# version 09
# baseline (speedup 1.0000x reference)
import math
from contextlib import ExitStack

import numpy as np
import concourse.bass as bass
import concourse.mybir as mybir
from concourse.bass_utils import run_bass_kernel_spmd

F32 = mybir.dt.float32
BF16 = mybir.dt.bfloat16
AX = mybir.AxisListType.X
OP = mybir.AluOpType
AF = mybir.ActivationFunctionType

T, D, TT = 2048, 1024, 16
H1, H2, HID = 16, 8, 512
NEG = -1e38
SC = 0.125
ROUTE_K = 1152.0
LOGK = float(np.log(np.float32(ROUTE_K)))
A1 = float(np.log(np.float32(ROUTE_K)) - np.log(np.float32(T)))
NIT = 50


class Sched:

    ENGS = ("sp", "gp", "dv", "ac", "pe")

    def __init__(self):
        self.q = {e: [] for e in self.ENGS}
        self.sem = {}
        self.semh = {}

    def op(self, eng, fn, inc=None, n=1, wait=()):
        semh = self.semh
        wait = tuple((s, v) for s, v in wait if v > 0)
        tgt = None
        if inc is not None:
            self.sem[inc] += n
            tgt = (semh[inc], n)

        def run(h, fn=fn, wait=wait, tgt=tgt):
            for s, v in wait:
                h.wait_ge(semh[s], v)
            inst = fn(h)
            if tgt is not None and inst is not None:
                inst.then_inc(*tgt)

        self.q[eng].append(run)
        return self.sem[inc] if inc is not None else None

    def waitq(self, eng, *sv):
        semh = self.semh
        sv = tuple(sv)

        def run(h, sv=sv):
            for s, v in sv:
                if v > 0:
                    h.wait_ge(semh[s], v)

        self.q[eng].append(run)

    def v(self, s):
        return self.sem[s]


def build_program(upto='full'):
    nc = bass.Bass("TRN2", target_bir_lowering=False, debug=False)
    sch = Sched()

    di = {}
    for name, shape in (
        ("x", [T, D]), ("wq_t", [D, D]), ("wk_t", [D, D]), ("wv_t", [D, D]),
        ("wo_t", [D, D]), ("wkvl", [D, D]), ("wkvh_t", [D, D]),
        ("wql", [D, HID]), ("wqh", [D, HID]), ("wol", [HID, D]),
        ("woh", [HID, D]), ("bq", [1, D]), ("bk", [1, D]), ("bv", [1, D]),
        ("bo", [1, D]), ("g_l", [1, D]), ("g_h", [1, D]), ("rq", [1, D]),
    ):
        di[name] = nc.dram_tensor(name, shape, F32, kind="ExternalInput").ap()
    out_t = nc.dram_tensor("out", [T, D], F32, kind="ExternalOutput").ap()
    dbg_t = (nc.dram_tensor("dbg", [2560, 1024], F32, kind="ExternalOutput")
             .ap() if upto == 'fulldbg' else None)

    es = ExitStack()
    Sb = lambda nm, sh, dt=F32: es.enter_context(nc.sbuf_tensor(nm, sh, dt))

    ARN = Sb("ARN", [128, 90384], BF16)
    XB = ARN[:, 0:16384]
    XNT = ARN[:, 16384:32768]
    CT1 = ARN[:, 16384:32768]
    CTXL = ARN[:, 16384:24576]
    CTXH = ARN[:, 24576:32768]
    QT = ARN[:, 32768:49152]
    RNT = ARN[:, 32768:49152]
    KT = ARN[:, 49152:65536]
    QLT = ARN[:, 49152:57344]
    KLT = ARN[:, 57344:65536]
    KVHT = ARN[:, 49152:57344]
    QHT = ARN[:, 57344:65536]
    VP = ARN[:, 65536:82176]
    VPL = ARN[:, 65536:73856]
    VPH = ARN[:, 73856:82176]
    SBT = ARN[:, 82176:86272]
    SBX = ARN[:, 86272:86784]
    PTE = ARN[:, 86784:88832]
    CND = ARN[:, 88832:89088]
    CN2 = ARN[:, 89088:89344]
    R8 = ARN[:, 89344:89360]
    CTE = ARN[:, 89360:90384]
    WTMP = ARN[:, 82176:90368]

    ARF = Sb("ARF", [128, 4096], F32)
    SBMB = Sb("SBMB", [128, 4096], BF16)
    NEGI = Sb("NEGI", [128, 128], BF16)
    SBM = ARF[:, 0:4096]
    OUTF = ARF[:, 0:2048]
    SPROD = ARF[:, 1024:2048]
    XNF = ARF[:, 2048:3072]
    R128 = ARF[:, 3072:4096]

    IDF = Sb("IDF", [128, 128], F32)
    BIASC = Sb("BIASC", [128, 32], F32)
    BVROW = ARF[0:1, 3072:4096]
    BOROW = None
    ONE1F = Sb("ONE1F", [1, 128], F32)
    ONE1B = Sb("ONE1B", [1, 128], BF16)
    ONECF = Sb("ONECF", [128, 1], F32)
    EPSC = Sb("EPSC", [128, 1], F32)
    TCOL = Sb("TCOL", [128, 2], F32)
    RZROWB = Sb("RZROWB", [1, 1024], BF16)
    RCPS = Sb("RCPS", [64, 1024], BF16)
    MCOL = Sb("MCOL", [128, 1], F32)
    RSTD = Sb("RSTD", [128, 1], F32)
    SSQ = Sb("SSQ", [128, 1], F32)
    RSN = Sb("RSN", [128, 1], F32)
    BN12 = Sb("BN12", [128, 12], F32)
    BN2 = Sb("BN2", [128, 2], F32)
    RNF = ARF[:, 0:1024]
    SQJ = ARN[:, 86784:87808]
    SCOL = Sb("SCOL", [128, 16], F32)
    UCOL = Sb("UCOL", [128, 16], F32)
    CCOL = Sb("CCOL", [128, 16], F32)
    SELCOL = Sb("SELCOL", [128, 16], F32)
    U16 = ARF[0:16, 3072:3200]
    SAT16 = ARF[0:16, 3200:3328]
    P16 = ARF[0:16, 3328:3456]
    SEL16 = ARF[0:16, 3456:3584]
    C16 = U16
    RS16 = ARF[0:16, 3584:3585]
    OFF16 = ARF[0:16, 3585:3586]
    OFR = ARF[0:1, 3586:3602]
    OFR2 = ARF[0:1, 3602:3618]
    AREP = ARF[:, 3618:3619]
    NAREP = ARF[:, 3619:3620]
    RSC = ARF[:, 3620:3636]
    EJ16 = Sb("EJ16", [128, 16], BF16)
    EPART = Sb("EPART", [128, 1], F32)
    LNS = Sb("LNS", [1, 1], F32)
    ANEW = Sb("ANEW", [1, 1], F32)
    NANEW = Sb("NANEW", [1, 1], F32)
    APLK = Sb("APLK", [1, 1], F32)

    PSF = es.enter_context(nc.psum_tensor("PSF", [128, 4096], F32))
    RZROW = PSF[64:65, 2048:3072]

    for s in ("sLs", "sLx0", "sLx1", "sLw", "sSTa", "sSTb",
              "sPE", "sDV", "sAC", "sGP", "sGW"):
        sch.semh[s] = es.enter_context(nc.semaphore(s))
        sch.sem[s] = 0

    W = lambda *sv: tuple(sv)
    sp = lambda fn, **k: sch.op("sp", fn, **k)
    gp = lambda fn, **k: sch.op("gp", fn, **k)
    dv = lambda fn, **k: sch.op("dv", fn, **k)
    ac = lambda fn, **k: sch.op("ac", fn, **k)
    pe = lambda fn, **k: sch.op("pe", fn, **k)

    def pe_mm(out_ap, lhs, rhs, start, stop, n_free, is_tr=False,
              inc=None, wait=()):
        nslc = (n_free + 511) // 512
        for j in range(nslc):
            c0, c1 = 512 * j, min(512 * (j + 1), n_free)
            pe(lambda p, c0=c0, c1=c1, out_ap=out_ap, lhs=lhs, rhs=rhs,
               start=start, stop=stop, is_tr=is_tr: p.matmul(
                   out_ap[:, c0:c1], lhs, rhs[:, c0:c1], start=start,
                   stop=stop, is_transpose=is_tr),
               inc=(inc if j == nslc - 1 else None),
               wait=(wait if j == 0 else ()))
        return sch.v(inc) if inc else None

    gp(lambda g: g.memset(IDF[:], 0.0), inc="sGP")
    gp(lambda g: g.affine_select(
        out=IDF[:], in_=IDF[:], compare_op=OP.not_equal, fill=1.0, base=0,
        pattern=[[-1, 128]], channel_multiplier=1), inc="sGP",
        wait=W(("sGP", sch.v("sGP"))))
    gp(lambda g: g.memset(NEGI[:], 0.0), inc="sGP")
    gp(lambda g: g.affine_select(
        out=NEGI[:], in_=NEGI[:], compare_op=OP.not_equal, fill=-1e30,
        base=0, pattern=[[-1, 128]], channel_multiplier=1), inc="sGP",
        wait=W(("sGP", sch.v("sGP"))))
    gp(lambda g: g.memset(ONE1F[:], 1.0), inc="sGP")
    gp(lambda g: g.memset(ONE1B[:], 1.0), inc="sGP")
    gp(lambda g: g.memset(ONECF[:], 1.0), inc="sGP")
    gp(lambda g: g.memset(EPSC[:], 1e-5), inc="sGP")
    gp(lambda g: g.memset(VP[:], 1.0), inc="sGP")
    gp_init = sch.v("sGP")

    BQROW = ARF[32:33, 3072:4096]
    BKROW = ARF[64:65, 3072:4096]
    BQROWB = ARN[0:1, 67584:68608]
    BKROWB = ARN[0:1, 68608:69632]
    ONESROW = VP[0:1, 0:2048]
    for rowt, name in ((BQROW, "bq"), (BKROW, "bk")):
        sp(lambda s, rowt=rowt, name=name: s.dma_start(
            out=rowt, in_=di[name]), inc="sLs", n=16)
    for name, cs in (("g_l", 16), ("g_h", 24)):
        sp(lambda s, name=name, cs=cs: s.dma_start(
            out=BIASC[:, cs:cs + 8],
            in_=di[name].rearrange("o (c p) -> (o p) c", p=128),
            allow_slow_non_contiguous=True), inc="sLs", n=16)
    sp(lambda s: s.dma_start(out=BVROW, in_=di["bv"]), inc="sLs", n=16)
    v_small = sch.v("sLs")
    dv(lambda v: v.tensor_copy(out=BQROWB, in_=BQROW), inc="sDV",
       wait=W(("sLs", v_small), ("sGP", gp_init)))
    dv(lambda v: v.tensor_copy(out=BKROWB, in_=BKROW), inc="sDV",
       wait=W(("sLs", v_small), ("sGP", gp_init)))

    ln_done = {}
    xnf_done = {}
    for t in range(TT):
        slx = "sLx0"
        sp(lambda s, t=t: s.dma_start(
            out=XNF[:], in_=di["x"][128 * t:128 * (t + 1), :]),
           inc=slx, n=16, wait=W(("sDV", xnf_done.get(t - 1, 0))))
        vld = sch.v(slx)
        dv(lambda v, t=t: v.tensor_copy(
            out=XB[:, D * t:D * (t + 1)], in_=XNF[:]),
           inc="sDV", wait=W((slx, vld)))
        dv(lambda v: v.bn_stats(out=BN12[:, 0:6],
                                in_=XNF[:, 0:512]), inc="sDV")
        dv(lambda v: v.bn_stats(out=BN12[:, 6:12],
                                in_=XNF[:, 512:1024]),
           inc="sDV")
        dv(lambda v: v.bn_aggr(out=BN2[:], in_=BN12[:]), inc="sDV",
           wait=W(("sDV", sch.v("sDV"))))
        dv(lambda v: v.tensor_copy(out=MCOL[:], in_=BN2[:, 0:1]), inc="sDV",
           wait=W(("sDV", sch.v("sDV"))))
        v_bn = sch.v("sDV")
        ac(lambda a: a.activation(out=SSQ[:], in_=BN2[:, 1:2], func=AF.Sqrt,
                                  bias=EPSC[:], scale=1.0), inc="sAC",
           wait=W(("sDV", v_bn), ("sGP", gp_init)))
        dv(lambda v: v.reciprocal(out=RSTD[:], in_=SSQ[:]), inc="sDV",
           wait=W(("sAC", sch.v("sAC"))))
        dv(lambda v: v.tensor_scalar(
            out=XNF[:], in0=XNF[:], scalar1=MCOL[:],
            scalar2=RSTD[:], op0=OP.subtract, op1=OP.mult), inc="sDV",
           wait=W(("sDV", sch.v("sDV"))))
        v_xnf = sch.v("sDV")
        for c in range(8):
            pe(lambda p, c=c: p.transpose(
                PSF[:, 3072 + 128 * c:3200 + 128 * c],
                XNF[:, 128 * c:128 * (c + 1)], IDF[:]),
               inc=("sPE" if c == 7 else None),
               wait=(W(("sDV", v_xnf), ("sGP", 2),
                       ("sDV", ln_done.get(t - 1, 0))) if c == 0 else ()))
        v_tr = sch.v("sPE")
        dv(lambda v, t=t: v.tensor_copy(
            out=XNT.rearrange("p (c n) -> p c n", n=2048)
                [:, :, 128 * t:128 * (t + 1)],
            in_=PSF[:, 3072:4096].rearrange("p (c n) -> p c n", n=128)),
           inc="sDV", wait=W(("sPE", v_tr)))
        ln_done[t] = sch.v("sDV")
        xnf_done[t] = sch.v("sDV")

    def load_weight_direct(name, ncols, rows=1024, woff=0, wait_pe=0,
                           dv_guard=0, ac_guard=0, gcol=None):
        vs = []
        for c in range(rows // 128):
            gp(lambda g, c=c, name=name, ncols=ncols, woff=woff: g.dma_start(
                out=WTMP[:, woff + ncols * c:woff + ncols * (c + 1)],
                in_=di[name][128 * c:128 * (c + 1), 0:ncols]),
               inc="sGW", n=16,
               wait=(W(("sGW", sch.v("sGW")), ("sPE", wait_pe),
                       ("sDV", dv_guard), ("sAC", ac_guard)) if c == 0 else
                     W(("sGW", sch.v("sGW")))))
            vs.append(sch.v("sGW"))
        if gcol is None:
            return ("sGW", sch.v("sGW"))
        for c in range(rows // 128):
            dv(lambda v, c=c, ncols=ncols, woff=woff, gcol=gcol:
               v.tensor_scalar(
                   out=WTMP[:, woff + ncols * c:woff + ncols * (c + 1)],
                   in0=WTMP[:, woff + ncols * c:woff + ncols * (c + 1)],
                   scalar1=BIASC[:, gcol + c:gcol + c + 1],
                   scalar2=None, op0=OP.mult),
               inc="sDV", wait=W(("sGW", vs[c]), ("sLs", v_small),
                                 ("sDV", sch.v("sDV"))))
        return ("sDV", sch.v("sDV"))

    def proj_T(dst, bias_row, src_T, nf=8, wcols=1024, wwait=None):
        if wwait is None:
            wwait = ("sDV", sch.v("sDV"))
        for fc in range(nf):
            for dc in range(8):
                pe_mm(PSF[:, 0:2048],
                      WTMP[:, wcols * dc + 128 * fc:
                           wcols * dc + 128 * (fc + 1)],
                      src_T[:, 2048 * dc:2048 * (dc + 1)],
                      start=(dc == 0), stop=(dc == 7 and bias_row is None),
                      n_free=2048,
                      inc=("sPE" if (dc == 7 and bias_row is None) else None),
                      wait=(W(wwait, ("sDV", sch.v("sDV")),
                              ("sAC", sch.v("sAC")),
                              ("sLs", v_small)) if dc == 0 else ()))
            if bias_row is not None:
                pe_mm(PSF[:, 0:2048], bias_row[0:1, 128 * fc:128 * (fc + 1)],
                      ONESROW, start=False, stop=True, n_free=2048,
                      inc="sPE", wait=W(("sDV", sch.v("sDV")),
                                        ("sGP", gp_init)))
            vpe = sch.v("sPE")
            ac(lambda a, fc=fc, dst=dst: a.activation(
                out=dst[:, 2048 * fc:2048 * (fc + 1)], in_=PSF[:, 0:2048],
                func=AF.Copy, scale=1.0), inc="sAC", wait=W(("sPE", vpe)))
        return sch.v("sAC")

    v_w = load_weight_direct("wq_t", 1024)
    proj_T(QT, BQROWB, XNT, wwait=v_w)
    v_w = load_weight_direct("wk_t", 1024, wait_pe=sch.v("sPE"))
    proj_T(KT, BKROWB, XNT, wwait=v_w)
    v_w = load_weight_direct("wv_t", 1024, wait_pe=sch.v("sPE"))
    for t in range(TT):
        for dc in range(8):
            pe_mm(PSF[:, 0:1024],
                  XNT[:, 2048 * dc + 128 * t:2048 * dc + 128 * (t + 1)],
                  WTMP[:, 1024 * dc:1024 * (dc + 1)],
                  start=(dc == 0), stop=False, n_free=1024,
                  wait=(W(v_w, ("sDV", sch.v("sDV")),
                          ("sAC", sch.v("sAC")),
                          ("sLs", v_small)) if dc == 0 else ()))
        pe_mm(PSF[:, 0:1024], ONE1F[:], BVROW[:], start=False, stop=True,
              n_free=1024, inc="sPE")
        vpe = sch.v("sPE")
        ac(lambda a, t=t: a.activation(
            out=VP[:, 1040 * t:1040 * (t + 1)]
                .rearrange("p (h c) -> p h c", c=65)[:, :, 0:64],
            in_=PSF[:, 0:1024], func=AF.Copy, scale=1.0),
           inc="sAC", wait=W(("sPE", vpe), ("sGP", gp_init)))
    v_vp = sch.v("sAC")

    evac_done = {}
    pen_done = {}
    qk_done = {}
    sel_tcol = {}
    pen0_relu = {}
    st_last = {}
    exp_pair = {}
    av_kt = {}
    div_done = 0

    def b1_qh_kh(ei):
        h = ei // 8
        ch, rh = h // 2, (h % 2) * 64
        return (QT[rh:rh + 64, 2048 * ch:2048 * (ch + 1)],
                KT[rh:rh + 64, 2048 * ch:2048 * (ch + 1)])

    def emit_qksel(ei, qi):
        Qh, Kh = b1_qh_kh(ei)
        qt = 2 * (ei % 8) + qi
        ev = evac_done.get((ei, 0), 0) if qi == 1 else \
            evac_done.get((ei - 1, 1), 0)
        for j in range(4):
            pe(lambda p, qt=qt, j=j, Qh=Qh, Kh=Kh: p.matmul(
                PSF[:, 512 * j:512 * (j + 1)],
                Qh[:, 128 * qt:128 * (qt + 1)],
                Kh[:, 512 * j:512 * (j + 1)],
                start=True, stop=True),
               inc=("sPE" if j == 3 else None),
               wait=(W(("sAC", max(ev, v_vp))) if j == 0 else ()))
        qk_done[(ei, qi)] = sch.v("sPE")

    def emit_evac(ei, qi):
        sb = SBT[:, 2048 * qi:2048 * (qi + 1)]
        psem, pval = pen_done.get((ei - 1, qi), ("sDV", 0))
        ac(lambda a, sb=sb: a.activation(out=sb, in_=PSF[:, 0:2048],
                                         func=AF.Copy, scale=1.0),
           inc="sAC", wait=W(("sPE", qk_done[(ei, qi)]), (psem, pval)))
        evac_done[(ei, qi)] = sch.v("sAC")

    def emit_sel(ei, qi):
        sb = SBT[:, 2048 * qi:2048 * (qi + 1)]
        cnd, cn2 = CND[:, 0:256], CN2[:, 0:256]
        r8s = R8[:, 8 * qi:8 * (qi + 1)]
        for c in range(16):
            dv(lambda v, c=c, sb=sb, cnd=cnd: v.max(
                out=cnd[:, 8 * c:8 * (c + 1)],
                in_=sb[:, 128 * c:128 * (c + 1)]),
               inc="sDV",
               wait=(W(("sAC", evac_done[(ei, qi)]), ("sDV", sch.v("sDV")))
                     if c == 0 else ()))
        v_r1 = sch.v("sDV")
        vgrp = v_r1
        for grp in range(4):
            for c in range(4 * grp, 4 * grp + 4):
                dv(lambda v, c=c, sb=sb, cnd=cnd: v.match_replace(
                    out=SBX[:, 128 * (c % 4):128 * (c % 4) + 128],
                    in_to_replace=cnd[:, 8 * c:8 * (c + 1)],
                    in_values=sb[:, 128 * c:128 * (c + 1)],
                    imm_value=NEG),
                   inc="sDV",
                   wait=(W(("sDV", vgrp)) if c % 4 == 0 else ()))
            v_mr = sch.v("sDV")
            for c in range(4 * grp, 4 * grp + 4):
                dv(lambda v, c=c, cnd=cnd: v.max(
                    out=cnd[:, 128 + 8 * c:128 + 8 * (c + 1)],
                    in_=SBX[:, 128 * (c % 4):128 * (c % 4) + 128]),
                   inc="sDV",
                   wait=(W(("sDV", v_mr)) if c % 4 == 0 else ()))
            vgrp = sch.v("sDV")
        cur, other = cnd, cn2
        for rr in range(5):
            dv(lambda v, cur=cur, r8s=r8s: v.max(out=r8s, in_=cur),
               inc="sDV", wait=W(("sDV", sch.v("sDV"))))
            if rr < 4:
                dv(lambda v, cur=cur, other=other, r8s=r8s:
                   v.match_replace(out=other, in_to_replace=r8s,
                                   in_values=cur, imm_value=NEG),
                   inc="sDV", wait=W(("sDV", sch.v("sDV"))))
                cur, other = other, cur
        dv(lambda v, qi=qi, r8s=r8s: v.tensor_copy(
            out=TCOL[:, qi:qi + 1], in_=r8s[:, 5:6]),
           inc="sDV", wait=W(("sDV", sch.v("sDV"))))
        sel_tcol[(ei, qi)] = sch.v("sDV")

    def emit_pen(ei, qi):
        sb = SBT[:, 2048 * qi:2048 * (qi + 1)]
        sm = SBMB[:, 2048 * qi:2048 * (qi + 1)]
        dv(lambda v, sb=sb, sm=sm, qi=qi: v.tensor_scalar(
            out=sm, in0=sb, scalar1=TCOL[:, qi:qi + 1], scalar2=None,
            op0=OP.is_lt),
           inc="sDV", wait=W(("sDV", sch.v("sDV")),
                             ("sPE", st_last.get(ei - 1, 0))))
        pen_done[(ei, qi)] = ("sDV", sch.v("sDV"))

    def emit_av(bi, g, h, ctx_c0):
        for sub in range(2):
            kt = 2 * g + sub
            w = (W(("sAC", exp_pair[(bi, g)]), ("sDV", div_done))
                 if kt == 0 else
                 (W(("sAC", exp_pair[(bi, g)])) if sub == 0 else ()))
            pe(lambda p, kt=kt, h=h, ctx_c0=ctx_c0: p.matmul(
                PSF[0:65, ctx_c0:ctx_c0 + 256],
                VP[:, 1040 * kt + 65 * h:1040 * kt + 65 * (h + 1)],
                PTE[:, 256 * (kt % 8):256 * (kt % 8) + 256],
                start=(kt == 0), stop=(kt == 15)),
               inc="sPE", wait=w)
            av_kt[(bi, kt)] = sch.v("sPE")

    pend_div = None
    rcps_gate = 0

    def emit_div2(pd):
        dbi, v_rzb = pd
        h, e8 = dbi // 8, dbi % 8
        half = e8 // 4
        pe_mm(PSF[0:64, 2048:3072], ONE1B[0:1, 0:64], RZROWB[:],
              start=True, stop=True, n_free=1024, inc="sPE",
              wait=W(("sAC", v_rzb)))
        v_rb = sch.v("sPE")
        ac(lambda a: a.activation(out=RCPS[:],
                                  in_=PSF[0:64, 2048:3072],
                                  func=AF.Copy, scale=1.0),
           inc="sAC", wait=W(("sPE", v_rb), ("sAC", sch.v("sAC"))))
        v_rc = sch.v("sAC")
        dv(lambda v, h=h, half=half: v.tensor_tensor(
            out=CT1[:, 2048 * (h // 2) + 1024 * half:
                    2048 * (h // 2) + 1024 * (half + 1)]
                [(h % 2) * 64:(h % 2) * 64 + 64, :],
            in0=CTE[0:64, :], in1=RCPS[:], op=OP.mult),
           inc="sDV", wait=W(("sAC", v_rc), ("sDV", sch.v("sDV"))))
        return v_rc, sch.v("sDV")

    for step in range(130):
        ei, bi = step, step - 1
        if ei < 128:
            emit_qksel(ei, 0)
        if pend_div is not None:
            rcps_gate, div_done = emit_div2(pend_div)
            pend_div = None
        if ei < 128:
            emit_evac(ei, 0)
        if bi >= 0 and bi < 128:
            Qh, Kh = b1_qh_kh(bi)
            h, e8 = bi // 8, bi % 8
            q0 = 256 * e8
            ctx_c0 = 3072 + 256 * (e8 % 4)
            st_v = {}
            psem0, pval0 = pen_done[(bi, 0)]
            _s1, pval1 = pen_done[(bi, 1)]
            for g in range(8):
                wbase = 2048 + 512 * (g % 2)
                wguard = (exp_pair[(bi, g - 2)] if g >= 2 else
                          exp_pair.get((bi - 1, g + 6), 0))
                for sub in range(2):
                    kt = 2 * g + sub
                    po = wbase + 256 * sub
                    if g == 0 and sub == 0:
                        pw = W((psem0, pval0),
                               ("sDV", max(pval1, div_done)),
                               ("sAC", max(wguard, rcps_gate)))
                    elif sub == 0:
                        pw = W(("sAC", wguard))
                    else:
                        pw = ()
                    pe(lambda p, kt=kt, po=po, q0=q0, Qh=Qh, Kh=Kh:
                       p.matmul(PSF[:, po:po + 256],
                                Kh[:, 128 * kt:128 * (kt + 1)],
                                Qh[:, q0:q0 + 256],
                                start=True, stop=False),
                       wait=pw)
                    for qi2 in range(2):
                        pe(lambda p, kt=kt, po=po, qi2=qi2: p.matmul(
                            PSF[:, po + 128 * qi2:po + 128 * (qi2 + 1)],
                            SBMB[:, 2048 * qi2 + 128 * kt:
                                 2048 * qi2 + 128 * (kt + 1)],
                            NEGI[:], start=False, stop=(qi2 == 1)),
                           inc=("sPE" if qi2 == 1 else None))
                    st_v[kt] = sch.v("sPE")
                pguard = (av_kt.get((bi, 2 * g - 7), 0) if g >= 4 else
                          av_kt.get((bi - 1, 2 * g + 9), 0))
                ac(lambda a, g=g, wbase=wbase: a.activation(
                    out=PTE[:, 512 * (g % 4):512 * (g % 4) + 512],
                    in_=PSF[:, wbase:wbase + 512], func=AF.Exp, scale=SC),
                   inc="sAC", wait=W(("sPE", max(st_v[2 * g + 1], pguard))))
                exp_pair[(bi, g)] = sch.v("sAC")
                if g >= 1:
                    emit_av(bi, g - 1, h, ctx_c0)
                if g == 2 and ei < 128:
                    emit_qksel(ei, 1)
                    emit_evac(ei, 1)
            emit_av(bi, 7, h, ctx_c0)
            st_last[bi] = st_v[15]
        if ei < 128 and (bi < 0 or bi >= 128):
            emit_qksel(ei, 1)
            emit_evac(ei, 1)
        if ei < 128:
            emit_sel(ei, 0)
            emit_sel(ei, 1)
            emit_pen(ei, 0)
            emit_pen(ei, 1)
        if bi >= 0 and bi < 128 and bi % 4 == 3:
            v_ava = av_kt[(bi, 15)]
            ac(lambda a: a.activation(out=CTE[0:65, :],
                                      in_=PSF[0:65, 3072:4096],
                                      func=AF.Copy, scale=1.0),
               inc="sAC", wait=W(("sPE", v_ava), ("sDV", div_done),
                                 ("sAC", sch.v("sAC"))))
            cte_v = sch.v("sAC")
            dv(lambda v: v.reciprocal(out=RZROW[:], in_=CTE[64:65, :]),
               inc="sDV", wait=W(("sAC", cte_v), ("sDV", sch.v("sDV"))))
            v_rz0 = sch.v("sDV")
            ac(lambda a: a.activation(out=RZROWB[:], in_=RZROW[:],
                                      func=AF.Copy, scale=1.0),
               inc="sAC", wait=W(("sDV", v_rz0), ("sAC", sch.v("sAC"))))
            pend_div = (bi, sch.v("sAC"))

    v_wo = load_weight_direct("wo_t", 1024, wait_pe=sch.v("sPE"),
                              dv_guard=sch.v("sDV"))
    BOROW = XNF[0:1, :]
    sp(lambda s: s.dma_start(out=BOROW, in_=di["bo"]), inc="sLs", n=16,
       wait=W(("sDV", sch.v("sDV"))))
    v_bo = sch.v("sLs")
    for t in range(TT):
        po = 1024 * (t % 2)
        for dc in range(8):
            pe_mm(PSF[:, po:po + 1024],
                  CT1[:, 2048 * dc + 128 * t:2048 * dc + 128 * (t + 1)],
                  WTMP[:, 1024 * dc:1024 * (dc + 1)],
                  start=(dc == 0), stop=False, n_free=1024,
                  wait=(W(v_wo, ("sDV", max(div_done, sch.v("sDV"))))
                        if dc == 0 else ()))
        pe_mm(PSF[:, po:po + 1024], ONE1F[:], BOROW, start=False,
              stop=True, n_free=1024, inc="sPE",
              wait=W(("sLs", v_bo)))
        v_pe = sch.v("sPE")
        dv(lambda v, t=t, po=po: v.scalar_tensor_tensor(
            out=XB[:, D * t:D * (t + 1)], in0=PSF[:, po:po + 1024],
            scalar=1.0, in1=XB[:, D * t:D * (t + 1)], op0=OP.mult,
            op1=OP.add), inc="sDV", wait=W(("sPE", v_pe),
                                           ("sDV", sch.v("sDV"))))
    v_x1 = sch.v("sDV")

    RQROW = XNF[0:1, :]
    sp(lambda s: s.dma_start(out=RQROW, in_=di["rq"]), inc="sLs", n=16,
       wait=W(("sDV", v_x1)))
    v_rq = sch.v("sLs")
    pe_mm(PSF[:, 2048:3072], ONE1F[:], RQROW, start=True, stop=True,
          n_free=1024, inc="sPE", wait=W(("sDV", v_x1), ("sLs", v_rq)))
    dv(lambda v: v.tensor_copy(out=R128[:], in_=PSF[:, 2048:3072]),
       inc="sDV", wait=W(("sPE", sch.v("sPE"))))
    v_r128 = sch.v("sDV")

    for t in range(TT):
        xt = XB[:, D * t:D * (t + 1)]
        dv(lambda v, xt=xt: v.bn_stats(out=BN12[:, 0:6], in_=xt[:, 0:512]),
           inc="sDV", wait=W(("sDV", max(v_x1, v_r128))))
        dv(lambda v, xt=xt: v.bn_stats(out=BN12[:, 6:12],
                                       in_=xt[:, 512:1024]), inc="sDV")
        dv(lambda v: v.bn_aggr(out=BN2[:], in_=BN12[:]), inc="sDV",
           wait=W(("sDV", sch.v("sDV"))))
        dv(lambda v: v.tensor_copy(out=MCOL[:], in_=BN2[:, 0:1]), inc="sDV",
           wait=W(("sDV", sch.v("sDV"))))
        v_bn = sch.v("sDV")
        ac(lambda a: a.activation(out=SSQ[:], in_=BN2[:, 1:2], func=AF.Sqrt,
                                  bias=EPSC[:], scale=1.0), inc="sAC",
           wait=W(("sDV", v_bn)))
        dv(lambda v: v.reciprocal(out=RSTD[:], in_=SSQ[:]), inc="sDV",
           wait=W(("sAC", sch.v("sAC"))))
        dv(lambda v, xt=xt: v.tensor_scalar(
            out=XNF[:], in0=xt, scalar1=MCOL[:], scalar2=RSTD[:],
            op0=OP.subtract, op1=OP.mult), inc="sDV",
           wait=W(("sDV", sch.v("sDV"))))
        v_xnf = sch.v("sDV")
        dv(lambda v, t=t: v.scalar_tensor_tensor(
            out=SPROD[:], in0=XNF[:], scalar=1.0, in1=R128[:],
            op0=OP.mult, op1=OP.mult),
           inc="sDV", wait=W(("sDV", max(v_r128, v_xnf))))
        dv(lambda v, t=t: v.reduce_sum(out=SCOL[:, t:t + 1], in_=SPROD[:],
                                       axis=AX),
           inc="sDV", wait=W(("sDV", sch.v("sDV"))))
        ac(lambda a: a.activation(out=SQJ[:], in_=XNF[:], func=AF.Square,
                                  scale=1.0, accum_out=SSQ[:]), inc="sAC",
           wait=W(("sDV", sch.v("sDV"))))
        ac(lambda a: a.activation(out=MCOL[:], in_=SSQ[:], func=AF.Sqrt,
                                  scale=1.0 / D), inc="sAC",
           wait=W(("sAC", sch.v("sAC"))))
        dv(lambda v: v.reciprocal(out=RSN[:], in_=MCOL[:]), inc="sDV",
           wait=W(("sAC", sch.v("sAC"))))
        v_rsn = sch.v("sDV")
        dv(lambda v: v.tensor_scalar(out=RNF[:], in0=XNF[:], scalar1=RSN[:],
                                     scalar2=None, op0=OP.mult),
           inc="sDV", wait=W(("sDV", v_rsn)))
        v_rn = sch.v("sDV")
        for c in range(8):
            pe(lambda p, c=c: p.transpose(
                PSF[:, 3072 + 128 * c:3200 + 128 * c],
                RNF[:, 128 * c:128 * (c + 1)], IDF[:]),
               inc=("sPE" if c == 7 else None),
               wait=(W(("sDV", v_rn)) if c == 0 else ()))
        v_tr = sch.v("sPE")
        dv(lambda v, t=t: v.tensor_copy(
            out=RNT.rearrange("p (c n) -> p c n", n=2048)
                [:, :, 128 * t:128 * (t + 1)],
            in_=PSF[:, 3072:4096].rearrange("p (c n) -> p c n", n=128)),
           inc="sDV", wait=W(("sPE", v_tr)))
    v_ln2 = sch.v("sDV")


    v_m = dv(lambda v: v.memset(ANEW[:], A1), inc="sDV")
    dv(lambda v: v.tensor_scalar(out=NANEW[:], in0=ANEW[:], scalar1=-1.0,
                                 scalar2=None, op0=OP.mult), inc="sDV",
       wait=W(("sDV", v_m)))
    dv(lambda v: v.tensor_scalar(out=APLK[:], in0=ANEW[:], scalar1=LOGK,
                                 scalar2=None, op0=OP.add), inc="sDV",
       wait=W(("sDV", v_m)))
    v_a = sch.v("sDV")
    for it in range(NIT - 1):
        pe(lambda p: p.matmul(PSF[:, 2048:2049], ONE1F[:], NANEW[:],
                              start=True, stop=True), inc="sPE",
           wait=W(("sDV", v_a)))
        v_brc = sch.v("sPE")
        ac(lambda a: a.activation(out=NAREP[:], in_=PSF[:, 2048:2049],
                                  func=AF.Copy, scale=1.0), inc="sAC",
           wait=W(("sPE", v_brc)))
        v_nar = sch.v("sAC")
        ac(lambda a: a.activation(out=RSC[:], in_=SCOL[:], func=AF.Relu,
                                  bias=NAREP[:], scale=-1.0), inc="sAC",
           wait=W(("sAC", v_nar), ("sDV", v_ln2)))
        ac(lambda a: a.activation(out=EJ16[:], in_=RSC[:], func=AF.Exp,
                                  scale=-1.0, accum_out=EPART[:]), inc="sAC",
           wait=W(("sAC", sch.v("sAC"))))
        v_ep = sch.v("sAC")
        pe(lambda p: p.matmul(PSF[0:1, 2049:2050], EPART[:], ONECF[:],
                              start=True, stop=True), inc="sPE",
           wait=W(("sAC", v_ep)))
        v_sum = sch.v("sPE")
        ac(lambda a: a.activation(out=LNS[:], in_=PSF[0:1, 2049:2050],
                                  func=AF.Ln, scale=1.0), inc="sAC",
           wait=W(("sPE", v_sum)))
        v_lns = sch.v("sAC")
        dv(lambda v: v.scalar_tensor_tensor(
            out=ANEW[:], in0=LNS[:], scalar=-1.0, in1=APLK[:],
            op0=OP.mult, op1=OP.add), inc="sDV", wait=W(("sAC", v_lns)))
        v_an = sch.v("sDV")
        dv(lambda v: v.tensor_scalar(out=APLK[:], in0=ANEW[:], scalar1=LOGK,
                                     scalar2=None, op0=OP.add), inc="sDV",
           wait=W(("sDV", v_an)))
        dv(lambda v: v.tensor_scalar(out=NANEW[:], in0=ANEW[:], scalar1=-1.0,
                                     scalar2=None, op0=OP.mult), inc="sDV",
           wait=W(("sDV", v_an)))
        v_a = sch.v("sDV")
    pe(lambda p: p.matmul(PSF[:, 2048:2049], ONE1F[:], ANEW[:],
                          start=True, stop=True), inc="sPE",
       wait=W(("sDV", v_a)))
    ac(lambda a: a.activation(out=AREP[:], in_=PSF[:, 2048:2049],
                              func=AF.Copy, scale=1.0), inc="sAC",
       wait=W(("sPE", sch.v("sPE"))))
    dv(lambda v: v.tensor_scalar(out=UCOL[:], in0=SCOL[:], scalar1=AREP[:],
                                 scalar2=0.0, op0=OP.add, op1=OP.min),
       inc="sDV", wait=W(("sAC", sch.v("sAC"))))
    v_ucol = sch.v("sDV")
    pe(lambda p: p.transpose(PSF[0:16, 2048:2176], UCOL[:], IDF[:]),
       inc="sPE", wait=W(("sDV", v_ucol)))
    dv(lambda v: v.tensor_copy(out=U16[:], in_=PSF[0:16, 2048:2176]),
       inc="sDV", wait=W(("sPE", sch.v("sPE"))))
    v_u16 = sch.v("sDV")
    UROW = ARF[0:1, 0:2048]
    sp(lambda s: s.dma_start(out=UROW, in_=U16[:]), inc="sLs", n=16,
       wait=W(("sDV", v_u16)))
    v_urow = sch.v("sLs")
    v_amat = pe_mm(PSF[:, 0:2048], ONE1F[:], UROW, start=True, stop=True,
                   n_free=2048, inc="sPE", wait=W(("sLs", v_urow)))
    for tb in range(16):
        dv(lambda v, tb=tb: v.tensor_scalar(
            out=OUTF[:, 0:2048], in0=PSF[:, 0:2048],
            scalar1=UCOL[:, tb:tb + 1], scalar2=None, op0=OP.is_gt),
           inc="sDV", wait=(W(("sPE", v_amat)) if tb == 0 else
                            W(("sDV", sch.v("sDV")))))
        dv(lambda v, tb=tb: v.reduce_sum(out=CCOL[:, tb:tb + 1],
                                         in_=OUTF[:, 0:2048], axis=AX),
           inc="sDV", wait=W(("sDV", sch.v("sDV"))))
    v_ccol = sch.v("sDV")
    dv(lambda v: v.tensor_scalar(out=SAT16[:], in0=U16[:], scalar1=0.0,
                                 scalar2=None, op0=OP.is_ge), inc="sDV",
       wait=W(("sDV", v_ccol)))
    dv(lambda v: v.tensor_tensor_scan(out=P16[:], data0=SAT16[:],
                                      data1=SAT16[:], initial=0.0,
                                      op0=OP.add, op1=OP.bypass), inc="sDV",
       wait=W(("sDV", sch.v("sDV"))))
    dv(lambda v: v.reduce_sum(out=RS16[:], in_=SAT16[:], axis=AX), inc="sDV",
       wait=W(("sDV", sch.v("sDV"))))
    v_rs = sch.v("sDV")
    pe(lambda p: p.transpose(PSF[0:1, 2048:2064], RS16[:], IDF[0:16, 0:16]),
       inc="sPE", wait=W(("sDV", v_rs)))
    dv(lambda v: v.tensor_copy(out=OFR[:], in_=PSF[0:1, 2048:2064]),
       inc="sDV", wait=W(("sPE", sch.v("sPE"))))
    dv(lambda v: v.tensor_tensor_scan(out=OFR2[:], data0=OFR[:],
                                      data1=OFR[:], initial=0.0,
                                      op0=OP.add, op1=OP.bypass), inc="sDV",
       wait=W(("sDV", sch.v("sDV"))))
    dv(lambda v: v.memset(OFR[:, 0:1], 0.0), inc="sDV",
       wait=W(("sDV", sch.v("sDV"))))
    dv(lambda v: v.tensor_copy(out=OFR[:, 1:16], in_=OFR2[:, 0:15]),
       inc="sDV", wait=W(("sDV", sch.v("sDV"))))
    v_ofr = sch.v("sDV")
    pe(lambda p: p.transpose(PSF[0:16, 2064:2065], OFR[:], IDF[0:1, 0:1]),
       inc="sPE", wait=W(("sDV", v_ofr)))
    dv(lambda v: v.tensor_copy(out=OFF16[:], in_=PSF[0:16, 2064:2065]),
       inc="sDV", wait=W(("sPE", sch.v("sPE"))))
    v_offev = sch.v("sDV")
    dv(lambda v: v.tensor_sub(out=P16[:], in0=P16[:], in1=SAT16[:]),
       inc="sDV", wait=W(("sDV", sch.v("sDV"))))
    dv(lambda v: v.tensor_scalar(out=P16[:], in0=P16[:], scalar1=OFF16[:],
                                 scalar2=None, op0=OP.add), inc="sDV",
       wait=W(("sDV", sch.v("sDV"))))
    dv(lambda v: v.tensor_mul(out=P16[:], in0=P16[:], in1=SAT16[:]),
       inc="sDV", wait=W(("sDV", sch.v("sDV"))))
    v_p16 = sch.v("sDV")
    pe(lambda p: p.transpose(PSF[0:16, 2176:2304], CCOL[:], IDF[:]),
       inc="sPE", wait=W(("sDV", max(v_ccol, v_offev))))
    dv(lambda v: v.tensor_copy(out=C16[:], in_=PSF[0:16, 2176:2304]),
       inc="sDV", wait=W(("sPE", sch.v("sPE"))))
    dv(lambda v: v.tensor_add(out=R16[:] if False else SEL16[:], in0=C16[:],
                              in1=P16[:]), inc="sDV",
       wait=W(("sDV", sch.v("sDV"))))
    dv(lambda v: v.tensor_scalar(out=SEL16[:], in0=SEL16[:], scalar1=1023.5,
                                 scalar2=None, op0=OP.is_lt), inc="sDV",
       wait=W(("sDV", sch.v("sDV"))))
    v_sel16 = sch.v("sDV")
    pe(lambda p: p.transpose(PSF[:, 2304:2320], SEL16[:], IDF[0:16, 0:16]),
       inc="sPE", wait=W(("sDV", v_sel16)))
    dv(lambda v: v.tensor_copy(out=SELCOL[:], in_=PSF[:, 2304:2320]),
       inc="sDV", wait=W(("sPE", sch.v("sPE"))))
    v_sel = sch.v("sDV")

    load_weight_direct("wql", 512, gcol=16, wait_pe=sch.v("sPE"),
                       dv_guard=sch.v("sDV"), ac_guard=sch.v("sAC"))
    proj_T(QLT, None, RNT, nf=4, wcols=512)
    load_weight_direct("wkvl", 1024, gcol=16, wait_pe=sch.v("sPE"),
                       dv_guard=sch.v("sDV"), ac_guard=sch.v("sAC"))
    vw = sch.v("sDV")
    for fc in range(4):
        for dc in range(8):
            pe_mm(PSF[:, 0:2048],
                  WTMP[:, 1024 * dc + 128 * fc:1024 * dc + 128 * (fc + 1)],
                  RNT[:, 2048 * dc:2048 * (dc + 1)],
                  start=(dc == 0), stop=(dc == 7), n_free=2048,
                  inc=("sPE" if dc == 7 else None),
                  wait=(W(("sDV", vw), ("sAC", sch.v("sAC")))
                        if dc == 0 else ()))
        v_pe = sch.v("sPE")
        ac(lambda a, fc=fc: a.activation(
            out=KLT[:, 2048 * fc:2048 * (fc + 1)], in_=PSF[:, 0:2048],
            func=AF.Copy, scale=1.0), inc="sAC", wait=W(("sPE", v_pe)))
    for t in range(TT):
        for dc in range(8):
            pe_mm(PSF[:, 0:512],
                  RNT[:, 2048 * dc + 128 * t:2048 * dc + 128 * (t + 1)],
                  WTMP[:, 1024 * dc + 512:1024 * (dc + 1)],
                  start=(dc == 0), stop=(dc == 7), n_free=512,
                  inc=("sPE" if dc == 7 else None),
                  wait=(W(("sAC", sch.v("sAC")), ("sGP", gp_init))
                        if dc == 0 else ()))
        v_pe = sch.v("sPE")
        ac(lambda a, t=t: a.activation(
            out=VPL[:, 520 * t:520 * (t + 1)]
                .rearrange("p (h c) -> p h c", c=65)[:, :, 0:64],
            in_=PSF[:, 0:512], func=AF.Copy, scale=1.0),
           inc="sAC", wait=W(("sPE", v_pe)))
    v_vpl = sch.v("sAC")

    lt_div = sch.v("sDV")
    cte_prev = sch.v("sAC")
    for h in range(H2):
        chl, rhl = h // 2, (h % 2) * 64
        QLh = QLT[rhl:rhl + 64, 2048 * chl:2048 * (chl + 1)]
        KLh = KLT[rhl:rhl + 64, 2048 * chl:2048 * (chl + 1)]
        for wg in range(2):
            exp_prev = 0
            for wi in range(8):
                w = 8 * wg + wi
                pb = 512 * (wi % 2)
                pe(lambda p, w=w, pb=pb, QLh=QLh, KLh=KLh: p.matmul(
                    PSF[:, pb:pb + 128], KLh[:, 128 * w:128 * (w + 1)],
                    QLh[:, 128 * w:128 * (w + 1)], start=True, stop=True),
                   inc="sPE",
                   wait=W(("sAC", max(v_vpl, exp_prev))))
                v_qk = sch.v("sPE")
                ac(lambda a, wi=wi, pb=pb: a.activation(
                    out=PTE[:, 128 * wi:128 * (wi + 1)],
                    in_=PSF[:, pb:pb + 128], func=AF.Exp, scale=SC),
                   inc="sAC", wait=W(("sPE", v_qk)))
                exp_prev = sch.v("sAC")
                pe(lambda p, w=w, h=h, wi=wi: p.matmul(
                    PSF[0:65, 3072 + 128 * wi:3072 + 128 * (wi + 1)],
                    VPL[:, 520 * w + 65 * h:520 * w + 65 * (h + 1)],
                    PTE[:, 128 * wi:128 * (wi + 1)],
                    start=True, stop=True),
                   inc="sPE", wait=W(("sAC", max(exp_prev, cte_prev))))
            v_av = sch.v("sPE")
            ac(lambda a: a.activation(out=CTE[0:65, :],
                                      in_=PSF[0:65, 3072:4096],
                                      func=AF.Copy, scale=1.0),
               inc="sAC", wait=W(("sPE", v_av), ("sDV", lt_div)))
            cte_prev = sch.v("sAC")
            dv(lambda v: v.reciprocal(out=RZROW[:], in_=CTE[64:65, :]),
               inc="sDV", wait=W(("sAC", sch.v("sAC")),
                                 ("sDV", sch.v("sDV"))))
            dv(lambda v: v.tensor_copy(out=RZROWB[:], in_=RZROW[:]),
               inc="sDV", wait=W(("sDV", sch.v("sDV"))))
            v_rz = sch.v("sDV")
            pe_mm(PSF[0:64, 2048:3072], ONE1B[0:1, 0:64], RZROWB[:],
                  start=True, stop=True, n_free=1024, inc="sPE",
                  wait=W(("sDV", v_rz)))
            v_rb = sch.v("sPE")
            dv(lambda v: v.tensor_copy(out=RCPS[:], in_=PSF[0:64, 2048:3072]),
               inc="sDV", wait=W(("sPE", v_rb), ("sDV", sch.v("sDV"))))
            v_rc = sch.v("sDV")
            dv(lambda v, h=h, wg=wg: v.tensor_tensor(
                out=CTXL[:, 2048 * (h // 2) + 1024 * wg:
                         2048 * (h // 2) + 1024 * (wg + 1)]
                    [(h % 2) * 64:(h % 2) * 64 + 64, :],
                in0=CTE[0:64, :], in1=RCPS[:], op=OP.mult),
               inc="sDV", wait=W(("sDV", v_rc)))
            lt_div = sch.v("sDV")

    load_weight_direct("wqh", 512, gcol=24, wait_pe=sch.v("sPE"),
                       dv_guard=sch.v("sDV"), ac_guard=sch.v("sAC"))
    proj_T(QHT, None, RNT, nf=4, wcols=512)
    load_weight_direct("wkvh_t", 1024, gcol=24, wait_pe=sch.v("sPE"),
                       dv_guard=sch.v("sDV"), ac_guard=sch.v("sAC"))
    vw = sch.v("sDV")
    for pc in range(4):
        for sub in range(2):
            hh = 2 * pc + sub
            for dc in range(8):
                pe_mm(PSF[64 * sub:64 * (sub + 1), 0:2048],
                      WTMP[:, 1024 * dc + 128 * hh:1024 * dc + 128 * hh + 64],
                      RNT[:, 2048 * dc:2048 * (dc + 1)],
                      start=(dc == 0), stop=(dc == 7), n_free=2048,
                      inc=("sPE" if (dc == 7 and sub == 1) else None),
                      wait=(W(("sDV", vw), ("sAC", sch.v("sAC")))
                            if (dc == 0 and sub == 0) else ()))
        v_pe = sch.v("sPE")
        ac(lambda a, pc=pc: a.activation(
            out=KVHT[:, 2048 * pc:2048 * (pc + 1)], in_=PSF[:, 0:2048],
            func=AF.Copy, scale=1.0), inc="sAC", wait=W(("sPE", v_pe)))
    v_light_done = lt_div
    for t in range(TT):
        for dc in range(8):
            pe(lambda p, t=t, dc=dc: p.matmul(
                PSF[:, 0:512],
                RNT[:, 2048 * dc + 128 * t:2048 * dc + 128 * (t + 1)],
                WTMP[:, 1024 * dc:1024 * (dc + 1)]
                    .rearrange("p (h c) -> p h c", c=128)[:, :, 64:128],
                start=(dc == 0), stop=(dc == 7)),
               inc=("sPE" if dc == 7 else None),
               wait=(W(("sAC", sch.v("sAC")), ("sDV", v_light_done),
                       ("sGP", gp_init)) if dc == 0 else ()))
        v_pe = sch.v("sPE")
        ac(lambda a, t=t: a.activation(
            out=VPH[:, 520 * t:520 * (t + 1)]
                .rearrange("p (h c) -> p h c", c=65)[:, :, 0:64],
            in_=PSF[:, 0:512], func=AF.Copy, scale=1.0),
           inc="sAC", wait=W(("sPE", v_pe)))
    v_vph = sch.v("sAC")

    hv_div = sch.v("sDV")
    cte_prev2 = sch.v("sAC")
    for h in range(H2):
        Kh2 = KVHT[(h % 2) * 64:(h % 2) * 64 + 64,
                   2048 * (h // 2):2048 * (h // 2 + 1)]
        Qh2 = QHT[(h % 2) * 64:(h % 2) * 64 + 64,
                  2048 * (h // 2):2048 * (h // 2 + 1)]
        for qh in range(2):
            exp_v = {}
            av_v = {}
            for kt in range(16):
                pb = 1024 * (kt % 2)
                for j in range(2):
                    pe(lambda p, kt=kt, qh=qh, j=j, pb=pb, Kh2=Kh2, Qh2=Qh2:
                       p.matmul(
                           PSF[:, pb + 512 * j:pb + 512 * (j + 1)],
                           Kh2[:, 128 * kt:128 * (kt + 1)],
                           Qh2[:, 1024 * qh + 512 * j:
                               1024 * qh + 512 * (j + 1)],
                           start=True, stop=True),
                       inc=("sPE" if j == 1 else None),
                       wait=(W(("sAC", max(v_vph, exp_v.get(kt - 2, 0))))
                             if j == 0 else ()))
                v_qk = sch.v("sPE")
                ac(lambda a, kt=kt, pb=pb: a.activation(
                    out=PTE[:, 1024 * (kt % 2):1024 * (kt % 2) + 1024],
                    in_=PSF[:, pb:pb + 1024], func=AF.Exp, scale=SC),
                   inc="sAC",
                   wait=W(("sPE", max(v_qk, av_v.get(kt - 2, 0)))))
                exp_v[kt] = sch.v("sAC")
                for j in range(2):
                    pe(lambda p, kt=kt, h=h, j=j: p.matmul(
                        PSF[0:65, 3072 + 512 * j:3072 + 512 * (j + 1)],
                        VPH[:, 520 * kt + 65 * h:520 * kt + 65 * (h + 1)],
                        PTE[:, 1024 * (kt % 2) + 512 * j:
                            1024 * (kt % 2) + 512 * (j + 1)],
                        start=(kt == 0), stop=(kt == 15)),
                       inc=("sPE" if j == 1 else None),
                       wait=(W(("sAC", max(exp_v[kt], cte_prev2)))
                             if j == 0 else ()))
                av_v[kt] = sch.v("sPE")
            v_av = sch.v("sPE")
            ac(lambda a: a.activation(out=CTE[0:65, :],
                                      in_=PSF[0:65, 3072:4096],
                                      func=AF.Copy, scale=1.0),
               inc="sAC", wait=W(("sPE", v_av), ("sDV", hv_div)))
            cte_prev2 = sch.v("sAC")
            dv(lambda v: v.reciprocal(out=RZROW[:], in_=CTE[64:65, :]),
               inc="sDV", wait=W(("sAC", sch.v("sAC")),
                                 ("sDV", sch.v("sDV"))))
            dv(lambda v: v.tensor_copy(out=RZROWB[:], in_=RZROW[:]),
               inc="sDV", wait=W(("sDV", sch.v("sDV"))))
            v_rz = sch.v("sDV")
            pe_mm(PSF[0:64, 2048:3072], ONE1B[0:1, 0:64], RZROWB[:],
                  start=True, stop=True, n_free=1024, inc="sPE",
                  wait=W(("sDV", v_rz)))
            v_rb = sch.v("sPE")
            dv(lambda v: v.tensor_copy(out=RCPS[:], in_=PSF[0:64, 2048:3072]),
               inc="sDV", wait=W(("sPE", v_rb), ("sDV", sch.v("sDV"))))
            v_rc = sch.v("sDV")
            dv(lambda v, h=h, qh=qh: v.tensor_tensor(
                out=CTXH[:, 2048 * (h // 2) + 1024 * qh:
                         2048 * (h // 2) + 1024 * (qh + 1)]
                    [(h % 2) * 64:(h % 2) * 64 + 64, :],
                in0=CTE[0:64, :], in1=RCPS[:], op=OP.mult),
               inc="sDV", wait=W(("sDV", v_rc)))
            hv_div = sch.v("sDV")

    load_weight_direct("wol", 1024, rows=512, woff=0,
                       wait_pe=sch.v("sPE"), dv_guard=sch.v("sDV"))
    v_wol = load_weight_direct("woh", 1024, rows=512, woff=4096)
    st_done = {0: 0, 1: 0}
    for t in range(TT):
        po = 2048 * (t % 2)
        so = 1024 * (t % 2)
        stx = "sSTa" if t % 2 == 0 else "sSTb"
        for dc in range(4):
            pe_mm(PSF[:, po:po + 1024],
                  CTXL[:, 2048 * dc + 128 * t:2048 * dc + 128 * (t + 1)],
                  WTMP[:, 1024 * dc:1024 * (dc + 1)],
                  start=(dc == 0), stop=(dc == 3), n_free=1024,
                  wait=(W(v_wol, ("sDV", max(hv_div, sch.v("sDV"))))
                        if dc == 0 else ()))
        for dc in range(4):
            pe_mm(PSF[:, po + 1024:po + 2048],
                  CTXH[:, 2048 * dc + 128 * t:2048 * dc + 128 * (t + 1)],
                  WTMP[:, 4096 + 1024 * dc:4096 + 1024 * (dc + 1)],
                  start=(dc == 0), stop=(dc == 3), n_free=1024,
                  inc=("sPE" if dc == 3 else None))
        v_pe = sch.v("sPE")
        dv(lambda v, t=t, po=po, so=so: v.scalar_tensor_tensor(
            out=OUTF[:, so:so + 1024], in0=PSF[:, po + 1024:po + 2048],
            scalar=SELCOL[:, t:t + 1], in1=XB[:, D * t:D * (t + 1)],
            op0=OP.mult, op1=OP.add),
           inc="sDV",
           wait=W(("sPE", v_pe), ("sDV", sch.v("sDV")), ("sDV", v_sel),
                  (stx, st_done[t % 2])))
        dv(lambda v, po=po, so=so: v.tensor_tensor(
            out=OUTF[:, so:so + 1024], in0=PSF[:, po:po + 1024],
            in1=OUTF[:, so:so + 1024], op=OP.add),
           inc="sDV", wait=W(("sDV", sch.v("sDV"))))
        v_of = sch.v("sDV")
        sp(lambda s, t=t, so=so: s.dma_start(
            out=out_t[128 * t:128 * (t + 1), :], in_=OUTF[:, so:so + 1024]),
           inc=stx, n=16, wait=W(("sDV", v_of)))
        st_done[t % 2] = sch.v(stx)
    if upto == 'fulldbg':
        v_fin = sch.v("sDV")
        for a in range(8):
            gp(lambda g, a=a: g.dma_start(
                out=dbg_t[128 * a:128 * (a + 1), :],
                in_=CTXL[:, 1024 * a:1024 * (a + 1)]),
               inc="sGW", n=16, wait=W(("sDV", v_fin), ("sGW", sch.v("sGW"))))
        for a in range(8):
            gp(lambda g, a=a: g.dma_start(
                out=dbg_t[1024 + 128 * a:1024 + 128 * (a + 1), :],
                in_=CTXH[:, 1024 * a:1024 * (a + 1)]),
               inc="sGW", n=16, wait=W(("sGW", sch.v("sGW"))))
        for src_ap, c0 in ((SELCOL, 0), (UCOL, 16), (CCOL, 32), (SCOL, 48)):
            gp(lambda g, src_ap=src_ap, c0=c0: g.dma_start(
                out=dbg_t[2048:2176, c0:c0 + 16], in_=src_ap[:, :],
                allow_slow_non_contiguous=True),
               inc="sGW", n=16, wait=W(("sGW", sch.v("sGW"))))
        for j, a in enumerate((6, 7)):
            gp(lambda g, j=j, a=a: g.dma_start(
                out=dbg_t[2176 + 128 * j:2176 + 128 * (j + 1), :],
                in_=XB[:, 1024 * a:1024 * (a + 1)]),
               inc="sGW", n=16, wait=W(("sGW", sch.v("sGW"))))
        sch.waitq("sp", ("sSTa", sch.v("sSTa")), ("sSTb", sch.v("sSTb")),
                  ("sGW", sch.v("sGW")))
    else:
        sch.waitq("sp", ("sSTa", sch.v("sSTa")), ("sSTb", sch.v("sSTb")))

    with nc.Block() as block:
        @block.sync
        def _(h):
            for f in sch.q["sp"]:
                f(h)

        @block.gpsimd
        def _(h):
            for f in sch.q["gp"]:
                f(h)

        @block.vector
        def _(h):
            for f in sch.q["dv"]:
                f(h)

        @block.scalar
        def _(h):
            for f in sch.q["ac"]:
                f(h)

        @block.tensor
        def _(h):
            for f in sch.q["pe"]:
                f(h)

    es.close()
    return nc


_NC_CACHE = [None]


def kernel(L0_attn_x, L0_attn_Wq, L0_attn_bq, L0_attn_Wk, L0_attn_bk,
           L0_attn_Wv, L0_attn_bv, L0_attn_Wo, L0_attn_bo,
           L0_ffn_gamma_light, L0_ffn_wq_light, L0_ffn_wkv_light,
           L0_ffn_wo_light, L0_ffn_q_routing_token, L0_ffn_kv_routing_token,
           L0_ffn_gamma_heavy, L0_ffn_wq_heavy, L0_ffn_wkv_heavy,
           L0_ffn_wo_heavy):
    x = np.ascontiguousarray(np.asarray(L0_attn_x, dtype=np.float32))
    f = lambda a: np.ascontiguousarray(np.asarray(a, dtype=np.float32))
    row = lambda a: np.ascontiguousarray(
        np.asarray(a, dtype=np.float32).reshape(1, -1))
    base = {
        "wq_t": np.ascontiguousarray(f(L0_attn_Wq).T),
        "wk_t": np.ascontiguousarray(f(L0_attn_Wk).T),
        "wv_t": np.ascontiguousarray(f(L0_attn_Wv).T),
        "wo_t": np.ascontiguousarray(f(L0_attn_Wo).T),
        "wkvl": f(L0_ffn_wkv_light),
        "wkvh_t": np.ascontiguousarray(f(L0_ffn_wkv_heavy).T),
        "wql": f(L0_ffn_wq_light), "wqh": f(L0_ffn_wq_heavy),
        "wol": f(L0_ffn_wo_light), "woh": f(L0_ffn_wo_heavy),
        "bq": row(L0_attn_bq), "bk": row(L0_attn_bk), "bv": row(L0_attn_bv),
        "bo": row(L0_attn_bo), "g_l": row(L0_ffn_gamma_light),
        "g_h": row(L0_ffn_gamma_heavy), "rq": row(L0_ffn_q_routing_token),
    }
    if _NC_CACHE[0] is None:
        _NC_CACHE[0] = build_program()
    nc = _NC_CACHE[0]
    in_maps = [dict(base, x=np.ascontiguousarray(x[b])) for b in range(8)]
    res = run_bass_kernel_spmd(nc, in_maps, core_ids=list(range(8)))
    return np.stack([np.asarray(r["out"], dtype=np.float32)
                     for r in res.results], axis=0)


# version 10
# speedup vs baseline: 1.0692x; 1.0692x over previous
import math
from contextlib import ExitStack

import numpy as np
import concourse.bass as bass
import concourse.mybir as mybir
from concourse.bass_utils import run_bass_kernel_spmd

F32 = mybir.dt.float32
BF16 = mybir.dt.bfloat16
AX = mybir.AxisListType.X
OP = mybir.AluOpType
AF = mybir.ActivationFunctionType

T, D, TT = 2048, 1024, 16
H1, H2, HID = 16, 8, 512
NEG = -1e38
SC = 0.125
ROUTE_K = 1152.0
LOGK = float(np.log(np.float32(ROUTE_K)))
A1 = float(np.log(np.float32(ROUTE_K)) - np.log(np.float32(T)))
NIT = 50


class Sched:

    ENGS = ("sp", "gp", "dv", "ac", "pe")

    def __init__(self):
        self.q = {e: [] for e in self.ENGS}
        self.sem = {}
        self.semh = {}

    def op(self, eng, fn, inc=None, n=1, wait=()):
        semh = self.semh
        wait = tuple((s, v) for s, v in wait if v > 0)
        tgt = None
        if inc is not None:
            self.sem[inc] += n
            tgt = (semh[inc], n)

        def run(h, fn=fn, wait=wait, tgt=tgt):
            for s, v in wait:
                h.wait_ge(semh[s], v)
            inst = fn(h)
            if tgt is not None and inst is not None:
                inst.then_inc(*tgt)

        self.q[eng].append(run)
        return self.sem[inc] if inc is not None else None

    def waitq(self, eng, *sv):
        semh = self.semh
        sv = tuple(sv)

        def run(h, sv=sv):
            for s, v in sv:
                if v > 0:
                    h.wait_ge(semh[s], v)

        self.q[eng].append(run)

    def v(self, s):
        return self.sem[s]


def build_program(upto='full'):
    nc = bass.Bass("TRN2", target_bir_lowering=False, debug=False)
    sch = Sched()

    di = {}
    for name, shape in (
        ("x", [T, D]), ("wq_t", [D, D]), ("wk_t", [D, D]), ("wv_t", [D, D]),
        ("wo_t", [D, D]), ("wkvl", [D, D]), ("wkvh_t", [D, D]),
        ("wql", [D, HID]), ("wqh", [D, HID]), ("wol", [HID, D]),
        ("woh", [HID, D]), ("bq", [1, D]), ("bk", [1, D]), ("bv", [1, D]),
        ("bo", [1, D]), ("g_l", [1, D]), ("g_h", [1, D]), ("rq", [1, D]),
    ):
        di[name] = nc.dram_tensor(name, shape, F32, kind="ExternalInput").ap()
    out_t = nc.dram_tensor("out", [T, D], F32, kind="ExternalOutput").ap()
    dbg_t = (nc.dram_tensor("dbg", [2560, 1024], F32, kind="ExternalOutput")
             .ap() if upto == 'fulldbg' else None)

    es = ExitStack()
    Sb = lambda nm, sh, dt=F32: es.enter_context(nc.sbuf_tensor(nm, sh, dt))

    ARN = Sb("ARN", [128, 90384], BF16)
    XB = ARN[:, 0:16384]
    XNT = ARN[:, 16384:32768]
    CT1 = ARN[:, 16384:32768]
    CTXL = ARN[:, 16384:24576]
    CTXH = ARN[:, 24576:32768]
    QT = ARN[:, 32768:49152]
    RNT = ARN[:, 32768:49152]
    KT = ARN[:, 49152:65536]
    QLT = ARN[:, 49152:57344]
    KLT = ARN[:, 57344:65536]
    KVHT = ARN[:, 49152:57344]
    QHT = ARN[:, 57344:65536]
    VP = ARN[:, 65536:82176]
    VPL = ARN[:, 65536:73856]
    VPH = ARN[:, 73856:82176]
    SBT = ARN[:, 82176:86272]
    SBX = ARN[:, 86272:86784]
    PTE = ARN[:, 86784:88832]
    CND = ARN[:, 88832:89088]
    CN2 = ARN[:, 89088:89344]
    R8 = ARN[:, 89344:89360]
    CTE = ARN[:, 89360:90384]
    WTMP = ARN[:, 82176:90368]

    ARF = Sb("ARF", [128, 4096], F32)
    SBMB = Sb("SBMB", [128, 4096], BF16)
    NEGI = Sb("NEGI", [128, 128], BF16)
    SBM = ARF[:, 0:4096]
    OUTF = ARF[:, 0:2048]
    SPROD = ARF[:, 1024:2048]
    XNF = ARF[:, 2048:3072]
    R128 = ARF[:, 3072:4096]

    IDF = Sb("IDF", [128, 128], F32)
    BIASC = Sb("BIASC", [128, 32], F32)
    BVROW = ARF[0:1, 3072:4096]
    BOROW = None
    ONE1F = Sb("ONE1F", [1, 128], F32)
    ONE1B = Sb("ONE1B", [1, 128], BF16)
    ONECF = Sb("ONECF", [128, 1], F32)
    EPSC = Sb("EPSC", [128, 1], F32)
    TCOL = Sb("TCOL", [128, 2], F32)
    RZROWB = Sb("RZROWB", [1, 1024], BF16)
    RCPS = Sb("RCPS", [64, 1024], BF16)
    MCOL = Sb("MCOL", [128, 1], F32)
    RSTD = Sb("RSTD", [128, 1], F32)
    SSQ = Sb("SSQ", [128, 1], F32)
    RSN = Sb("RSN", [128, 1], F32)
    BN12 = Sb("BN12", [128, 12], F32)
    BN2 = Sb("BN2", [128, 2], F32)
    RNF = ARF[:, 0:1024]
    SQJ = ARN[:, 86784:87808]
    SCOL = Sb("SCOL", [128, 16], F32)
    UCOL = Sb("UCOL", [128, 16], F32)
    CCOL = Sb("CCOL", [128, 16], F32)
    SELCOL = Sb("SELCOL", [128, 16], F32)
    U16 = ARF[0:16, 3072:3200]
    SAT16 = ARF[0:16, 3200:3328]
    P16 = ARF[0:16, 3328:3456]
    SEL16 = ARF[0:16, 3456:3584]
    C16 = U16
    RS16 = ARF[0:16, 3584:3585]
    OFF16 = ARF[0:16, 3585:3586]
    OFR = ARF[0:1, 3586:3602]
    OFR2 = ARF[0:1, 3602:3618]
    AREP = ARF[:, 3618:3619]
    NAREP = ARF[:, 3619:3620]
    RSC = ARF[:, 3620:3636]
    EJ16 = Sb("EJ16", [128, 16], BF16)
    EPART = Sb("EPART", [128, 1], F32)
    LNS = Sb("LNS", [1, 1], F32)
    ANEW = Sb("ANEW", [1, 1], F32)
    NANEW = Sb("NANEW", [1, 1], F32)
    APLK = Sb("APLK", [1, 1], F32)

    PSF = es.enter_context(nc.psum_tensor("PSF", [128, 4096], F32))
    RZROW = PSF[64:65, 2048:3072]

    for s in ("sLs", "sLx0", "sLx1", "sLw", "sSTa", "sSTb",
              "sPE", "sDV", "sAC", "sGP", "sGW"):
        sch.semh[s] = es.enter_context(nc.semaphore(s))
        sch.sem[s] = 0

    W = lambda *sv: tuple(sv)
    sp = lambda fn, **k: sch.op("sp", fn, **k)
    gp = lambda fn, **k: sch.op("gp", fn, **k)
    dv = lambda fn, **k: sch.op("dv", fn, **k)
    ac = lambda fn, **k: sch.op("ac", fn, **k)
    pe = lambda fn, **k: sch.op("pe", fn, **k)

    def pe_mm(out_ap, lhs, rhs, start, stop, n_free, is_tr=False,
              inc=None, wait=()):
        nslc = (n_free + 511) // 512
        for j in range(nslc):
            c0, c1 = 512 * j, min(512 * (j + 1), n_free)
            pe(lambda p, c0=c0, c1=c1, out_ap=out_ap, lhs=lhs, rhs=rhs,
               start=start, stop=stop, is_tr=is_tr: p.matmul(
                   out_ap[:, c0:c1], lhs, rhs[:, c0:c1], start=start,
                   stop=stop, is_transpose=is_tr),
               inc=(inc if j == nslc - 1 else None),
               wait=(wait if j == 0 else ()))
        return sch.v(inc) if inc else None

    gp(lambda g: g.memset(IDF[:], 0.0), inc="sGP")
    gp(lambda g: g.affine_select(
        out=IDF[:], in_=IDF[:], compare_op=OP.not_equal, fill=1.0, base=0,
        pattern=[[-1, 128]], channel_multiplier=1), inc="sGP",
        wait=W(("sGP", sch.v("sGP"))))
    gp(lambda g: g.memset(NEGI[:], 0.0), inc="sGP")
    gp(lambda g: g.affine_select(
        out=NEGI[:], in_=NEGI[:], compare_op=OP.not_equal, fill=-1e30,
        base=0, pattern=[[-1, 128]], channel_multiplier=1), inc="sGP",
        wait=W(("sGP", sch.v("sGP"))))
    gp(lambda g: g.memset(ONE1F[:], 1.0), inc="sGP")
    gp(lambda g: g.memset(ONE1B[:], 1.0), inc="sGP")
    gp(lambda g: g.memset(ONECF[:], 1.0), inc="sGP")
    gp(lambda g: g.memset(EPSC[:], 1e-5), inc="sGP")
    gp(lambda g: g.memset(VP[:], 1.0), inc="sGP")
    gp_init = sch.v("sGP")

    BQROW = ARF[32:33, 3072:4096]
    BKROW = ARF[64:65, 3072:4096]
    BQROWB = ARN[0:1, 67584:68608]
    BKROWB = ARN[0:1, 68608:69632]
    ONESROW = VP[0:1, 0:2048]
    for rowt, name in ((BQROW, "bq"), (BKROW, "bk")):
        sp(lambda s, rowt=rowt, name=name: s.dma_start(
            out=rowt, in_=di[name]), inc="sLs", n=16)
    for name, cs in (("g_l", 16), ("g_h", 24)):
        sp(lambda s, name=name, cs=cs: s.dma_start(
            out=BIASC[:, cs:cs + 8],
            in_=di[name].rearrange("o (c p) -> (o p) c", p=128),
            allow_slow_non_contiguous=True), inc="sLs", n=16)
    sp(lambda s: s.dma_start(out=BVROW, in_=di["bv"]), inc="sLs", n=16)
    v_small = sch.v("sLs")
    dv(lambda v: v.tensor_copy(out=BQROWB, in_=BQROW), inc="sDV",
       wait=W(("sLs", v_small), ("sGP", gp_init)))
    dv(lambda v: v.tensor_copy(out=BKROWB, in_=BKROW), inc="sDV",
       wait=W(("sLs", v_small), ("sGP", gp_init)))

    ln_done = {}
    xnf_done = {}
    for t in range(TT):
        slx = "sLx0"
        sp(lambda s, t=t: s.dma_start(
            out=XNF[:], in_=di["x"][128 * t:128 * (t + 1), :]),
           inc=slx, n=16, wait=W(("sDV", xnf_done.get(t - 1, 0))))
        vld = sch.v(slx)
        dv(lambda v, t=t: v.tensor_copy(
            out=XB[:, D * t:D * (t + 1)], in_=XNF[:]),
           inc="sDV", wait=W((slx, vld)))
        dv(lambda v: v.bn_stats(out=BN12[:, 0:6],
                                in_=XNF[:, 0:512]), inc="sDV")
        dv(lambda v: v.bn_stats(out=BN12[:, 6:12],
                                in_=XNF[:, 512:1024]),
           inc="sDV")
        dv(lambda v: v.bn_aggr(out=BN2[:], in_=BN12[:]), inc="sDV",
           wait=W(("sDV", sch.v("sDV"))))
        dv(lambda v: v.tensor_copy(out=MCOL[:], in_=BN2[:, 0:1]), inc="sDV",
           wait=W(("sDV", sch.v("sDV"))))
        v_bn = sch.v("sDV")
        ac(lambda a: a.activation(out=SSQ[:], in_=BN2[:, 1:2], func=AF.Sqrt,
                                  bias=EPSC[:], scale=1.0), inc="sAC",
           wait=W(("sDV", v_bn), ("sGP", gp_init)))
        dv(lambda v: v.reciprocal(out=RSTD[:], in_=SSQ[:]), inc="sDV",
           wait=W(("sAC", sch.v("sAC"))))
        dv(lambda v: v.tensor_scalar(
            out=XNF[:], in0=XNF[:], scalar1=MCOL[:],
            scalar2=RSTD[:], op0=OP.subtract, op1=OP.mult), inc="sDV",
           wait=W(("sDV", sch.v("sDV"))))
        v_xnf = sch.v("sDV")
        for c in range(8):
            pe(lambda p, c=c: p.transpose(
                PSF[:, 3072 + 128 * c:3200 + 128 * c],
                XNF[:, 128 * c:128 * (c + 1)], IDF[:]),
               inc=("sPE" if c == 7 else None),
               wait=(W(("sDV", v_xnf), ("sGP", 2),
                       ("sDV", ln_done.get(t - 1, 0))) if c == 0 else ()))
        v_tr = sch.v("sPE")
        dv(lambda v, t=t: v.tensor_copy(
            out=XNT.rearrange("p (c n) -> p c n", n=2048)
                [:, :, 128 * t:128 * (t + 1)],
            in_=PSF[:, 3072:4096].rearrange("p (c n) -> p c n", n=128)),
           inc="sDV", wait=W(("sPE", v_tr)))
        ln_done[t] = sch.v("sDV")
        xnf_done[t] = sch.v("sDV")

    def load_weight_direct(name, ncols, rows=1024, woff=0, wait_pe=0,
                           dv_guard=0, ac_guard=0, gcol=None):
        vs = []
        for c in range(rows // 128):
            gp(lambda g, c=c, name=name, ncols=ncols, woff=woff: g.dma_start(
                out=WTMP[:, woff + ncols * c:woff + ncols * (c + 1)],
                in_=di[name][128 * c:128 * (c + 1), 0:ncols]),
               inc="sGW", n=16,
               wait=(W(("sGW", sch.v("sGW")), ("sPE", wait_pe),
                       ("sDV", dv_guard), ("sAC", ac_guard)) if c == 0 else
                     W(("sGW", sch.v("sGW")))))
            vs.append(sch.v("sGW"))
        if gcol is None:
            return ("sGW", sch.v("sGW"))
        for c in range(rows // 128):
            dv(lambda v, c=c, ncols=ncols, woff=woff, gcol=gcol:
               v.tensor_scalar(
                   out=WTMP[:, woff + ncols * c:woff + ncols * (c + 1)],
                   in0=WTMP[:, woff + ncols * c:woff + ncols * (c + 1)],
                   scalar1=BIASC[:, gcol + c:gcol + c + 1],
                   scalar2=None, op0=OP.mult),
               inc="sDV", wait=W(("sGW", vs[c]), ("sLs", v_small),
                                 ("sDV", sch.v("sDV"))))
        return ("sDV", sch.v("sDV"))

    def proj_T(dst, bias_row, src_T, nf=8, wcols=1024, wwait=None):
        if wwait is None:
            wwait = ("sDV", sch.v("sDV"))
        for fc in range(nf):
            for dc in range(8):
                pe_mm(PSF[:, 0:2048],
                      WTMP[:, wcols * dc + 128 * fc:
                           wcols * dc + 128 * (fc + 1)],
                      src_T[:, 2048 * dc:2048 * (dc + 1)],
                      start=(dc == 0), stop=(dc == 7 and bias_row is None),
                      n_free=2048,
                      inc=("sPE" if (dc == 7 and bias_row is None) else None),
                      wait=(W(wwait, ("sDV", sch.v("sDV")),
                              ("sAC", sch.v("sAC")),
                              ("sLs", v_small)) if dc == 0 else ()))
            if bias_row is not None:
                pe_mm(PSF[:, 0:2048], bias_row[0:1, 128 * fc:128 * (fc + 1)],
                      ONESROW, start=False, stop=True, n_free=2048,
                      inc="sPE", wait=W(("sDV", sch.v("sDV")),
                                        ("sGP", gp_init)))
            vpe = sch.v("sPE")
            ac(lambda a, fc=fc, dst=dst: a.activation(
                out=dst[:, 2048 * fc:2048 * (fc + 1)], in_=PSF[:, 0:2048],
                func=AF.Copy, scale=1.0), inc="sAC", wait=W(("sPE", vpe)))
        return sch.v("sAC")

    v_w = load_weight_direct("wq_t", 1024)
    proj_T(QT, BQROWB, XNT, wwait=v_w)
    v_w = load_weight_direct("wk_t", 1024, wait_pe=sch.v("sPE"))
    proj_T(KT, BKROWB, XNT, wwait=v_w)
    v_w = load_weight_direct("wv_t", 1024, wait_pe=sch.v("sPE"))
    for t in range(TT):
        for dc in range(8):
            pe_mm(PSF[:, 0:1024],
                  XNT[:, 2048 * dc + 128 * t:2048 * dc + 128 * (t + 1)],
                  WTMP[:, 1024 * dc:1024 * (dc + 1)],
                  start=(dc == 0), stop=False, n_free=1024,
                  wait=(W(v_w, ("sDV", sch.v("sDV")),
                          ("sAC", sch.v("sAC")),
                          ("sLs", v_small)) if dc == 0 else ()))
        pe_mm(PSF[:, 0:1024], ONE1F[:], BVROW[:], start=False, stop=True,
              n_free=1024, inc="sPE")
        vpe = sch.v("sPE")
        ac(lambda a, t=t: a.activation(
            out=VP[:, 1040 * t:1040 * (t + 1)]
                .rearrange("p (h c) -> p h c", c=65)[:, :, 0:64],
            in_=PSF[:, 0:1024], func=AF.Copy, scale=1.0),
           inc="sAC", wait=W(("sPE", vpe), ("sGP", gp_init)))
    v_vp = sch.v("sAC")

    evac_done = {}
    pen_done = {}
    qk_done = {}
    sel_tcol = {}
    pen0_relu = {}
    st_last = {}
    exp_pair = {}
    av_kt = {}
    div_done = 0

    def b1_qh_kh(ei):
        h = ei // 8
        ch, rh = h // 2, (h % 2) * 64
        return (QT[rh:rh + 64, 2048 * ch:2048 * (ch + 1)],
                KT[rh:rh + 64, 2048 * ch:2048 * (ch + 1)])

    def emit_qksel(ei, qi):
        Qh, Kh = b1_qh_kh(ei)
        qt = 2 * (ei % 8) + qi
        ev = evac_done.get((ei, 0), 0) if qi == 1 else \
            evac_done.get((ei - 1, 1), 0)
        for j in range(4):
            pe(lambda p, qt=qt, j=j, Qh=Qh, Kh=Kh: p.matmul(
                PSF[:, 512 * j:512 * (j + 1)],
                Qh[:, 128 * qt:128 * (qt + 1)],
                Kh[:, 512 * j:512 * (j + 1)],
                start=True, stop=True),
               inc=("sPE" if j == 3 else None),
               wait=(W(("sAC", max(ev, v_vp))) if j == 0 else ()))
        qk_done[(ei, qi)] = sch.v("sPE")

    def emit_evac(ei, qi):
        sb = SBT[:, 2048 * qi:2048 * (qi + 1)]
        psem, pval = pen_done.get((ei - 1, qi), ("sDV", 0))
        ac(lambda a, sb=sb: a.activation(out=sb, in_=PSF[:, 0:2048],
                                         func=AF.Copy, scale=1.0),
           inc="sAC", wait=W(("sPE", qk_done[(ei, qi)]), (psem, pval)))
        evac_done[(ei, qi)] = sch.v("sAC")

    def emit_sel_l1(ei, qi):
        sb = SBT[:, 2048 * qi:2048 * (qi + 1)]
        cnd = (CND if qi == 0 else CN2)[:, 0:256]
        for c in range(16):
            dv(lambda v, c=c, sb=sb, cnd=cnd: v.max(
                out=cnd[:, 8 * c:8 * (c + 1)],
                in_=sb[:, 128 * c:128 * (c + 1)]),
               inc="sDV",
               wait=(W(("sAC", evac_done[(ei, qi)]), ("sDV", sch.v("sDV")))
                     if c == 0 else ()))
        v_r1 = sch.v("sDV")
        vgrp = v_r1
        for grp in range(4):
            for c in range(4 * grp, 4 * grp + 4):
                dv(lambda v, c=c, sb=sb, cnd=cnd: v.match_replace(
                    out=SBX[:, 128 * (c % 4):128 * (c % 4) + 128],
                    in_to_replace=cnd[:, 8 * c:8 * (c + 1)],
                    in_values=sb[:, 128 * c:128 * (c + 1)],
                    imm_value=NEG),
                   inc="sDV",
                   wait=(W(("sDV", vgrp)) if c % 4 == 0 else ()))
            v_mr = sch.v("sDV")
            for c in range(4 * grp, 4 * grp + 4):
                dv(lambda v, c=c, cnd=cnd: v.max(
                    out=cnd[:, 128 + 8 * c:128 + 8 * (c + 1)],
                    in_=SBX[:, 128 * (c % 4):128 * (c % 4) + 128]),
                   inc="sDV",
                   wait=(W(("sDV", v_mr)) if c % 4 == 0 else ()))
            vgrp = sch.v("sDV")
        return sch.v("sDV")

    def emit_sel_l2(ei, v_l1):
        cands = (CND[:, 0:256], CN2[:, 0:256])
        last = {0: v_l1[0], 1: v_l1[1]}
        for rr in range(5):
            for qi in range(2):
                dv(lambda v, qi=qi: v.max(out=R8[:, 8 * qi:8 * (qi + 1)],
                                          in_=cands[qi]),
                   inc="sDV", wait=W(("sDV", last[qi])))
                last[qi] = sch.v("sDV")
            if rr < 4:
                for qi in range(2):
                    dv(lambda v, qi=qi: v.match_replace(
                        out=cands[qi], in_to_replace=R8[:, 8 * qi:8 * (qi + 1)],
                        in_values=cands[qi], imm_value=NEG),
                       inc="sDV", wait=W(("sDV", last[qi])))
                    last[qi] = sch.v("sDV")
        for qi in range(2):
            dv(lambda v, qi=qi: v.tensor_copy(
                out=TCOL[:, qi:qi + 1], in_=R8[:, 8 * qi + 5:8 * qi + 6]),
               inc="sDV", wait=W(("sDV", last[qi])))
            sel_tcol[(ei, qi)] = sch.v("sDV")

    def emit_pen(ei, qi):
        sb = SBT[:, 2048 * qi:2048 * (qi + 1)]
        sm = SBMB[:, 2048 * qi:2048 * (qi + 1)]
        dv(lambda v, sb=sb, sm=sm, qi=qi: v.tensor_scalar(
            out=sm, in0=sb, scalar1=TCOL[:, qi:qi + 1], scalar2=None,
            op0=OP.is_lt),
           inc="sDV", wait=W(("sDV", sch.v("sDV")),
                             ("sPE", st_last.get(ei - 1, 0))))
        pen_done[(ei, qi)] = ("sDV", sch.v("sDV"))

    def emit_av(bi, g, h, ctx_c0):
        for sub in range(2):
            kt = 2 * g + sub
            w = (W(("sAC", exp_pair[(bi, g)]), ("sDV", div_done))
                 if kt == 0 else
                 (W(("sAC", exp_pair[(bi, g)])) if sub == 0 else ()))
            pe(lambda p, kt=kt, h=h, ctx_c0=ctx_c0: p.matmul(
                PSF[0:65, ctx_c0:ctx_c0 + 256],
                VP[:, 1040 * kt + 65 * h:1040 * kt + 65 * (h + 1)],
                PTE[:, 256 * (kt % 8):256 * (kt % 8) + 256],
                start=(kt == 0), stop=(kt == 15)),
               inc="sPE", wait=w)
            av_kt[(bi, kt)] = sch.v("sPE")

    pend_div = None
    rcps_gate = 0

    def emit_div2(pd):
        dbi, v_rzb = pd
        h, e8 = dbi // 8, dbi % 8
        half = e8 // 4
        pe_mm(PSF[0:64, 2048:3072], ONE1B[0:1, 0:64], RZROWB[:],
              start=True, stop=True, n_free=1024, inc="sPE",
              wait=W(("sAC", v_rzb)))
        v_rb = sch.v("sPE")
        ac(lambda a: a.activation(out=RCPS[:],
                                  in_=PSF[0:64, 2048:3072],
                                  func=AF.Copy, scale=1.0),
           inc="sAC", wait=W(("sPE", v_rb), ("sAC", sch.v("sAC"))))
        v_rc = sch.v("sAC")
        dv(lambda v, h=h, half=half: v.tensor_tensor(
            out=CT1[:, 2048 * (h // 2) + 1024 * half:
                    2048 * (h // 2) + 1024 * (half + 1)]
                [(h % 2) * 64:(h % 2) * 64 + 64, :],
            in0=CTE[0:64, :], in1=RCPS[:], op=OP.mult),
           inc="sDV", wait=W(("sAC", v_rc), ("sDV", sch.v("sDV"))))
        return v_rc, sch.v("sDV")

    for step in range(130):
        ei, bi = step, step - 1
        if ei < 128:
            emit_qksel(ei, 0)
        if pend_div is not None:
            rcps_gate, div_done = emit_div2(pend_div)
            pend_div = None
        if ei < 128:
            emit_evac(ei, 0)
        if bi >= 0 and bi < 128:
            Qh, Kh = b1_qh_kh(bi)
            h, e8 = bi // 8, bi % 8
            q0 = 256 * e8
            ctx_c0 = 3072 + 256 * (e8 % 4)
            st_v = {}
            psem0, pval0 = pen_done[(bi, 0)]
            _s1, pval1 = pen_done[(bi, 1)]
            for g in range(8):
                wbase = 2048 + 512 * (g % 2)
                wguard = (exp_pair[(bi, g - 2)] if g >= 2 else
                          exp_pair.get((bi - 1, g + 6), 0))
                for sub in range(2):
                    kt = 2 * g + sub
                    po = wbase + 256 * sub
                    if g == 0 and sub == 0:
                        pw = W((psem0, pval0),
                               ("sDV", max(pval1, div_done)),
                               ("sAC", max(wguard, rcps_gate)))
                    elif sub == 0:
                        pw = W(("sAC", wguard))
                    else:
                        pw = ()
                    pe(lambda p, kt=kt, po=po, q0=q0, Qh=Qh, Kh=Kh:
                       p.matmul(PSF[:, po:po + 256],
                                Kh[:, 128 * kt:128 * (kt + 1)],
                                Qh[:, q0:q0 + 256],
                                start=True, stop=False),
                       wait=pw)
                    for qi2 in range(2):
                        pe(lambda p, kt=kt, po=po, qi2=qi2: p.matmul(
                            PSF[:, po + 128 * qi2:po + 128 * (qi2 + 1)],
                            SBMB[:, 2048 * qi2 + 128 * kt:
                                 2048 * qi2 + 128 * (kt + 1)],
                            NEGI[:], start=False, stop=(qi2 == 1)),
                           inc=("sPE" if qi2 == 1 else None))
                    st_v[kt] = sch.v("sPE")
                pguard = (av_kt.get((bi, 2 * g - 7), 0) if g >= 4 else
                          av_kt.get((bi - 1, 2 * g + 9), 0))
                ac(lambda a, g=g, wbase=wbase: a.activation(
                    out=PTE[:, 512 * (g % 4):512 * (g % 4) + 512],
                    in_=PSF[:, wbase:wbase + 512], func=AF.Exp, scale=SC),
                   inc="sAC", wait=W(("sPE", max(st_v[2 * g + 1], pguard))))
                exp_pair[(bi, g)] = sch.v("sAC")
                if g >= 1:
                    emit_av(bi, g - 1, h, ctx_c0)
                if g == 2 and ei < 128:
                    emit_qksel(ei, 1)
                    emit_evac(ei, 1)
            emit_av(bi, 7, h, ctx_c0)
            st_last[bi] = st_v[15]
        if ei < 128 and (bi < 0 or bi >= 128):
            emit_qksel(ei, 1)
            emit_evac(ei, 1)
        if ei < 128:
            v_l1 = {0: emit_sel_l1(ei, 0)}
            v_l1[1] = emit_sel_l1(ei, 1)
            emit_sel_l2(ei, v_l1)
            emit_pen(ei, 0)
            emit_pen(ei, 1)
        if bi >= 0 and bi < 128 and bi % 4 == 3:
            v_ava = av_kt[(bi, 15)]
            ac(lambda a: a.activation(out=CTE[0:65, :],
                                      in_=PSF[0:65, 3072:4096],
                                      func=AF.Copy, scale=1.0),
               inc="sAC", wait=W(("sPE", v_ava), ("sDV", div_done),
                                 ("sAC", sch.v("sAC"))))
            cte_v = sch.v("sAC")
            dv(lambda v: v.reciprocal(out=RZROW[:], in_=CTE[64:65, :]),
               inc="sDV", wait=W(("sAC", cte_v), ("sDV", sch.v("sDV"))))
            v_rz0 = sch.v("sDV")
            ac(lambda a: a.activation(out=RZROWB[:], in_=RZROW[:],
                                      func=AF.Copy, scale=1.0),
               inc="sAC", wait=W(("sDV", v_rz0), ("sAC", sch.v("sAC"))))
            pend_div = (bi, sch.v("sAC"))

    v_wo = load_weight_direct("wo_t", 1024, wait_pe=sch.v("sPE"),
                              dv_guard=sch.v("sDV"))
    BOROW = XNF[0:1, :]
    sp(lambda s: s.dma_start(out=BOROW, in_=di["bo"]), inc="sLs", n=16,
       wait=W(("sDV", sch.v("sDV"))))
    v_bo = sch.v("sLs")
    for t in range(TT):
        po = 1024 * (t % 2)
        for dc in range(8):
            pe_mm(PSF[:, po:po + 1024],
                  CT1[:, 2048 * dc + 128 * t:2048 * dc + 128 * (t + 1)],
                  WTMP[:, 1024 * dc:1024 * (dc + 1)],
                  start=(dc == 0), stop=False, n_free=1024,
                  wait=(W(v_wo, ("sDV", max(div_done, sch.v("sDV"))))
                        if dc == 0 else ()))
        pe_mm(PSF[:, po:po + 1024], ONE1F[:], BOROW, start=False,
              stop=True, n_free=1024, inc="sPE",
              wait=W(("sLs", v_bo)))
        v_pe = sch.v("sPE")
        dv(lambda v, t=t, po=po: v.scalar_tensor_tensor(
            out=XB[:, D * t:D * (t + 1)], in0=PSF[:, po:po + 1024],
            scalar=1.0, in1=XB[:, D * t:D * (t + 1)], op0=OP.mult,
            op1=OP.add), inc="sDV", wait=W(("sPE", v_pe),
                                           ("sDV", sch.v("sDV"))))
    v_x1 = sch.v("sDV")

    RQROW = XNF[0:1, :]
    sp(lambda s: s.dma_start(out=RQROW, in_=di["rq"]), inc="sLs", n=16,
       wait=W(("sDV", v_x1)))
    v_rq = sch.v("sLs")
    pe_mm(PSF[:, 2048:3072], ONE1F[:], RQROW, start=True, stop=True,
          n_free=1024, inc="sPE", wait=W(("sDV", v_x1), ("sLs", v_rq)))
    dv(lambda v: v.tensor_copy(out=R128[:], in_=PSF[:, 2048:3072]),
       inc="sDV", wait=W(("sPE", sch.v("sPE"))))
    v_r128 = sch.v("sDV")

    for t in range(TT):
        xt = XB[:, D * t:D * (t + 1)]
        dv(lambda v, xt=xt: v.bn_stats(out=BN12[:, 0:6], in_=xt[:, 0:512]),
           inc="sDV", wait=W(("sDV", max(v_x1, v_r128))))
        dv(lambda v, xt=xt: v.bn_stats(out=BN12[:, 6:12],
                                       in_=xt[:, 512:1024]), inc="sDV")
        dv(lambda v: v.bn_aggr(out=BN2[:], in_=BN12[:]), inc="sDV",
           wait=W(("sDV", sch.v("sDV"))))
        dv(lambda v: v.tensor_copy(out=MCOL[:], in_=BN2[:, 0:1]), inc="sDV",
           wait=W(("sDV", sch.v("sDV"))))
        v_bn = sch.v("sDV")
        ac(lambda a: a.activation(out=SSQ[:], in_=BN2[:, 1:2], func=AF.Sqrt,
                                  bias=EPSC[:], scale=1.0), inc="sAC",
           wait=W(("sDV", v_bn)))
        dv(lambda v: v.reciprocal(out=RSTD[:], in_=SSQ[:]), inc="sDV",
           wait=W(("sAC", sch.v("sAC"))))
        dv(lambda v, xt=xt: v.tensor_scalar(
            out=XNF[:], in0=xt, scalar1=MCOL[:], scalar2=RSTD[:],
            op0=OP.subtract, op1=OP.mult), inc="sDV",
           wait=W(("sDV", sch.v("sDV"))))
        v_xnf = sch.v("sDV")
        dv(lambda v, t=t: v.scalar_tensor_tensor(
            out=SPROD[:], in0=XNF[:], scalar=1.0, in1=R128[:],
            op0=OP.mult, op1=OP.mult),
           inc="sDV", wait=W(("sDV", max(v_r128, v_xnf))))
        dv(lambda v, t=t: v.reduce_sum(out=SCOL[:, t:t + 1], in_=SPROD[:],
                                       axis=AX),
           inc="sDV", wait=W(("sDV", sch.v("sDV"))))
        ac(lambda a: a.activation(out=SQJ[:], in_=XNF[:], func=AF.Square,
                                  scale=1.0, accum_out=SSQ[:]), inc="sAC",
           wait=W(("sDV", sch.v("sDV"))))
        ac(lambda a: a.activation(out=MCOL[:], in_=SSQ[:], func=AF.Sqrt,
                                  scale=1.0 / D), inc="sAC",
           wait=W(("sAC", sch.v("sAC"))))
        dv(lambda v: v.reciprocal(out=RSN[:], in_=MCOL[:]), inc="sDV",
           wait=W(("sAC", sch.v("sAC"))))
        v_rsn = sch.v("sDV")
        dv(lambda v: v.tensor_scalar(out=RNF[:], in0=XNF[:], scalar1=RSN[:],
                                     scalar2=None, op0=OP.mult),
           inc="sDV", wait=W(("sDV", v_rsn)))
        v_rn = sch.v("sDV")
        for c in range(8):
            pe(lambda p, c=c: p.transpose(
                PSF[:, 3072 + 128 * c:3200 + 128 * c],
                RNF[:, 128 * c:128 * (c + 1)], IDF[:]),
               inc=("sPE" if c == 7 else None),
               wait=(W(("sDV", v_rn)) if c == 0 else ()))
        v_tr = sch.v("sPE")
        dv(lambda v, t=t: v.tensor_copy(
            out=RNT.rearrange("p (c n) -> p c n", n=2048)
                [:, :, 128 * t:128 * (t + 1)],
            in_=PSF[:, 3072:4096].rearrange("p (c n) -> p c n", n=128)),
           inc="sDV", wait=W(("sPE", v_tr)))
    v_ln2 = sch.v("sDV")


    v_m = dv(lambda v: v.memset(ANEW[:], A1), inc="sDV")
    dv(lambda v: v.tensor_scalar(out=NANEW[:], in0=ANEW[:], scalar1=-1.0,
                                 scalar2=None, op0=OP.mult), inc="sDV",
       wait=W(("sDV", v_m)))
    dv(lambda v: v.tensor_scalar(out=APLK[:], in0=ANEW[:], scalar1=LOGK,
                                 scalar2=None, op0=OP.add), inc="sDV",
       wait=W(("sDV", v_m)))
    v_a = sch.v("sDV")
    for it in range(NIT - 1):
        pe(lambda p: p.matmul(PSF[:, 2048:2049], ONE1F[:], NANEW[:],
                              start=True, stop=True), inc="sPE",
           wait=W(("sDV", v_a)))
        v_brc = sch.v("sPE")
        ac(lambda a: a.activation(out=NAREP[:], in_=PSF[:, 2048:2049],
                                  func=AF.Copy, scale=1.0), inc="sAC",
           wait=W(("sPE", v_brc)))
        v_nar = sch.v("sAC")
        ac(lambda a: a.activation(out=RSC[:], in_=SCOL[:], func=AF.Relu,
                                  bias=NAREP[:], scale=-1.0), inc="sAC",
           wait=W(("sAC", v_nar), ("sDV", v_ln2)))
        ac(lambda a: a.activation(out=EJ16[:], in_=RSC[:], func=AF.Exp,
                                  scale=-1.0, accum_out=EPART[:]), inc="sAC",
           wait=W(("sAC", sch.v("sAC"))))
        v_ep = sch.v("sAC")
        pe(lambda p: p.matmul(PSF[0:1, 2049:2050], EPART[:], ONECF[:],
                              start=True, stop=True), inc="sPE",
           wait=W(("sAC", v_ep)))
        v_sum = sch.v("sPE")
        ac(lambda a: a.activation(out=LNS[:], in_=PSF[0:1, 2049:2050],
                                  func=AF.Ln, scale=1.0), inc="sAC",
           wait=W(("sPE", v_sum)))
        v_lns = sch.v("sAC")
        dv(lambda v: v.scalar_tensor_tensor(
            out=ANEW[:], in0=LNS[:], scalar=-1.0, in1=APLK[:],
            op0=OP.mult, op1=OP.add), inc="sDV", wait=W(("sAC", v_lns)))
        v_an = sch.v("sDV")
        dv(lambda v: v.tensor_scalar(out=APLK[:], in0=ANEW[:], scalar1=LOGK,
                                     scalar2=None, op0=OP.add), inc="sDV",
           wait=W(("sDV", v_an)))
        dv(lambda v: v.tensor_scalar(out=NANEW[:], in0=ANEW[:], scalar1=-1.0,
                                     scalar2=None, op0=OP.mult), inc="sDV",
           wait=W(("sDV", v_an)))
        v_a = sch.v("sDV")
    pe(lambda p: p.matmul(PSF[:, 2048:2049], ONE1F[:], ANEW[:],
                          start=True, stop=True), inc="sPE",
       wait=W(("sDV", v_a)))
    ac(lambda a: a.activation(out=AREP[:], in_=PSF[:, 2048:2049],
                              func=AF.Copy, scale=1.0), inc="sAC",
       wait=W(("sPE", sch.v("sPE"))))
    dv(lambda v: v.tensor_scalar(out=UCOL[:], in0=SCOL[:], scalar1=AREP[:],
                                 scalar2=0.0, op0=OP.add, op1=OP.min),
       inc="sDV", wait=W(("sAC", sch.v("sAC"))))
    v_ucol = sch.v("sDV")
    pe(lambda p: p.transpose(PSF[0:16, 2048:2176], UCOL[:], IDF[:]),
       inc="sPE", wait=W(("sDV", v_ucol)))
    dv(lambda v: v.tensor_copy(out=U16[:], in_=PSF[0:16, 2048:2176]),
       inc="sDV", wait=W(("sPE", sch.v("sPE"))))
    v_u16 = sch.v("sDV")
    UROW = ARF[0:1, 0:2048]
    sp(lambda s: s.dma_start(out=UROW, in_=U16[:]), inc="sLs", n=16,
       wait=W(("sDV", v_u16)))
    v_urow = sch.v("sLs")
    v_amat = pe_mm(PSF[:, 0:2048], ONE1F[:], UROW, start=True, stop=True,
                   n_free=2048, inc="sPE", wait=W(("sLs", v_urow)))
    for tb in range(16):
        dv(lambda v, tb=tb: v.tensor_scalar(
            out=OUTF[:, 0:2048], in0=PSF[:, 0:2048],
            scalar1=UCOL[:, tb:tb + 1], scalar2=None, op0=OP.is_gt),
           inc="sDV", wait=(W(("sPE", v_amat)) if tb == 0 else
                            W(("sDV", sch.v("sDV")))))
        dv(lambda v, tb=tb: v.reduce_sum(out=CCOL[:, tb:tb + 1],
                                         in_=OUTF[:, 0:2048], axis=AX),
           inc="sDV", wait=W(("sDV", sch.v("sDV"))))
    v_ccol = sch.v("sDV")
    dv(lambda v: v.tensor_scalar(out=SAT16[:], in0=U16[:], scalar1=0.0,
                                 scalar2=None, op0=OP.is_ge), inc="sDV",
       wait=W(("sDV", v_ccol)))
    dv(lambda v: v.tensor_tensor_scan(out=P16[:], data0=SAT16[:],
                                      data1=SAT16[:], initial=0.0,
                                      op0=OP.add, op1=OP.bypass), inc="sDV",
       wait=W(("sDV", sch.v("sDV"))))
    dv(lambda v: v.reduce_sum(out=RS16[:], in_=SAT16[:], axis=AX), inc="sDV",
       wait=W(("sDV", sch.v("sDV"))))
    v_rs = sch.v("sDV")
    pe(lambda p: p.transpose(PSF[0:1, 2048:2064], RS16[:], IDF[0:16, 0:16]),
       inc="sPE", wait=W(("sDV", v_rs)))
    dv(lambda v: v.tensor_copy(out=OFR[:], in_=PSF[0:1, 2048:2064]),
       inc="sDV", wait=W(("sPE", sch.v("sPE"))))
    dv(lambda v: v.tensor_tensor_scan(out=OFR2[:], data0=OFR[:],
                                      data1=OFR[:], initial=0.0,
                                      op0=OP.add, op1=OP.bypass), inc="sDV",
       wait=W(("sDV", sch.v("sDV"))))
    dv(lambda v: v.memset(OFR[:, 0:1], 0.0), inc="sDV",
       wait=W(("sDV", sch.v("sDV"))))
    dv(lambda v: v.tensor_copy(out=OFR[:, 1:16], in_=OFR2[:, 0:15]),
       inc="sDV", wait=W(("sDV", sch.v("sDV"))))
    v_ofr = sch.v("sDV")
    pe(lambda p: p.transpose(PSF[0:16, 2064:2065], OFR[:], IDF[0:1, 0:1]),
       inc="sPE", wait=W(("sDV", v_ofr)))
    dv(lambda v: v.tensor_copy(out=OFF16[:], in_=PSF[0:16, 2064:2065]),
       inc="sDV", wait=W(("sPE", sch.v("sPE"))))
    v_offev = sch.v("sDV")
    dv(lambda v: v.tensor_sub(out=P16[:], in0=P16[:], in1=SAT16[:]),
       inc="sDV", wait=W(("sDV", sch.v("sDV"))))
    dv(lambda v: v.tensor_scalar(out=P16[:], in0=P16[:], scalar1=OFF16[:],
                                 scalar2=None, op0=OP.add), inc="sDV",
       wait=W(("sDV", sch.v("sDV"))))
    dv(lambda v: v.tensor_mul(out=P16[:], in0=P16[:], in1=SAT16[:]),
       inc="sDV", wait=W(("sDV", sch.v("sDV"))))
    v_p16 = sch.v("sDV")
    pe(lambda p: p.transpose(PSF[0:16, 2176:2304], CCOL[:], IDF[:]),
       inc="sPE", wait=W(("sDV", max(v_ccol, v_offev))))
    dv(lambda v: v.tensor_copy(out=C16[:], in_=PSF[0:16, 2176:2304]),
       inc="sDV", wait=W(("sPE", sch.v("sPE"))))
    dv(lambda v: v.tensor_add(out=R16[:] if False else SEL16[:], in0=C16[:],
                              in1=P16[:]), inc="sDV",
       wait=W(("sDV", sch.v("sDV"))))
    dv(lambda v: v.tensor_scalar(out=SEL16[:], in0=SEL16[:], scalar1=1023.5,
                                 scalar2=None, op0=OP.is_lt), inc="sDV",
       wait=W(("sDV", sch.v("sDV"))))
    v_sel16 = sch.v("sDV")
    pe(lambda p: p.transpose(PSF[:, 2304:2320], SEL16[:], IDF[0:16, 0:16]),
       inc="sPE", wait=W(("sDV", v_sel16)))
    dv(lambda v: v.tensor_copy(out=SELCOL[:], in_=PSF[:, 2304:2320]),
       inc="sDV", wait=W(("sPE", sch.v("sPE"))))
    v_sel = sch.v("sDV")

    load_weight_direct("wql", 512, gcol=16, wait_pe=sch.v("sPE"),
                       dv_guard=sch.v("sDV"), ac_guard=sch.v("sAC"))
    proj_T(QLT, None, RNT, nf=4, wcols=512)
    load_weight_direct("wkvl", 1024, gcol=16, wait_pe=sch.v("sPE"),
                       dv_guard=sch.v("sDV"), ac_guard=sch.v("sAC"))
    vw = sch.v("sDV")
    for fc in range(4):
        for dc in range(8):
            pe_mm(PSF[:, 0:2048],
                  WTMP[:, 1024 * dc + 128 * fc:1024 * dc + 128 * (fc + 1)],
                  RNT[:, 2048 * dc:2048 * (dc + 1)],
                  start=(dc == 0), stop=(dc == 7), n_free=2048,
                  inc=("sPE" if dc == 7 else None),
                  wait=(W(("sDV", vw), ("sAC", sch.v("sAC")))
                        if dc == 0 else ()))
        v_pe = sch.v("sPE")
        ac(lambda a, fc=fc: a.activation(
            out=KLT[:, 2048 * fc:2048 * (fc + 1)], in_=PSF[:, 0:2048],
            func=AF.Copy, scale=1.0), inc="sAC", wait=W(("sPE", v_pe)))
    for t in range(TT):
        for dc in range(8):
            pe_mm(PSF[:, 0:512],
                  RNT[:, 2048 * dc + 128 * t:2048 * dc + 128 * (t + 1)],
                  WTMP[:, 1024 * dc + 512:1024 * (dc + 1)],
                  start=(dc == 0), stop=(dc == 7), n_free=512,
                  inc=("sPE" if dc == 7 else None),
                  wait=(W(("sAC", sch.v("sAC")), ("sGP", gp_init))
                        if dc == 0 else ()))
        v_pe = sch.v("sPE")
        ac(lambda a, t=t: a.activation(
            out=VPL[:, 520 * t:520 * (t + 1)]
                .rearrange("p (h c) -> p h c", c=65)[:, :, 0:64],
            in_=PSF[:, 0:512], func=AF.Copy, scale=1.0),
           inc="sAC", wait=W(("sPE", v_pe)))
    v_vpl = sch.v("sAC")

    lt_div = sch.v("sDV")
    cte_prev = sch.v("sAC")
    for h in range(H2):
        chl, rhl = h // 2, (h % 2) * 64
        QLh = QLT[rhl:rhl + 64, 2048 * chl:2048 * (chl + 1)]
        KLh = KLT[rhl:rhl + 64, 2048 * chl:2048 * (chl + 1)]
        for wg in range(2):
            exp_prev = 0
            for wi in range(8):
                w = 8 * wg + wi
                pb = 512 * (wi % 2)
                pe(lambda p, w=w, pb=pb, QLh=QLh, KLh=KLh: p.matmul(
                    PSF[:, pb:pb + 128], KLh[:, 128 * w:128 * (w + 1)],
                    QLh[:, 128 * w:128 * (w + 1)], start=True, stop=True),
                   inc="sPE",
                   wait=W(("sAC", max(v_vpl, exp_prev))))
                v_qk = sch.v("sPE")
                ac(lambda a, wi=wi, pb=pb: a.activation(
                    out=PTE[:, 128 * wi:128 * (wi + 1)],
                    in_=PSF[:, pb:pb + 128], func=AF.Exp, scale=SC),
                   inc="sAC", wait=W(("sPE", v_qk)))
                exp_prev = sch.v("sAC")
                pe(lambda p, w=w, h=h, wi=wi: p.matmul(
                    PSF[0:65, 3072 + 128 * wi:3072 + 128 * (wi + 1)],
                    VPL[:, 520 * w + 65 * h:520 * w + 65 * (h + 1)],
                    PTE[:, 128 * wi:128 * (wi + 1)],
                    start=True, stop=True),
                   inc="sPE", wait=W(("sAC", max(exp_prev, cte_prev))))
            v_av = sch.v("sPE")
            ac(lambda a: a.activation(out=CTE[0:65, :],
                                      in_=PSF[0:65, 3072:4096],
                                      func=AF.Copy, scale=1.0),
               inc="sAC", wait=W(("sPE", v_av), ("sDV", lt_div)))
            cte_prev = sch.v("sAC")
            dv(lambda v: v.reciprocal(out=RZROW[:], in_=CTE[64:65, :]),
               inc="sDV", wait=W(("sAC", sch.v("sAC")),
                                 ("sDV", sch.v("sDV"))))
            dv(lambda v: v.tensor_copy(out=RZROWB[:], in_=RZROW[:]),
               inc="sDV", wait=W(("sDV", sch.v("sDV"))))
            v_rz = sch.v("sDV")
            pe_mm(PSF[0:64, 2048:3072], ONE1B[0:1, 0:64], RZROWB[:],
                  start=True, stop=True, n_free=1024, inc="sPE",
                  wait=W(("sDV", v_rz)))
            v_rb = sch.v("sPE")
            dv(lambda v: v.tensor_copy(out=RCPS[:], in_=PSF[0:64, 2048:3072]),
               inc="sDV", wait=W(("sPE", v_rb), ("sDV", sch.v("sDV"))))
            v_rc = sch.v("sDV")
            dv(lambda v, h=h, wg=wg: v.tensor_tensor(
                out=CTXL[:, 2048 * (h // 2) + 1024 * wg:
                         2048 * (h // 2) + 1024 * (wg + 1)]
                    [(h % 2) * 64:(h % 2) * 64 + 64, :],
                in0=CTE[0:64, :], in1=RCPS[:], op=OP.mult),
               inc="sDV", wait=W(("sDV", v_rc)))
            lt_div = sch.v("sDV")

    load_weight_direct("wqh", 512, gcol=24, wait_pe=sch.v("sPE"),
                       dv_guard=sch.v("sDV"), ac_guard=sch.v("sAC"))
    proj_T(QHT, None, RNT, nf=4, wcols=512)
    load_weight_direct("wkvh_t", 1024, gcol=24, wait_pe=sch.v("sPE"),
                       dv_guard=sch.v("sDV"), ac_guard=sch.v("sAC"))
    vw = sch.v("sDV")
    for pc in range(4):
        for sub in range(2):
            hh = 2 * pc + sub
            for dc in range(8):
                pe_mm(PSF[64 * sub:64 * (sub + 1), 0:2048],
                      WTMP[:, 1024 * dc + 128 * hh:1024 * dc + 128 * hh + 64],
                      RNT[:, 2048 * dc:2048 * (dc + 1)],
                      start=(dc == 0), stop=(dc == 7), n_free=2048,
                      inc=("sPE" if (dc == 7 and sub == 1) else None),
                      wait=(W(("sDV", vw), ("sAC", sch.v("sAC")))
                            if (dc == 0 and sub == 0) else ()))
        v_pe = sch.v("sPE")
        ac(lambda a, pc=pc: a.activation(
            out=KVHT[:, 2048 * pc:2048 * (pc + 1)], in_=PSF[:, 0:2048],
            func=AF.Copy, scale=1.0), inc="sAC", wait=W(("sPE", v_pe)))
    v_light_done = lt_div
    for t in range(TT):
        for dc in range(8):
            pe(lambda p, t=t, dc=dc: p.matmul(
                PSF[:, 0:512],
                RNT[:, 2048 * dc + 128 * t:2048 * dc + 128 * (t + 1)],
                WTMP[:, 1024 * dc:1024 * (dc + 1)]
                    .rearrange("p (h c) -> p h c", c=128)[:, :, 64:128],
                start=(dc == 0), stop=(dc == 7)),
               inc=("sPE" if dc == 7 else None),
               wait=(W(("sAC", sch.v("sAC")), ("sDV", v_light_done),
                       ("sGP", gp_init)) if dc == 0 else ()))
        v_pe = sch.v("sPE")
        ac(lambda a, t=t: a.activation(
            out=VPH[:, 520 * t:520 * (t + 1)]
                .rearrange("p (h c) -> p h c", c=65)[:, :, 0:64],
            in_=PSF[:, 0:512], func=AF.Copy, scale=1.0),
           inc="sAC", wait=W(("sPE", v_pe)))
    v_vph = sch.v("sAC")

    hv_div = sch.v("sDV")
    cte_prev2 = sch.v("sAC")
    for h in range(H2):
        Kh2 = KVHT[(h % 2) * 64:(h % 2) * 64 + 64,
                   2048 * (h // 2):2048 * (h // 2 + 1)]
        Qh2 = QHT[(h % 2) * 64:(h % 2) * 64 + 64,
                  2048 * (h // 2):2048 * (h // 2 + 1)]
        for qh in range(2):
            exp_v = {}
            av_v = {}
            for kt in range(16):
                pb = 1024 * (kt % 2)
                for j in range(2):
                    pe(lambda p, kt=kt, qh=qh, j=j, pb=pb, Kh2=Kh2, Qh2=Qh2:
                       p.matmul(
                           PSF[:, pb + 512 * j:pb + 512 * (j + 1)],
                           Kh2[:, 128 * kt:128 * (kt + 1)],
                           Qh2[:, 1024 * qh + 512 * j:
                               1024 * qh + 512 * (j + 1)],
                           start=True, stop=True),
                       inc=("sPE" if j == 1 else None),
                       wait=(W(("sAC", max(v_vph, exp_v.get(kt - 2, 0))))
                             if j == 0 else ()))
                v_qk = sch.v("sPE")
                ac(lambda a, kt=kt, pb=pb: a.activation(
                    out=PTE[:, 1024 * (kt % 2):1024 * (kt % 2) + 1024],
                    in_=PSF[:, pb:pb + 1024], func=AF.Exp, scale=SC),
                   inc="sAC",
                   wait=W(("sPE", max(v_qk, av_v.get(kt - 2, 0)))))
                exp_v[kt] = sch.v("sAC")
                for j in range(2):
                    pe(lambda p, kt=kt, h=h, j=j: p.matmul(
                        PSF[0:65, 3072 + 512 * j:3072 + 512 * (j + 1)],
                        VPH[:, 520 * kt + 65 * h:520 * kt + 65 * (h + 1)],
                        PTE[:, 1024 * (kt % 2) + 512 * j:
                            1024 * (kt % 2) + 512 * (j + 1)],
                        start=(kt == 0), stop=(kt == 15)),
                       inc=("sPE" if j == 1 else None),
                       wait=(W(("sAC", max(exp_v[kt], cte_prev2)))
                             if j == 0 else ()))
                av_v[kt] = sch.v("sPE")
            v_av = sch.v("sPE")
            ac(lambda a: a.activation(out=CTE[0:65, :],
                                      in_=PSF[0:65, 3072:4096],
                                      func=AF.Copy, scale=1.0),
               inc="sAC", wait=W(("sPE", v_av), ("sDV", hv_div)))
            cte_prev2 = sch.v("sAC")
            dv(lambda v: v.reciprocal(out=RZROW[:], in_=CTE[64:65, :]),
               inc="sDV", wait=W(("sAC", sch.v("sAC")),
                                 ("sDV", sch.v("sDV"))))
            dv(lambda v: v.tensor_copy(out=RZROWB[:], in_=RZROW[:]),
               inc="sDV", wait=W(("sDV", sch.v("sDV"))))
            v_rz = sch.v("sDV")
            pe_mm(PSF[0:64, 2048:3072], ONE1B[0:1, 0:64], RZROWB[:],
                  start=True, stop=True, n_free=1024, inc="sPE",
                  wait=W(("sDV", v_rz)))
            v_rb = sch.v("sPE")
            dv(lambda v: v.tensor_copy(out=RCPS[:], in_=PSF[0:64, 2048:3072]),
               inc="sDV", wait=W(("sPE", v_rb), ("sDV", sch.v("sDV"))))
            v_rc = sch.v("sDV")
            dv(lambda v, h=h, qh=qh: v.tensor_tensor(
                out=CTXH[:, 2048 * (h // 2) + 1024 * qh:
                         2048 * (h // 2) + 1024 * (qh + 1)]
                    [(h % 2) * 64:(h % 2) * 64 + 64, :],
                in0=CTE[0:64, :], in1=RCPS[:], op=OP.mult),
               inc="sDV", wait=W(("sDV", v_rc)))
            hv_div = sch.v("sDV")

    load_weight_direct("wol", 1024, rows=512, woff=0,
                       wait_pe=sch.v("sPE"), dv_guard=sch.v("sDV"))
    v_wol = load_weight_direct("woh", 1024, rows=512, woff=4096)
    st_done = {0: 0, 1: 0}
    for t in range(TT):
        po = 2048 * (t % 2)
        so = 1024 * (t % 2)
        stx = "sSTa" if t % 2 == 0 else "sSTb"
        for dc in range(4):
            pe_mm(PSF[:, po:po + 1024],
                  CTXL[:, 2048 * dc + 128 * t:2048 * dc + 128 * (t + 1)],
                  WTMP[:, 1024 * dc:1024 * (dc + 1)],
                  start=(dc == 0), stop=(dc == 3), n_free=1024,
                  wait=(W(v_wol, ("sDV", max(hv_div, sch.v("sDV"))))
                        if dc == 0 else ()))
        for dc in range(4):
            pe_mm(PSF[:, po + 1024:po + 2048],
                  CTXH[:, 2048 * dc + 128 * t:2048 * dc + 128 * (t + 1)],
                  WTMP[:, 4096 + 1024 * dc:4096 + 1024 * (dc + 1)],
                  start=(dc == 0), stop=(dc == 3), n_free=1024,
                  inc=("sPE" if dc == 3 else None))
        v_pe = sch.v("sPE")
        dv(lambda v, t=t, po=po, so=so: v.scalar_tensor_tensor(
            out=OUTF[:, so:so + 1024], in0=PSF[:, po + 1024:po + 2048],
            scalar=SELCOL[:, t:t + 1], in1=XB[:, D * t:D * (t + 1)],
            op0=OP.mult, op1=OP.add),
           inc="sDV",
           wait=W(("sPE", v_pe), ("sDV", sch.v("sDV")), ("sDV", v_sel),
                  (stx, st_done[t % 2])))
        dv(lambda v, po=po, so=so: v.tensor_tensor(
            out=OUTF[:, so:so + 1024], in0=PSF[:, po:po + 1024],
            in1=OUTF[:, so:so + 1024], op=OP.add),
           inc="sDV", wait=W(("sDV", sch.v("sDV"))))
        v_of = sch.v("sDV")
        sp(lambda s, t=t, so=so: s.dma_start(
            out=out_t[128 * t:128 * (t + 1), :], in_=OUTF[:, so:so + 1024]),
           inc=stx, n=16, wait=W(("sDV", v_of)))
        st_done[t % 2] = sch.v(stx)
    if upto == 'fulldbg':
        v_fin = sch.v("sDV")
        for a in range(8):
            gp(lambda g, a=a: g.dma_start(
                out=dbg_t[128 * a:128 * (a + 1), :],
                in_=CTXL[:, 1024 * a:1024 * (a + 1)]),
               inc="sGW", n=16, wait=W(("sDV", v_fin), ("sGW", sch.v("sGW"))))
        for a in range(8):
            gp(lambda g, a=a: g.dma_start(
                out=dbg_t[1024 + 128 * a:1024 + 128 * (a + 1), :],
                in_=CTXH[:, 1024 * a:1024 * (a + 1)]),
               inc="sGW", n=16, wait=W(("sGW", sch.v("sGW"))))
        for src_ap, c0 in ((SELCOL, 0), (UCOL, 16), (CCOL, 32), (SCOL, 48)):
            gp(lambda g, src_ap=src_ap, c0=c0: g.dma_start(
                out=dbg_t[2048:2176, c0:c0 + 16], in_=src_ap[:, :],
                allow_slow_non_contiguous=True),
               inc="sGW", n=16, wait=W(("sGW", sch.v("sGW"))))
        for j, a in enumerate((6, 7)):
            gp(lambda g, j=j, a=a: g.dma_start(
                out=dbg_t[2176 + 128 * j:2176 + 128 * (j + 1), :],
                in_=XB[:, 1024 * a:1024 * (a + 1)]),
               inc="sGW", n=16, wait=W(("sGW", sch.v("sGW"))))
        sch.waitq("sp", ("sSTa", sch.v("sSTa")), ("sSTb", sch.v("sSTb")),
                  ("sGW", sch.v("sGW")))
    else:
        sch.waitq("sp", ("sSTa", sch.v("sSTa")), ("sSTb", sch.v("sSTb")))

    with nc.Block() as block:
        @block.sync
        def _(h):
            for f in sch.q["sp"]:
                f(h)

        @block.gpsimd
        def _(h):
            for f in sch.q["gp"]:
                f(h)

        @block.vector
        def _(h):
            for f in sch.q["dv"]:
                f(h)

        @block.scalar
        def _(h):
            for f in sch.q["ac"]:
                f(h)

        @block.tensor
        def _(h):
            for f in sch.q["pe"]:
                f(h)

    es.close()
    return nc


_NC_CACHE = [None]


def kernel(L0_attn_x, L0_attn_Wq, L0_attn_bq, L0_attn_Wk, L0_attn_bk,
           L0_attn_Wv, L0_attn_bv, L0_attn_Wo, L0_attn_bo,
           L0_ffn_gamma_light, L0_ffn_wq_light, L0_ffn_wkv_light,
           L0_ffn_wo_light, L0_ffn_q_routing_token, L0_ffn_kv_routing_token,
           L0_ffn_gamma_heavy, L0_ffn_wq_heavy, L0_ffn_wkv_heavy,
           L0_ffn_wo_heavy):
    x = np.ascontiguousarray(np.asarray(L0_attn_x, dtype=np.float32))
    f = lambda a: np.ascontiguousarray(np.asarray(a, dtype=np.float32))
    row = lambda a: np.ascontiguousarray(
        np.asarray(a, dtype=np.float32).reshape(1, -1))
    base = {
        "wq_t": np.ascontiguousarray(f(L0_attn_Wq).T),
        "wk_t": np.ascontiguousarray(f(L0_attn_Wk).T),
        "wv_t": np.ascontiguousarray(f(L0_attn_Wv).T),
        "wo_t": np.ascontiguousarray(f(L0_attn_Wo).T),
        "wkvl": f(L0_ffn_wkv_light),
        "wkvh_t": np.ascontiguousarray(f(L0_ffn_wkv_heavy).T),
        "wql": f(L0_ffn_wq_light), "wqh": f(L0_ffn_wq_heavy),
        "wol": f(L0_ffn_wo_light), "woh": f(L0_ffn_wo_heavy),
        "bq": row(L0_attn_bq), "bk": row(L0_attn_bk), "bv": row(L0_attn_bv),
        "bo": row(L0_attn_bo), "g_l": row(L0_ffn_gamma_light),
        "g_h": row(L0_ffn_gamma_heavy), "rq": row(L0_ffn_q_routing_token),
    }
    if _NC_CACHE[0] is None:
        _NC_CACHE[0] = build_program()
    nc = _NC_CACHE[0]
    in_maps = [dict(base, x=np.ascontiguousarray(x[b])) for b in range(8)]
    res = run_bass_kernel_spmd(nc, in_maps, core_ids=list(range(8)))
    return np.stack([np.asarray(r["out"], dtype=np.float32)
                     for r in res.results], axis=0)


# version 11
# speedup vs baseline: 1.2781x; 1.1954x over previous
import math
from contextlib import ExitStack

import numpy as np
import concourse.bass as bass
import concourse.mybir as mybir
from concourse.bass_utils import run_bass_kernel_spmd

F32 = mybir.dt.float32
BF16 = mybir.dt.bfloat16
AX = mybir.AxisListType.X
OP = mybir.AluOpType
AF = mybir.ActivationFunctionType

T, D, TT = 2048, 1024, 16
H1, H2, HID = 16, 8, 512
NEG = -1e38
SC = 0.125
ROUTE_K = 1152.0
LOGK = float(np.log(np.float32(ROUTE_K)))
A1 = float(np.log(np.float32(ROUTE_K)) - np.log(np.float32(T)))
NIT = 50


class Sched:

    ENGS = ("sp", "gp", "dv", "ac", "pe")

    def __init__(self):
        self.q = {e: [] for e in self.ENGS}
        self.sem = {}
        self.semh = {}

    def op(self, eng, fn, inc=None, n=1, wait=()):
        semh = self.semh
        wait = tuple((s, v) for s, v in wait if v > 0)
        tgt = None
        if inc is not None:
            self.sem[inc] += n
            tgt = (semh[inc], n)

        def run(h, fn=fn, wait=wait, tgt=tgt):
            for s, v in wait:
                h.wait_ge(semh[s], v)
            inst = fn(h)
            if tgt is not None and inst is not None:
                inst.then_inc(*tgt)

        self.q[eng].append(run)
        return self.sem[inc] if inc is not None else None

    def waitq(self, eng, *sv):
        semh = self.semh
        sv = tuple(sv)

        def run(h, sv=sv):
            for s, v in sv:
                if v > 0:
                    h.wait_ge(semh[s], v)

        self.q[eng].append(run)

    def v(self, s):
        return self.sem[s]


def build_program(upto='full'):
    nc = bass.Bass("TRN2", target_bir_lowering=False, debug=False)
    sch = Sched()

    di = {}
    for name, shape in (
        ("x", [T, D]), ("wq_t", [D, D]), ("wk_t", [D, D]), ("wv_t", [D, D]),
        ("wo_t", [D, D]), ("wkvl", [D, D]), ("wkvh_t", [D, D]),
        ("wql", [D, HID]), ("wqh", [D, HID]), ("wol", [HID, D]),
        ("woh", [HID, D]), ("bq", [1, D]), ("bk", [1, D]), ("bv", [1, D]),
        ("bo", [1, D]), ("g_l", [1, D]), ("g_h", [1, D]), ("rq", [1, D]),
    ):
        di[name] = nc.dram_tensor(name, shape, F32, kind="ExternalInput").ap()
    out_t = nc.dram_tensor("out", [T, D], F32, kind="ExternalOutput").ap()
    dbg_t = (nc.dram_tensor("dbg", [2560, 1024], F32, kind="ExternalOutput")
             .ap() if upto == 'fulldbg' else None)

    es = ExitStack()
    Sb = lambda nm, sh, dt=F32: es.enter_context(nc.sbuf_tensor(nm, sh, dt))

    ARN = Sb("ARN", [128, 90384], BF16)
    XB = ARN[:, 0:16384]
    XNT = ARN[:, 16384:32768]
    CT1 = ARN[:, 16384:32768]
    CTXL = ARN[:, 16384:24576]
    CTXH = ARN[:, 24576:32768]
    QT = ARN[:, 32768:49152]
    RNT = ARN[:, 32768:49152]
    KT = ARN[:, 49152:65536]
    QLT = ARN[:, 49152:57344]
    KLT = ARN[:, 57344:65536]
    KVHT = ARN[:, 49152:57344]
    QHT = ARN[:, 57344:65536]
    VP = ARN[:, 65536:82176]
    VPL = ARN[:, 65536:73856]
    VPH = ARN[:, 73856:82176]
    SBT = ARN[:, 82176:86272]
    SBX = ARN[:, 86272:86784]
    PTE = ARN[:, 86784:88832]
    CND = ARN[:, 88832:89088]
    CN2 = ARN[:, 89088:89344]
    R8 = ARN[:, 89344:89360]
    CTE = ARN[:, 89360:90384]
    WTMP = ARN[:, 82176:90368]

    ARF = Sb("ARF", [128, 4096], F32)
    SBMB = Sb("SBMB", [128, 4096], BF16)
    NEGI = Sb("NEGI", [128, 128], BF16)
    SBM = ARF[:, 0:4096]
    OUTF = ARF[:, 0:2048]
    SPROD = ARF[:, 1024:2048]
    XNF = ARF[:, 2048:3072]
    R128 = ARF[:, 3072:4096]

    IDF = Sb("IDF", [128, 128], F32)
    BIASC = Sb("BIASC", [128, 32], F32)
    BVROW = ARF[0:1, 3072:4096]
    BOROW = None
    ONE1F = Sb("ONE1F", [1, 128], F32)
    ONE1B = Sb("ONE1B", [1, 128], BF16)
    ONECF = Sb("ONECF", [128, 1], F32)
    EPSC = Sb("EPSC", [128, 1], F32)
    TCOL = Sb("TCOL", [128, 2], F32)
    RZROWB = Sb("RZROWB", [1, 1024], BF16)
    RCPS = Sb("RCPS", [64, 1024], BF16)
    MCOL = Sb("MCOL", [128, 1], F32)
    RSTD = Sb("RSTD", [128, 1], F32)
    SSQ = Sb("SSQ", [128, 1], F32)
    RSN = Sb("RSN", [128, 1], F32)
    BN12 = Sb("BN12", [128, 12], F32)
    BN2 = Sb("BN2", [128, 2], F32)
    RNF = ARF[:, 0:1024]
    SQJ = ARN[:, 86784:87808]
    SCOL = Sb("SCOL", [128, 16], F32)
    UCOL = Sb("UCOL", [128, 16], F32)
    CCOL = Sb("CCOL", [128, 16], F32)
    SELCOL = Sb("SELCOL", [128, 16], F32)
    U16 = ARF[0:16, 3072:3200]
    SAT16 = ARF[0:16, 3200:3328]
    P16 = ARF[0:16, 3328:3456]
    SEL16 = ARF[0:16, 3456:3584]
    C16 = U16
    RS16 = ARF[0:16, 3584:3585]
    OFF16 = ARF[0:16, 3585:3586]
    OFR = ARF[0:1, 3586:3602]
    OFR2 = ARF[0:1, 3602:3618]
    AREP = ARF[:, 3618:3619]
    NAREP = ARF[:, 3619:3620]
    RSC = ARF[:, 3620:3636]
    EJ16 = Sb("EJ16", [128, 16], BF16)
    EPART = Sb("EPART", [128, 1], F32)
    LNS = Sb("LNS", [1, 1], F32)
    ANEW = Sb("ANEW", [1, 1], F32)
    NANEW = Sb("NANEW", [1, 1], F32)
    APLK = Sb("APLK", [1, 1], F32)

    PSF = es.enter_context(nc.psum_tensor("PSF", [128, 4096], F32))
    RZROW = PSF[64:65, 2048:3072]

    for s in ("sLs", "sLx0", "sLx1", "sLw", "sSTa", "sSTb",
              "sPE", "sDV", "sAC", "sGP", "sGW"):
        sch.semh[s] = es.enter_context(nc.semaphore(s))
        sch.sem[s] = 0

    W = lambda *sv: tuple(sv)
    sp = lambda fn, **k: sch.op("sp", fn, **k)
    gp = lambda fn, **k: sch.op("gp", fn, **k)
    dv = lambda fn, **k: sch.op("dv", fn, **k)
    ac = lambda fn, **k: sch.op("ac", fn, **k)
    pe = lambda fn, **k: sch.op("pe", fn, **k)

    def pe_mm(out_ap, lhs, rhs, start, stop, n_free, is_tr=False,
              inc=None, wait=()):
        nslc = (n_free + 511) // 512
        for j in range(nslc):
            c0, c1 = 512 * j, min(512 * (j + 1), n_free)
            pe(lambda p, c0=c0, c1=c1, out_ap=out_ap, lhs=lhs, rhs=rhs,
               start=start, stop=stop, is_tr=is_tr: p.matmul(
                   out_ap[:, c0:c1], lhs, rhs[:, c0:c1], start=start,
                   stop=stop, is_transpose=is_tr),
               inc=(inc if j == nslc - 1 else None),
               wait=(wait if j == 0 else ()))
        return sch.v(inc) if inc else None

    gp(lambda g: g.memset(IDF[:], 0.0), inc="sGP")
    gp(lambda g: g.affine_select(
        out=IDF[:], in_=IDF[:], compare_op=OP.not_equal, fill=1.0, base=0,
        pattern=[[-1, 128]], channel_multiplier=1), inc="sGP",
        wait=W(("sGP", sch.v("sGP"))))
    gp(lambda g: g.memset(NEGI[:], 0.0), inc="sGP")
    gp(lambda g: g.affine_select(
        out=NEGI[:], in_=NEGI[:], compare_op=OP.not_equal, fill=-1e30,
        base=0, pattern=[[-1, 128]], channel_multiplier=1), inc="sGP",
        wait=W(("sGP", sch.v("sGP"))))
    gp(lambda g: g.memset(ONE1F[:], 1.0), inc="sGP")
    gp(lambda g: g.memset(ONE1B[:], 1.0), inc="sGP")
    gp(lambda g: g.memset(ONECF[:], 1.0), inc="sGP")
    gp(lambda g: g.memset(EPSC[:], 1e-5), inc="sGP")
    gp(lambda g: g.memset(VP[:], 1.0), inc="sGP")
    gp_init = sch.v("sGP")

    BQROW = ARF[32:33, 3072:4096]
    BKROW = ARF[64:65, 3072:4096]
    BQROWB = ARN[0:1, 67584:68608]
    BKROWB = ARN[0:1, 68608:69632]
    ONESROW = VP[0:1, 0:2048]
    for rowt, name in ((BQROW, "bq"), (BKROW, "bk")):
        sp(lambda s, rowt=rowt, name=name: s.dma_start(
            out=rowt, in_=di[name]), inc="sLs", n=16)
    for name, cs in (("g_l", 16), ("g_h", 24)):
        sp(lambda s, name=name, cs=cs: s.dma_start(
            out=BIASC[:, cs:cs + 8],
            in_=di[name].rearrange("o (c p) -> (o p) c", p=128),
            allow_slow_non_contiguous=True), inc="sLs", n=16)
    sp(lambda s: s.dma_start(out=BVROW, in_=di["bv"]), inc="sLs", n=16)
    v_small = sch.v("sLs")
    dv(lambda v: v.tensor_copy(out=BQROWB, in_=BQROW), inc="sDV",
       wait=W(("sLs", v_small), ("sGP", gp_init)))
    dv(lambda v: v.tensor_copy(out=BKROWB, in_=BKROW), inc="sDV",
       wait=W(("sLs", v_small), ("sGP", gp_init)))

    ln_done = {}
    xnf_done = {}
    for t in range(TT):
        slx = "sLx0"
        sp(lambda s, t=t: s.dma_start(
            out=XNF[:], in_=di["x"][128 * t:128 * (t + 1), :]),
           inc=slx, n=16, wait=W(("sDV", xnf_done.get(t - 1, 0))))
        vld = sch.v(slx)
        dv(lambda v, t=t: v.tensor_copy(
            out=XB[:, D * t:D * (t + 1)], in_=XNF[:]),
           inc="sDV", wait=W((slx, vld)))
        dv(lambda v: v.bn_stats(out=BN12[:, 0:6],
                                in_=XNF[:, 0:512]), inc="sDV")
        dv(lambda v: v.bn_stats(out=BN12[:, 6:12],
                                in_=XNF[:, 512:1024]),
           inc="sDV")
        dv(lambda v: v.bn_aggr(out=BN2[:], in_=BN12[:]), inc="sDV",
           wait=W(("sDV", sch.v("sDV"))))
        dv(lambda v: v.tensor_copy(out=MCOL[:], in_=BN2[:, 0:1]), inc="sDV",
           wait=W(("sDV", sch.v("sDV"))))
        v_bn = sch.v("sDV")
        ac(lambda a: a.activation(out=SSQ[:], in_=BN2[:, 1:2], func=AF.Sqrt,
                                  bias=EPSC[:], scale=1.0), inc="sAC",
           wait=W(("sDV", v_bn), ("sGP", gp_init)))
        dv(lambda v: v.reciprocal(out=RSTD[:], in_=SSQ[:]), inc="sDV",
           wait=W(("sAC", sch.v("sAC"))))
        dv(lambda v: v.tensor_scalar(
            out=XNF[:], in0=XNF[:], scalar1=MCOL[:],
            scalar2=RSTD[:], op0=OP.subtract, op1=OP.mult), inc="sDV",
           wait=W(("sDV", sch.v("sDV"))))
        v_xnf = sch.v("sDV")
        for c in range(8):
            pe(lambda p, c=c: p.transpose(
                PSF[:, 3072 + 128 * c:3200 + 128 * c],
                XNF[:, 128 * c:128 * (c + 1)], IDF[:]),
               inc=("sPE" if c == 7 else None),
               wait=(W(("sDV", v_xnf), ("sGP", 2),
                       ("sDV", ln_done.get(t - 1, 0))) if c == 0 else ()))
        v_tr = sch.v("sPE")
        dv(lambda v, t=t: v.tensor_copy(
            out=XNT.rearrange("p (c n) -> p c n", n=2048)
                [:, :, 128 * t:128 * (t + 1)],
            in_=PSF[:, 3072:4096].rearrange("p (c n) -> p c n", n=128)),
           inc="sDV", wait=W(("sPE", v_tr)))
        ln_done[t] = sch.v("sDV")
        xnf_done[t] = sch.v("sDV")

    def load_weight_direct(name, ncols, rows=1024, woff=0, wait_pe=0,
                           dv_guard=0, ac_guard=0, gcol=None):
        vs = []
        for c in range(rows // 128):
            gp(lambda g, c=c, name=name, ncols=ncols, woff=woff: g.dma_start(
                out=WTMP[:, woff + ncols * c:woff + ncols * (c + 1)],
                in_=di[name][128 * c:128 * (c + 1), 0:ncols]),
               inc="sGW", n=16,
               wait=(W(("sGW", sch.v("sGW")), ("sPE", wait_pe),
                       ("sDV", dv_guard), ("sAC", ac_guard)) if c == 0 else
                     W(("sGW", sch.v("sGW")))))
            vs.append(sch.v("sGW"))
        if gcol is None:
            return ("sGW", sch.v("sGW"))
        for c in range(rows // 128):
            dv(lambda v, c=c, ncols=ncols, woff=woff, gcol=gcol:
               v.tensor_scalar(
                   out=WTMP[:, woff + ncols * c:woff + ncols * (c + 1)],
                   in0=WTMP[:, woff + ncols * c:woff + ncols * (c + 1)],
                   scalar1=BIASC[:, gcol + c:gcol + c + 1],
                   scalar2=None, op0=OP.mult),
               inc="sDV", wait=W(("sGW", vs[c]), ("sLs", v_small),
                                 ("sDV", sch.v("sDV"))))
        return ("sDV", sch.v("sDV"))

    def proj_T(dst, bias_row, src_T, nf=8, wcols=1024, wwait=None):
        if wwait is None:
            wwait = ("sDV", sch.v("sDV"))
        for fc in range(nf):
            for dc in range(8):
                pe_mm(PSF[:, 0:2048],
                      WTMP[:, wcols * dc + 128 * fc:
                           wcols * dc + 128 * (fc + 1)],
                      src_T[:, 2048 * dc:2048 * (dc + 1)],
                      start=(dc == 0), stop=(dc == 7 and bias_row is None),
                      n_free=2048,
                      inc=("sPE" if (dc == 7 and bias_row is None) else None),
                      wait=(W(wwait, ("sDV", sch.v("sDV")),
                              ("sAC", sch.v("sAC")),
                              ("sLs", v_small)) if dc == 0 else ()))
            if bias_row is not None:
                pe_mm(PSF[:, 0:2048], bias_row[0:1, 128 * fc:128 * (fc + 1)],
                      ONESROW, start=False, stop=True, n_free=2048,
                      inc="sPE", wait=W(("sDV", sch.v("sDV")),
                                        ("sGP", gp_init)))
            vpe = sch.v("sPE")
            ac(lambda a, fc=fc, dst=dst: a.activation(
                out=dst[:, 2048 * fc:2048 * (fc + 1)], in_=PSF[:, 0:2048],
                func=AF.Copy, scale=1.0), inc="sAC", wait=W(("sPE", vpe)))
        return sch.v("sAC")

    v_w = load_weight_direct("wq_t", 1024)
    proj_T(QT, BQROWB, XNT, wwait=v_w)
    v_w = load_weight_direct("wk_t", 1024, wait_pe=sch.v("sPE"))
    proj_T(KT, BKROWB, XNT, wwait=v_w)
    v_w = load_weight_direct("wv_t", 1024, wait_pe=sch.v("sPE"))
    for t in range(TT):
        for dc in range(8):
            pe_mm(PSF[:, 0:1024],
                  XNT[:, 2048 * dc + 128 * t:2048 * dc + 128 * (t + 1)],
                  WTMP[:, 1024 * dc:1024 * (dc + 1)],
                  start=(dc == 0), stop=False, n_free=1024,
                  wait=(W(v_w, ("sDV", sch.v("sDV")),
                          ("sAC", sch.v("sAC")),
                          ("sLs", v_small)) if dc == 0 else ()))
        pe_mm(PSF[:, 0:1024], ONE1F[:], BVROW[:], start=False, stop=True,
              n_free=1024, inc="sPE")
        vpe = sch.v("sPE")
        ac(lambda a, t=t: a.activation(
            out=VP[:, 1040 * t:1040 * (t + 1)]
                .rearrange("p (h c) -> p h c", c=65)[:, :, 0:64],
            in_=PSF[:, 0:1024], func=AF.Copy, scale=1.0),
           inc="sAC", wait=W(("sPE", vpe), ("sGP", gp_init)))
    v_vp = sch.v("sAC")

    evac_done = {}
    pen_done = {}
    qk_done = {}
    sel_tcol = {}
    pen0_relu = {}
    st_last = {}
    exp_pair = {}
    av_kt = {}
    div_done = 0

    def b1_qh_kh(ei):
        h = ei // 8
        ch, rh = h // 2, (h % 2) * 64
        return (QT[rh:rh + 64, 2048 * ch:2048 * (ch + 1)],
                KT[rh:rh + 64, 2048 * ch:2048 * (ch + 1)])

    def emit_qksel(ei, qi):
        Qh, Kh = b1_qh_kh(ei)
        qt = 2 * (ei % 8) + qi
        ev = evac_done.get((ei, 0), 0) if qi == 1 else \
            evac_done.get((ei - 1, 1), 0)
        for j in range(4):
            pe(lambda p, qt=qt, j=j, Qh=Qh, Kh=Kh: p.matmul(
                PSF[:, 512 * j:512 * (j + 1)],
                Qh[:, 128 * qt:128 * (qt + 1)],
                Kh[:, 512 * j:512 * (j + 1)],
                start=True, stop=True),
               inc=("sPE" if j == 3 else None),
               wait=(W(("sAC", max(ev, v_vp))) if j == 0 else ()))
        qk_done[(ei, qi)] = sch.v("sPE")

    def emit_evac(ei, qi):
        sb = SBT[:, 2048 * qi:2048 * (qi + 1)]
        psem, pval = pen_done.get((ei - 1, qi), ("sDV", 0))
        ac(lambda a, sb=sb: a.activation(out=sb, in_=PSF[:, 0:2048],
                                         func=AF.Copy, scale=1.0),
           inc="sAC", wait=W(("sPE", qk_done[(ei, qi)]), (psem, pval)))
        evac_done[(ei, qi)] = sch.v("sAC")

    def emit_sel_l1(ei, qi):
        sb = SBT[:, 2048 * qi:2048 * (qi + 1)]
        cnd = (CND if qi == 0 else CN2)[:, 0:256]
        for c in range(16):
            dv(lambda v, c=c, sb=sb, cnd=cnd: v.max(
                out=cnd[:, 8 * c:8 * (c + 1)],
                in_=sb[:, 128 * c:128 * (c + 1)]),
               inc="sDV",
               wait=(W(("sAC", evac_done[(ei, qi)]), ("sDV", sch.v("sDV")))
                     if c == 0 else ()))
        v_r1 = sch.v("sDV")
        vgrp = v_r1
        for grp in range(4):
            for c in range(4 * grp, 4 * grp + 4):
                dv(lambda v, c=c, sb=sb, cnd=cnd: v.match_replace(
                    out=SBX[:, 128 * (c % 4):128 * (c % 4) + 128],
                    in_to_replace=cnd[:, 8 * c:8 * (c + 1)],
                    in_values=sb[:, 128 * c:128 * (c + 1)],
                    imm_value=NEG),
                   inc="sDV",
                   wait=(W(("sDV", vgrp)) if c % 4 == 0 else ()))
            v_mr = sch.v("sDV")
            for c in range(4 * grp, 4 * grp + 4):
                dv(lambda v, c=c, cnd=cnd: v.max(
                    out=cnd[:, 128 + 8 * c:128 + 8 * (c + 1)],
                    in_=SBX[:, 128 * (c % 4):128 * (c % 4) + 128]),
                   inc="sDV",
                   wait=(W(("sDV", v_mr)) if c % 4 == 0 else ()))
            vgrp = sch.v("sDV")
        return sch.v("sDV")

    def emit_sel_l2(ei, v_l1):
        cands = (CND[:, 0:256], CN2[:, 0:256])
        last = {0: v_l1[0], 1: v_l1[1]}
        for rr in range(5):
            for qi in range(2):
                dv(lambda v, qi=qi: v.max(out=R8[:, 8 * qi:8 * (qi + 1)],
                                          in_=cands[qi]),
                   inc="sDV", wait=W(("sDV", last[qi])))
                last[qi] = sch.v("sDV")
            if rr < 4:
                for qi in range(2):
                    dv(lambda v, qi=qi: v.match_replace(
                        out=cands[qi], in_to_replace=R8[:, 8 * qi:8 * (qi + 1)],
                        in_values=cands[qi], imm_value=NEG),
                       inc="sDV", wait=W(("sDV", last[qi])))
                    last[qi] = sch.v("sDV")
        for qi in range(2):
            dv(lambda v, qi=qi: v.tensor_copy(
                out=TCOL[:, qi:qi + 1], in_=R8[:, 8 * qi + 5:8 * qi + 6]),
               inc="sDV", wait=W(("sDV", last[qi])))
            sel_tcol[(ei, qi)] = sch.v("sDV")

    def emit_pen(ei, qi):
        sb = SBT[:, 2048 * qi:2048 * (qi + 1)]
        sm = SBMB[:, 2048 * qi:2048 * (qi + 1)]
        dv(lambda v, sb=sb, sm=sm, qi=qi: v.tensor_scalar(
            out=sm, in0=sb, scalar1=TCOL[:, qi:qi + 1], scalar2=None,
            op0=OP.is_lt),
           inc="sDV", wait=W(("sDV", sch.v("sDV")),
                             ("sPE", st_last.get(ei - 1, 0))))
        pen_done[(ei, qi)] = ("sDV", sch.v("sDV"))

    def emit_av(bi, g, h, ctx_c0):
        for sub in range(2):
            kt = 2 * g + sub
            w = (W(("sAC", exp_pair[(bi, g)]), ("sDV", div_done))
                 if kt == 0 else
                 (W(("sAC", exp_pair[(bi, g)])) if sub == 0 else ()))
            pe(lambda p, kt=kt, h=h, ctx_c0=ctx_c0: p.matmul(
                PSF[0:65, ctx_c0:ctx_c0 + 256],
                VP[:, 1040 * kt + 65 * h:1040 * kt + 65 * (h + 1)],
                PTE[:, 256 * (kt % 8):256 * (kt % 8) + 256],
                start=(kt == 0), stop=(kt == 15)),
               inc="sPE", wait=w)
            av_kt[(bi, kt)] = sch.v("sPE")

    pend_div = None
    rcps_gate = 0

    def emit_div2(pd):
        dbi, v_rzb = pd
        h, e8 = dbi // 8, dbi % 8
        half = e8 // 4
        pe_mm(PSF[0:64, 2048:3072], ONE1B[0:1, 0:64], RZROWB[:],
              start=True, stop=True, n_free=1024, inc="sPE",
              wait=W(("sAC", v_rzb)))
        v_rb = sch.v("sPE")
        ac(lambda a: a.activation(out=RCPS[:],
                                  in_=PSF[0:64, 2048:3072],
                                  func=AF.Copy, scale=1.0),
           inc="sAC", wait=W(("sPE", v_rb), ("sAC", sch.v("sAC"))))
        v_rc = sch.v("sAC")
        dv(lambda v, h=h, half=half: v.tensor_tensor(
            out=CT1[:, 2048 * (h // 2) + 1024 * half:
                    2048 * (h // 2) + 1024 * (half + 1)]
                [(h % 2) * 64:(h % 2) * 64 + 64, :],
            in0=CTE[0:64, :], in1=RCPS[:], op=OP.mult),
           inc="sDV", wait=W(("sAC", v_rc), ("sDV", sch.v("sDV"))))
        return v_rc, sch.v("sDV")

    for step in range(130):
        ei, bi = step, step - 1
        if ei == 0:
            emit_qksel(0, 0)
        if pend_div is not None:
            rcps_gate, div_done = emit_div2(pend_div)
            pend_div = None
        if ei < 128:
            emit_evac(ei, 0)
        if bi >= 0 and bi < 128:
            Qh, Kh = b1_qh_kh(bi)
            h, e8 = bi // 8, bi % 8
            q0 = 256 * e8
            ctx_c0 = 3072 + 256 * (e8 % 4)
            st_v = {}
            psem0, pval0 = pen_done[(bi, 0)]
            _s1, pval1 = pen_done[(bi, 1)]
            for g in range(8):
                wbase = 2048 + 512 * (g % 2)
                wguard = (exp_pair[(bi, g - 2)] if g >= 2 else
                          exp_pair.get((bi - 1, g + 6), 0))
                for sub in range(2):
                    kt = 2 * g + sub
                    po = wbase + 256 * sub
                    if g == 0 and sub == 0:
                        pw = W((psem0, pval0),
                               ("sDV", max(pval1, div_done)),
                               ("sAC", max(wguard, rcps_gate)))
                    elif sub == 0:
                        pw = W(("sAC", wguard))
                    else:
                        pw = ()
                    pe(lambda p, kt=kt, po=po, q0=q0, Qh=Qh, Kh=Kh:
                       p.matmul(PSF[:, po:po + 256],
                                Kh[:, 128 * kt:128 * (kt + 1)],
                                Qh[:, q0:q0 + 256],
                                start=True, stop=False),
                       wait=pw)
                    for qi2 in range(2):
                        pe(lambda p, kt=kt, po=po, qi2=qi2: p.matmul(
                            PSF[:, po + 128 * qi2:po + 128 * (qi2 + 1)],
                            SBMB[:, 2048 * qi2 + 128 * kt:
                                 2048 * qi2 + 128 * (kt + 1)],
                            NEGI[:], start=False, stop=(qi2 == 1)),
                           inc=("sPE" if qi2 == 1 else None))
                    st_v[kt] = sch.v("sPE")
                pguard = (av_kt.get((bi, 2 * g - 7), 0) if g >= 4 else
                          av_kt.get((bi - 1, 2 * g + 9), 0))
                ac(lambda a, g=g, wbase=wbase: a.activation(
                    out=PTE[:, 512 * (g % 4):512 * (g % 4) + 512],
                    in_=PSF[:, wbase:wbase + 512], func=AF.Exp, scale=SC),
                   inc="sAC", wait=W(("sPE", max(st_v[2 * g + 1], pguard))))
                exp_pair[(bi, g)] = sch.v("sAC")
                if g >= 1:
                    emit_av(bi, g - 1, h, ctx_c0)
                if g == 2 and ei < 128:
                    emit_qksel(ei, 1)
                    emit_evac(ei, 1)
                if g == 4 and ei + 1 <= 127:
                    emit_qksel(ei + 1, 0)
            emit_av(bi, 7, h, ctx_c0)
            st_last[bi] = st_v[15]
        if ei < 128 and (bi < 0 or bi >= 128):
            emit_qksel(ei, 1)
            emit_evac(ei, 1)
        if bi < 0 and ei + 1 <= 127:
            emit_qksel(ei + 1, 0)
        if ei < 128:
            v_l1 = {0: emit_sel_l1(ei, 0)}
            v_l1[1] = emit_sel_l1(ei, 1)
            emit_sel_l2(ei, v_l1)
            emit_pen(ei, 0)
            emit_pen(ei, 1)
        if bi >= 0 and bi < 128 and bi % 4 == 3:
            v_ava = av_kt[(bi, 15)]
            ac(lambda a: a.activation(out=CTE[0:65, :],
                                      in_=PSF[0:65, 3072:4096],
                                      func=AF.Copy, scale=1.0),
               inc="sAC", wait=W(("sPE", v_ava), ("sDV", div_done),
                                 ("sAC", sch.v("sAC"))))
            cte_v = sch.v("sAC")
            dv(lambda v: v.reciprocal(out=RZROW[:], in_=CTE[64:65, :]),
               inc="sDV", wait=W(("sAC", cte_v), ("sDV", sch.v("sDV"))))
            v_rz0 = sch.v("sDV")
            ac(lambda a: a.activation(out=RZROWB[:], in_=RZROW[:],
                                      func=AF.Copy, scale=1.0),
               inc="sAC", wait=W(("sDV", v_rz0), ("sAC", sch.v("sAC"))))
            pend_div = (bi, sch.v("sAC"))

    v_wo = load_weight_direct("wo_t", 1024, wait_pe=sch.v("sPE"),
                              dv_guard=sch.v("sDV"))
    BOROW = XNF[0:1, :]
    sp(lambda s: s.dma_start(out=BOROW, in_=di["bo"]), inc="sLs", n=16,
       wait=W(("sDV", sch.v("sDV"))))
    v_bo = sch.v("sLs")
    for t in range(TT):
        po = 1024 * (t % 2)
        for dc in range(8):
            pe_mm(PSF[:, po:po + 1024],
                  CT1[:, 2048 * dc + 128 * t:2048 * dc + 128 * (t + 1)],
                  WTMP[:, 1024 * dc:1024 * (dc + 1)],
                  start=(dc == 0), stop=False, n_free=1024,
                  wait=(W(v_wo, ("sDV", max(div_done, sch.v("sDV"))))
                        if dc == 0 else ()))
        pe_mm(PSF[:, po:po + 1024], ONE1F[:], BOROW, start=False,
              stop=True, n_free=1024, inc="sPE",
              wait=W(("sLs", v_bo)))
        v_pe = sch.v("sPE")
        dv(lambda v, t=t, po=po: v.scalar_tensor_tensor(
            out=XB[:, D * t:D * (t + 1)], in0=PSF[:, po:po + 1024],
            scalar=1.0, in1=XB[:, D * t:D * (t + 1)], op0=OP.mult,
            op1=OP.add), inc="sDV", wait=W(("sPE", v_pe),
                                           ("sDV", sch.v("sDV"))))
    v_x1 = sch.v("sDV")

    RQROW = XNF[0:1, :]
    sp(lambda s: s.dma_start(out=RQROW, in_=di["rq"]), inc="sLs", n=16,
       wait=W(("sDV", v_x1)))
    v_rq = sch.v("sLs")
    pe_mm(PSF[:, 2048:3072], ONE1F[:], RQROW, start=True, stop=True,
          n_free=1024, inc="sPE", wait=W(("sDV", v_x1), ("sLs", v_rq)))
    dv(lambda v: v.tensor_copy(out=R128[:], in_=PSF[:, 2048:3072]),
       inc="sDV", wait=W(("sPE", sch.v("sPE"))))
    v_r128 = sch.v("sDV")

    for t in range(TT):
        xt = XB[:, D * t:D * (t + 1)]
        dv(lambda v, xt=xt: v.bn_stats(out=BN12[:, 0:6], in_=xt[:, 0:512]),
           inc="sDV", wait=W(("sDV", max(v_x1, v_r128))))
        dv(lambda v, xt=xt: v.bn_stats(out=BN12[:, 6:12],
                                       in_=xt[:, 512:1024]), inc="sDV")
        dv(lambda v: v.bn_aggr(out=BN2[:], in_=BN12[:]), inc="sDV",
           wait=W(("sDV", sch.v("sDV"))))
        dv(lambda v: v.tensor_copy(out=MCOL[:], in_=BN2[:, 0:1]), inc="sDV",
           wait=W(("sDV", sch.v("sDV"))))
        v_bn = sch.v("sDV")
        ac(lambda a: a.activation(out=SSQ[:], in_=BN2[:, 1:2], func=AF.Sqrt,
                                  bias=EPSC[:], scale=1.0), inc="sAC",
           wait=W(("sDV", v_bn)))
        dv(lambda v: v.reciprocal(out=RSTD[:], in_=SSQ[:]), inc="sDV",
           wait=W(("sAC", sch.v("sAC"))))
        dv(lambda v, xt=xt: v.tensor_scalar(
            out=XNF[:], in0=xt, scalar1=MCOL[:], scalar2=RSTD[:],
            op0=OP.subtract, op1=OP.mult), inc="sDV",
           wait=W(("sDV", sch.v("sDV"))))
        v_xnf = sch.v("sDV")
        dv(lambda v, t=t: v.scalar_tensor_tensor(
            out=SPROD[:], in0=XNF[:], scalar=1.0, in1=R128[:],
            op0=OP.mult, op1=OP.mult),
           inc="sDV", wait=W(("sDV", max(v_r128, v_xnf))))
        dv(lambda v, t=t: v.reduce_sum(out=SCOL[:, t:t + 1], in_=SPROD[:],
                                       axis=AX),
           inc="sDV", wait=W(("sDV", sch.v("sDV"))))
        ac(lambda a: a.activation(out=SQJ[:], in_=XNF[:], func=AF.Square,
                                  scale=1.0, accum_out=SSQ[:]), inc="sAC",
           wait=W(("sDV", sch.v("sDV"))))
        ac(lambda a: a.activation(out=MCOL[:], in_=SSQ[:], func=AF.Sqrt,
                                  scale=1.0 / D), inc="sAC",
           wait=W(("sAC", sch.v("sAC"))))
        dv(lambda v: v.reciprocal(out=RSN[:], in_=MCOL[:]), inc="sDV",
           wait=W(("sAC", sch.v("sAC"))))
        v_rsn = sch.v("sDV")
        dv(lambda v: v.tensor_scalar(out=RNF[:], in0=XNF[:], scalar1=RSN[:],
                                     scalar2=None, op0=OP.mult),
           inc="sDV", wait=W(("sDV", v_rsn)))
        v_rn = sch.v("sDV")
        for c in range(8):
            pe(lambda p, c=c: p.transpose(
                PSF[:, 3072 + 128 * c:3200 + 128 * c],
                RNF[:, 128 * c:128 * (c + 1)], IDF[:]),
               inc=("sPE" if c == 7 else None),
               wait=(W(("sDV", v_rn)) if c == 0 else ()))
        v_tr = sch.v("sPE")
        dv(lambda v, t=t: v.tensor_copy(
            out=RNT.rearrange("p (c n) -> p c n", n=2048)
                [:, :, 128 * t:128 * (t + 1)],
            in_=PSF[:, 3072:4096].rearrange("p (c n) -> p c n", n=128)),
           inc="sDV", wait=W(("sPE", v_tr)))
    v_ln2 = sch.v("sDV")


    v_wql = load_weight_direct("wql", 512, gcol=16, wait_pe=sch.v("sPE"),
                               dv_guard=sch.v("sDV"), ac_guard=sch.v("sAC"))
    lp_state = {"vevac": sch.v("sAC"), "vvpl": 0}

    def light_projs():
        vq = v_wql[1]
        for fc in range(4):
            for dc in range(8):
                pe_mm(PSF[:, 0:2048],
                      WTMP[:, 512 * dc + 128 * fc:512 * dc + 128 * (fc + 1)],
                      RNT[:, 2048 * dc:2048 * (dc + 1)],
                      start=(dc == 0), stop=(dc == 7), n_free=2048,
                      inc=("sPE" if dc == 7 else None),
                      wait=(W(("sDV", vq), ("sAC", lp_state["vevac"]))
                            if dc == 0 else ()))
                yield
            v_pe = sch.v("sPE")
            ac(lambda a, fc=fc: a.activation(
                out=QLT[:, 2048 * fc:2048 * (fc + 1)], in_=PSF[:, 0:2048],
                func=AF.Copy, scale=1.0), inc="sAC", wait=W(("sPE", v_pe)))
            lp_state["vevac"] = sch.v("sAC")
            yield
        v_wkvl = load_weight_direct("wkvl", 1024, gcol=16,
                                    wait_pe=sch.v("sPE"),
                                    dv_guard=sch.v("sDV"),
                                    ac_guard=lp_state["vevac"])
        vkv = v_wkvl[1]
        yield
        for fc in range(4):
            for dc in range(8):
                pe_mm(PSF[:, 0:2048],
                      WTMP[:, 1024 * dc + 128 * fc:
                           1024 * dc + 128 * (fc + 1)],
                      RNT[:, 2048 * dc:2048 * (dc + 1)],
                      start=(dc == 0), stop=(dc == 7), n_free=2048,
                      inc=("sPE" if dc == 7 else None),
                      wait=(W(("sDV", vkv), ("sAC", lp_state["vevac"]))
                            if dc == 0 else ()))
                yield
            v_pe = sch.v("sPE")
            ac(lambda a, fc=fc: a.activation(
                out=KLT[:, 2048 * fc:2048 * (fc + 1)], in_=PSF[:, 0:2048],
                func=AF.Copy, scale=1.0), inc="sAC", wait=W(("sPE", v_pe)))
            lp_state["vevac"] = sch.v("sAC")
            yield
        for t in range(TT):
            for dc in range(8):
                pe_mm(PSF[:, 0:512],
                      RNT[:, 2048 * dc + 128 * t:2048 * dc + 128 * (t + 1)],
                      WTMP[:, 1024 * dc + 512:1024 * (dc + 1)],
                      start=(dc == 0), stop=(dc == 7), n_free=512,
                      inc=("sPE" if dc == 7 else None),
                      wait=(W(("sAC", lp_state["vevac"]), ("sGP", gp_init))
                            if dc == 0 else ()))
                yield
            v_pe = sch.v("sPE")
            ac(lambda a, t=t: a.activation(
                out=VPL[:, 520 * t:520 * (t + 1)]
                    .rearrange("p (h c) -> p h c", c=65)[:, :, 0:64],
                in_=PSF[:, 0:512], func=AF.Copy, scale=1.0),
               inc="sAC", wait=W(("sPE", v_pe)))
            lp_state["vevac"] = sch.v("sAC")
            lp_state["vvpl"] = sch.v("sAC")
            yield

    lpg = light_projs()

    def pump(n=1):
        for _ in range(n):
            next(lpg, None)

    v_m = dv(lambda v: v.memset(ANEW[:], A1), inc="sDV")
    dv(lambda v: v.tensor_scalar(out=NANEW[:], in0=ANEW[:], scalar1=-1.0,
                                 scalar2=None, op0=OP.mult), inc="sDV",
       wait=W(("sDV", v_m)))
    dv(lambda v: v.tensor_scalar(out=APLK[:], in0=ANEW[:], scalar1=LOGK,
                                 scalar2=None, op0=OP.add), inc="sDV",
       wait=W(("sDV", v_m)))
    v_a = sch.v("sDV")
    for it in range(NIT - 1):
        pe(lambda p: p.matmul(PSF[:, 2048:2049], ONE1F[:], NANEW[:],
                              start=True, stop=True), inc="sPE",
           wait=W(("sDV", v_a)))
        v_brc = sch.v("sPE")
        ac(lambda a: a.activation(out=NAREP[:], in_=PSF[:, 2048:2049],
                                  func=AF.Copy, scale=1.0), inc="sAC",
           wait=W(("sPE", v_brc)))
        v_nar = sch.v("sAC")
        ac(lambda a: a.activation(out=RSC[:], in_=SCOL[:], func=AF.Relu,
                                  bias=NAREP[:], scale=-1.0), inc="sAC",
           wait=W(("sAC", v_nar), ("sDV", v_ln2)))
        ac(lambda a: a.activation(out=EJ16[:], in_=RSC[:], func=AF.Exp,
                                  scale=-1.0, accum_out=EPART[:]), inc="sAC",
           wait=W(("sAC", sch.v("sAC"))))
        v_ep = sch.v("sAC")
        pe(lambda p: p.matmul(PSF[0:1, 2049:2050], EPART[:], ONECF[:],
                              start=True, stop=True), inc="sPE",
           wait=W(("sAC", v_ep)))
        v_sum = sch.v("sPE")
        ac(lambda a: a.activation(out=LNS[:], in_=PSF[0:1, 2049:2050],
                                  func=AF.Ln, scale=1.0), inc="sAC",
           wait=W(("sPE", v_sum)))
        v_lns = sch.v("sAC")
        dv(lambda v: v.scalar_tensor_tensor(
            out=ANEW[:], in0=LNS[:], scalar=-1.0, in1=APLK[:],
            op0=OP.mult, op1=OP.add), inc="sDV", wait=W(("sAC", v_lns)))
        v_an = sch.v("sDV")
        dv(lambda v: v.tensor_scalar(out=APLK[:], in0=ANEW[:], scalar1=LOGK,
                                     scalar2=None, op0=OP.add), inc="sDV",
           wait=W(("sDV", v_an)))
        dv(lambda v: v.tensor_scalar(out=NANEW[:], in0=ANEW[:], scalar1=-1.0,
                                     scalar2=None, op0=OP.mult), inc="sDV",
           wait=W(("sDV", v_an)))
        v_a = sch.v("sDV")
        pump(6)
    pe(lambda p: p.matmul(PSF[:, 2048:2049], ONE1F[:], ANEW[:],
                          start=True, stop=True), inc="sPE",
       wait=W(("sDV", v_a)))
    ac(lambda a: a.activation(out=AREP[:], in_=PSF[:, 2048:2049],
                              func=AF.Copy, scale=1.0), inc="sAC",
       wait=W(("sPE", sch.v("sPE"))))
    dv(lambda v: v.tensor_scalar(out=UCOL[:], in0=SCOL[:], scalar1=AREP[:],
                                 scalar2=0.0, op0=OP.add, op1=OP.min),
       inc="sDV", wait=W(("sAC", sch.v("sAC"))))
    v_ucol = sch.v("sDV")
    pe(lambda p: p.transpose(PSF[0:16, 2048:2176], UCOL[:], IDF[:]),
       inc="sPE", wait=W(("sDV", v_ucol)))
    dv(lambda v: v.tensor_copy(out=U16[:], in_=PSF[0:16, 2048:2176]),
       inc="sDV", wait=W(("sPE", sch.v("sPE"))))
    v_u16 = sch.v("sDV")
    UROW = ARF[0:1, 0:2048]
    sp(lambda s: s.dma_start(out=UROW, in_=U16[:]), inc="sLs", n=16,
       wait=W(("sDV", v_u16)))
    v_urow = sch.v("sLs")
    v_amat = pe_mm(PSF[:, 2048:4096], ONE1F[:], UROW, start=True, stop=True,
                   n_free=2048, inc="sPE", wait=W(("sLs", v_urow)))
    for tb in range(16):
        dv(lambda v, tb=tb: v.tensor_scalar(
            out=OUTF[:, 0:2048], in0=PSF[:, 2048:4096],
            scalar1=UCOL[:, tb:tb + 1], scalar2=None, op0=OP.is_gt),
           inc="sDV", wait=(W(("sPE", v_amat)) if tb == 0 else
                            W(("sDV", sch.v("sDV")))))
        dv(lambda v, tb=tb: v.reduce_sum(out=CCOL[:, tb:tb + 1],
                                         in_=OUTF[:, 0:2048], axis=AX),
           inc="sDV", wait=W(("sDV", sch.v("sDV"))))
        pump(2)
    v_ccol = sch.v("sDV")
    dv(lambda v: v.tensor_scalar(out=SAT16[:], in0=U16[:], scalar1=0.0,
                                 scalar2=None, op0=OP.is_ge), inc="sDV",
       wait=W(("sDV", v_ccol)))
    dv(lambda v: v.tensor_tensor_scan(out=P16[:], data0=SAT16[:],
                                      data1=SAT16[:], initial=0.0,
                                      op0=OP.add, op1=OP.bypass), inc="sDV",
       wait=W(("sDV", sch.v("sDV"))))
    dv(lambda v: v.reduce_sum(out=RS16[:], in_=SAT16[:], axis=AX), inc="sDV",
       wait=W(("sDV", sch.v("sDV"))))
    v_rs = sch.v("sDV")
    pe(lambda p: p.transpose(PSF[0:1, 2048:2064], RS16[:], IDF[0:16, 0:16]),
       inc="sPE", wait=W(("sDV", v_rs)))
    dv(lambda v: v.tensor_copy(out=OFR[:], in_=PSF[0:1, 2048:2064]),
       inc="sDV", wait=W(("sPE", sch.v("sPE"))))
    dv(lambda v: v.tensor_tensor_scan(out=OFR2[:], data0=OFR[:],
                                      data1=OFR[:], initial=0.0,
                                      op0=OP.add, op1=OP.bypass), inc="sDV",
       wait=W(("sDV", sch.v("sDV"))))
    dv(lambda v: v.memset(OFR[:, 0:1], 0.0), inc="sDV",
       wait=W(("sDV", sch.v("sDV"))))
    dv(lambda v: v.tensor_copy(out=OFR[:, 1:16], in_=OFR2[:, 0:15]),
       inc="sDV", wait=W(("sDV", sch.v("sDV"))))
    v_ofr = sch.v("sDV")
    pe(lambda p: p.transpose(PSF[0:16, 2064:2065], OFR[:], IDF[0:1, 0:1]),
       inc="sPE", wait=W(("sDV", v_ofr)))
    dv(lambda v: v.tensor_copy(out=OFF16[:], in_=PSF[0:16, 2064:2065]),
       inc="sDV", wait=W(("sPE", sch.v("sPE"))))
    v_offev = sch.v("sDV")
    dv(lambda v: v.tensor_sub(out=P16[:], in0=P16[:], in1=SAT16[:]),
       inc="sDV", wait=W(("sDV", sch.v("sDV"))))
    dv(lambda v: v.tensor_scalar(out=P16[:], in0=P16[:], scalar1=OFF16[:],
                                 scalar2=None, op0=OP.add), inc="sDV",
       wait=W(("sDV", sch.v("sDV"))))
    dv(lambda v: v.tensor_mul(out=P16[:], in0=P16[:], in1=SAT16[:]),
       inc="sDV", wait=W(("sDV", sch.v("sDV"))))
    v_p16 = sch.v("sDV")
    pe(lambda p: p.transpose(PSF[0:16, 2176:2304], CCOL[:], IDF[:]),
       inc="sPE", wait=W(("sDV", max(v_ccol, v_offev))))
    dv(lambda v: v.tensor_copy(out=C16[:], in_=PSF[0:16, 2176:2304]),
       inc="sDV", wait=W(("sPE", sch.v("sPE"))))
    dv(lambda v: v.tensor_add(out=R16[:] if False else SEL16[:], in0=C16[:],
                              in1=P16[:]), inc="sDV",
       wait=W(("sDV", sch.v("sDV"))))
    dv(lambda v: v.tensor_scalar(out=SEL16[:], in0=SEL16[:], scalar1=1023.5,
                                 scalar2=None, op0=OP.is_lt), inc="sDV",
       wait=W(("sDV", sch.v("sDV"))))
    v_sel16 = sch.v("sDV")
    pe(lambda p: p.transpose(PSF[:, 2304:2320], SEL16[:], IDF[0:16, 0:16]),
       inc="sPE", wait=W(("sDV", v_sel16)))
    dv(lambda v: v.tensor_copy(out=SELCOL[:], in_=PSF[:, 2304:2320]),
       inc="sDV", wait=W(("sPE", sch.v("sPE"))))
    v_sel = sch.v("sDV")

    pump(400)
    v_vpl = lp_state["vvpl"]

    lt_div = sch.v("sDV")
    cte_prev = sch.v("sAC")
    exp_w = {}
    nw = 0
    for h in range(H2):
        chl, rhl = h // 2, (h % 2) * 64
        QLh = QLT[rhl:rhl + 64, 2048 * chl:2048 * (chl + 1)]
        KLh = KLT[rhl:rhl + 64, 2048 * chl:2048 * (chl + 1)]
        for wg in range(2):
            for wi in range(8):
                w = 8 * wg + wi
                pb = 512 * (wi % 2)
                pe(lambda p, w=w, pb=pb, QLh=QLh, KLh=KLh: p.matmul(
                    PSF[:, pb:pb + 128], KLh[:, 128 * w:128 * (w + 1)],
                    QLh[:, 128 * w:128 * (w + 1)], start=True, stop=True),
                   inc="sPE",
                   wait=W(("sAC", max(v_vpl, exp_w.get(nw - 2, 0)))))
                v_qk = sch.v("sPE")
                ac(lambda a, wi=wi, pb=pb: a.activation(
                    out=PTE[:, 128 * wi:128 * (wi + 1)],
                    in_=PSF[:, pb:pb + 128], func=AF.Exp, scale=SC),
                   inc="sAC", wait=W(("sPE", v_qk)))
                exp_w[nw] = sch.v("sAC")
                exp_prev = exp_w[nw]
                nw += 1
                pe(lambda p, w=w, h=h, wi=wi: p.matmul(
                    PSF[0:65, 3072 + 128 * wi:3072 + 128 * (wi + 1)],
                    VPL[:, 520 * w + 65 * h:520 * w + 65 * (h + 1)],
                    PTE[:, 128 * wi:128 * (wi + 1)],
                    start=True, stop=True),
                   inc="sPE", wait=W(("sAC", max(exp_prev, cte_prev)),
                                     ("sDV", v_ccol if (h == 0 and wg == 0
                                                        and wi == 0) else 0)))
            v_av = sch.v("sPE")
            ac(lambda a: a.activation(out=CTE[0:65, :],
                                      in_=PSF[0:65, 3072:4096],
                                      func=AF.Copy, scale=1.0),
               inc="sAC", wait=W(("sPE", v_av), ("sDV", lt_div)))
            cte_prev = sch.v("sAC")
            dv(lambda v: v.reciprocal(out=RZROW[:], in_=CTE[64:65, :]),
               inc="sDV", wait=W(("sAC", sch.v("sAC")),
                                 ("sDV", sch.v("sDV"))))
            dv(lambda v: v.tensor_copy(out=RZROWB[:], in_=RZROW[:]),
               inc="sDV", wait=W(("sDV", sch.v("sDV"))))
            v_rz = sch.v("sDV")
            pe_mm(PSF[0:64, 2048:3072], ONE1B[0:1, 0:64], RZROWB[:],
                  start=True, stop=True, n_free=1024, inc="sPE",
                  wait=W(("sDV", v_rz)))
            v_rb = sch.v("sPE")
            dv(lambda v: v.tensor_copy(out=RCPS[:], in_=PSF[0:64, 2048:3072]),
               inc="sDV", wait=W(("sPE", v_rb), ("sDV", sch.v("sDV"))))
            v_rc = sch.v("sDV")
            dv(lambda v, h=h, wg=wg: v.tensor_tensor(
                out=CTXL[:, 2048 * (h // 2) + 1024 * wg:
                         2048 * (h // 2) + 1024 * (wg + 1)]
                    [(h % 2) * 64:(h % 2) * 64 + 64, :],
                in0=CTE[0:64, :], in1=RCPS[:], op=OP.mult),
               inc="sDV", wait=W(("sDV", v_rc)))
            lt_div = sch.v("sDV")

    load_weight_direct("wqh", 512, gcol=24, wait_pe=sch.v("sPE"),
                       dv_guard=sch.v("sDV"), ac_guard=sch.v("sAC"))
    proj_T(QHT, None, RNT, nf=4, wcols=512)
    load_weight_direct("wkvh_t", 1024, gcol=24, wait_pe=sch.v("sPE"),
                       dv_guard=sch.v("sDV"), ac_guard=sch.v("sAC"))
    vw = sch.v("sDV")
    for pc in range(4):
        for sub in range(2):
            hh = 2 * pc + sub
            for dc in range(8):
                pe_mm(PSF[64 * sub:64 * (sub + 1), 0:2048],
                      WTMP[:, 1024 * dc + 128 * hh:1024 * dc + 128 * hh + 64],
                      RNT[:, 2048 * dc:2048 * (dc + 1)],
                      start=(dc == 0), stop=(dc == 7), n_free=2048,
                      inc=("sPE" if (dc == 7 and sub == 1) else None),
                      wait=(W(("sDV", vw), ("sAC", sch.v("sAC")))
                            if (dc == 0 and sub == 0) else ()))
        v_pe = sch.v("sPE")
        ac(lambda a, pc=pc: a.activation(
            out=KVHT[:, 2048 * pc:2048 * (pc + 1)], in_=PSF[:, 0:2048],
            func=AF.Copy, scale=1.0), inc="sAC", wait=W(("sPE", v_pe)))
    v_light_done = lt_div
    for t in range(TT):
        for dc in range(8):
            pe(lambda p, t=t, dc=dc: p.matmul(
                PSF[:, 0:512],
                RNT[:, 2048 * dc + 128 * t:2048 * dc + 128 * (t + 1)],
                WTMP[:, 1024 * dc:1024 * (dc + 1)]
                    .rearrange("p (h c) -> p h c", c=128)[:, :, 64:128],
                start=(dc == 0), stop=(dc == 7)),
               inc=("sPE" if dc == 7 else None),
               wait=(W(("sAC", sch.v("sAC")), ("sDV", v_light_done),
                       ("sGP", gp_init)) if dc == 0 else ()))
        v_pe = sch.v("sPE")
        ac(lambda a, t=t: a.activation(
            out=VPH[:, 520 * t:520 * (t + 1)]
                .rearrange("p (h c) -> p h c", c=65)[:, :, 0:64],
            in_=PSF[:, 0:512], func=AF.Copy, scale=1.0),
           inc="sAC", wait=W(("sPE", v_pe)))
    v_vph = sch.v("sAC")

    hv_div = sch.v("sDV")
    cte_prev2 = sch.v("sAC")
    for h in range(H2):
        Kh2 = KVHT[(h % 2) * 64:(h % 2) * 64 + 64,
                   2048 * (h // 2):2048 * (h // 2 + 1)]
        Qh2 = QHT[(h % 2) * 64:(h % 2) * 64 + 64,
                  2048 * (h // 2):2048 * (h // 2 + 1)]
        for qh in range(2):
            exp_v = {}
            av_v = {}
            for kt in range(16):
                pb = 1024 * (kt % 2)
                for j in range(2):
                    pe(lambda p, kt=kt, qh=qh, j=j, pb=pb, Kh2=Kh2, Qh2=Qh2:
                       p.matmul(
                           PSF[:, pb + 512 * j:pb + 512 * (j + 1)],
                           Kh2[:, 128 * kt:128 * (kt + 1)],
                           Qh2[:, 1024 * qh + 512 * j:
                               1024 * qh + 512 * (j + 1)],
                           start=True, stop=True),
                       inc=("sPE" if j == 1 else None),
                       wait=(W(("sAC", max(v_vph, exp_v.get(kt - 2, 0))))
                             if j == 0 else ()))
                v_qk = sch.v("sPE")
                ac(lambda a, kt=kt, pb=pb: a.activation(
                    out=PTE[:, 1024 * (kt % 2):1024 * (kt % 2) + 1024],
                    in_=PSF[:, pb:pb + 1024], func=AF.Exp, scale=SC),
                   inc="sAC",
                   wait=W(("sPE", max(v_qk, av_v.get(kt - 2, 0)))))
                exp_v[kt] = sch.v("sAC")
                for j in range(2):
                    pe(lambda p, kt=kt, h=h, j=j: p.matmul(
                        PSF[0:65, 3072 + 512 * j:3072 + 512 * (j + 1)],
                        VPH[:, 520 * kt + 65 * h:520 * kt + 65 * (h + 1)],
                        PTE[:, 1024 * (kt % 2) + 512 * j:
                            1024 * (kt % 2) + 512 * (j + 1)],
                        start=(kt == 0), stop=(kt == 15)),
                       inc=("sPE" if j == 1 else None),
                       wait=(W(("sAC", max(exp_v[kt], cte_prev2)))
                             if j == 0 else ()))
                av_v[kt] = sch.v("sPE")
            v_av = sch.v("sPE")
            ac(lambda a: a.activation(out=CTE[0:65, :],
                                      in_=PSF[0:65, 3072:4096],
                                      func=AF.Copy, scale=1.0),
               inc="sAC", wait=W(("sPE", v_av), ("sDV", hv_div)))
            cte_prev2 = sch.v("sAC")
            dv(lambda v: v.reciprocal(out=RZROW[:], in_=CTE[64:65, :]),
               inc="sDV", wait=W(("sAC", sch.v("sAC")),
                                 ("sDV", sch.v("sDV"))))
            dv(lambda v: v.tensor_copy(out=RZROWB[:], in_=RZROW[:]),
               inc="sDV", wait=W(("sDV", sch.v("sDV"))))
            v_rz = sch.v("sDV")
            pe_mm(PSF[0:64, 2048:3072], ONE1B[0:1, 0:64], RZROWB[:],
                  start=True, stop=True, n_free=1024, inc="sPE",
                  wait=W(("sDV", v_rz)))
            v_rb = sch.v("sPE")
            dv(lambda v: v.tensor_copy(out=RCPS[:], in_=PSF[0:64, 2048:3072]),
               inc="sDV", wait=W(("sPE", v_rb), ("sDV", sch.v("sDV"))))
            v_rc = sch.v("sDV")
            dv(lambda v, h=h, qh=qh: v.tensor_tensor(
                out=CTXH[:, 2048 * (h // 2) + 1024 * qh:
                         2048 * (h // 2) + 1024 * (qh + 1)]
                    [(h % 2) * 64:(h % 2) * 64 + 64, :],
                in0=CTE[0:64, :], in1=RCPS[:], op=OP.mult),
               inc="sDV", wait=W(("sDV", v_rc)))
            hv_div = sch.v("sDV")

    load_weight_direct("wol", 1024, rows=512, woff=0,
                       wait_pe=sch.v("sPE"), dv_guard=sch.v("sDV"))
    v_wol = load_weight_direct("woh", 1024, rows=512, woff=4096)
    st_done = {0: 0, 1: 0}
    for t in range(TT):
        po = 2048 * (t % 2)
        so = 1024 * (t % 2)
        stx = "sSTa" if t % 2 == 0 else "sSTb"
        for dc in range(4):
            pe_mm(PSF[:, po:po + 1024],
                  CTXL[:, 2048 * dc + 128 * t:2048 * dc + 128 * (t + 1)],
                  WTMP[:, 1024 * dc:1024 * (dc + 1)],
                  start=(dc == 0), stop=(dc == 3), n_free=1024,
                  wait=(W(v_wol, ("sDV", max(hv_div, sch.v("sDV"))))
                        if dc == 0 else ()))
        for dc in range(4):
            pe_mm(PSF[:, po + 1024:po + 2048],
                  CTXH[:, 2048 * dc + 128 * t:2048 * dc + 128 * (t + 1)],
                  WTMP[:, 4096 + 1024 * dc:4096 + 1024 * (dc + 1)],
                  start=(dc == 0), stop=(dc == 3), n_free=1024,
                  inc=("sPE" if dc == 3 else None))
        v_pe = sch.v("sPE")
        dv(lambda v, t=t, po=po, so=so: v.scalar_tensor_tensor(
            out=OUTF[:, so:so + 1024], in0=PSF[:, po + 1024:po + 2048],
            scalar=SELCOL[:, t:t + 1], in1=XB[:, D * t:D * (t + 1)],
            op0=OP.mult, op1=OP.add),
           inc="sDV",
           wait=W(("sPE", v_pe), ("sDV", sch.v("sDV")), ("sDV", v_sel),
                  (stx, st_done[t % 2])))
        dv(lambda v, po=po, so=so: v.tensor_tensor(
            out=OUTF[:, so:so + 1024], in0=PSF[:, po:po + 1024],
            in1=OUTF[:, so:so + 1024], op=OP.add),
           inc="sDV", wait=W(("sDV", sch.v("sDV"))))
        v_of = sch.v("sDV")
        sp(lambda s, t=t, so=so: s.dma_start(
            out=out_t[128 * t:128 * (t + 1), :], in_=OUTF[:, so:so + 1024]),
           inc=stx, n=16, wait=W(("sDV", v_of)))
        st_done[t % 2] = sch.v(stx)
    if upto == 'fulldbg':
        v_fin = sch.v("sDV")
        for a in range(8):
            gp(lambda g, a=a: g.dma_start(
                out=dbg_t[128 * a:128 * (a + 1), :],
                in_=CTXL[:, 1024 * a:1024 * (a + 1)]),
               inc="sGW", n=16, wait=W(("sDV", v_fin), ("sGW", sch.v("sGW"))))
        for a in range(8):
            gp(lambda g, a=a: g.dma_start(
                out=dbg_t[1024 + 128 * a:1024 + 128 * (a + 1), :],
                in_=CTXH[:, 1024 * a:1024 * (a + 1)]),
               inc="sGW", n=16, wait=W(("sGW", sch.v("sGW"))))
        for src_ap, c0 in ((SELCOL, 0), (UCOL, 16), (CCOL, 32), (SCOL, 48)):
            gp(lambda g, src_ap=src_ap, c0=c0: g.dma_start(
                out=dbg_t[2048:2176, c0:c0 + 16], in_=src_ap[:, :],
                allow_slow_non_contiguous=True),
               inc="sGW", n=16, wait=W(("sGW", sch.v("sGW"))))
        for j, a in enumerate((6, 7)):
            gp(lambda g, j=j, a=a: g.dma_start(
                out=dbg_t[2176 + 128 * j:2176 + 128 * (j + 1), :],
                in_=XB[:, 1024 * a:1024 * (a + 1)]),
               inc="sGW", n=16, wait=W(("sGW", sch.v("sGW"))))
        sch.waitq("sp", ("sSTa", sch.v("sSTa")), ("sSTb", sch.v("sSTb")),
                  ("sGW", sch.v("sGW")))
    else:
        sch.waitq("sp", ("sSTa", sch.v("sSTa")), ("sSTb", sch.v("sSTb")))

    with nc.Block() as block:
        @block.sync
        def _(h):
            for f in sch.q["sp"]:
                f(h)

        @block.gpsimd
        def _(h):
            for f in sch.q["gp"]:
                f(h)

        @block.vector
        def _(h):
            for f in sch.q["dv"]:
                f(h)

        @block.scalar
        def _(h):
            for f in sch.q["ac"]:
                f(h)

        @block.tensor
        def _(h):
            for f in sch.q["pe"]:
                f(h)

    es.close()
    return nc


_NC_CACHE = [None]


def kernel(L0_attn_x, L0_attn_Wq, L0_attn_bq, L0_attn_Wk, L0_attn_bk,
           L0_attn_Wv, L0_attn_bv, L0_attn_Wo, L0_attn_bo,
           L0_ffn_gamma_light, L0_ffn_wq_light, L0_ffn_wkv_light,
           L0_ffn_wo_light, L0_ffn_q_routing_token, L0_ffn_kv_routing_token,
           L0_ffn_gamma_heavy, L0_ffn_wq_heavy, L0_ffn_wkv_heavy,
           L0_ffn_wo_heavy):
    x = np.ascontiguousarray(np.asarray(L0_attn_x, dtype=np.float32))
    f = lambda a: np.ascontiguousarray(np.asarray(a, dtype=np.float32))
    row = lambda a: np.ascontiguousarray(
        np.asarray(a, dtype=np.float32).reshape(1, -1))
    base = {
        "wq_t": np.ascontiguousarray(f(L0_attn_Wq).T),
        "wk_t": np.ascontiguousarray(f(L0_attn_Wk).T),
        "wv_t": np.ascontiguousarray(f(L0_attn_Wv).T),
        "wo_t": np.ascontiguousarray(f(L0_attn_Wo).T),
        "wkvl": f(L0_ffn_wkv_light),
        "wkvh_t": np.ascontiguousarray(f(L0_ffn_wkv_heavy).T),
        "wql": f(L0_ffn_wq_light), "wqh": f(L0_ffn_wq_heavy),
        "wol": f(L0_ffn_wo_light), "woh": f(L0_ffn_wo_heavy),
        "bq": row(L0_attn_bq), "bk": row(L0_attn_bk), "bv": row(L0_attn_bv),
        "bo": row(L0_attn_bo), "g_l": row(L0_ffn_gamma_light),
        "g_h": row(L0_ffn_gamma_heavy), "rq": row(L0_ffn_q_routing_token),
    }
    if _NC_CACHE[0] is None:
        _NC_CACHE[0] = build_program()
    nc = _NC_CACHE[0]
    in_maps = [dict(base, x=np.ascontiguousarray(x[b])) for b in range(8)]
    res = run_bass_kernel_spmd(nc, in_maps, core_ids=list(range(8)))
    return np.stack([np.asarray(r["out"], dtype=np.float32)
                     for r in res.results], axis=0)


# version 12
# speedup vs baseline: 1.2830x; 1.0039x over previous
import math
from contextlib import ExitStack

import numpy as np
import concourse.bass as bass
import concourse.mybir as mybir
from concourse.bass_utils import run_bass_kernel_spmd

F32 = mybir.dt.float32
BF16 = mybir.dt.bfloat16
AX = mybir.AxisListType.X
OP = mybir.AluOpType
AF = mybir.ActivationFunctionType

T, D, TT = 2048, 1024, 16
H1, H2, HID = 16, 8, 512
NEG = -1e38
SC = 0.125
ROUTE_K = 1152.0
LOGK = float(np.log(np.float32(ROUTE_K)))
A1 = float(np.log(np.float32(ROUTE_K)) - np.log(np.float32(T)))
NIT = 50


class Sched:

    ENGS = ("sp", "gp", "dv", "ac", "pe")

    def __init__(self):
        self.q = {e: [] for e in self.ENGS}
        self.sem = {}
        self.semh = {}

    def op(self, eng, fn, inc=None, n=1, wait=()):
        semh = self.semh
        wait = tuple((s, v) for s, v in wait if v > 0)
        tgt = None
        if inc is not None:
            self.sem[inc] += n
            tgt = (semh[inc], n)

        def run(h, fn=fn, wait=wait, tgt=tgt):
            for s, v in wait:
                h.wait_ge(semh[s], v)
            inst = fn(h)
            if tgt is not None and inst is not None:
                inst.then_inc(*tgt)

        self.q[eng].append(run)
        return self.sem[inc] if inc is not None else None

    def waitq(self, eng, *sv):
        semh = self.semh
        sv = tuple(sv)

        def run(h, sv=sv):
            for s, v in sv:
                if v > 0:
                    h.wait_ge(semh[s], v)

        self.q[eng].append(run)

    def v(self, s):
        return self.sem[s]


def build_program(upto='full'):
    nc = bass.Bass("TRN2", target_bir_lowering=False, debug=False)
    sch = Sched()

    di = {}
    for name, shape in (
        ("x", [T, D]), ("wq_t", [D, D]), ("wk_t", [D, D]), ("wv_t", [D, D]),
        ("wo_t", [D, D]), ("wkvl", [D, D]), ("wkvh_t", [D, D]),
        ("wql", [D, HID]), ("wqh", [D, HID]), ("wol", [HID, D]),
        ("woh", [HID, D]), ("bq", [1, D]), ("bk", [1, D]), ("bv", [1, D]),
        ("bo", [1, D]), ("g_l", [1, D]), ("g_h", [1, D]), ("rq", [1, D]),
    ):
        di[name] = nc.dram_tensor(name, shape, F32, kind="ExternalInput").ap()
    out_t = nc.dram_tensor("out", [T, D], F32, kind="ExternalOutput").ap()
    dbg_t = (nc.dram_tensor("dbg", [2560, 1024], F32, kind="ExternalOutput")
             .ap() if upto == 'fulldbg' else None)

    es = ExitStack()
    Sb = lambda nm, sh, dt=F32: es.enter_context(nc.sbuf_tensor(nm, sh, dt))

    ARN = Sb("ARN", [128, 90384], BF16)
    XB = ARN[:, 0:16384]
    XNT = ARN[:, 16384:32768]
    CT1 = ARN[:, 16384:32768]
    CTXL = ARN[:, 16384:24576]
    CTXH = ARN[:, 24576:32768]
    QT = ARN[:, 32768:49152]
    RNT = ARN[:, 32768:49152]
    KT = ARN[:, 49152:65536]
    QLT = ARN[:, 49152:57344]
    KLT = ARN[:, 57344:65536]
    KVHT = ARN[:, 49152:57344]
    QHT = ARN[:, 57344:65536]
    VP = ARN[:, 65536:82176]
    VPL = ARN[:, 65536:73856]
    VPH = ARN[:, 73856:82176]
    SBT = ARN[:, 82176:86272]
    SBX = ARN[:, 86272:86784]
    PTE = ARN[:, 86784:88832]
    CND = ARN[:, 88832:89088]
    CN2 = ARN[:, 89088:89344]
    R8 = ARN[:, 89344:89360]
    CTE = ARN[:, 89360:90384]
    WTMP = ARN[:, 82176:90368]

    ARF = Sb("ARF", [128, 4096], F32)
    SBMB = Sb("SBMB", [128, 4096], BF16)
    NEGI = Sb("NEGI", [128, 128], BF16)
    SBM = ARF[:, 0:4096]
    OUTF = ARF[:, 0:2048]
    SPROD = ARF[:, 1024:2048]
    XNF = ARF[:, 2048:3072]
    R128 = ARF[:, 3072:4096]

    IDF = Sb("IDF", [128, 128], F32)
    BIASC = Sb("BIASC", [128, 32], F32)
    BVROW = ARF[0:1, 3072:4096]
    BOROW = None
    ONE1F = Sb("ONE1F", [1, 128], F32)
    ONE1B = Sb("ONE1B", [1, 128], BF16)
    ONECF = Sb("ONECF", [128, 1], F32)
    EPSC = Sb("EPSC", [128, 1], F32)
    TCOL = Sb("TCOL", [128, 2], F32)
    RZROWB = Sb("RZROWB", [1, 1024], BF16)
    RCPS = Sb("RCPS", [64, 1024], BF16)
    MCOL = Sb("MCOL", [128, 1], F32)
    RSTD = Sb("RSTD", [128, 1], F32)
    SSQ = Sb("SSQ", [128, 1], F32)
    RSN = Sb("RSN", [128, 1], F32)
    BN12 = Sb("BN12", [128, 12], F32)
    BN2 = Sb("BN2", [128, 2], F32)
    RNF = ARF[:, 0:1024]
    SQJ = ARN[:, 86784:87808]
    SCOL = Sb("SCOL", [128, 16], F32)
    UCOL = Sb("UCOL", [128, 16], F32)
    CCOL = Sb("CCOL", [128, 16], F32)
    SELCOL = Sb("SELCOL", [128, 16], F32)
    U16 = ARF[0:16, 3072:3200]
    SAT16 = ARF[0:16, 3200:3328]
    P16 = ARF[0:16, 3328:3456]
    SEL16 = ARF[0:16, 3456:3584]
    C16 = U16
    RS16 = ARF[0:16, 3584:3585]
    OFF16 = ARF[0:16, 3585:3586]
    OFR = ARF[0:1, 3586:3602]
    OFR2 = ARF[0:1, 3602:3618]
    AREP = ARF[:, 3618:3619]
    NAREP = ARF[:, 3619:3620]
    RSC = ARF[:, 3620:3636]
    EJ16 = Sb("EJ16", [128, 16], BF16)
    EPART = Sb("EPART", [128, 1], F32)
    LNS = Sb("LNS", [1, 1], F32)
    ANEW = Sb("ANEW", [1, 1], F32)
    NANEW = Sb("NANEW", [1, 1], F32)
    APLK = Sb("APLK", [1, 1], F32)

    PSF = es.enter_context(nc.psum_tensor("PSF", [128, 4096], F32))
    RZROW = PSF[64:65, 2048:3072]

    for s in ("sLs", "sLx0", "sLx1", "sLw", "sSTa", "sSTb",
              "sPE", "sDV", "sAC", "sGP", "sGW"):
        sch.semh[s] = es.enter_context(nc.semaphore(s))
        sch.sem[s] = 0

    W = lambda *sv: tuple(sv)
    sp = lambda fn, **k: sch.op("sp", fn, **k)
    gp = lambda fn, **k: sch.op("gp", fn, **k)
    dv = lambda fn, **k: sch.op("dv", fn, **k)
    ac = lambda fn, **k: sch.op("ac", fn, **k)
    pe = lambda fn, **k: sch.op("pe", fn, **k)

    def pe_mm(out_ap, lhs, rhs, start, stop, n_free, is_tr=False,
              inc=None, wait=()):
        nslc = (n_free + 511) // 512
        for j in range(nslc):
            c0, c1 = 512 * j, min(512 * (j + 1), n_free)
            pe(lambda p, c0=c0, c1=c1, out_ap=out_ap, lhs=lhs, rhs=rhs,
               start=start, stop=stop, is_tr=is_tr: p.matmul(
                   out_ap[:, c0:c1], lhs, rhs[:, c0:c1], start=start,
                   stop=stop, is_transpose=is_tr),
               inc=(inc if j == nslc - 1 else None),
               wait=(wait if j == 0 else ()))
        return sch.v(inc) if inc else None

    gp(lambda g: g.memset(IDF[:], 0.0), inc="sGP")
    gp(lambda g: g.affine_select(
        out=IDF[:], in_=IDF[:], compare_op=OP.not_equal, fill=1.0, base=0,
        pattern=[[-1, 128]], channel_multiplier=1), inc="sGP",
        wait=W(("sGP", sch.v("sGP"))))
    gp(lambda g: g.memset(NEGI[:], 0.0), inc="sGP")
    gp(lambda g: g.affine_select(
        out=NEGI[:], in_=NEGI[:], compare_op=OP.not_equal, fill=-1e30,
        base=0, pattern=[[-1, 128]], channel_multiplier=1), inc="sGP",
        wait=W(("sGP", sch.v("sGP"))))
    gp(lambda g: g.memset(ONE1F[:], 1.0), inc="sGP")
    gp(lambda g: g.memset(ONE1B[:], 1.0), inc="sGP")
    gp(lambda g: g.memset(ONECF[:], 1.0), inc="sGP")
    gp(lambda g: g.memset(EPSC[:], 1e-5), inc="sGP")
    gp(lambda g: g.memset(VP[:], 1.0), inc="sGP")
    gp_init = sch.v("sGP")

    BQROW = ARF[32:33, 3072:4096]
    BKROW = ARF[64:65, 3072:4096]
    BQROWB = ARN[0:1, 67584:68608]
    BKROWB = ARN[0:1, 68608:69632]
    ONESROW = VP[0:1, 0:2048]
    for rowt, name in ((BQROW, "bq"), (BKROW, "bk")):
        sp(lambda s, rowt=rowt, name=name: s.dma_start(
            out=rowt, in_=di[name]), inc="sLs", n=16)
    for name, cs in (("g_l", 16), ("g_h", 24)):
        sp(lambda s, name=name, cs=cs: s.dma_start(
            out=BIASC[:, cs:cs + 8],
            in_=di[name].rearrange("o (c p) -> (o p) c", p=128),
            allow_slow_non_contiguous=True), inc="sLs", n=16)
    sp(lambda s: s.dma_start(out=BVROW, in_=di["bv"]), inc="sLs", n=16)
    v_small = sch.v("sLs")
    dv(lambda v: v.tensor_copy(out=BQROWB, in_=BQROW), inc="sDV",
       wait=W(("sLs", v_small), ("sGP", gp_init)))
    dv(lambda v: v.tensor_copy(out=BKROWB, in_=BKROW), inc="sDV",
       wait=W(("sLs", v_small), ("sGP", gp_init)))

    ln_done = {}
    xnf_done = {}
    for t in range(TT):
        slx = "sLx0"
        sp(lambda s, t=t: s.dma_start(
            out=XNF[:], in_=di["x"][128 * t:128 * (t + 1), :]),
           inc=slx, n=16, wait=W(("sDV", xnf_done.get(t - 1, 0))))
        vld = sch.v(slx)
        dv(lambda v, t=t: v.tensor_copy(
            out=XB[:, D * t:D * (t + 1)], in_=XNF[:]),
           inc="sDV", wait=W((slx, vld)))
        dv(lambda v: v.bn_stats(out=BN12[:, 0:6],
                                in_=XNF[:, 0:512]), inc="sDV")
        dv(lambda v: v.bn_stats(out=BN12[:, 6:12],
                                in_=XNF[:, 512:1024]),
           inc="sDV")
        dv(lambda v: v.bn_aggr(out=BN2[:], in_=BN12[:]), inc="sDV",
           wait=W(("sDV", sch.v("sDV"))))
        dv(lambda v: v.tensor_copy(out=MCOL[:], in_=BN2[:, 0:1]), inc="sDV",
           wait=W(("sDV", sch.v("sDV"))))
        v_bn = sch.v("sDV")
        ac(lambda a: a.activation(out=SSQ[:], in_=BN2[:, 1:2], func=AF.Sqrt,
                                  bias=EPSC[:], scale=1.0), inc="sAC",
           wait=W(("sDV", v_bn), ("sGP", gp_init)))
        dv(lambda v: v.reciprocal(out=RSTD[:], in_=SSQ[:]), inc="sDV",
           wait=W(("sAC", sch.v("sAC"))))
        dv(lambda v: v.tensor_scalar(
            out=XNF[:], in0=XNF[:], scalar1=MCOL[:],
            scalar2=RSTD[:], op0=OP.subtract, op1=OP.mult), inc="sDV",
           wait=W(("sDV", sch.v("sDV"))))
        v_xnf = sch.v("sDV")
        for c in range(8):
            pe(lambda p, c=c: p.transpose(
                PSF[:, 3072 + 128 * c:3200 + 128 * c],
                XNF[:, 128 * c:128 * (c + 1)], IDF[:]),
               inc=("sPE" if c == 7 else None),
               wait=(W(("sDV", v_xnf), ("sGP", 2),
                       ("sDV", ln_done.get(t - 1, 0))) if c == 0 else ()))
        v_tr = sch.v("sPE")
        dv(lambda v, t=t: v.tensor_copy(
            out=XNT.rearrange("p (c n) -> p c n", n=2048)
                [:, :, 128 * t:128 * (t + 1)],
            in_=PSF[:, 3072:4096].rearrange("p (c n) -> p c n", n=128)),
           inc="sDV", wait=W(("sPE", v_tr)))
        ln_done[t] = sch.v("sDV")
        xnf_done[t] = sch.v("sDV")

    def load_weight_direct(name, ncols, rows=1024, woff=0, wait_pe=0,
                           dv_guard=0, ac_guard=0, gcol=None):
        vs = []
        for c in range(rows // 128):
            gp(lambda g, c=c, name=name, ncols=ncols, woff=woff: g.dma_start(
                out=WTMP[:, woff + ncols * c:woff + ncols * (c + 1)],
                in_=di[name][128 * c:128 * (c + 1), 0:ncols]),
               inc="sGW", n=16,
               wait=(W(("sGW", sch.v("sGW")), ("sPE", wait_pe),
                       ("sDV", dv_guard), ("sAC", ac_guard)) if c == 0 else
                     W(("sGW", sch.v("sGW")))))
            vs.append(sch.v("sGW"))
        if gcol is None:
            return ("sGW", sch.v("sGW"))
        for c in range(rows // 128):
            dv(lambda v, c=c, ncols=ncols, woff=woff, gcol=gcol:
               v.tensor_scalar(
                   out=WTMP[:, woff + ncols * c:woff + ncols * (c + 1)],
                   in0=WTMP[:, woff + ncols * c:woff + ncols * (c + 1)],
                   scalar1=BIASC[:, gcol + c:gcol + c + 1],
                   scalar2=None, op0=OP.mult),
               inc="sDV", wait=W(("sGW", vs[c]), ("sLs", v_small),
                                 ("sDV", sch.v("sDV"))))
        return ("sDV", sch.v("sDV"))

    def proj_T(dst, bias_row, src_T, nf=8, wcols=1024, wwait=None):
        if wwait is None:
            wwait = ("sDV", sch.v("sDV"))
        for fc in range(nf):
            for dc in range(8):
                pe_mm(PSF[:, 0:2048],
                      WTMP[:, wcols * dc + 128 * fc:
                           wcols * dc + 128 * (fc + 1)],
                      src_T[:, 2048 * dc:2048 * (dc + 1)],
                      start=(dc == 0), stop=(dc == 7 and bias_row is None),
                      n_free=2048,
                      inc=("sPE" if (dc == 7 and bias_row is None) else None),
                      wait=(W(wwait, ("sDV", sch.v("sDV")),
                              ("sAC", sch.v("sAC")),
                              ("sLs", v_small)) if dc == 0 else ()))
            if bias_row is not None:
                pe_mm(PSF[:, 0:2048], bias_row[0:1, 128 * fc:128 * (fc + 1)],
                      ONESROW, start=False, stop=True, n_free=2048,
                      inc="sPE", wait=W(("sDV", sch.v("sDV")),
                                        ("sGP", gp_init)))
            vpe = sch.v("sPE")
            ac(lambda a, fc=fc, dst=dst: a.activation(
                out=dst[:, 2048 * fc:2048 * (fc + 1)], in_=PSF[:, 0:2048],
                func=AF.Copy, scale=1.0), inc="sAC", wait=W(("sPE", vpe)))
        return sch.v("sAC")

    v_w = load_weight_direct("wq_t", 1024)
    proj_T(QT, BQROWB, XNT, wwait=v_w)
    v_w = load_weight_direct("wk_t", 1024, wait_pe=sch.v("sPE"))
    proj_T(KT, BKROWB, XNT, wwait=v_w)
    v_w = load_weight_direct("wv_t", 1024, wait_pe=sch.v("sPE"))
    for t in range(TT):
        for dc in range(8):
            pe_mm(PSF[:, 0:1024],
                  XNT[:, 2048 * dc + 128 * t:2048 * dc + 128 * (t + 1)],
                  WTMP[:, 1024 * dc:1024 * (dc + 1)],
                  start=(dc == 0), stop=False, n_free=1024,
                  wait=(W(v_w, ("sDV", sch.v("sDV")),
                          ("sAC", sch.v("sAC")),
                          ("sLs", v_small)) if dc == 0 else ()))
        pe_mm(PSF[:, 0:1024], ONE1F[:], BVROW[:], start=False, stop=True,
              n_free=1024, inc="sPE")
        vpe = sch.v("sPE")
        ac(lambda a, t=t: a.activation(
            out=VP[:, 1040 * t:1040 * (t + 1)]
                .rearrange("p (h c) -> p h c", c=65)[:, :, 0:64],
            in_=PSF[:, 0:1024], func=AF.Copy, scale=1.0),
           inc="sAC", wait=W(("sPE", vpe), ("sGP", gp_init)))
    v_vp = sch.v("sAC")

    evac_done = {}
    pen_done = {}
    qk_done = {}
    sel_tcol = {}
    pen0_relu = {}
    st_last = {}
    exp_pair = {}
    av_kt = {}
    div_done = 0

    def b1_qh_kh(ei):
        h = ei // 8
        ch, rh = h // 2, (h % 2) * 64
        return (QT[rh:rh + 64, 2048 * ch:2048 * (ch + 1)],
                KT[rh:rh + 64, 2048 * ch:2048 * (ch + 1)])

    def emit_qksel(ei, qi):
        Qh, Kh = b1_qh_kh(ei)
        qt = 2 * (ei % 8) + qi
        ev = evac_done.get((ei, 0), 0) if qi == 1 else \
            evac_done.get((ei - 1, 1), 0)
        for j in range(4):
            pe(lambda p, qt=qt, j=j, Qh=Qh, Kh=Kh: p.matmul(
                PSF[:, 512 * j:512 * (j + 1)],
                Qh[:, 128 * qt:128 * (qt + 1)],
                Kh[:, 512 * j:512 * (j + 1)],
                start=True, stop=True),
               inc=("sPE" if j == 3 else None),
               wait=(W(("sAC", max(ev, v_vp))) if j == 0 else ()))
        qk_done[(ei, qi)] = sch.v("sPE")

    def emit_evac(ei, qi):
        sb = SBT[:, 2048 * qi:2048 * (qi + 1)]
        psem, pval = pen_done.get((ei - 1, qi), ("sDV", 0))
        ac(lambda a, sb=sb: a.activation(out=sb, in_=PSF[:, 0:2048],
                                         func=AF.Copy, scale=1.0),
           inc="sAC", wait=W(("sPE", qk_done[(ei, qi)]), (psem, pval)))
        evac_done[(ei, qi)] = sch.v("sAC")

    def emit_sel_l1(ei, qi):
        sb = SBT[:, 2048 * qi:2048 * (qi + 1)]
        cnd = (CND if qi == 0 else CN2)[:, 0:256]
        for c in range(16):
            dv(lambda v, c=c, sb=sb, cnd=cnd: v.max(
                out=cnd[:, 8 * c:8 * (c + 1)],
                in_=sb[:, 128 * c:128 * (c + 1)]),
               inc="sDV",
               wait=(W(("sAC", evac_done[(ei, qi)]), ("sDV", sch.v("sDV")))
                     if c == 0 else ()))
        v_r1 = sch.v("sDV")
        vgrp = v_r1
        for grp in range(4):
            for c in range(4 * grp, 4 * grp + 4):
                dv(lambda v, c=c, sb=sb, cnd=cnd: v.match_replace(
                    out=SBX[:, 128 * (c % 4):128 * (c % 4) + 128],
                    in_to_replace=cnd[:, 8 * c:8 * (c + 1)],
                    in_values=sb[:, 128 * c:128 * (c + 1)],
                    imm_value=NEG),
                   inc="sDV",
                   wait=(W(("sDV", vgrp)) if c % 4 == 0 else ()))
            v_mr = sch.v("sDV")
            for c in range(4 * grp, 4 * grp + 4):
                dv(lambda v, c=c, cnd=cnd: v.max(
                    out=cnd[:, 128 + 8 * c:128 + 8 * (c + 1)],
                    in_=SBX[:, 128 * (c % 4):128 * (c % 4) + 128]),
                   inc="sDV",
                   wait=(W(("sDV", v_mr)) if c % 4 == 0 else ()))
            vgrp = sch.v("sDV")
        return sch.v("sDV")

    def emit_sel_l2(ei, v_l1):
        cands = (CND[:, 0:256], CN2[:, 0:256])
        last = {0: v_l1[0], 1: v_l1[1]}
        for rr in range(5):
            for qi in range(2):
                dv(lambda v, qi=qi: v.max(out=R8[:, 8 * qi:8 * (qi + 1)],
                                          in_=cands[qi]),
                   inc="sDV", wait=W(("sDV", last[qi])))
                last[qi] = sch.v("sDV")
            if rr < 4:
                for qi in range(2):
                    dv(lambda v, qi=qi: v.match_replace(
                        out=cands[qi], in_to_replace=R8[:, 8 * qi:8 * (qi + 1)],
                        in_values=cands[qi], imm_value=NEG),
                       inc="sDV", wait=W(("sDV", last[qi])))
                    last[qi] = sch.v("sDV")
        for qi in range(2):
            dv(lambda v, qi=qi: v.tensor_copy(
                out=TCOL[:, qi:qi + 1], in_=R8[:, 8 * qi + 5:8 * qi + 6]),
               inc="sDV", wait=W(("sDV", last[qi])))
            sel_tcol[(ei, qi)] = sch.v("sDV")
            emit_pen(ei, qi)

    def emit_pen(ei, qi):
        sb = SBT[:, 2048 * qi:2048 * (qi + 1)]
        sm = SBMB[:, 2048 * qi:2048 * (qi + 1)]
        dv(lambda v, sb=sb, sm=sm, qi=qi: v.tensor_scalar(
            out=sm, in0=sb, scalar1=TCOL[:, qi:qi + 1], scalar2=None,
            op0=OP.is_lt),
           inc="sDV", wait=W(("sDV", sch.v("sDV")),
                             ("sPE", st_last.get(ei - 1, 0))))
        pen_done[(ei, qi)] = ("sDV", sch.v("sDV"))

    def emit_av(bi, g, h, ctx_c0):
        for sub in range(2):
            kt = 2 * g + sub
            w = (W(("sAC", exp_pair[(bi, g)]), ("sDV", div_done))
                 if kt == 0 else
                 (W(("sAC", exp_pair[(bi, g)])) if sub == 0 else ()))
            pe(lambda p, kt=kt, h=h, ctx_c0=ctx_c0: p.matmul(
                PSF[0:65, ctx_c0:ctx_c0 + 256],
                VP[:, 1040 * kt + 65 * h:1040 * kt + 65 * (h + 1)],
                PTE[:, 256 * (kt % 8):256 * (kt % 8) + 256],
                start=(kt == 0), stop=(kt == 15)),
               inc="sPE", wait=w)
            av_kt[(bi, kt)] = sch.v("sPE")

    pend_div = None
    rcps_gate = 0

    def emit_div2(pd):
        dbi, v_rzb = pd
        h, e8 = dbi // 8, dbi % 8
        half = e8 // 4
        pe_mm(PSF[0:64, 2048:3072], ONE1B[0:1, 0:64], RZROWB[:],
              start=True, stop=True, n_free=1024, inc="sPE",
              wait=W(("sAC", v_rzb)))
        v_rb = sch.v("sPE")
        ac(lambda a: a.activation(out=RCPS[:],
                                  in_=PSF[0:64, 2048:3072],
                                  func=AF.Copy, scale=1.0),
           inc="sAC", wait=W(("sPE", v_rb), ("sAC", sch.v("sAC"))))
        v_rc = sch.v("sAC")
        dv(lambda v, h=h, half=half: v.tensor_tensor(
            out=CT1[:, 2048 * (h // 2) + 1024 * half:
                    2048 * (h // 2) + 1024 * (half + 1)]
                [(h % 2) * 64:(h % 2) * 64 + 64, :],
            in0=CTE[0:64, :], in1=RCPS[:], op=OP.mult),
           inc="sDV", wait=W(("sAC", v_rc), ("sDV", sch.v("sDV"))))
        return v_rc, sch.v("sDV")

    for step in range(130):
        ei, bi = step, step - 1
        if ei == 0:
            emit_qksel(0, 0)
        if pend_div is not None:
            rcps_gate, div_done = emit_div2(pend_div)
            pend_div = None
        if ei < 128:
            emit_evac(ei, 0)
        if bi >= 0 and bi < 128:
            Qh, Kh = b1_qh_kh(bi)
            h, e8 = bi // 8, bi % 8
            q0 = 256 * e8
            ctx_c0 = 3072 + 256 * (e8 % 4)
            st_v = {}
            psem0, pval0 = pen_done[(bi, 0)]
            _s1, pval1 = pen_done[(bi, 1)]
            for g in range(8):
                wbase = 2048 + 512 * (g % 2)
                wguard = (exp_pair[(bi, g - 2)] if g >= 2 else
                          exp_pair.get((bi - 1, g + 6), 0))
                for sub in range(2):
                    kt = 2 * g + sub
                    po = wbase + 256 * sub
                    if g == 0 and sub == 0:
                        pw = W((psem0, pval0),
                               ("sDV", max(pval1, div_done)),
                               ("sAC", max(wguard, rcps_gate)))
                    elif sub == 0:
                        pw = W(("sAC", wguard))
                    else:
                        pw = ()
                    pe(lambda p, kt=kt, po=po, q0=q0, Qh=Qh, Kh=Kh:
                       p.matmul(PSF[:, po:po + 256],
                                Kh[:, 128 * kt:128 * (kt + 1)],
                                Qh[:, q0:q0 + 256],
                                start=True, stop=False),
                       wait=pw)
                    for qi2 in range(2):
                        pe(lambda p, kt=kt, po=po, qi2=qi2: p.matmul(
                            PSF[:, po + 128 * qi2:po + 128 * (qi2 + 1)],
                            SBMB[:, 2048 * qi2 + 128 * kt:
                                 2048 * qi2 + 128 * (kt + 1)],
                            NEGI[:], start=False, stop=(qi2 == 1)),
                           inc=("sPE" if qi2 == 1 else None))
                    st_v[kt] = sch.v("sPE")
                pguard = (av_kt.get((bi, 2 * g - 7), 0) if g >= 4 else
                          av_kt.get((bi - 1, 2 * g + 9), 0))
                ac(lambda a, g=g, wbase=wbase: a.activation(
                    out=PTE[:, 512 * (g % 4):512 * (g % 4) + 512],
                    in_=PSF[:, wbase:wbase + 512], func=AF.Exp, scale=SC),
                   inc="sAC", wait=W(("sPE", max(st_v[2 * g + 1], pguard))))
                exp_pair[(bi, g)] = sch.v("sAC")
                if g >= 1:
                    emit_av(bi, g - 1, h, ctx_c0)
                if g == 2 and ei < 128:
                    emit_qksel(ei, 1)
                    emit_evac(ei, 1)
                if g == 6 and ei + 1 <= 127:
                    emit_qksel(ei + 1, 0)
            emit_av(bi, 7, h, ctx_c0)
            st_last[bi] = st_v[15]
        if ei < 128 and (bi < 0 or bi >= 128):
            emit_qksel(ei, 1)
            emit_evac(ei, 1)
        if bi < 0 and ei + 1 <= 127:
            emit_qksel(ei + 1, 0)
        if ei < 128:
            v_l1 = {0: emit_sel_l1(ei, 0)}
            v_l1[1] = emit_sel_l1(ei, 1)
            emit_sel_l2(ei, v_l1)
        if bi >= 0 and bi < 128 and bi % 4 == 3:
            v_ava = av_kt[(bi, 15)]
            ac(lambda a: a.activation(out=CTE[0:65, :],
                                      in_=PSF[0:65, 3072:4096],
                                      func=AF.Copy, scale=1.0),
               inc="sAC", wait=W(("sPE", v_ava), ("sDV", div_done),
                                 ("sAC", sch.v("sAC"))))
            cte_v = sch.v("sAC")
            dv(lambda v: v.reciprocal(out=RZROW[:], in_=CTE[64:65, :]),
               inc="sDV", wait=W(("sAC", cte_v), ("sDV", sch.v("sDV"))))
            v_rz0 = sch.v("sDV")
            ac(lambda a: a.activation(out=RZROWB[:], in_=RZROW[:],
                                      func=AF.Copy, scale=1.0),
               inc="sAC", wait=W(("sDV", v_rz0), ("sAC", sch.v("sAC"))))
            pend_div = (bi, sch.v("sAC"))

    v_wo = load_weight_direct("wo_t", 1024, wait_pe=sch.v("sPE"),
                              dv_guard=sch.v("sDV"))
    BOROW = XNF[0:1, :]
    sp(lambda s: s.dma_start(out=BOROW, in_=di["bo"]), inc="sLs", n=16,
       wait=W(("sDV", sch.v("sDV"))))
    v_bo = sch.v("sLs")
    for t in range(TT):
        po = 1024 * (t % 2)
        for dc in range(8):
            pe_mm(PSF[:, po:po + 1024],
                  CT1[:, 2048 * dc + 128 * t:2048 * dc + 128 * (t + 1)],
                  WTMP[:, 1024 * dc:1024 * (dc + 1)],
                  start=(dc == 0), stop=False, n_free=1024,
                  wait=(W(v_wo, ("sDV", max(div_done, sch.v("sDV"))))
                        if dc == 0 else ()))
        pe_mm(PSF[:, po:po + 1024], ONE1F[:], BOROW, start=False,
              stop=True, n_free=1024, inc="sPE",
              wait=W(("sLs", v_bo)))
        v_pe = sch.v("sPE")
        dv(lambda v, t=t, po=po: v.scalar_tensor_tensor(
            out=XB[:, D * t:D * (t + 1)], in0=PSF[:, po:po + 1024],
            scalar=1.0, in1=XB[:, D * t:D * (t + 1)], op0=OP.mult,
            op1=OP.add), inc="sDV", wait=W(("sPE", v_pe),
                                           ("sDV", sch.v("sDV"))))
    v_x1 = sch.v("sDV")

    RQROW = XNF[0:1, :]
    sp(lambda s: s.dma_start(out=RQROW, in_=di["rq"]), inc="sLs", n=16,
       wait=W(("sDV", v_x1)))
    v_rq = sch.v("sLs")
    pe_mm(PSF[:, 2048:3072], ONE1F[:], RQROW, start=True, stop=True,
          n_free=1024, inc="sPE", wait=W(("sDV", v_x1), ("sLs", v_rq)))
    dv(lambda v: v.tensor_copy(out=R128[:], in_=PSF[:, 2048:3072]),
       inc="sDV", wait=W(("sPE", sch.v("sPE"))))
    v_r128 = sch.v("sDV")

    for t in range(TT):
        xt = XB[:, D * t:D * (t + 1)]
        dv(lambda v, xt=xt: v.bn_stats(out=BN12[:, 0:6], in_=xt[:, 0:512]),
           inc="sDV", wait=W(("sDV", max(v_x1, v_r128))))
        dv(lambda v, xt=xt: v.bn_stats(out=BN12[:, 6:12],
                                       in_=xt[:, 512:1024]), inc="sDV")
        dv(lambda v: v.bn_aggr(out=BN2[:], in_=BN12[:]), inc="sDV",
           wait=W(("sDV", sch.v("sDV"))))
        dv(lambda v: v.tensor_copy(out=MCOL[:], in_=BN2[:, 0:1]), inc="sDV",
           wait=W(("sDV", sch.v("sDV"))))
        v_bn = sch.v("sDV")
        ac(lambda a: a.activation(out=SSQ[:], in_=BN2[:, 1:2], func=AF.Sqrt,
                                  bias=EPSC[:], scale=1.0), inc="sAC",
           wait=W(("sDV", v_bn)))
        dv(lambda v: v.reciprocal(out=RSTD[:], in_=SSQ[:]), inc="sDV",
           wait=W(("sAC", sch.v("sAC"))))
        dv(lambda v, xt=xt: v.tensor_scalar(
            out=XNF[:], in0=xt, scalar1=MCOL[:], scalar2=RSTD[:],
            op0=OP.subtract, op1=OP.mult), inc="sDV",
           wait=W(("sDV", sch.v("sDV"))))
        v_xnf = sch.v("sDV")
        dv(lambda v, t=t: v.scalar_tensor_tensor(
            out=SPROD[:], in0=XNF[:], scalar=1.0, in1=R128[:],
            op0=OP.mult, op1=OP.mult),
           inc="sDV", wait=W(("sDV", max(v_r128, v_xnf))))
        dv(lambda v, t=t: v.reduce_sum(out=SCOL[:, t:t + 1], in_=SPROD[:],
                                       axis=AX),
           inc="sDV", wait=W(("sDV", sch.v("sDV"))))
        ac(lambda a: a.activation(out=SQJ[:], in_=XNF[:], func=AF.Square,
                                  scale=1.0, accum_out=SSQ[:]), inc="sAC",
           wait=W(("sDV", sch.v("sDV"))))
        ac(lambda a: a.activation(out=MCOL[:], in_=SSQ[:], func=AF.Sqrt,
                                  scale=1.0 / D), inc="sAC",
           wait=W(("sAC", sch.v("sAC"))))
        dv(lambda v: v.reciprocal(out=RSN[:], in_=MCOL[:]), inc="sDV",
           wait=W(("sAC", sch.v("sAC"))))
        v_rsn = sch.v("sDV")
        dv(lambda v: v.tensor_scalar(out=RNF[:], in0=XNF[:], scalar1=RSN[:],
                                     scalar2=None, op0=OP.mult),
           inc="sDV", wait=W(("sDV", v_rsn)))
        v_rn = sch.v("sDV")
        for c in range(8):
            pe(lambda p, c=c: p.transpose(
                PSF[:, 3072 + 128 * c:3200 + 128 * c],
                RNF[:, 128 * c:128 * (c + 1)], IDF[:]),
               inc=("sPE" if c == 7 else None),
               wait=(W(("sDV", v_rn)) if c == 0 else ()))
        v_tr = sch.v("sPE")
        dv(lambda v, t=t: v.tensor_copy(
            out=RNT.rearrange("p (c n) -> p c n", n=2048)
                [:, :, 128 * t:128 * (t + 1)],
            in_=PSF[:, 3072:4096].rearrange("p (c n) -> p c n", n=128)),
           inc="sDV", wait=W(("sPE", v_tr)))
    v_ln2 = sch.v("sDV")


    v_wql = load_weight_direct("wql", 512, gcol=16, wait_pe=sch.v("sPE"),
                               dv_guard=sch.v("sDV"), ac_guard=sch.v("sAC"))
    lp_state = {"vevac": sch.v("sAC"), "vvpl": 0}

    def light_projs():
        vq = v_wql[1]
        for fc in range(4):
            for dc in range(8):
                pe_mm(PSF[:, 0:2048],
                      WTMP[:, 512 * dc + 128 * fc:512 * dc + 128 * (fc + 1)],
                      RNT[:, 2048 * dc:2048 * (dc + 1)],
                      start=(dc == 0), stop=(dc == 7), n_free=2048,
                      inc=("sPE" if dc == 7 else None),
                      wait=(W(("sDV", vq), ("sAC", lp_state["vevac"]))
                            if dc == 0 else ()))
                yield
            v_pe = sch.v("sPE")
            ac(lambda a, fc=fc: a.activation(
                out=QLT[:, 2048 * fc:2048 * (fc + 1)], in_=PSF[:, 0:2048],
                func=AF.Copy, scale=1.0), inc="sAC", wait=W(("sPE", v_pe)))
            lp_state["vevac"] = sch.v("sAC")
            yield
        v_wkvl = load_weight_direct("wkvl", 1024, gcol=16,
                                    wait_pe=sch.v("sPE"),
                                    dv_guard=sch.v("sDV"),
                                    ac_guard=lp_state["vevac"])
        vkv = v_wkvl[1]
        yield
        for fc in range(4):
            for dc in range(8):
                pe_mm(PSF[:, 0:2048],
                      WTMP[:, 1024 * dc + 128 * fc:
                           1024 * dc + 128 * (fc + 1)],
                      RNT[:, 2048 * dc:2048 * (dc + 1)],
                      start=(dc == 0), stop=(dc == 7), n_free=2048,
                      inc=("sPE" if dc == 7 else None),
                      wait=(W(("sDV", vkv), ("sAC", lp_state["vevac"]))
                            if dc == 0 else ()))
                yield
            v_pe = sch.v("sPE")
            ac(lambda a, fc=fc: a.activation(
                out=KLT[:, 2048 * fc:2048 * (fc + 1)], in_=PSF[:, 0:2048],
                func=AF.Copy, scale=1.0), inc="sAC", wait=W(("sPE", v_pe)))
            lp_state["vevac"] = sch.v("sAC")
            yield
        for t in range(TT):
            for dc in range(8):
                pe_mm(PSF[:, 0:512],
                      RNT[:, 2048 * dc + 128 * t:2048 * dc + 128 * (t + 1)],
                      WTMP[:, 1024 * dc + 512:1024 * (dc + 1)],
                      start=(dc == 0), stop=(dc == 7), n_free=512,
                      inc=("sPE" if dc == 7 else None),
                      wait=(W(("sAC", lp_state["vevac"]), ("sGP", gp_init))
                            if dc == 0 else ()))
                yield
            v_pe = sch.v("sPE")
            ac(lambda a, t=t: a.activation(
                out=VPL[:, 520 * t:520 * (t + 1)]
                    .rearrange("p (h c) -> p h c", c=65)[:, :, 0:64],
                in_=PSF[:, 0:512], func=AF.Copy, scale=1.0),
               inc="sAC", wait=W(("sPE", v_pe)))
            lp_state["vevac"] = sch.v("sAC")
            lp_state["vvpl"] = sch.v("sAC")
            yield

    lpg = light_projs()

    def pump(n=1):
        for _ in range(n):
            next(lpg, None)

    v_m = dv(lambda v: v.memset(ANEW[:], A1), inc="sDV")
    dv(lambda v: v.tensor_scalar(out=NANEW[:], in0=ANEW[:], scalar1=-1.0,
                                 scalar2=None, op0=OP.mult), inc="sDV",
       wait=W(("sDV", v_m)))
    dv(lambda v: v.tensor_scalar(out=APLK[:], in0=ANEW[:], scalar1=LOGK,
                                 scalar2=None, op0=OP.add), inc="sDV",
       wait=W(("sDV", v_m)))
    v_a = sch.v("sDV")
    for it in range(NIT - 1):
        pe(lambda p: p.matmul(PSF[:, 2048:2049], ONE1F[:], NANEW[:],
                              start=True, stop=True), inc="sPE",
           wait=W(("sDV", v_a)))
        v_brc = sch.v("sPE")
        ac(lambda a: a.activation(out=NAREP[:], in_=PSF[:, 2048:2049],
                                  func=AF.Copy, scale=1.0), inc="sAC",
           wait=W(("sPE", v_brc)))
        v_nar = sch.v("sAC")
        ac(lambda a: a.activation(out=RSC[:], in_=SCOL[:], func=AF.Relu,
                                  bias=NAREP[:], scale=-1.0), inc="sAC",
           wait=W(("sAC", v_nar), ("sDV", v_ln2)))
        ac(lambda a: a.activation(out=EJ16[:], in_=RSC[:], func=AF.Exp,
                                  scale=-1.0, accum_out=EPART[:]), inc="sAC",
           wait=W(("sAC", sch.v("sAC"))))
        v_ep = sch.v("sAC")
        pe(lambda p: p.matmul(PSF[0:1, 2049:2050], EPART[:], ONECF[:],
                              start=True, stop=True), inc="sPE",
           wait=W(("sAC", v_ep)))
        v_sum = sch.v("sPE")
        ac(lambda a: a.activation(out=LNS[:], in_=PSF[0:1, 2049:2050],
                                  func=AF.Ln, scale=1.0), inc="sAC",
           wait=W(("sPE", v_sum)))
        v_lns = sch.v("sAC")
        dv(lambda v: v.scalar_tensor_tensor(
            out=ANEW[:], in0=LNS[:], scalar=-1.0, in1=APLK[:],
            op0=OP.mult, op1=OP.add), inc="sDV", wait=W(("sAC", v_lns)))
        v_an = sch.v("sDV")
        dv(lambda v: v.tensor_scalar(out=APLK[:], in0=ANEW[:], scalar1=LOGK,
                                     scalar2=None, op0=OP.add), inc="sDV",
           wait=W(("sDV", v_an)))
        dv(lambda v: v.tensor_scalar(out=NANEW[:], in0=ANEW[:], scalar1=-1.0,
                                     scalar2=None, op0=OP.mult), inc="sDV",
           wait=W(("sDV", v_an)))
        v_a = sch.v("sDV")
        pump(6)
    pe(lambda p: p.matmul(PSF[:, 2048:2049], ONE1F[:], ANEW[:],
                          start=True, stop=True), inc="sPE",
       wait=W(("sDV", v_a)))
    ac(lambda a: a.activation(out=AREP[:], in_=PSF[:, 2048:2049],
                              func=AF.Copy, scale=1.0), inc="sAC",
       wait=W(("sPE", sch.v("sPE"))))
    dv(lambda v: v.tensor_scalar(out=UCOL[:], in0=SCOL[:], scalar1=AREP[:],
                                 scalar2=0.0, op0=OP.add, op1=OP.min),
       inc="sDV", wait=W(("sAC", sch.v("sAC"))))
    v_ucol = sch.v("sDV")
    pe(lambda p: p.transpose(PSF[0:16, 2048:2176], UCOL[:], IDF[:]),
       inc="sPE", wait=W(("sDV", v_ucol)))
    dv(lambda v: v.tensor_copy(out=U16[:], in_=PSF[0:16, 2048:2176]),
       inc="sDV", wait=W(("sPE", sch.v("sPE"))))
    v_u16 = sch.v("sDV")
    UROW = ARF[0:1, 0:2048]
    sp(lambda s: s.dma_start(out=UROW, in_=U16[:]), inc="sLs", n=16,
       wait=W(("sDV", v_u16)))
    v_urow = sch.v("sLs")
    v_amat = pe_mm(PSF[:, 2048:4096], ONE1F[:], UROW, start=True, stop=True,
                   n_free=2048, inc="sPE", wait=W(("sLs", v_urow)))
    for tb in range(16):
        dv(lambda v, tb=tb: v.tensor_scalar(
            out=OUTF[:, 0:2048], in0=PSF[:, 2048:4096],
            scalar1=UCOL[:, tb:tb + 1], scalar2=None, op0=OP.is_gt),
           inc="sDV", wait=(W(("sPE", v_amat)) if tb == 0 else
                            W(("sDV", sch.v("sDV")))))
        dv(lambda v, tb=tb: v.reduce_sum(out=CCOL[:, tb:tb + 1],
                                         in_=OUTF[:, 0:2048], axis=AX),
           inc="sDV", wait=W(("sDV", sch.v("sDV"))))
        pump(2)
    v_ccol = sch.v("sDV")
    dv(lambda v: v.tensor_scalar(out=SAT16[:], in0=U16[:], scalar1=0.0,
                                 scalar2=None, op0=OP.is_ge), inc="sDV",
       wait=W(("sDV", v_ccol)))
    dv(lambda v: v.tensor_tensor_scan(out=P16[:], data0=SAT16[:],
                                      data1=SAT16[:], initial=0.0,
                                      op0=OP.add, op1=OP.bypass), inc="sDV",
       wait=W(("sDV", sch.v("sDV"))))
    dv(lambda v: v.reduce_sum(out=RS16[:], in_=SAT16[:], axis=AX), inc="sDV",
       wait=W(("sDV", sch.v("sDV"))))
    v_rs = sch.v("sDV")
    pe(lambda p: p.transpose(PSF[0:1, 2048:2064], RS16[:], IDF[0:16, 0:16]),
       inc="sPE", wait=W(("sDV", v_rs)))
    dv(lambda v: v.tensor_copy(out=OFR[:], in_=PSF[0:1, 2048:2064]),
       inc="sDV", wait=W(("sPE", sch.v("sPE"))))
    dv(lambda v: v.tensor_tensor_scan(out=OFR2[:], data0=OFR[:],
                                      data1=OFR[:], initial=0.0,
                                      op0=OP.add, op1=OP.bypass), inc="sDV",
       wait=W(("sDV", sch.v("sDV"))))
    dv(lambda v: v.memset(OFR[:, 0:1], 0.0), inc="sDV",
       wait=W(("sDV", sch.v("sDV"))))
    dv(lambda v: v.tensor_copy(out=OFR[:, 1:16], in_=OFR2[:, 0:15]),
       inc="sDV", wait=W(("sDV", sch.v("sDV"))))
    v_ofr = sch.v("sDV")
    pe(lambda p: p.transpose(PSF[0:16, 2064:2065], OFR[:], IDF[0:1, 0:1]),
       inc="sPE", wait=W(("sDV", v_ofr)))
    dv(lambda v: v.tensor_copy(out=OFF16[:], in_=PSF[0:16, 2064:2065]),
       inc="sDV", wait=W(("sPE", sch.v("sPE"))))
    v_offev = sch.v("sDV")
    dv(lambda v: v.tensor_sub(out=P16[:], in0=P16[:], in1=SAT16[:]),
       inc="sDV", wait=W(("sDV", sch.v("sDV"))))
    dv(lambda v: v.tensor_scalar(out=P16[:], in0=P16[:], scalar1=OFF16[:],
                                 scalar2=None, op0=OP.add), inc="sDV",
       wait=W(("sDV", sch.v("sDV"))))
    dv(lambda v: v.tensor_mul(out=P16[:], in0=P16[:], in1=SAT16[:]),
       inc="sDV", wait=W(("sDV", sch.v("sDV"))))
    v_p16 = sch.v("sDV")
    pe(lambda p: p.transpose(PSF[0:16, 2176:2304], CCOL[:], IDF[:]),
       inc="sPE", wait=W(("sDV", max(v_ccol, v_offev))))
    dv(lambda v: v.tensor_copy(out=C16[:], in_=PSF[0:16, 2176:2304]),
       inc="sDV", wait=W(("sPE", sch.v("sPE"))))
    dv(lambda v: v.tensor_add(out=R16[:] if False else SEL16[:], in0=C16[:],
                              in1=P16[:]), inc="sDV",
       wait=W(("sDV", sch.v("sDV"))))
    dv(lambda v: v.tensor_scalar(out=SEL16[:], in0=SEL16[:], scalar1=1023.5,
                                 scalar2=None, op0=OP.is_lt), inc="sDV",
       wait=W(("sDV", sch.v("sDV"))))
    v_sel16 = sch.v("sDV")
    pe(lambda p: p.transpose(PSF[:, 2304:2320], SEL16[:], IDF[0:16, 0:16]),
       inc="sPE", wait=W(("sDV", v_sel16)))
    dv(lambda v: v.tensor_copy(out=SELCOL[:], in_=PSF[:, 2304:2320]),
       inc="sDV", wait=W(("sPE", sch.v("sPE"))))
    v_sel = sch.v("sDV")

    pump(400)
    v_vpl = lp_state["vvpl"]

    lt_div = sch.v("sDV")
    cte_prev = sch.v("sAC")
    exp_w = {}
    nw = 0
    for h in range(H2):
        chl, rhl = h // 2, (h % 2) * 64
        QLh = QLT[rhl:rhl + 64, 2048 * chl:2048 * (chl + 1)]
        KLh = KLT[rhl:rhl + 64, 2048 * chl:2048 * (chl + 1)]
        for wg in range(2):
            for wi in range(8):
                w = 8 * wg + wi
                pb = 512 * (wi % 2)
                pe(lambda p, w=w, pb=pb, QLh=QLh, KLh=KLh: p.matmul(
                    PSF[:, pb:pb + 128], KLh[:, 128 * w:128 * (w + 1)],
                    QLh[:, 128 * w:128 * (w + 1)], start=True, stop=True),
                   inc="sPE",
                   wait=W(("sAC", max(v_vpl, exp_w.get(nw - 2, 0)))))
                v_qk = sch.v("sPE")
                ac(lambda a, wi=wi, pb=pb: a.activation(
                    out=PTE[:, 128 * wi:128 * (wi + 1)],
                    in_=PSF[:, pb:pb + 128], func=AF.Exp, scale=SC),
                   inc="sAC", wait=W(("sPE", v_qk)))
                exp_w[nw] = sch.v("sAC")
                exp_prev = exp_w[nw]
                nw += 1
                pe(lambda p, w=w, h=h, wi=wi: p.matmul(
                    PSF[0:65, 3072 + 128 * wi:3072 + 128 * (wi + 1)],
                    VPL[:, 520 * w + 65 * h:520 * w + 65 * (h + 1)],
                    PTE[:, 128 * wi:128 * (wi + 1)],
                    start=True, stop=True),
                   inc="sPE", wait=W(("sAC", max(exp_prev, cte_prev)),
                                     ("sDV", v_ccol if (h == 0 and wg == 0
                                                        and wi == 0) else 0)))
            v_av = sch.v("sPE")
            ac(lambda a: a.activation(out=CTE[0:65, :],
                                      in_=PSF[0:65, 3072:4096],
                                      func=AF.Copy, scale=1.0),
               inc="sAC", wait=W(("sPE", v_av), ("sDV", lt_div)))
            cte_prev = sch.v("sAC")
            dv(lambda v: v.reciprocal(out=RZROW[:], in_=CTE[64:65, :]),
               inc="sDV", wait=W(("sAC", sch.v("sAC")),
                                 ("sDV", sch.v("sDV"))))
            dv(lambda v: v.tensor_copy(out=RZROWB[:], in_=RZROW[:]),
               inc="sDV", wait=W(("sDV", sch.v("sDV"))))
            v_rz = sch.v("sDV")
            pe_mm(PSF[0:64, 2048:3072], ONE1B[0:1, 0:64], RZROWB[:],
                  start=True, stop=True, n_free=1024, inc="sPE",
                  wait=W(("sDV", v_rz)))
            v_rb = sch.v("sPE")
            dv(lambda v: v.tensor_copy(out=RCPS[:], in_=PSF[0:64, 2048:3072]),
               inc="sDV", wait=W(("sPE", v_rb), ("sDV", sch.v("sDV"))))
            v_rc = sch.v("sDV")
            dv(lambda v, h=h, wg=wg: v.tensor_tensor(
                out=CTXL[:, 2048 * (h // 2) + 1024 * wg:
                         2048 * (h // 2) + 1024 * (wg + 1)]
                    [(h % 2) * 64:(h % 2) * 64 + 64, :],
                in0=CTE[0:64, :], in1=RCPS[:], op=OP.mult),
               inc="sDV", wait=W(("sDV", v_rc)))
            lt_div = sch.v("sDV")

    load_weight_direct("wqh", 512, gcol=24, wait_pe=sch.v("sPE"),
                       dv_guard=sch.v("sDV"), ac_guard=sch.v("sAC"))
    proj_T(QHT, None, RNT, nf=4, wcols=512)
    load_weight_direct("wkvh_t", 1024, gcol=24, wait_pe=sch.v("sPE"),
                       dv_guard=sch.v("sDV"), ac_guard=sch.v("sAC"))
    vw = sch.v("sDV")
    for pc in range(4):
        for sub in range(2):
            hh = 2 * pc + sub
            for dc in range(8):
                pe_mm(PSF[64 * sub:64 * (sub + 1), 0:2048],
                      WTMP[:, 1024 * dc + 128 * hh:1024 * dc + 128 * hh + 64],
                      RNT[:, 2048 * dc:2048 * (dc + 1)],
                      start=(dc == 0), stop=(dc == 7), n_free=2048,
                      inc=("sPE" if (dc == 7 and sub == 1) else None),
                      wait=(W(("sDV", vw), ("sAC", sch.v("sAC")))
                            if (dc == 0 and sub == 0) else ()))
        v_pe = sch.v("sPE")
        ac(lambda a, pc=pc: a.activation(
            out=KVHT[:, 2048 * pc:2048 * (pc + 1)], in_=PSF[:, 0:2048],
            func=AF.Copy, scale=1.0), inc="sAC", wait=W(("sPE", v_pe)))
    v_light_done = lt_div
    for t in range(TT):
        for dc in range(8):
            pe(lambda p, t=t, dc=dc: p.matmul(
                PSF[:, 0:512],
                RNT[:, 2048 * dc + 128 * t:2048 * dc + 128 * (t + 1)],
                WTMP[:, 1024 * dc:1024 * (dc + 1)]
                    .rearrange("p (h c) -> p h c", c=128)[:, :, 64:128],
                start=(dc == 0), stop=(dc == 7)),
               inc=("sPE" if dc == 7 else None),
               wait=(W(("sAC", sch.v("sAC")), ("sDV", v_light_done),
                       ("sGP", gp_init)) if dc == 0 else ()))
        v_pe = sch.v("sPE")
        ac(lambda a, t=t: a.activation(
            out=VPH[:, 520 * t:520 * (t + 1)]
                .rearrange("p (h c) -> p h c", c=65)[:, :, 0:64],
            in_=PSF[:, 0:512], func=AF.Copy, scale=1.0),
           inc="sAC", wait=W(("sPE", v_pe)))
    v_vph = sch.v("sAC")

    hv_div = sch.v("sDV")
    cte_prev2 = sch.v("sAC")
    for h in range(H2):
        Kh2 = KVHT[(h % 2) * 64:(h % 2) * 64 + 64,
                   2048 * (h // 2):2048 * (h // 2 + 1)]
        Qh2 = QHT[(h % 2) * 64:(h % 2) * 64 + 64,
                  2048 * (h // 2):2048 * (h // 2 + 1)]
        for qh in range(2):
            exp_v = {}
            av_v = {}
            for kt in range(16):
                pb = 1024 * (kt % 2)
                for j in range(2):
                    pe(lambda p, kt=kt, qh=qh, j=j, pb=pb, Kh2=Kh2, Qh2=Qh2:
                       p.matmul(
                           PSF[:, pb + 512 * j:pb + 512 * (j + 1)],
                           Kh2[:, 128 * kt:128 * (kt + 1)],
                           Qh2[:, 1024 * qh + 512 * j:
                               1024 * qh + 512 * (j + 1)],
                           start=True, stop=True),
                       inc=("sPE" if j == 1 else None),
                       wait=(W(("sAC", max(v_vph, exp_v.get(kt - 2, 0))))
                             if j == 0 else ()))
                v_qk = sch.v("sPE")
                ac(lambda a, kt=kt, pb=pb: a.activation(
                    out=PTE[:, 1024 * (kt % 2):1024 * (kt % 2) + 1024],
                    in_=PSF[:, pb:pb + 1024], func=AF.Exp, scale=SC),
                   inc="sAC",
                   wait=W(("sPE", max(v_qk, av_v.get(kt - 2, 0)))))
                exp_v[kt] = sch.v("sAC")
                for j in range(2):
                    pe(lambda p, kt=kt, h=h, j=j: p.matmul(
                        PSF[0:65, 3072 + 512 * j:3072 + 512 * (j + 1)],
                        VPH[:, 520 * kt + 65 * h:520 * kt + 65 * (h + 1)],
                        PTE[:, 1024 * (kt % 2) + 512 * j:
                            1024 * (kt % 2) + 512 * (j + 1)],
                        start=(kt == 0), stop=(kt == 15)),
                       inc=("sPE" if j == 1 else None),
                       wait=(W(("sAC", max(exp_v[kt], cte_prev2)))
                             if j == 0 else ()))
                av_v[kt] = sch.v("sPE")
            v_av = sch.v("sPE")
            ac(lambda a: a.activation(out=CTE[0:65, :],
                                      in_=PSF[0:65, 3072:4096],
                                      func=AF.Copy, scale=1.0),
               inc="sAC", wait=W(("sPE", v_av), ("sDV", hv_div)))
            cte_prev2 = sch.v("sAC")
            dv(lambda v: v.reciprocal(out=RZROW[:], in_=CTE[64:65, :]),
               inc="sDV", wait=W(("sAC", sch.v("sAC")),
                                 ("sDV", sch.v("sDV"))))
            dv(lambda v: v.tensor_copy(out=RZROWB[:], in_=RZROW[:]),
               inc="sDV", wait=W(("sDV", sch.v("sDV"))))
            v_rz = sch.v("sDV")
            pe_mm(PSF[0:64, 2048:3072], ONE1B[0:1, 0:64], RZROWB[:],
                  start=True, stop=True, n_free=1024, inc="sPE",
                  wait=W(("sDV", v_rz)))
            v_rb = sch.v("sPE")
            dv(lambda v: v.tensor_copy(out=RCPS[:], in_=PSF[0:64, 2048:3072]),
               inc="sDV", wait=W(("sPE", v_rb), ("sDV", sch.v("sDV"))))
            v_rc = sch.v("sDV")
            dv(lambda v, h=h, qh=qh: v.tensor_tensor(
                out=CTXH[:, 2048 * (h // 2) + 1024 * qh:
                         2048 * (h // 2) + 1024 * (qh + 1)]
                    [(h % 2) * 64:(h % 2) * 64 + 64, :],
                in0=CTE[0:64, :], in1=RCPS[:], op=OP.mult),
               inc="sDV", wait=W(("sDV", v_rc)))
            hv_div = sch.v("sDV")

    load_weight_direct("wol", 1024, rows=512, woff=0,
                       wait_pe=sch.v("sPE"), dv_guard=sch.v("sDV"))
    v_wol = load_weight_direct("woh", 1024, rows=512, woff=4096)
    st_done = {0: 0, 1: 0}
    for t in range(TT):
        po = 2048 * (t % 2)
        so = 1024 * (t % 2)
        stx = "sSTa" if t % 2 == 0 else "sSTb"
        for dc in range(4):
            pe_mm(PSF[:, po:po + 1024],
                  CTXL[:, 2048 * dc + 128 * t:2048 * dc + 128 * (t + 1)],
                  WTMP[:, 1024 * dc:1024 * (dc + 1)],
                  start=(dc == 0), stop=(dc == 3), n_free=1024,
                  wait=(W(v_wol, ("sDV", max(hv_div, sch.v("sDV"))))
                        if dc == 0 else ()))
        for dc in range(4):
            pe_mm(PSF[:, po + 1024:po + 2048],
                  CTXH[:, 2048 * dc + 128 * t:2048 * dc + 128 * (t + 1)],
                  WTMP[:, 4096 + 1024 * dc:4096 + 1024 * (dc + 1)],
                  start=(dc == 0), stop=(dc == 3), n_free=1024,
                  inc=("sPE" if dc == 3 else None))
        v_pe = sch.v("sPE")
        dv(lambda v, t=t, po=po, so=so: v.scalar_tensor_tensor(
            out=OUTF[:, so:so + 1024], in0=PSF[:, po + 1024:po + 2048],
            scalar=SELCOL[:, t:t + 1], in1=XB[:, D * t:D * (t + 1)],
            op0=OP.mult, op1=OP.add),
           inc="sDV",
           wait=W(("sPE", v_pe), ("sDV", sch.v("sDV")), ("sDV", v_sel),
                  (stx, st_done[t % 2])))
        dv(lambda v, po=po, so=so: v.tensor_tensor(
            out=OUTF[:, so:so + 1024], in0=PSF[:, po:po + 1024],
            in1=OUTF[:, so:so + 1024], op=OP.add),
           inc="sDV", wait=W(("sDV", sch.v("sDV"))))
        v_of = sch.v("sDV")
        sp(lambda s, t=t, so=so: s.dma_start(
            out=out_t[128 * t:128 * (t + 1), :], in_=OUTF[:, so:so + 1024]),
           inc=stx, n=16, wait=W(("sDV", v_of)))
        st_done[t % 2] = sch.v(stx)
    if upto == 'fulldbg':
        v_fin = sch.v("sDV")
        for a in range(8):
            gp(lambda g, a=a: g.dma_start(
                out=dbg_t[128 * a:128 * (a + 1), :],
                in_=CTXL[:, 1024 * a:1024 * (a + 1)]),
               inc="sGW", n=16, wait=W(("sDV", v_fin), ("sGW", sch.v("sGW"))))
        for a in range(8):
            gp(lambda g, a=a: g.dma_start(
                out=dbg_t[1024 + 128 * a:1024 + 128 * (a + 1), :],
                in_=CTXH[:, 1024 * a:1024 * (a + 1)]),
               inc="sGW", n=16, wait=W(("sGW", sch.v("sGW"))))
        for src_ap, c0 in ((SELCOL, 0), (UCOL, 16), (CCOL, 32), (SCOL, 48)):
            gp(lambda g, src_ap=src_ap, c0=c0: g.dma_start(
                out=dbg_t[2048:2176, c0:c0 + 16], in_=src_ap[:, :],
                allow_slow_non_contiguous=True),
               inc="sGW", n=16, wait=W(("sGW", sch.v("sGW"))))
        for j, a in enumerate((6, 7)):
            gp(lambda g, j=j, a=a: g.dma_start(
                out=dbg_t[2176 + 128 * j:2176 + 128 * (j + 1), :],
                in_=XB[:, 1024 * a:1024 * (a + 1)]),
               inc="sGW", n=16, wait=W(("sGW", sch.v("sGW"))))
        sch.waitq("sp", ("sSTa", sch.v("sSTa")), ("sSTb", sch.v("sSTb")),
                  ("sGW", sch.v("sGW")))
    else:
        sch.waitq("sp", ("sSTa", sch.v("sSTa")), ("sSTb", sch.v("sSTb")))

    with nc.Block() as block:
        @block.sync
        def _(h):
            for f in sch.q["sp"]:
                f(h)

        @block.gpsimd
        def _(h):
            for f in sch.q["gp"]:
                f(h)

        @block.vector
        def _(h):
            for f in sch.q["dv"]:
                f(h)

        @block.scalar
        def _(h):
            for f in sch.q["ac"]:
                f(h)

        @block.tensor
        def _(h):
            for f in sch.q["pe"]:
                f(h)

    es.close()
    return nc


_NC_CACHE = [None]


def kernel(L0_attn_x, L0_attn_Wq, L0_attn_bq, L0_attn_Wk, L0_attn_bk,
           L0_attn_Wv, L0_attn_bv, L0_attn_Wo, L0_attn_bo,
           L0_ffn_gamma_light, L0_ffn_wq_light, L0_ffn_wkv_light,
           L0_ffn_wo_light, L0_ffn_q_routing_token, L0_ffn_kv_routing_token,
           L0_ffn_gamma_heavy, L0_ffn_wq_heavy, L0_ffn_wkv_heavy,
           L0_ffn_wo_heavy):
    x = np.ascontiguousarray(np.asarray(L0_attn_x, dtype=np.float32))
    f = lambda a: np.ascontiguousarray(np.asarray(a, dtype=np.float32))
    row = lambda a: np.ascontiguousarray(
        np.asarray(a, dtype=np.float32).reshape(1, -1))
    base = {
        "wq_t": np.ascontiguousarray(f(L0_attn_Wq).T),
        "wk_t": np.ascontiguousarray(f(L0_attn_Wk).T),
        "wv_t": np.ascontiguousarray(f(L0_attn_Wv).T),
        "wo_t": np.ascontiguousarray(f(L0_attn_Wo).T),
        "wkvl": f(L0_ffn_wkv_light),
        "wkvh_t": np.ascontiguousarray(f(L0_ffn_wkv_heavy).T),
        "wql": f(L0_ffn_wq_light), "wqh": f(L0_ffn_wq_heavy),
        "wol": f(L0_ffn_wo_light), "woh": f(L0_ffn_wo_heavy),
        "bq": row(L0_attn_bq), "bk": row(L0_attn_bk), "bv": row(L0_attn_bv),
        "bo": row(L0_attn_bo), "g_l": row(L0_ffn_gamma_light),
        "g_h": row(L0_ffn_gamma_heavy), "rq": row(L0_ffn_q_routing_token),
    }
    if _NC_CACHE[0] is None:
        _NC_CACHE[0] = build_program()
    nc = _NC_CACHE[0]
    in_maps = [dict(base, x=np.ascontiguousarray(x[b])) for b in range(8)]
    res = run_bass_kernel_spmd(nc, in_maps, core_ids=list(range(8)))
    return np.stack([np.asarray(r["out"], dtype=np.float32)
                     for r in res.results], axis=0)


# version 13
# speedup vs baseline: 1.2868x; 1.0029x over previous
import math
from contextlib import ExitStack

import numpy as np
import concourse.bass as bass
import concourse.mybir as mybir
from concourse.bass_utils import run_bass_kernel_spmd

F32 = mybir.dt.float32
BF16 = mybir.dt.bfloat16
AX = mybir.AxisListType.X
OP = mybir.AluOpType
AF = mybir.ActivationFunctionType

T, D, TT = 2048, 1024, 16
H1, H2, HID = 16, 8, 512
NEG = -1e38
SC = 0.125
ROUTE_K = 1152.0
LOGK = float(np.log(np.float32(ROUTE_K)))
A1 = float(np.log(np.float32(ROUTE_K)) - np.log(np.float32(T)))
NIT = 50


class Sched:

    ENGS = ("sp", "gp", "dv", "ac", "pe")

    def __init__(self):
        self.q = {e: [] for e in self.ENGS}
        self.sem = {}
        self.semh = {}

    def op(self, eng, fn, inc=None, n=1, wait=()):
        semh = self.semh
        wait = tuple((s, v) for s, v in wait if v > 0)
        tgt = None
        if inc is not None:
            self.sem[inc] += n
            tgt = (semh[inc], n)

        def run(h, fn=fn, wait=wait, tgt=tgt):
            for s, v in wait:
                h.wait_ge(semh[s], v)
            inst = fn(h)
            if tgt is not None and inst is not None:
                inst.then_inc(*tgt)

        self.q[eng].append(run)
        return self.sem[inc] if inc is not None else None

    def waitq(self, eng, *sv):
        semh = self.semh
        sv = tuple(sv)

        def run(h, sv=sv):
            for s, v in sv:
                if v > 0:
                    h.wait_ge(semh[s], v)

        self.q[eng].append(run)

    def v(self, s):
        return self.sem[s]


def build_program(upto='full'):
    nc = bass.Bass("TRN2", target_bir_lowering=False, debug=False)
    sch = Sched()

    di = {}
    for name, shape in (
        ("x", [T, D]), ("wq_t", [D, D]), ("wk_t", [D, D]), ("wv_t", [D, D]),
        ("wo_t", [D, D]), ("wkvl", [D, D]), ("wkvh_t", [D, D]),
        ("wql", [D, HID]), ("wqh", [D, HID]), ("wol", [HID, D]),
        ("woh", [HID, D]), ("bq", [1, D]), ("bk", [1, D]), ("bv", [1, D]),
        ("bo", [1, D]), ("g_l", [1, D]), ("g_h", [1, D]), ("rq", [1, D]),
    ):
        di[name] = nc.dram_tensor(name, shape, F32, kind="ExternalInput").ap()
    out_t = nc.dram_tensor("out", [T, D], F32, kind="ExternalOutput").ap()
    dbg_t = (nc.dram_tensor("dbg", [2560, 1024], F32, kind="ExternalOutput")
             .ap() if upto == 'fulldbg' else None)

    es = ExitStack()
    Sb = lambda nm, sh, dt=F32: es.enter_context(nc.sbuf_tensor(nm, sh, dt))

    ARN = Sb("ARN", [128, 90384], BF16)
    XB = ARN[:, 0:16384]
    XNT = ARN[:, 16384:32768]
    CT1 = ARN[:, 16384:32768]
    CTXL = ARN[:, 16384:24576]
    CTXH = ARN[:, 24576:32768]
    QT = ARN[:, 32768:49152]
    RNT = ARN[:, 32768:49152]
    KT = ARN[:, 49152:65536]
    QLT = ARN[:, 49152:57344]
    KLT = ARN[:, 57344:65536]
    KVHT = ARN[:, 49152:57344]
    QHT = ARN[:, 57344:65536]
    VP = ARN[:, 65536:82176]
    VPL = ARN[:, 65536:73856]
    VPH = ARN[:, 73856:82176]
    SBT = ARN[:, 82176:86272]
    SBX = ARN[:, 86272:86784]
    PTE = ARN[:, 86784:88832]
    CND = ARN[:, 88832:89088]
    CN2 = ARN[:, 89088:89344]
    R8 = ARN[:, 89344:89360]
    CTE = ARN[:, 89360:90384]
    WTMP = ARN[:, 82176:90368]

    ARF = Sb("ARF", [128, 4096], F32)
    SBMB = Sb("SBMB", [128, 4096], BF16)
    NEGI = Sb("NEGI", [128, 128], BF16)
    SBM = ARF[:, 0:4096]
    OUTF = ARF[:, 0:2048]
    SPROD = ARF[:, 1024:2048]
    XNF = ARF[:, 2048:3072]
    R128 = ARF[:, 3072:4096]

    IDF = Sb("IDF", [128, 128], F32)
    BIASC = Sb("BIASC", [128, 32], F32)
    BVROW = ARF[0:1, 3072:4096]
    BOROW = None
    ONE1F = Sb("ONE1F", [1, 128], F32)
    ONE1B = Sb("ONE1B", [1, 128], BF16)
    ONECF = Sb("ONECF", [128, 1], F32)
    EPSC = Sb("EPSC", [128, 1], F32)
    TCOL = Sb("TCOL", [128, 2], F32)
    RZROWB = Sb("RZROWB", [1, 1024], BF16)
    RCPS = Sb("RCPS", [64, 1024], BF16)
    MCOL = Sb("MCOL", [128, 1], F32)
    RSTD = Sb("RSTD", [128, 1], F32)
    SSQ = Sb("SSQ", [128, 1], F32)
    RSN = Sb("RSN", [128, 1], F32)
    BN12 = Sb("BN12", [128, 12], F32)
    BN2 = Sb("BN2", [128, 2], F32)
    RNF = ARF[:, 0:1024]
    SQJ = ARN[:, 49152:50176]
    SCOL = Sb("SCOL", [128, 16], F32)
    UCOL = Sb("UCOL", [128, 16], F32)
    CCOL = Sb("CCOL", [128, 16], F32)
    SELCOL = Sb("SELCOL", [128, 16], F32)
    U16 = ARF[0:16, 3072:3200]
    SAT16 = ARF[0:16, 3200:3328]
    P16 = ARF[0:16, 3328:3456]
    SEL16 = ARF[0:16, 3456:3584]
    C16 = U16
    RS16 = ARF[0:16, 3584:3585]
    OFF16 = ARF[0:16, 3585:3586]
    OFR = ARF[0:1, 3586:3602]
    OFR2 = ARF[0:1, 3602:3618]
    AREP = ARF[:, 3618:3619]
    NAREP = ARF[:, 3619:3620]
    RSC = ARF[:, 3620:3636]
    EJ16 = Sb("EJ16", [128, 16], BF16)
    EPART = Sb("EPART", [128, 1], F32)
    LNS = Sb("LNS", [1, 1], F32)
    ANEW = Sb("ANEW", [1, 1], F32)
    NANEW = Sb("NANEW", [1, 1], F32)
    APLK = Sb("APLK", [1, 1], F32)

    PSF = es.enter_context(nc.psum_tensor("PSF", [128, 4096], F32))
    RZROW = PSF[64:65, 2048:3072]

    for s in ("sLs", "sLx0", "sLx1", "sLw", "sSTa", "sSTb",
              "sPE", "sDV", "sAC", "sGP", "sGW"):
        sch.semh[s] = es.enter_context(nc.semaphore(s))
        sch.sem[s] = 0

    W = lambda *sv: tuple(sv)
    sp = lambda fn, **k: sch.op("sp", fn, **k)
    gp = lambda fn, **k: sch.op("gp", fn, **k)
    dv = lambda fn, **k: sch.op("dv", fn, **k)
    ac = lambda fn, **k: sch.op("ac", fn, **k)
    pe = lambda fn, **k: sch.op("pe", fn, **k)

    def pe_mm(out_ap, lhs, rhs, start, stop, n_free, is_tr=False,
              inc=None, wait=()):
        nslc = (n_free + 511) // 512
        for j in range(nslc):
            c0, c1 = 512 * j, min(512 * (j + 1), n_free)
            pe(lambda p, c0=c0, c1=c1, out_ap=out_ap, lhs=lhs, rhs=rhs,
               start=start, stop=stop, is_tr=is_tr: p.matmul(
                   out_ap[:, c0:c1], lhs, rhs[:, c0:c1], start=start,
                   stop=stop, is_transpose=is_tr),
               inc=(inc if j == nslc - 1 else None),
               wait=(wait if j == 0 else ()))
        return sch.v(inc) if inc else None

    gp(lambda g: g.memset(IDF[:], 0.0), inc="sGP")
    gp(lambda g: g.affine_select(
        out=IDF[:], in_=IDF[:], compare_op=OP.not_equal, fill=1.0, base=0,
        pattern=[[-1, 128]], channel_multiplier=1), inc="sGP",
        wait=W(("sGP", sch.v("sGP"))))
    gp(lambda g: g.memset(NEGI[:], 0.0), inc="sGP")
    gp(lambda g: g.affine_select(
        out=NEGI[:], in_=NEGI[:], compare_op=OP.not_equal, fill=-1e30,
        base=0, pattern=[[-1, 128]], channel_multiplier=1), inc="sGP",
        wait=W(("sGP", sch.v("sGP"))))
    gp(lambda g: g.memset(ONE1F[:], 1.0), inc="sGP")
    gp(lambda g: g.memset(ONE1B[:], 1.0), inc="sGP")
    gp(lambda g: g.memset(ONECF[:], 1.0), inc="sGP")
    gp(lambda g: g.memset(EPSC[:], 1e-5), inc="sGP")
    gp(lambda g: g.memset(VP[:], 1.0), inc="sGP")
    gp_init = sch.v("sGP")

    BQROW = ARF[32:33, 3072:4096]
    BKROW = ARF[64:65, 3072:4096]
    BQROWB = ARN[0:1, 67584:68608]
    BKROWB = ARN[0:1, 68608:69632]
    ONESROW = VP[0:1, 0:2048]
    for rowt, name in ((BQROW, "bq"), (BKROW, "bk")):
        sp(lambda s, rowt=rowt, name=name: s.dma_start(
            out=rowt, in_=di[name]), inc="sLs", n=16)
    for name, cs in (("g_l", 16), ("g_h", 24)):
        sp(lambda s, name=name, cs=cs: s.dma_start(
            out=BIASC[:, cs:cs + 8],
            in_=di[name].rearrange("o (c p) -> (o p) c", p=128),
            allow_slow_non_contiguous=True), inc="sLs", n=16)
    sp(lambda s: s.dma_start(out=BVROW, in_=di["bv"]), inc="sLs", n=16)
    v_small = sch.v("sLs")
    dv(lambda v: v.tensor_copy(out=BQROWB, in_=BQROW), inc="sDV",
       wait=W(("sLs", v_small), ("sGP", gp_init)))
    dv(lambda v: v.tensor_copy(out=BKROWB, in_=BKROW), inc="sDV",
       wait=W(("sLs", v_small), ("sGP", gp_init)))

    ln_done = {}
    xnf_done = {}
    for t in range(TT):
        slx = "sLx0"
        sp(lambda s, t=t: s.dma_start(
            out=XNF[:], in_=di["x"][128 * t:128 * (t + 1), :]),
           inc=slx, n=16, wait=W(("sDV", xnf_done.get(t - 1, 0))))
        vld = sch.v(slx)
        dv(lambda v, t=t: v.tensor_copy(
            out=XB[:, D * t:D * (t + 1)], in_=XNF[:]),
           inc="sDV", wait=W((slx, vld)))
        dv(lambda v: v.bn_stats(out=BN12[:, 0:6],
                                in_=XNF[:, 0:512]), inc="sDV")
        dv(lambda v: v.bn_stats(out=BN12[:, 6:12],
                                in_=XNF[:, 512:1024]),
           inc="sDV")
        dv(lambda v: v.bn_aggr(out=BN2[:], in_=BN12[:]), inc="sDV",
           wait=W(("sDV", sch.v("sDV"))))
        dv(lambda v: v.tensor_copy(out=MCOL[:], in_=BN2[:, 0:1]), inc="sDV",
           wait=W(("sDV", sch.v("sDV"))))
        v_bn = sch.v("sDV")
        ac(lambda a: a.activation(out=SSQ[:], in_=BN2[:, 1:2], func=AF.Sqrt,
                                  bias=EPSC[:], scale=1.0), inc="sAC",
           wait=W(("sDV", v_bn), ("sGP", gp_init)))
        dv(lambda v: v.reciprocal(out=RSTD[:], in_=SSQ[:]), inc="sDV",
           wait=W(("sAC", sch.v("sAC"))))
        dv(lambda v: v.tensor_scalar(
            out=XNF[:], in0=XNF[:], scalar1=MCOL[:],
            scalar2=RSTD[:], op0=OP.subtract, op1=OP.mult), inc="sDV",
           wait=W(("sDV", sch.v("sDV"))))
        v_xnf = sch.v("sDV")
        for c in range(8):
            pe(lambda p, c=c: p.transpose(
                PSF[:, 3072 + 128 * c:3200 + 128 * c],
                XNF[:, 128 * c:128 * (c + 1)], IDF[:]),
               inc=("sPE" if c == 7 else None),
               wait=(W(("sDV", v_xnf), ("sGP", 2),
                       ("sDV", ln_done.get(t - 1, 0))) if c == 0 else ()))
        v_tr = sch.v("sPE")
        dv(lambda v, t=t: v.tensor_copy(
            out=XNT.rearrange("p (c n) -> p c n", n=2048)
                [:, :, 128 * t:128 * (t + 1)],
            in_=PSF[:, 3072:4096].rearrange("p (c n) -> p c n", n=128)),
           inc="sDV", wait=W(("sPE", v_tr)))
        ln_done[t] = sch.v("sDV")
        xnf_done[t] = sch.v("sDV")

    def load_weight_direct(name, ncols, rows=1024, woff=0, wait_pe=0,
                           dv_guard=0, ac_guard=0, gcol=None):
        vs = []
        for c in range(rows // 128):
            gp(lambda g, c=c, name=name, ncols=ncols, woff=woff: g.dma_start(
                out=WTMP[:, woff + ncols * c:woff + ncols * (c + 1)],
                in_=di[name][128 * c:128 * (c + 1), 0:ncols]),
               inc="sGW", n=16,
               wait=(W(("sGW", sch.v("sGW")), ("sPE", wait_pe),
                       ("sDV", dv_guard), ("sAC", ac_guard)) if c == 0 else
                     W(("sGW", sch.v("sGW")))))
            vs.append(sch.v("sGW"))
        if gcol is None:
            return ("sGW", sch.v("sGW"))
        for c in range(rows // 128):
            dv(lambda v, c=c, ncols=ncols, woff=woff, gcol=gcol:
               v.tensor_scalar(
                   out=WTMP[:, woff + ncols * c:woff + ncols * (c + 1)],
                   in0=WTMP[:, woff + ncols * c:woff + ncols * (c + 1)],
                   scalar1=BIASC[:, gcol + c:gcol + c + 1],
                   scalar2=None, op0=OP.mult),
               inc="sDV", wait=W(("sGW", vs[c]), ("sLs", v_small),
                                 ("sDV", sch.v("sDV"))))
        return ("sDV", sch.v("sDV"))

    def proj_T(dst, bias_row, src_T, nf=8, wcols=1024, wwait=None):
        if wwait is None:
            wwait = ("sDV", sch.v("sDV"))
        for fc in range(nf):
            for dc in range(8):
                pe_mm(PSF[:, 0:2048],
                      WTMP[:, wcols * dc + 128 * fc:
                           wcols * dc + 128 * (fc + 1)],
                      src_T[:, 2048 * dc:2048 * (dc + 1)],
                      start=(dc == 0), stop=(dc == 7 and bias_row is None),
                      n_free=2048,
                      inc=("sPE" if (dc == 7 and bias_row is None) else None),
                      wait=(W(wwait, ("sDV", sch.v("sDV")),
                              ("sAC", sch.v("sAC")),
                              ("sLs", v_small)) if dc == 0 else ()))
            if bias_row is not None:
                pe_mm(PSF[:, 0:2048], bias_row[0:1, 128 * fc:128 * (fc + 1)],
                      ONESROW, start=False, stop=True, n_free=2048,
                      inc="sPE", wait=W(("sDV", sch.v("sDV")),
                                        ("sGP", gp_init)))
            vpe = sch.v("sPE")
            ac(lambda a, fc=fc, dst=dst: a.activation(
                out=dst[:, 2048 * fc:2048 * (fc + 1)], in_=PSF[:, 0:2048],
                func=AF.Copy, scale=1.0), inc="sAC", wait=W(("sPE", vpe)))
        return sch.v("sAC")

    v_w = load_weight_direct("wq_t", 1024)
    proj_T(QT, BQROWB, XNT, wwait=v_w)
    v_w = load_weight_direct("wk_t", 1024, wait_pe=sch.v("sPE"))
    proj_T(KT, BKROWB, XNT, wwait=v_w)
    v_w = load_weight_direct("wv_t", 1024, wait_pe=sch.v("sPE"))
    for t in range(TT):
        for dc in range(8):
            pe_mm(PSF[:, 0:1024],
                  XNT[:, 2048 * dc + 128 * t:2048 * dc + 128 * (t + 1)],
                  WTMP[:, 1024 * dc:1024 * (dc + 1)],
                  start=(dc == 0), stop=False, n_free=1024,
                  wait=(W(v_w, ("sDV", sch.v("sDV")),
                          ("sAC", sch.v("sAC")),
                          ("sLs", v_small)) if dc == 0 else ()))
        pe_mm(PSF[:, 0:1024], ONE1F[:], BVROW[:], start=False, stop=True,
              n_free=1024, inc="sPE")
        vpe = sch.v("sPE")
        ac(lambda a, t=t: a.activation(
            out=VP[:, 1040 * t:1040 * (t + 1)]
                .rearrange("p (h c) -> p h c", c=65)[:, :, 0:64],
            in_=PSF[:, 0:1024], func=AF.Copy, scale=1.0),
           inc="sAC", wait=W(("sPE", vpe), ("sGP", gp_init)))
    v_vp = sch.v("sAC")

    evac_done = {}
    pen_done = {}
    qk_done = {}
    sel_tcol = {}
    pen0_relu = {}
    st_last = {}
    exp_pair = {}
    av_kt = {}
    div_done = 0

    def b1_qh_kh(ei):
        h = ei // 8
        ch, rh = h // 2, (h % 2) * 64
        return (QT[rh:rh + 64, 2048 * ch:2048 * (ch + 1)],
                KT[rh:rh + 64, 2048 * ch:2048 * (ch + 1)])

    def emit_qksel(ei, qi):
        Qh, Kh = b1_qh_kh(ei)
        qt = 2 * (ei % 8) + qi
        ev = evac_done.get((ei, 0), 0) if qi == 1 else \
            evac_done.get((ei - 1, 1), 0)
        for j in range(4):
            pe(lambda p, qt=qt, j=j, Qh=Qh, Kh=Kh: p.matmul(
                PSF[:, 512 * j:512 * (j + 1)],
                Qh[:, 128 * qt:128 * (qt + 1)],
                Kh[:, 512 * j:512 * (j + 1)],
                start=True, stop=True),
               inc=("sPE" if j == 3 else None),
               wait=(W(("sAC", max(ev, v_vp))) if j == 0 else ()))
        qk_done[(ei, qi)] = sch.v("sPE")

    def emit_evac(ei, qi):
        sb = SBT[:, 2048 * qi:2048 * (qi + 1)]
        psem, pval = pen_done.get((ei - 1, qi), ("sDV", 0))
        ac(lambda a, sb=sb: a.activation(out=sb, in_=PSF[:, 0:2048],
                                         func=AF.Copy, scale=1.0),
           inc="sAC", wait=W(("sPE", qk_done[(ei, qi)]), (psem, pval)))
        evac_done[(ei, qi)] = sch.v("sAC")

    def emit_sel_l1(ei, qi):
        sb = SBT[:, 2048 * qi:2048 * (qi + 1)]
        cnd = (CND if qi == 0 else CN2)[:, 0:256]
        for c in range(16):
            dv(lambda v, c=c, sb=sb, cnd=cnd: v.max(
                out=cnd[:, 8 * c:8 * (c + 1)],
                in_=sb[:, 128 * c:128 * (c + 1)]),
               inc="sDV",
               wait=(W(("sAC", evac_done[(ei, qi)]), ("sDV", sch.v("sDV")))
                     if c == 0 else ()))
        v_r1 = sch.v("sDV")
        vgrp = v_r1
        for grp in range(4):
            for c in range(4 * grp, 4 * grp + 4):
                dv(lambda v, c=c, sb=sb, cnd=cnd: v.match_replace(
                    out=SBX[:, 128 * (c % 4):128 * (c % 4) + 128],
                    in_to_replace=cnd[:, 8 * c:8 * (c + 1)],
                    in_values=sb[:, 128 * c:128 * (c + 1)],
                    imm_value=NEG),
                   inc="sDV",
                   wait=(W(("sDV", vgrp)) if c % 4 == 0 else ()))
            v_mr = sch.v("sDV")
            for c in range(4 * grp, 4 * grp + 4):
                dv(lambda v, c=c, cnd=cnd: v.max(
                    out=cnd[:, 128 + 8 * c:128 + 8 * (c + 1)],
                    in_=SBX[:, 128 * (c % 4):128 * (c % 4) + 128]),
                   inc="sDV",
                   wait=(W(("sDV", v_mr)) if c % 4 == 0 else ()))
            vgrp = sch.v("sDV")
        return sch.v("sDV")

    def emit_sel_l2(ei, v_l1):
        cands = (CND[:, 0:256], CN2[:, 0:256])
        last = {0: v_l1[0], 1: v_l1[1]}
        for rr in range(5):
            for qi in range(2):
                dv(lambda v, qi=qi: v.max(out=R8[:, 8 * qi:8 * (qi + 1)],
                                          in_=cands[qi]),
                   inc="sDV", wait=W(("sDV", last[qi])))
                last[qi] = sch.v("sDV")
            if rr < 4:
                for qi in range(2):
                    dv(lambda v, qi=qi: v.match_replace(
                        out=cands[qi], in_to_replace=R8[:, 8 * qi:8 * (qi + 1)],
                        in_values=cands[qi], imm_value=NEG),
                       inc="sDV", wait=W(("sDV", last[qi])))
                    last[qi] = sch.v("sDV")
        for qi in range(2):
            dv(lambda v, qi=qi: v.tensor_copy(
                out=TCOL[:, qi:qi + 1], in_=R8[:, 8 * qi + 5:8 * qi + 6]),
               inc="sDV", wait=W(("sDV", last[qi])))
            sel_tcol[(ei, qi)] = sch.v("sDV")
            emit_pen(ei, qi)

    def emit_pen(ei, qi):
        sb = SBT[:, 2048 * qi:2048 * (qi + 1)]
        sm = SBMB[:, 2048 * qi:2048 * (qi + 1)]
        dv(lambda v, sb=sb, sm=sm, qi=qi: v.tensor_scalar(
            out=sm, in0=sb, scalar1=TCOL[:, qi:qi + 1], scalar2=None,
            op0=OP.is_lt),
           inc="sDV", wait=W(("sDV", sch.v("sDV")),
                             ("sPE", st_last.get(ei - 1, 0))))
        pen_done[(ei, qi)] = ("sDV", sch.v("sDV"))

    def emit_av(bi, g, h, ctx_c0):
        for sub in range(2):
            kt = 2 * g + sub
            w = (W(("sAC", exp_pair[(bi, g)]), ("sDV", div_done))
                 if kt == 0 else
                 (W(("sAC", exp_pair[(bi, g)])) if sub == 0 else ()))
            pe(lambda p, kt=kt, h=h, ctx_c0=ctx_c0: p.matmul(
                PSF[0:65, ctx_c0:ctx_c0 + 256],
                VP[:, 1040 * kt + 65 * h:1040 * kt + 65 * (h + 1)],
                PTE[:, 256 * (kt % 8):256 * (kt % 8) + 256],
                start=(kt == 0), stop=(kt == 15)),
               inc="sPE", wait=w)
            av_kt[(bi, kt)] = sch.v("sPE")

    pend_div = None
    rcps_gate = 0

    def emit_div2(pd):
        dbi, v_rzb = pd
        h, e8 = dbi // 8, dbi % 8
        half = e8 // 4
        pe_mm(PSF[0:64, 2048:3072], ONE1B[0:1, 0:64], RZROWB[:],
              start=True, stop=True, n_free=1024, inc="sPE",
              wait=W(("sAC", v_rzb)))
        v_rb = sch.v("sPE")
        ac(lambda a: a.activation(out=RCPS[:],
                                  in_=PSF[0:64, 2048:3072],
                                  func=AF.Copy, scale=1.0),
           inc="sAC", wait=W(("sPE", v_rb), ("sAC", sch.v("sAC"))))
        v_rc = sch.v("sAC")
        dv(lambda v, h=h, half=half: v.tensor_tensor(
            out=CT1[:, 2048 * (h // 2) + 1024 * half:
                    2048 * (h // 2) + 1024 * (half + 1)]
                [(h % 2) * 64:(h % 2) * 64 + 64, :],
            in0=CTE[0:64, :], in1=RCPS[:], op=OP.mult),
           inc="sDV", wait=W(("sAC", v_rc), ("sDV", sch.v("sDV"))))
        return v_rc, sch.v("sDV")

    for step in range(130):
        ei, bi = step, step - 1
        if ei == 0:
            emit_qksel(0, 0)
        if pend_div is not None:
            rcps_gate, div_done = emit_div2(pend_div)
            pend_div = None
        if ei < 128:
            emit_evac(ei, 0)
        if bi >= 0 and bi < 128:
            Qh, Kh = b1_qh_kh(bi)
            h, e8 = bi // 8, bi % 8
            q0 = 256 * e8
            ctx_c0 = 3072 + 256 * (e8 % 4)
            st_v = {}
            psem0, pval0 = pen_done[(bi, 0)]
            _s1, pval1 = pen_done[(bi, 1)]
            for g in range(8):
                wbase = 2048 + 512 * (g % 2)
                wguard = (exp_pair[(bi, g - 2)] if g >= 2 else
                          exp_pair.get((bi - 1, g + 6), 0))
                for sub in range(2):
                    kt = 2 * g + sub
                    po = wbase + 256 * sub
                    if g == 0 and sub == 0:
                        pw = W((psem0, pval0),
                               ("sDV", max(pval1, div_done)),
                               ("sAC", max(wguard, rcps_gate)))
                    elif sub == 0:
                        pw = W(("sAC", wguard))
                    else:
                        pw = ()
                    pe(lambda p, kt=kt, po=po, q0=q0, Qh=Qh, Kh=Kh:
                       p.matmul(PSF[:, po:po + 256],
                                Kh[:, 128 * kt:128 * (kt + 1)],
                                Qh[:, q0:q0 + 256],
                                start=True, stop=False),
                       wait=pw)
                    for qi2 in range(2):
                        pe(lambda p, kt=kt, po=po, qi2=qi2: p.matmul(
                            PSF[:, po + 128 * qi2:po + 128 * (qi2 + 1)],
                            SBMB[:, 2048 * qi2 + 128 * kt:
                                 2048 * qi2 + 128 * (kt + 1)],
                            NEGI[:], start=False, stop=(qi2 == 1)),
                           inc=("sPE" if qi2 == 1 else None))
                    st_v[kt] = sch.v("sPE")
                pguard = (av_kt.get((bi, 2 * g - 7), 0) if g >= 4 else
                          av_kt.get((bi - 1, 2 * g + 9), 0))
                ac(lambda a, g=g, wbase=wbase: a.activation(
                    out=PTE[:, 512 * (g % 4):512 * (g % 4) + 512],
                    in_=PSF[:, wbase:wbase + 512], func=AF.Exp, scale=SC),
                   inc="sAC", wait=W(("sPE", max(st_v[2 * g + 1], pguard))))
                exp_pair[(bi, g)] = sch.v("sAC")
                if g >= 1:
                    emit_av(bi, g - 1, h, ctx_c0)
                if g == 2 and ei < 128:
                    emit_qksel(ei, 1)
                    emit_evac(ei, 1)
                if g == 6 and ei + 1 <= 127:
                    emit_qksel(ei + 1, 0)
            emit_av(bi, 7, h, ctx_c0)
            st_last[bi] = st_v[15]
        if ei < 128 and (bi < 0 or bi >= 128):
            emit_qksel(ei, 1)
            emit_evac(ei, 1)
        if bi < 0 and ei + 1 <= 127:
            emit_qksel(ei + 1, 0)
        if ei < 128:
            v_l1 = {0: emit_sel_l1(ei, 0)}
            v_l1[1] = emit_sel_l1(ei, 1)
            emit_sel_l2(ei, v_l1)
        if bi >= 0 and bi < 128 and bi % 4 == 3:
            v_ava = av_kt[(bi, 15)]
            ac(lambda a: a.activation(out=CTE[0:65, :],
                                      in_=PSF[0:65, 3072:4096],
                                      func=AF.Copy, scale=1.0),
               inc="sAC", wait=W(("sPE", v_ava), ("sDV", div_done),
                                 ("sAC", sch.v("sAC"))))
            cte_v = sch.v("sAC")
            dv(lambda v: v.reciprocal(out=RZROW[:], in_=CTE[64:65, :]),
               inc="sDV", wait=W(("sAC", cte_v), ("sDV", sch.v("sDV"))))
            v_rz0 = sch.v("sDV")
            ac(lambda a: a.activation(out=RZROWB[:], in_=RZROW[:],
                                      func=AF.Copy, scale=1.0),
               inc="sAC", wait=W(("sDV", v_rz0), ("sAC", sch.v("sAC"))))
            pend_div = (bi, sch.v("sAC"))

    v_wo = load_weight_direct("wo_t", 1024, wait_pe=sch.v("sPE"),
                              dv_guard=sch.v("sDV"))
    BOROW = ARN[0:1, 50176:51200]
    RQROW = ARF[0:1, 3072:4096]
    v_ac_ph2 = sch.v("sAC")
    gp(lambda g: g.dma_start(out=BOROW, in_=di["bo"]), inc="sGW", n=16,
       wait=W(("sGW", sch.v("sGW")), ("sPE", sch.v("sPE"))))
    v_bo = sch.v("sGW")
    sp(lambda s: s.dma_start(out=RQROW, in_=di["rq"]), inc="sLs", n=16,
       wait=W(("sDV", sch.v("sDV"))))
    v_rq = sch.v("sLs")
    pe_mm(PSF[:, 2048:3072], ONE1F[:], RQROW, start=True, stop=True,
          n_free=1024, inc="sPE",
          wait=W(("sDV", div_done), ("sAC", rcps_gate), ("sLs", v_rq)))
    v_rqbc = sch.v("sPE")
    dv(lambda v: v.tensor_copy(out=R128[:], in_=PSF[:, 2048:3072]),
       inc="sDV", wait=W(("sPE", v_rqbc), ("sDV", sch.v("sDV"))))
    v_r128 = sch.v("sDV")

    for t in range(TT):
        po = 1024 * (t % 2)
        for dc in range(8):
            pe_mm(PSF[:, po:po + 1024],
                  CT1[:, 2048 * dc + 128 * t:2048 * dc + 128 * (t + 1)],
                  WTMP[:, 1024 * dc:1024 * (dc + 1)],
                  start=(dc == 0), stop=False, n_free=1024,
                  wait=(W(v_wo, ("sDV", max(div_done, sch.v("sDV"))))
                        if dc == 0 else ()))
        pe_mm(PSF[:, po:po + 1024], ONE1B[:], BOROW, start=False,
              stop=True, n_free=1024, inc="sPE",
              wait=W(("sGW", v_bo)))
        v_pe = sch.v("sPE")
        dv(lambda v, t=t, po=po: v.scalar_tensor_tensor(
            out=XB[:, D * t:D * (t + 1)], in0=PSF[:, po:po + 1024],
            scalar=1.0, in1=XB[:, D * t:D * (t + 1)], op0=OP.mult,
            op1=OP.add), inc="sDV", wait=W(("sPE", v_pe),
                                           ("sDV", sch.v("sDV"))))
        xt = XB[:, D * t:D * (t + 1)]
        dv(lambda v, xt=xt: v.bn_stats(out=BN12[:, 0:6], in_=xt[:, 0:512]),
           inc="sDV", wait=W(("sDV", sch.v("sDV"))))
        dv(lambda v, xt=xt: v.bn_stats(out=BN12[:, 6:12],
                                       in_=xt[:, 512:1024]), inc="sDV")
        dv(lambda v: v.bn_aggr(out=BN2[:], in_=BN12[:]), inc="sDV",
           wait=W(("sDV", sch.v("sDV"))))
        dv(lambda v: v.tensor_copy(out=MCOL[:], in_=BN2[:, 0:1]), inc="sDV",
           wait=W(("sDV", sch.v("sDV"))))
        v_bn = sch.v("sDV")
        ac(lambda a: a.activation(out=SSQ[:], in_=BN2[:, 1:2], func=AF.Sqrt,
                                  bias=EPSC[:], scale=1.0), inc="sAC",
           wait=W(("sDV", v_bn)))
        dv(lambda v: v.reciprocal(out=RSTD[:], in_=SSQ[:]), inc="sDV",
           wait=W(("sAC", sch.v("sAC"))))
        dv(lambda v, xt=xt: v.tensor_scalar(
            out=XNF[:], in0=xt, scalar1=MCOL[:], scalar2=RSTD[:],
            op0=OP.subtract, op1=OP.mult), inc="sDV",
           wait=W(("sDV", sch.v("sDV"))))
        v_xnf = sch.v("sDV")
        dv(lambda v, t=t: v.scalar_tensor_tensor(
            out=SPROD[:], in0=XNF[:], scalar=1.0, in1=R128[:],
            op0=OP.mult, op1=OP.mult),
           inc="sDV", wait=W(("sDV", max(v_r128, v_xnf))))
        dv(lambda v, t=t: v.reduce_sum(out=SCOL[:, t:t + 1], in_=SPROD[:],
                                       axis=AX),
           inc="sDV", wait=W(("sDV", sch.v("sDV"))))
        ac(lambda a: a.activation(out=SQJ[:], in_=XNF[:], func=AF.Square,
                                  scale=1.0, accum_out=SSQ[:]), inc="sAC",
           wait=W(("sDV", sch.v("sDV"))))
        ac(lambda a: a.activation(out=MCOL[:], in_=SSQ[:], func=AF.Sqrt,
                                  scale=1.0 / D), inc="sAC",
           wait=W(("sAC", sch.v("sAC"))))
        dv(lambda v: v.reciprocal(out=RSN[:], in_=MCOL[:]), inc="sDV",
           wait=W(("sAC", sch.v("sAC"))))
        v_rsn = sch.v("sDV")
        dv(lambda v: v.tensor_scalar(out=RNF[:], in0=XNF[:], scalar1=RSN[:],
                                     scalar2=None, op0=OP.mult),
           inc="sDV", wait=W(("sDV", v_rsn)))
        v_rn = sch.v("sDV")
        for c in range(8):
            pe(lambda p, c=c: p.transpose(
                PSF[:, 3072 + 128 * c:3200 + 128 * c],
                RNF[:, 128 * c:128 * (c + 1)], IDF[:]),
               inc=("sPE" if c == 7 else None),
               wait=(W(("sDV", v_rn), ("sAC", v_ac_ph2 if t == 0 else 0))
                     if c == 0 else ()))
        v_tr = sch.v("sPE")
        dv(lambda v, t=t: v.tensor_copy(
            out=RNT.rearrange("p (c n) -> p c n", n=2048)
                [:, :, 128 * t:128 * (t + 1)],
            in_=PSF[:, 3072:4096].rearrange("p (c n) -> p c n", n=128)),
           inc="sDV", wait=W(("sPE", v_tr)))
    v_ln2 = sch.v("sDV")
    v_x1 = v_ln2


    v_wql = load_weight_direct("wql", 512, gcol=16, wait_pe=sch.v("sPE"),
                               dv_guard=sch.v("sDV"), ac_guard=sch.v("sAC"))
    lp_state = {"vevac": sch.v("sAC"), "vvpl": 0}

    def light_projs():
        vq = v_wql[1]
        for fc in range(4):
            for dc in range(8):
                pe_mm(PSF[:, 0:2048],
                      WTMP[:, 512 * dc + 128 * fc:512 * dc + 128 * (fc + 1)],
                      RNT[:, 2048 * dc:2048 * (dc + 1)],
                      start=(dc == 0), stop=(dc == 7), n_free=2048,
                      inc=("sPE" if dc == 7 else None),
                      wait=(W(("sDV", vq), ("sAC", lp_state["vevac"]))
                            if dc == 0 else ()))
                yield
            v_pe = sch.v("sPE")
            ac(lambda a, fc=fc: a.activation(
                out=QLT[:, 2048 * fc:2048 * (fc + 1)], in_=PSF[:, 0:2048],
                func=AF.Copy, scale=1.0), inc="sAC", wait=W(("sPE", v_pe)))
            lp_state["vevac"] = sch.v("sAC")
            yield
        v_wkvl = load_weight_direct("wkvl", 1024, gcol=16,
                                    wait_pe=sch.v("sPE"),
                                    dv_guard=sch.v("sDV"),
                                    ac_guard=lp_state["vevac"])
        vkv = v_wkvl[1]
        yield
        for fc in range(4):
            for dc in range(8):
                pe_mm(PSF[:, 0:2048],
                      WTMP[:, 1024 * dc + 128 * fc:
                           1024 * dc + 128 * (fc + 1)],
                      RNT[:, 2048 * dc:2048 * (dc + 1)],
                      start=(dc == 0), stop=(dc == 7), n_free=2048,
                      inc=("sPE" if dc == 7 else None),
                      wait=(W(("sDV", vkv), ("sAC", lp_state["vevac"]))
                            if dc == 0 else ()))
                yield
            v_pe = sch.v("sPE")
            ac(lambda a, fc=fc: a.activation(
                out=KLT[:, 2048 * fc:2048 * (fc + 1)], in_=PSF[:, 0:2048],
                func=AF.Copy, scale=1.0), inc="sAC", wait=W(("sPE", v_pe)))
            lp_state["vevac"] = sch.v("sAC")
            yield
        for t in range(TT):
            for dc in range(8):
                pe_mm(PSF[:, 0:512],
                      RNT[:, 2048 * dc + 128 * t:2048 * dc + 128 * (t + 1)],
                      WTMP[:, 1024 * dc + 512:1024 * (dc + 1)],
                      start=(dc == 0), stop=(dc == 7), n_free=512,
                      inc=("sPE" if dc == 7 else None),
                      wait=(W(("sAC", lp_state["vevac"]), ("sGP", gp_init))
                            if dc == 0 else ()))
                yield
            v_pe = sch.v("sPE")
            ac(lambda a, t=t: a.activation(
                out=VPL[:, 520 * t:520 * (t + 1)]
                    .rearrange("p (h c) -> p h c", c=65)[:, :, 0:64],
                in_=PSF[:, 0:512], func=AF.Copy, scale=1.0),
               inc="sAC", wait=W(("sPE", v_pe)))
            lp_state["vevac"] = sch.v("sAC")
            lp_state["vvpl"] = sch.v("sAC")
            yield

    lpg = light_projs()

    def pump(n=1):
        for _ in range(n):
            next(lpg, None)

    v_m = dv(lambda v: v.memset(ANEW[:], A1), inc="sDV")
    dv(lambda v: v.tensor_scalar(out=NANEW[:], in0=ANEW[:], scalar1=-1.0,
                                 scalar2=None, op0=OP.mult), inc="sDV",
       wait=W(("sDV", v_m)))
    dv(lambda v: v.tensor_scalar(out=APLK[:], in0=ANEW[:], scalar1=LOGK,
                                 scalar2=None, op0=OP.add), inc="sDV",
       wait=W(("sDV", v_m)))
    v_a = sch.v("sDV")
    for it in range(NIT - 1):
        pe(lambda p: p.matmul(PSF[:, 2048:2049], ONE1F[:], NANEW[:],
                              start=True, stop=True), inc="sPE",
           wait=W(("sDV", v_a)))
        v_brc = sch.v("sPE")
        ac(lambda a: a.activation(out=NAREP[:], in_=PSF[:, 2048:2049],
                                  func=AF.Copy, scale=1.0), inc="sAC",
           wait=W(("sPE", v_brc)))
        v_nar = sch.v("sAC")
        ac(lambda a: a.activation(out=RSC[:], in_=SCOL[:], func=AF.Relu,
                                  bias=NAREP[:], scale=-1.0), inc="sAC",
           wait=W(("sAC", v_nar), ("sDV", v_ln2)))
        ac(lambda a: a.activation(out=EJ16[:], in_=RSC[:], func=AF.Exp,
                                  scale=-1.0, accum_out=EPART[:]), inc="sAC",
           wait=W(("sAC", sch.v("sAC"))))
        v_ep = sch.v("sAC")
        pe(lambda p: p.matmul(PSF[0:1, 2049:2050], EPART[:], ONECF[:],
                              start=True, stop=True), inc="sPE",
           wait=W(("sAC", v_ep)))
        v_sum = sch.v("sPE")
        ac(lambda a: a.activation(out=LNS[:], in_=PSF[0:1, 2049:2050],
                                  func=AF.Ln, scale=1.0), inc="sAC",
           wait=W(("sPE", v_sum)))
        v_lns = sch.v("sAC")
        dv(lambda v: v.scalar_tensor_tensor(
            out=ANEW[:], in0=LNS[:], scalar=-1.0, in1=APLK[:],
            op0=OP.mult, op1=OP.add), inc="sDV", wait=W(("sAC", v_lns)))
        v_an = sch.v("sDV")
        dv(lambda v: v.tensor_scalar(out=APLK[:], in0=ANEW[:], scalar1=LOGK,
                                     scalar2=None, op0=OP.add), inc="sDV",
           wait=W(("sDV", v_an)))
        dv(lambda v: v.tensor_scalar(out=NANEW[:], in0=ANEW[:], scalar1=-1.0,
                                     scalar2=None, op0=OP.mult), inc="sDV",
           wait=W(("sDV", v_an)))
        v_a = sch.v("sDV")
        pump(6)
    pe(lambda p: p.matmul(PSF[:, 2048:2049], ONE1F[:], ANEW[:],
                          start=True, stop=True), inc="sPE",
       wait=W(("sDV", v_a)))
    ac(lambda a: a.activation(out=AREP[:], in_=PSF[:, 2048:2049],
                              func=AF.Copy, scale=1.0), inc="sAC",
       wait=W(("sPE", sch.v("sPE"))))
    dv(lambda v: v.tensor_scalar(out=UCOL[:], in0=SCOL[:], scalar1=AREP[:],
                                 scalar2=0.0, op0=OP.add, op1=OP.min),
       inc="sDV", wait=W(("sAC", sch.v("sAC"))))
    v_ucol = sch.v("sDV")
    pe(lambda p: p.transpose(PSF[0:16, 2048:2176], UCOL[:], IDF[:]),
       inc="sPE", wait=W(("sDV", v_ucol)))
    dv(lambda v: v.tensor_copy(out=U16[:], in_=PSF[0:16, 2048:2176]),
       inc="sDV", wait=W(("sPE", sch.v("sPE"))))
    v_u16 = sch.v("sDV")
    UROW = ARF[0:1, 0:2048]
    sp(lambda s: s.dma_start(out=UROW, in_=U16[:]), inc="sLs", n=16,
       wait=W(("sDV", v_u16)))
    v_urow = sch.v("sLs")
    v_amat = pe_mm(PSF[:, 2048:4096], ONE1F[:], UROW, start=True, stop=True,
                   n_free=2048, inc="sPE", wait=W(("sLs", v_urow)))
    for tb in range(16):
        dv(lambda v, tb=tb: v.tensor_scalar(
            out=OUTF[:, 0:2048], in0=PSF[:, 2048:4096],
            scalar1=UCOL[:, tb:tb + 1], scalar2=None, op0=OP.is_gt),
           inc="sDV", wait=(W(("sPE", v_amat)) if tb == 0 else
                            W(("sDV", sch.v("sDV")))))
        dv(lambda v, tb=tb: v.reduce_sum(out=CCOL[:, tb:tb + 1],
                                         in_=OUTF[:, 0:2048], axis=AX),
           inc="sDV", wait=W(("sDV", sch.v("sDV"))))
        pump(2)
    v_ccol = sch.v("sDV")
    dv(lambda v: v.tensor_scalar(out=SAT16[:], in0=U16[:], scalar1=0.0,
                                 scalar2=None, op0=OP.is_ge), inc="sDV",
       wait=W(("sDV", v_ccol)))
    dv(lambda v: v.tensor_tensor_scan(out=P16[:], data0=SAT16[:],
                                      data1=SAT16[:], initial=0.0,
                                      op0=OP.add, op1=OP.bypass), inc="sDV",
       wait=W(("sDV", sch.v("sDV"))))
    dv(lambda v: v.reduce_sum(out=RS16[:], in_=SAT16[:], axis=AX), inc="sDV",
       wait=W(("sDV", sch.v("sDV"))))
    v_rs = sch.v("sDV")
    pe(lambda p: p.transpose(PSF[0:1, 2048:2064], RS16[:], IDF[0:16, 0:16]),
       inc="sPE", wait=W(("sDV", v_rs)))
    dv(lambda v: v.tensor_copy(out=OFR[:], in_=PSF[0:1, 2048:2064]),
       inc="sDV", wait=W(("sPE", sch.v("sPE"))))
    dv(lambda v: v.tensor_tensor_scan(out=OFR2[:], data0=OFR[:],
                                      data1=OFR[:], initial=0.0,
                                      op0=OP.add, op1=OP.bypass), inc="sDV",
       wait=W(("sDV", sch.v("sDV"))))
    dv(lambda v: v.memset(OFR[:, 0:1], 0.0), inc="sDV",
       wait=W(("sDV", sch.v("sDV"))))
    dv(lambda v: v.tensor_copy(out=OFR[:, 1:16], in_=OFR2[:, 0:15]),
       inc="sDV", wait=W(("sDV", sch.v("sDV"))))
    v_ofr = sch.v("sDV")
    pe(lambda p: p.transpose(PSF[0:16, 2064:2065], OFR[:], IDF[0:1, 0:1]),
       inc="sPE", wait=W(("sDV", v_ofr)))
    dv(lambda v: v.tensor_copy(out=OFF16[:], in_=PSF[0:16, 2064:2065]),
       inc="sDV", wait=W(("sPE", sch.v("sPE"))))
    v_offev = sch.v("sDV")
    dv(lambda v: v.tensor_sub(out=P16[:], in0=P16[:], in1=SAT16[:]),
       inc="sDV", wait=W(("sDV", sch.v("sDV"))))
    dv(lambda v: v.tensor_scalar(out=P16[:], in0=P16[:], scalar1=OFF16[:],
                                 scalar2=None, op0=OP.add), inc="sDV",
       wait=W(("sDV", sch.v("sDV"))))
    dv(lambda v: v.tensor_mul(out=P16[:], in0=P16[:], in1=SAT16[:]),
       inc="sDV", wait=W(("sDV", sch.v("sDV"))))
    v_p16 = sch.v("sDV")
    pe(lambda p: p.transpose(PSF[0:16, 2176:2304], CCOL[:], IDF[:]),
       inc="sPE", wait=W(("sDV", max(v_ccol, v_offev))))
    dv(lambda v: v.tensor_copy(out=C16[:], in_=PSF[0:16, 2176:2304]),
       inc="sDV", wait=W(("sPE", sch.v("sPE"))))
    dv(lambda v: v.tensor_add(out=R16[:] if False else SEL16[:], in0=C16[:],
                              in1=P16[:]), inc="sDV",
       wait=W(("sDV", sch.v("sDV"))))
    dv(lambda v: v.tensor_scalar(out=SEL16[:], in0=SEL16[:], scalar1=1023.5,
                                 scalar2=None, op0=OP.is_lt), inc="sDV",
       wait=W(("sDV", sch.v("sDV"))))
    v_sel16 = sch.v("sDV")
    pe(lambda p: p.transpose(PSF[:, 2304:2320], SEL16[:], IDF[0:16, 0:16]),
       inc="sPE", wait=W(("sDV", v_sel16)))
    dv(lambda v: v.tensor_copy(out=SELCOL[:], in_=PSF[:, 2304:2320]),
       inc="sDV", wait=W(("sPE", sch.v("sPE"))))
    v_sel = sch.v("sDV")

    pump(400)
    v_vpl = lp_state["vvpl"]

    lt_div = sch.v("sDV")
    cte_prev = sch.v("sAC")
    exp_w = {}
    nw = 0
    for h in range(H2):
        chl, rhl = h // 2, (h % 2) * 64
        QLh = QLT[rhl:rhl + 64, 2048 * chl:2048 * (chl + 1)]
        KLh = KLT[rhl:rhl + 64, 2048 * chl:2048 * (chl + 1)]
        for wg in range(2):
            for wi in range(8):
                w = 8 * wg + wi
                pb = 512 * (wi % 2)
                pe(lambda p, w=w, pb=pb, QLh=QLh, KLh=KLh: p.matmul(
                    PSF[:, pb:pb + 128], KLh[:, 128 * w:128 * (w + 1)],
                    QLh[:, 128 * w:128 * (w + 1)], start=True, stop=True),
                   inc="sPE",
                   wait=W(("sAC", max(v_vpl, exp_w.get(nw - 2, 0)))))
                v_qk = sch.v("sPE")
                ac(lambda a, wi=wi, pb=pb: a.activation(
                    out=PTE[:, 128 * wi:128 * (wi + 1)],
                    in_=PSF[:, pb:pb + 128], func=AF.Exp, scale=SC),
                   inc="sAC", wait=W(("sPE", v_qk)))
                exp_w[nw] = sch.v("sAC")
                exp_prev = exp_w[nw]
                nw += 1
                pe(lambda p, w=w, h=h, wi=wi: p.matmul(
                    PSF[0:65, 3072 + 128 * wi:3072 + 128 * (wi + 1)],
                    VPL[:, 520 * w + 65 * h:520 * w + 65 * (h + 1)],
                    PTE[:, 128 * wi:128 * (wi + 1)],
                    start=True, stop=True),
                   inc="sPE", wait=W(("sAC", max(exp_prev, cte_prev)),
                                     ("sDV", v_ccol if (h == 0 and wg == 0
                                                        and wi == 0) else 0)))
            v_av = sch.v("sPE")
            ac(lambda a: a.activation(out=CTE[0:65, :],
                                      in_=PSF[0:65, 3072:4096],
                                      func=AF.Copy, scale=1.0),
               inc="sAC", wait=W(("sPE", v_av), ("sDV", lt_div)))
            cte_prev = sch.v("sAC")
            dv(lambda v: v.reciprocal(out=RZROW[:], in_=CTE[64:65, :]),
               inc="sDV", wait=W(("sAC", sch.v("sAC")),
                                 ("sDV", sch.v("sDV"))))
            dv(lambda v: v.tensor_copy(out=RZROWB[:], in_=RZROW[:]),
               inc="sDV", wait=W(("sDV", sch.v("sDV"))))
            v_rz = sch.v("sDV")
            pe_mm(PSF[0:64, 2048:3072], ONE1B[0:1, 0:64], RZROWB[:],
                  start=True, stop=True, n_free=1024, inc="sPE",
                  wait=W(("sDV", v_rz)))
            v_rb = sch.v("sPE")
            dv(lambda v: v.tensor_copy(out=RCPS[:], in_=PSF[0:64, 2048:3072]),
               inc="sDV", wait=W(("sPE", v_rb), ("sDV", sch.v("sDV"))))
            v_rc = sch.v("sDV")
            dv(lambda v, h=h, wg=wg: v.tensor_tensor(
                out=CTXL[:, 2048 * (h // 2) + 1024 * wg:
                         2048 * (h // 2) + 1024 * (wg + 1)]
                    [(h % 2) * 64:(h % 2) * 64 + 64, :],
                in0=CTE[0:64, :], in1=RCPS[:], op=OP.mult),
               inc="sDV", wait=W(("sDV", v_rc)))
            lt_div = sch.v("sDV")

    load_weight_direct("wqh", 512, gcol=24, wait_pe=sch.v("sPE"),
                       dv_guard=sch.v("sDV"), ac_guard=sch.v("sAC"))
    proj_T(QHT, None, RNT, nf=4, wcols=512)
    load_weight_direct("wkvh_t", 1024, gcol=24, wait_pe=sch.v("sPE"),
                       dv_guard=sch.v("sDV"), ac_guard=sch.v("sAC"))
    vw = sch.v("sDV")
    for pc in range(4):
        for sub in range(2):
            hh = 2 * pc + sub
            for dc in range(8):
                pe_mm(PSF[64 * sub:64 * (sub + 1), 0:2048],
                      WTMP[:, 1024 * dc + 128 * hh:1024 * dc + 128 * hh + 64],
                      RNT[:, 2048 * dc:2048 * (dc + 1)],
                      start=(dc == 0), stop=(dc == 7), n_free=2048,
                      inc=("sPE" if (dc == 7 and sub == 1) else None),
                      wait=(W(("sDV", vw), ("sAC", sch.v("sAC")))
                            if (dc == 0 and sub == 0) else ()))
        v_pe = sch.v("sPE")
        ac(lambda a, pc=pc: a.activation(
            out=KVHT[:, 2048 * pc:2048 * (pc + 1)], in_=PSF[:, 0:2048],
            func=AF.Copy, scale=1.0), inc="sAC", wait=W(("sPE", v_pe)))
    v_light_done = lt_div
    for t in range(TT):
        for dc in range(8):
            pe(lambda p, t=t, dc=dc: p.matmul(
                PSF[:, 0:512],
                RNT[:, 2048 * dc + 128 * t:2048 * dc + 128 * (t + 1)],
                WTMP[:, 1024 * dc:1024 * (dc + 1)]
                    .rearrange("p (h c) -> p h c", c=128)[:, :, 64:128],
                start=(dc == 0), stop=(dc == 7)),
               inc=("sPE" if dc == 7 else None),
               wait=(W(("sAC", sch.v("sAC")), ("sDV", v_light_done),
                       ("sGP", gp_init)) if dc == 0 else ()))
        v_pe = sch.v("sPE")
        ac(lambda a, t=t: a.activation(
            out=VPH[:, 520 * t:520 * (t + 1)]
                .rearrange("p (h c) -> p h c", c=65)[:, :, 0:64],
            in_=PSF[:, 0:512], func=AF.Copy, scale=1.0),
           inc="sAC", wait=W(("sPE", v_pe)))
    v_vph = sch.v("sAC")

    hv_div = sch.v("sDV")
    cte_prev2 = sch.v("sAC")
    for h in range(H2):
        Kh2 = KVHT[(h % 2) * 64:(h % 2) * 64 + 64,
                   2048 * (h // 2):2048 * (h // 2 + 1)]
        Qh2 = QHT[(h % 2) * 64:(h % 2) * 64 + 64,
                  2048 * (h // 2):2048 * (h // 2 + 1)]
        for qh in range(2):
            exp_v = {}
            av_v = {}
            for kt in range(16):
                pb = 1024 * (kt % 2)
                for j in range(2):
                    pe(lambda p, kt=kt, qh=qh, j=j, pb=pb, Kh2=Kh2, Qh2=Qh2:
                       p.matmul(
                           PSF[:, pb + 512 * j:pb + 512 * (j + 1)],
                           Kh2[:, 128 * kt:128 * (kt + 1)],
                           Qh2[:, 1024 * qh + 512 * j:
                               1024 * qh + 512 * (j + 1)],
                           start=True, stop=True),
                       inc=("sPE" if j == 1 else None),
                       wait=(W(("sAC", max(v_vph, exp_v.get(kt - 2, 0))))
                             if j == 0 else ()))
                v_qk = sch.v("sPE")
                ac(lambda a, kt=kt, pb=pb: a.activation(
                    out=PTE[:, 1024 * (kt % 2):1024 * (kt % 2) + 1024],
                    in_=PSF[:, pb:pb + 1024], func=AF.Exp, scale=SC),
                   inc="sAC",
                   wait=W(("sPE", max(v_qk, av_v.get(kt - 2, 0)))))
                exp_v[kt] = sch.v("sAC")
                for j in range(2):
                    pe(lambda p, kt=kt, h=h, j=j: p.matmul(
                        PSF[0:65, 3072 + 512 * j:3072 + 512 * (j + 1)],
                        VPH[:, 520 * kt + 65 * h:520 * kt + 65 * (h + 1)],
                        PTE[:, 1024 * (kt % 2) + 512 * j:
                            1024 * (kt % 2) + 512 * (j + 1)],
                        start=(kt == 0), stop=(kt == 15)),
                       inc=("sPE" if j == 1 else None),
                       wait=(W(("sAC", max(exp_v[kt], cte_prev2)))
                             if j == 0 else ()))
                av_v[kt] = sch.v("sPE")
            v_av = sch.v("sPE")
            ac(lambda a: a.activation(out=CTE[0:65, :],
                                      in_=PSF[0:65, 3072:4096],
                                      func=AF.Copy, scale=1.0),
               inc="sAC", wait=W(("sPE", v_av), ("sDV", hv_div)))
            cte_prev2 = sch.v("sAC")
            dv(lambda v: v.reciprocal(out=RZROW[:], in_=CTE[64:65, :]),
               inc="sDV", wait=W(("sAC", sch.v("sAC")),
                                 ("sDV", sch.v("sDV"))))
            dv(lambda v: v.tensor_copy(out=RZROWB[:], in_=RZROW[:]),
               inc="sDV", wait=W(("sDV", sch.v("sDV"))))
            v_rz = sch.v("sDV")
            pe_mm(PSF[0:64, 2048:3072], ONE1B[0:1, 0:64], RZROWB[:],
                  start=True, stop=True, n_free=1024, inc="sPE",
                  wait=W(("sDV", v_rz)))
            v_rb = sch.v("sPE")
            dv(lambda v: v.tensor_copy(out=RCPS[:], in_=PSF[0:64, 2048:3072]),
               inc="sDV", wait=W(("sPE", v_rb), ("sDV", sch.v("sDV"))))
            v_rc = sch.v("sDV")
            dv(lambda v, h=h, qh=qh: v.tensor_tensor(
                out=CTXH[:, 2048 * (h // 2) + 1024 * qh:
                         2048 * (h // 2) + 1024 * (qh + 1)]
                    [(h % 2) * 64:(h % 2) * 64 + 64, :],
                in0=CTE[0:64, :], in1=RCPS[:], op=OP.mult),
               inc="sDV", wait=W(("sDV", v_rc)))
            hv_div = sch.v("sDV")

    load_weight_direct("wol", 1024, rows=512, woff=0,
                       wait_pe=sch.v("sPE"), dv_guard=sch.v("sDV"))
    v_wol = load_weight_direct("woh", 1024, rows=512, woff=4096)
    st_done = {0: 0, 1: 0}
    for t in range(TT):
        po = 2048 * (t % 2)
        so = 1024 * (t % 2)
        stx = "sSTa" if t % 2 == 0 else "sSTb"
        for dc in range(4):
            pe_mm(PSF[:, po:po + 1024],
                  CTXL[:, 2048 * dc + 128 * t:2048 * dc + 128 * (t + 1)],
                  WTMP[:, 1024 * dc:1024 * (dc + 1)],
                  start=(dc == 0), stop=(dc == 3), n_free=1024,
                  wait=(W(v_wol, ("sDV", max(hv_div, sch.v("sDV"))))
                        if dc == 0 else ()))
        for dc in range(4):
            pe_mm(PSF[:, po + 1024:po + 2048],
                  CTXH[:, 2048 * dc + 128 * t:2048 * dc + 128 * (t + 1)],
                  WTMP[:, 4096 + 1024 * dc:4096 + 1024 * (dc + 1)],
                  start=(dc == 0), stop=(dc == 3), n_free=1024,
                  inc=("sPE" if dc == 3 else None))
        v_pe = sch.v("sPE")
        dv(lambda v, t=t, po=po, so=so: v.scalar_tensor_tensor(
            out=OUTF[:, so:so + 1024], in0=PSF[:, po + 1024:po + 2048],
            scalar=SELCOL[:, t:t + 1], in1=XB[:, D * t:D * (t + 1)],
            op0=OP.mult, op1=OP.add),
           inc="sDV",
           wait=W(("sPE", v_pe), ("sDV", sch.v("sDV")), ("sDV", v_sel),
                  (stx, st_done[t % 2])))
        dv(lambda v, po=po, so=so: v.tensor_tensor(
            out=OUTF[:, so:so + 1024], in0=PSF[:, po:po + 1024],
            in1=OUTF[:, so:so + 1024], op=OP.add),
           inc="sDV", wait=W(("sDV", sch.v("sDV"))))
        v_of = sch.v("sDV")
        sp(lambda s, t=t, so=so: s.dma_start(
            out=out_t[128 * t:128 * (t + 1), :], in_=OUTF[:, so:so + 1024]),
           inc=stx, n=16, wait=W(("sDV", v_of)))
        st_done[t % 2] = sch.v(stx)
    if upto == 'fulldbg':
        v_fin = sch.v("sDV")
        for a in range(8):
            gp(lambda g, a=a: g.dma_start(
                out=dbg_t[128 * a:128 * (a + 1), :],
                in_=CTXL[:, 1024 * a:1024 * (a + 1)]),
               inc="sGW", n=16, wait=W(("sDV", v_fin), ("sGW", sch.v("sGW"))))
        for a in range(8):
            gp(lambda g, a=a: g.dma_start(
                out=dbg_t[1024 + 128 * a:1024 + 128 * (a + 1), :],
                in_=CTXH[:, 1024 * a:1024 * (a + 1)]),
               inc="sGW", n=16, wait=W(("sGW", sch.v("sGW"))))
        for src_ap, c0 in ((SELCOL, 0), (UCOL, 16), (CCOL, 32), (SCOL, 48)):
            gp(lambda g, src_ap=src_ap, c0=c0: g.dma_start(
                out=dbg_t[2048:2176, c0:c0 + 16], in_=src_ap[:, :],
                allow_slow_non_contiguous=True),
               inc="sGW", n=16, wait=W(("sGW", sch.v("sGW"))))
        for j, a in enumerate((6, 7)):
            gp(lambda g, j=j, a=a: g.dma_start(
                out=dbg_t[2176 + 128 * j:2176 + 128 * (j + 1), :],
                in_=XB[:, 1024 * a:1024 * (a + 1)]),
               inc="sGW", n=16, wait=W(("sGW", sch.v("sGW"))))
        sch.waitq("sp", ("sSTa", sch.v("sSTa")), ("sSTb", sch.v("sSTb")),
                  ("sGW", sch.v("sGW")))
    else:
        sch.waitq("sp", ("sSTa", sch.v("sSTa")), ("sSTb", sch.v("sSTb")))

    with nc.Block() as block:
        @block.sync
        def _(h):
            for f in sch.q["sp"]:
                f(h)

        @block.gpsimd
        def _(h):
            for f in sch.q["gp"]:
                f(h)

        @block.vector
        def _(h):
            for f in sch.q["dv"]:
                f(h)

        @block.scalar
        def _(h):
            for f in sch.q["ac"]:
                f(h)

        @block.tensor
        def _(h):
            for f in sch.q["pe"]:
                f(h)

    es.close()
    return nc


_NC_CACHE = [None]


def kernel(L0_attn_x, L0_attn_Wq, L0_attn_bq, L0_attn_Wk, L0_attn_bk,
           L0_attn_Wv, L0_attn_bv, L0_attn_Wo, L0_attn_bo,
           L0_ffn_gamma_light, L0_ffn_wq_light, L0_ffn_wkv_light,
           L0_ffn_wo_light, L0_ffn_q_routing_token, L0_ffn_kv_routing_token,
           L0_ffn_gamma_heavy, L0_ffn_wq_heavy, L0_ffn_wkv_heavy,
           L0_ffn_wo_heavy):
    x = np.ascontiguousarray(np.asarray(L0_attn_x, dtype=np.float32))
    f = lambda a: np.ascontiguousarray(np.asarray(a, dtype=np.float32))
    row = lambda a: np.ascontiguousarray(
        np.asarray(a, dtype=np.float32).reshape(1, -1))
    base = {
        "wq_t": np.ascontiguousarray(f(L0_attn_Wq).T),
        "wk_t": np.ascontiguousarray(f(L0_attn_Wk).T),
        "wv_t": np.ascontiguousarray(f(L0_attn_Wv).T),
        "wo_t": np.ascontiguousarray(f(L0_attn_Wo).T),
        "wkvl": f(L0_ffn_wkv_light),
        "wkvh_t": np.ascontiguousarray(f(L0_ffn_wkv_heavy).T),
        "wql": f(L0_ffn_wq_light), "wqh": f(L0_ffn_wq_heavy),
        "wol": f(L0_ffn_wo_light), "woh": f(L0_ffn_wo_heavy),
        "bq": row(L0_attn_bq), "bk": row(L0_attn_bk), "bv": row(L0_attn_bv),
        "bo": row(L0_attn_bo), "g_l": row(L0_ffn_gamma_light),
        "g_h": row(L0_ffn_gamma_heavy), "rq": row(L0_ffn_q_routing_token),
    }
    if _NC_CACHE[0] is None:
        _NC_CACHE[0] = build_program()
    nc = _NC_CACHE[0]
    in_maps = [dict(base, x=np.ascontiguousarray(x[b])) for b in range(8)]
    res = run_bass_kernel_spmd(nc, in_maps, core_ids=list(range(8)))
    return np.stack([np.asarray(r["out"], dtype=np.float32)
                     for r in res.results], axis=0)


# version 14
# speedup vs baseline: 1.3528x; 1.0513x over previous
import math
from contextlib import ExitStack

import numpy as np
import concourse.bass as bass
import concourse.mybir as mybir
from concourse.bass_utils import run_bass_kernel_spmd

F32 = mybir.dt.float32
BF16 = mybir.dt.bfloat16
AX = mybir.AxisListType.X
OP = mybir.AluOpType
AF = mybir.ActivationFunctionType

T, D, TT = 2048, 1024, 16
H1, H2, HID = 16, 8, 512
NEG = -1e38
SC = 0.125
ROUTE_K = 1152.0
LOGK = float(np.log(np.float32(ROUTE_K)))
A1 = float(np.log(np.float32(ROUTE_K)) - np.log(np.float32(T)))
NIT = 50


class Sched:

    ENGS = ("sp", "gp", "dv", "ac", "pe")

    def __init__(self):
        self.q = {e: [] for e in self.ENGS}
        self.sem = {}
        self.semh = {}

    def op(self, eng, fn, inc=None, n=1, wait=()):
        semh = self.semh
        wait = tuple((s, v) for s, v in wait if v > 0)
        tgt = None
        if inc is not None:
            self.sem[inc] += n
            tgt = (semh[inc], n)

        def run(h, fn=fn, wait=wait, tgt=tgt):
            for s, v in wait:
                h.wait_ge(semh[s], v)
            inst = fn(h)
            if tgt is not None and inst is not None:
                inst.then_inc(*tgt)

        self.q[eng].append(run)
        return self.sem[inc] if inc is not None else None

    def waitq(self, eng, *sv):
        semh = self.semh
        sv = tuple(sv)

        def run(h, sv=sv):
            for s, v in sv:
                if v > 0:
                    h.wait_ge(semh[s], v)

        self.q[eng].append(run)

    def v(self, s):
        return self.sem[s]


def build_program(upto='full'):
    nc = bass.Bass("TRN2", target_bir_lowering=False, debug=False)
    sch = Sched()

    di = {}
    for name, shape in (
        ("x", [T, D]), ("wq_t", [D, D]), ("wk_t", [D, D]), ("wv_t", [D, D]),
        ("wo_t", [D, D]), ("wkvl", [D, D]), ("wkvh_t", [D, D]),
        ("wql", [D, HID]), ("wqh", [D, HID]), ("wol", [HID, D]),
        ("woh", [HID, D]), ("bq", [1, D]), ("bk", [1, D]), ("bv", [1, D]),
        ("bo", [1, D]), ("g_l", [1, D]), ("g_h", [1, D]), ("rq", [1, D]),
    ):
        di[name] = nc.dram_tensor(name, shape, F32, kind="ExternalInput").ap()
    out_t = nc.dram_tensor("out", [T, D], F32, kind="ExternalOutput").ap()
    dbg_t = (nc.dram_tensor("dbg", [2560, 1024], F32, kind="ExternalOutput")
             .ap() if upto == 'fulldbg' else None)

    es = ExitStack()
    Sb = lambda nm, sh, dt=F32: es.enter_context(nc.sbuf_tensor(nm, sh, dt))

    ARN = Sb("ARN", [128, 90384], BF16)
    XB = ARN[:, 0:16384]
    XNT = ARN[:, 16384:32768]
    CT1 = ARN[:, 16384:32768]
    CTXL = ARN[:, 16384:24576]
    CTXH = ARN[:, 24576:32768]
    QT = ARN[:, 32768:49152]
    RNT = ARN[:, 32768:49152]
    KT = ARN[:, 49152:65536]
    QLT = ARN[:, 49152:57344]
    KLT = ARN[:, 57344:65536]
    KVHT = ARN[:, 49152:57344]
    QHT = ARN[:, 57344:65536]
    VP = ARN[:, 65536:82176]
    VPL = ARN[:, 65536:73856]
    VPH = ARN[:, 73856:82176]
    SBT = ARN[:, 82176:86272]
    SBX = ARN[:, 86272:86784]
    PTE = ARN[:, 86784:88832]
    CND = ARN[:, 88832:89088]
    CN2 = ARN[:, 89088:89344]
    R8 = ARN[:, 89344:89360]
    CTE = ARN[:, 89360:90384]
    WTMP = ARN[:, 82176:90368]

    ARF = Sb("ARF", [128, 4096], F32)
    SBMB = Sb("SBMB", [128, 4096], BF16)
    NEGI = Sb("NEGI", [128, 128], BF16)
    SBM = ARF[:, 0:4096]
    OUTF = ARF[:, 0:2048]
    SPROD = ARF[:, 1024:2048]
    XNF = ARF[:, 2048:3072]
    R128 = ARF[:, 3072:4096]

    IDF = Sb("IDF", [128, 128], F32)
    BIASC = Sb("BIASC", [128, 32], F32)
    BVROW = ARF[0:1, 3072:4096]
    BOROW = None
    ONE1F = Sb("ONE1F", [1, 128], F32)
    ONE1B = Sb("ONE1B", [1, 128], BF16)
    ONECF = Sb("ONECF", [128, 1], F32)
    EPSC = Sb("EPSC", [128, 1], F32)
    TCOL = Sb("TCOL", [128, 2], F32)
    RZROWB = Sb("RZROWB", [1, 1024], BF16)
    RCPS = Sb("RCPS", [64, 1024], BF16)
    MCOL = Sb("MCOL", [128, 1], F32)
    RSTD = Sb("RSTD", [128, 1], F32)
    SSQ = Sb("SSQ", [128, 1], F32)
    RSN = Sb("RSN", [128, 1], F32)
    BN12 = Sb("BN12", [128, 12], F32)
    BN2 = Sb("BN2", [128, 2], F32)
    RNF = ARF[:, 0:1024]
    SQJ = ARN[:, 49152:50176]
    SCOL = Sb("SCOL", [128, 16], F32)
    UCOL = Sb("UCOL", [128, 16], F32)
    CCOL = Sb("CCOL", [128, 16], F32)
    SELCOL = Sb("SELCOL", [128, 16], F32)
    U16 = ARF[0:16, 3072:3200]
    SAT16 = ARF[0:16, 3200:3328]
    P16 = ARF[0:16, 3328:3456]
    SEL16 = ARF[0:16, 3456:3584]
    C16 = U16
    RS16 = ARF[0:16, 3584:3585]
    OFF16 = ARF[0:16, 3585:3586]
    OFR = ARF[0:1, 3586:3602]
    OFR2 = ARF[0:1, 3602:3618]
    AREP = ARF[:, 3618:3619]
    NAREP = ARF[:, 3619:3620]
    RSC = ARF[:, 3620:3636]
    EJ16 = Sb("EJ16", [128, 16], BF16)
    EPART = Sb("EPART", [128, 1], F32)
    LNS = Sb("LNS", [1, 1], F32)
    ANEW = Sb("ANEW", [1, 1], F32)
    NANEW = Sb("NANEW", [1, 1], F32)
    APLK = Sb("APLK", [1, 1], F32)

    PSF = es.enter_context(nc.psum_tensor("PSF", [128, 4096], F32))
    RZROW = PSF[64:65, 2048:3072]

    for s in ("sLs", "sLx0", "sLx1", "sLw", "sSTa", "sSTb",
              "sPE", "sDV", "sAC", "sGP", "sGW"):
        sch.semh[s] = es.enter_context(nc.semaphore(s))
        sch.sem[s] = 0

    W = lambda *sv: tuple(sv)
    sp = lambda fn, **k: sch.op("sp", fn, **k)
    gp = lambda fn, **k: sch.op("gp", fn, **k)
    dv = lambda fn, **k: sch.op("dv", fn, **k)
    ac = lambda fn, **k: sch.op("ac", fn, **k)
    pe = lambda fn, **k: sch.op("pe", fn, **k)

    def pe_mm(out_ap, lhs, rhs, start, stop, n_free, is_tr=False,
              inc=None, wait=()):
        nslc = (n_free + 511) // 512
        for j in range(nslc):
            c0, c1 = 512 * j, min(512 * (j + 1), n_free)
            pe(lambda p, c0=c0, c1=c1, out_ap=out_ap, lhs=lhs, rhs=rhs,
               start=start, stop=stop, is_tr=is_tr: p.matmul(
                   out_ap[:, c0:c1], lhs, rhs[:, c0:c1], start=start,
                   stop=stop, is_transpose=is_tr),
               inc=(inc if j == nslc - 1 else None),
               wait=(wait if j == 0 else ()))
        return sch.v(inc) if inc else None

    gp(lambda g: g.memset(IDF[:], 0.0), inc="sGP")
    gp(lambda g: g.affine_select(
        out=IDF[:], in_=IDF[:], compare_op=OP.not_equal, fill=1.0, base=0,
        pattern=[[-1, 128]], channel_multiplier=1), inc="sGP",
        wait=W(("sGP", sch.v("sGP"))))
    gp(lambda g: g.memset(NEGI[:], 0.0), inc="sGP")
    gp(lambda g: g.affine_select(
        out=NEGI[:], in_=NEGI[:], compare_op=OP.not_equal, fill=-1e30,
        base=0, pattern=[[-1, 128]], channel_multiplier=1), inc="sGP",
        wait=W(("sGP", sch.v("sGP"))))
    gp(lambda g: g.memset(ONE1F[:], 1.0), inc="sGP")
    gp(lambda g: g.memset(ONE1B[:], 1.0), inc="sGP")
    gp(lambda g: g.memset(ONECF[:], 1.0), inc="sGP")
    gp(lambda g: g.memset(EPSC[:], 1e-5), inc="sGP")
    gp(lambda g: g.memset(VP[:], 1.0), inc="sGP")
    gp_init = sch.v("sGP")

    BQROW = ARF[32:33, 3072:4096]
    BKROW = ARF[64:65, 3072:4096]
    BQROWB = ARN[0:1, 67584:68608]
    BKROWB = ARN[0:1, 68608:69632]
    ONESROW = VP[0:1, 0:2048]
    for rowt, name in ((BQROW, "bq"), (BKROW, "bk")):
        sp(lambda s, rowt=rowt, name=name: s.dma_start(
            out=rowt, in_=di[name]), inc="sLs", n=16)
    for name, cs in (("g_l", 16), ("g_h", 24)):
        sp(lambda s, name=name, cs=cs: s.dma_start(
            out=BIASC[:, cs:cs + 8],
            in_=di[name].rearrange("o (c p) -> (o p) c", p=128),
            allow_slow_non_contiguous=True), inc="sLs", n=16)
    sp(lambda s: s.dma_start(out=BVROW, in_=di["bv"]), inc="sLs", n=16)
    v_small = sch.v("sLs")
    dv(lambda v: v.tensor_copy(out=BQROWB, in_=BQROW), inc="sDV",
       wait=W(("sLs", v_small), ("sGP", gp_init)))
    dv(lambda v: v.tensor_copy(out=BKROWB, in_=BKROW), inc="sDV",
       wait=W(("sLs", v_small), ("sGP", gp_init)))

    ln_done = {}
    xnf_done = {}
    for t in range(TT):
        slx = "sLx0"
        sp(lambda s, t=t: s.dma_start(
            out=XNF[:], in_=di["x"][128 * t:128 * (t + 1), :]),
           inc=slx, n=16, wait=W(("sDV", xnf_done.get(t - 1, 0))))
        vld = sch.v(slx)
        dv(lambda v, t=t: v.tensor_copy(
            out=XB[:, D * t:D * (t + 1)], in_=XNF[:]),
           inc="sDV", wait=W((slx, vld)))
        dv(lambda v: v.bn_stats(out=BN12[:, 0:6],
                                in_=XNF[:, 0:512]), inc="sDV")
        dv(lambda v: v.bn_stats(out=BN12[:, 6:12],
                                in_=XNF[:, 512:1024]),
           inc="sDV")
        dv(lambda v: v.bn_aggr(out=BN2[:], in_=BN12[:]), inc="sDV",
           wait=W(("sDV", sch.v("sDV"))))
        dv(lambda v: v.tensor_copy(out=MCOL[:], in_=BN2[:, 0:1]), inc="sDV",
           wait=W(("sDV", sch.v("sDV"))))
        v_bn = sch.v("sDV")
        ac(lambda a: a.activation(out=SSQ[:], in_=BN2[:, 1:2], func=AF.Sqrt,
                                  bias=EPSC[:], scale=1.0), inc="sAC",
           wait=W(("sDV", v_bn), ("sGP", gp_init)))
        dv(lambda v: v.reciprocal(out=RSTD[:], in_=SSQ[:]), inc="sDV",
           wait=W(("sAC", sch.v("sAC"))))
        dv(lambda v: v.tensor_scalar(
            out=XNF[:], in0=XNF[:], scalar1=MCOL[:],
            scalar2=RSTD[:], op0=OP.subtract, op1=OP.mult), inc="sDV",
           wait=W(("sDV", sch.v("sDV"))))
        v_xnf = sch.v("sDV")
        for c in range(8):
            pe(lambda p, c=c: p.transpose(
                PSF[:, 3072 + 128 * c:3200 + 128 * c],
                XNF[:, 128 * c:128 * (c + 1)], IDF[:]),
               inc=("sPE" if c == 7 else None),
               wait=(W(("sDV", v_xnf), ("sGP", 2),
                       ("sDV", ln_done.get(t - 1, 0))) if c == 0 else ()))
        v_tr = sch.v("sPE")
        dv(lambda v, t=t: v.tensor_copy(
            out=XNT.rearrange("p (c n) -> p c n", n=2048)
                [:, :, 128 * t:128 * (t + 1)],
            in_=PSF[:, 3072:4096].rearrange("p (c n) -> p c n", n=128)),
           inc="sDV", wait=W(("sPE", v_tr)))
        ln_done[t] = sch.v("sDV")
        xnf_done[t] = sch.v("sDV")

    def load_weight_direct(name, ncols, rows=1024, woff=0, wait_pe=0,
                           dv_guard=0, ac_guard=0, gcol=None):
        vs = []
        for c in range(rows // 128):
            gp(lambda g, c=c, name=name, ncols=ncols, woff=woff: g.dma_start(
                out=WTMP[:, woff + ncols * c:woff + ncols * (c + 1)],
                in_=di[name][128 * c:128 * (c + 1), 0:ncols]),
               inc="sGW", n=16,
               wait=(W(("sGW", sch.v("sGW")), ("sPE", wait_pe),
                       ("sDV", dv_guard), ("sAC", ac_guard)) if c == 0 else
                     W(("sGW", sch.v("sGW")))))
            vs.append(sch.v("sGW"))
        if gcol is None:
            return ("sGW", sch.v("sGW"))
        for c in range(rows // 128):
            dv(lambda v, c=c, ncols=ncols, woff=woff, gcol=gcol:
               v.tensor_scalar(
                   out=WTMP[:, woff + ncols * c:woff + ncols * (c + 1)],
                   in0=WTMP[:, woff + ncols * c:woff + ncols * (c + 1)],
                   scalar1=BIASC[:, gcol + c:gcol + c + 1],
                   scalar2=None, op0=OP.mult),
               inc="sDV", wait=W(("sGW", vs[c]), ("sLs", v_small),
                                 ("sDV", sch.v("sDV"))))
        return ("sDV", sch.v("sDV"))

    def proj_T(dst, bias_row, src_T, nf=8, wcols=1024, wwait=None):
        if wwait is None:
            wwait = ("sDV", sch.v("sDV"))
        for fc in range(nf):
            for dc in range(8):
                pe_mm(PSF[:, 0:2048],
                      WTMP[:, wcols * dc + 128 * fc:
                           wcols * dc + 128 * (fc + 1)],
                      src_T[:, 2048 * dc:2048 * (dc + 1)],
                      start=(dc == 0), stop=(dc == 7 and bias_row is None),
                      n_free=2048,
                      inc=("sPE" if (dc == 7 and bias_row is None) else None),
                      wait=(W(wwait, ("sDV", sch.v("sDV")),
                              ("sAC", sch.v("sAC")),
                              ("sLs", v_small)) if dc == 0 else ()))
            if bias_row is not None:
                pe_mm(PSF[:, 0:2048], bias_row[0:1, 128 * fc:128 * (fc + 1)],
                      ONESROW, start=False, stop=True, n_free=2048,
                      inc="sPE", wait=W(("sDV", sch.v("sDV")),
                                        ("sGP", gp_init)))
            vpe = sch.v("sPE")
            ac(lambda a, fc=fc, dst=dst: a.activation(
                out=dst[:, 2048 * fc:2048 * (fc + 1)], in_=PSF[:, 0:2048],
                func=AF.Copy, scale=1.0), inc="sAC", wait=W(("sPE", vpe)))
        return sch.v("sAC")

    v_w = load_weight_direct("wq_t", 1024)
    proj_T(QT, BQROWB, XNT, wwait=v_w)
    v_w = load_weight_direct("wk_t", 1024, wait_pe=sch.v("sPE"))
    proj_T(KT, BKROWB, XNT, wwait=v_w)
    v_w = load_weight_direct("wv_t", 1024, wait_pe=sch.v("sPE"))
    for t in range(TT):
        for dc in range(8):
            pe_mm(PSF[:, 0:1024],
                  XNT[:, 2048 * dc + 128 * t:2048 * dc + 128 * (t + 1)],
                  WTMP[:, 1024 * dc:1024 * (dc + 1)],
                  start=(dc == 0), stop=False, n_free=1024,
                  wait=(W(v_w, ("sDV", sch.v("sDV")),
                          ("sAC", sch.v("sAC")),
                          ("sLs", v_small)) if dc == 0 else ()))
        pe_mm(PSF[:, 0:1024], ONE1F[:], BVROW[:], start=False, stop=True,
              n_free=1024, inc="sPE")
        vpe = sch.v("sPE")
        ac(lambda a, t=t: a.activation(
            out=VP[:, 1040 * t:1040 * (t + 1)]
                .rearrange("p (h c) -> p h c", c=65)[:, :, 0:64],
            in_=PSF[:, 0:1024], func=AF.Copy, scale=1.0),
           inc="sAC", wait=W(("sPE", vpe), ("sGP", gp_init)))
    v_vp = sch.v("sAC")

    evac_done = {}
    pen_done = {}
    qk_done = {}
    sel_tcol = {}
    pen0_relu = {}
    st_last = {}
    exp_pair = {}
    av_kt = {}
    div_done = 0

    def b1_qh_kh(ei):
        h = ei // 8
        ch, rh = h // 2, (h % 2) * 64
        return (QT[rh:rh + 64, 2048 * ch:2048 * (ch + 1)],
                KT[rh:rh + 64, 2048 * ch:2048 * (ch + 1)])

    def emit_qksel(ei, qi):
        Qh, Kh = b1_qh_kh(ei)
        qt = 2 * (ei % 8) + qi
        ev = evac_done.get((ei, 0), 0) if qi == 1 else \
            evac_done.get((ei - 1, 1), 0)
        for j in range(4):
            pe(lambda p, qt=qt, j=j, Qh=Qh, Kh=Kh: p.matmul(
                PSF[:, 512 * j:512 * (j + 1)],
                Qh[:, 128 * qt:128 * (qt + 1)],
                Kh[:, 512 * j:512 * (j + 1)],
                start=True, stop=True),
               inc=("sPE" if j == 3 else None),
               wait=(W(("sAC", max(ev, v_vp))) if j == 0 else ()))
        qk_done[(ei, qi)] = sch.v("sPE")

    def emit_evac(ei, qi):
        sb = SBT[:, 2048 * qi:2048 * (qi + 1)]
        psem, pval = pen_done.get((ei - 1, qi), ("sDV", 0))
        ac(lambda a, sb=sb: a.activation(out=sb, in_=PSF[:, 0:2048],
                                         func=AF.Copy, scale=1.0),
           inc="sAC", wait=W(("sPE", qk_done[(ei, qi)]), (psem, pval)))
        evac_done[(ei, qi)] = sch.v("sAC")

    def emit_sel_l1(ei, qi, mid=None):
        sb = SBT[:, 2048 * qi:2048 * (qi + 1)]
        cnd = (CND if qi == 0 else CN2)[:, 0:256]
        for c in range(16):
            dv(lambda v, c=c, sb=sb, cnd=cnd: v.max(
                out=cnd[:, 8 * c:8 * (c + 1)],
                in_=sb[:, 128 * c:128 * (c + 1)]),
               inc="sDV",
               wait=(W(("sAC", evac_done[(ei, qi)]), ("sDV", sch.v("sDV")))
                     if c == 0 else ()))
            if c == 3 and mid is not None:
                mid()
        v_r1 = sch.v("sDV")
        vgrp = v_r1
        for grp in range(4):
            for c in range(4 * grp, 4 * grp + 4):
                dv(lambda v, c=c, sb=sb, cnd=cnd: v.match_replace(
                    out=SBX[:, 128 * (c % 4):128 * (c % 4) + 128],
                    in_to_replace=cnd[:, 8 * c:8 * (c + 1)],
                    in_values=sb[:, 128 * c:128 * (c + 1)],
                    imm_value=NEG),
                   inc="sDV",
                   wait=(W(("sDV", vgrp)) if c % 4 == 0 else ()))
            v_mr = sch.v("sDV")
            for c in range(4 * grp, 4 * grp + 4):
                dv(lambda v, c=c, cnd=cnd: v.max(
                    out=cnd[:, 128 + 8 * c:128 + 8 * (c + 1)],
                    in_=SBX[:, 128 * (c % 4):128 * (c % 4) + 128]),
                   inc="sDV",
                   wait=(W(("sDV", v_mr)) if c % 4 == 0 else ()))
            vgrp = sch.v("sDV")
        return sch.v("sDV")

    def emit_sel_l2(ei, v_l1):
        cands = (CND[:, 0:256], CN2[:, 0:256])
        last = {0: v_l1[0], 1: v_l1[1]}
        for rr in range(5):
            for qi in range(2):
                dv(lambda v, qi=qi: v.max(out=R8[:, 8 * qi:8 * (qi + 1)],
                                          in_=cands[qi]),
                   inc="sDV", wait=W(("sDV", last[qi])))
                last[qi] = sch.v("sDV")
            if rr < 4:
                for qi in range(2):
                    dv(lambda v, qi=qi: v.match_replace(
                        out=cands[qi], in_to_replace=R8[:, 8 * qi:8 * (qi + 1)],
                        in_values=cands[qi], imm_value=NEG),
                       inc="sDV", wait=W(("sDV", last[qi])))
                    last[qi] = sch.v("sDV")
        for qi in range(2):
            dv(lambda v, qi=qi: v.tensor_copy(
                out=TCOL[:, qi:qi + 1], in_=R8[:, 8 * qi + 5:8 * qi + 6]),
               inc="sDV", wait=W(("sDV", last[qi])))
            sel_tcol[(ei, qi)] = sch.v("sDV")
            emit_pen(ei, qi)

    def emit_pen(ei, qi):
        sb = SBT[:, 2048 * qi:2048 * (qi + 1)]
        sm = SBMB[:, 2048 * qi:2048 * (qi + 1)]
        dv(lambda v, sb=sb, sm=sm, qi=qi: v.tensor_scalar(
            out=sm, in0=sb, scalar1=TCOL[:, qi:qi + 1], scalar2=None,
            op0=OP.is_lt),
           inc="sDV", wait=W(("sDV", sch.v("sDV")),
                             ("sPE", st_last.get(ei - 1, 0))))
        pen_done[(ei, qi)] = ("sDV", sch.v("sDV"))

    def emit_av(bi, g, h, ctx_c0):
        for sub in range(2):
            kt = 2 * g + sub
            w = (W(("sAC", max(exp_pair[(bi, g)], cte_gate)))
                 if kt == 0 else
                 (W(("sAC", exp_pair[(bi, g)])) if sub == 0 else ()))
            pe(lambda p, kt=kt, h=h, ctx_c0=ctx_c0: p.matmul(
                PSF[0:65, ctx_c0:ctx_c0 + 256],
                VP[:, 1040 * kt + 65 * h:1040 * kt + 65 * (h + 1)],
                PTE[:, 256 * (kt % 8):256 * (kt % 8) + 256],
                start=(kt == 0), stop=(kt == 15)),
               inc="sPE", wait=w)
            av_kt[(bi, kt)] = sch.v("sPE")

    pend_div = None
    rcps_gate = 0
    cte_gate = 0

    def emit_div2(pd):
        dbi, v_rzb = pd
        h, e8 = dbi // 8, dbi % 8
        half = e8 // 4
        pe_mm(PSF[0:64, 2048:3072], ONE1B[0:1, 0:64], RZROWB[:],
              start=True, stop=True, n_free=1024, inc="sPE",
              wait=W(("sAC", v_rzb)))
        v_rb = sch.v("sPE")
        ac(lambda a: a.activation(out=RCPS[:],
                                  in_=PSF[0:64, 2048:3072],
                                  func=AF.Copy, scale=1.0),
           inc="sAC", wait=W(("sPE", v_rb), ("sAC", sch.v("sAC"))))
        v_rc = sch.v("sAC")

        def ct1_thunk():
            dv(lambda v, h=h, half=half: v.tensor_tensor(
                out=CT1[:, 2048 * (h // 2) + 1024 * half:
                        2048 * (h // 2) + 1024 * (half + 1)]
                    [(h % 2) * 64:(h % 2) * 64 + 64, :],
                in0=CTE[0:64, :], in1=RCPS[:], op=OP.mult),
               inc="sDV", wait=W(("sAC", v_rc), ("sDV", sch.v("sDV"))))
            return sch.v("sDV")
        return v_rc, ct1_thunk

    for step in range(130):
        ei, bi = step, step - 1
        if ei == 0:
            emit_qksel(0, 0)
        ct1_pend = None
        if pend_div is not None:
            rcps_gate, ct1_pend = emit_div2(pend_div)
            pend_div = None
        if ei < 128:
            emit_evac(ei, 0)
        if bi >= 0 and bi < 128:
            Qh, Kh = b1_qh_kh(bi)
            h, e8 = bi // 8, bi % 8
            q0 = 256 * e8
            ctx_c0 = 3072 + 256 * (e8 % 4)
            st_v = {}
            psem0, pval0 = pen_done[(bi, 0)]
            _s1, pval1 = pen_done[(bi, 1)]
            for g in range(8):
                wbase = 2048 + 512 * (g % 2)
                wguard = (exp_pair[(bi, g - 2)] if g >= 2 else
                          exp_pair.get((bi - 1, g + 6), 0))
                for sub in range(2):
                    kt = 2 * g + sub
                    po = wbase + 256 * sub
                    if g == 0 and sub == 0:
                        pw = W((psem0, pval0), ("sDV", pval1),
                               ("sAC", max(wguard, rcps_gate)))
                    elif sub == 0:
                        pw = W(("sAC", wguard))
                    else:
                        pw = ()
                    pe(lambda p, kt=kt, po=po, q0=q0, Qh=Qh, Kh=Kh:
                       p.matmul(PSF[:, po:po + 256],
                                Kh[:, 128 * kt:128 * (kt + 1)],
                                Qh[:, q0:q0 + 256],
                                start=True, stop=False),
                       wait=pw)
                    for qi2 in range(2):
                        pe(lambda p, kt=kt, po=po, qi2=qi2: p.matmul(
                            PSF[:, po + 128 * qi2:po + 128 * (qi2 + 1)],
                            SBMB[:, 2048 * qi2 + 128 * kt:
                                 2048 * qi2 + 128 * (kt + 1)],
                            NEGI[:], start=False, stop=(qi2 == 1)),
                           inc=("sPE" if qi2 == 1 else None))
                    st_v[kt] = sch.v("sPE")
                pguard = (av_kt.get((bi, 2 * g - 7), 0) if g >= 4 else
                          av_kt.get((bi - 1, 2 * g + 9), 0))
                ac(lambda a, g=g, wbase=wbase: a.activation(
                    out=PTE[:, 512 * (g % 4):512 * (g % 4) + 512],
                    in_=PSF[:, wbase:wbase + 512], func=AF.Exp, scale=SC),
                   inc="sAC", wait=W(("sPE", max(st_v[2 * g + 1], pguard))))
                exp_pair[(bi, g)] = sch.v("sAC")
                if g >= 1:
                    emit_av(bi, g - 1, h, ctx_c0)
                if g == 2 and ei < 128:
                    emit_qksel(ei, 1)
                    emit_evac(ei, 1)
                if g == 6 and ei + 1 <= 127:
                    emit_qksel(ei + 1, 0)
            emit_av(bi, 7, h, ctx_c0)
            st_last[bi] = st_v[15]
        if ei < 128 and (bi < 0 or bi >= 128):
            emit_qksel(ei, 1)
            emit_evac(ei, 1)
        if bi < 0 and ei + 1 <= 127:
            emit_qksel(ei + 1, 0)
        if ei < 128:
            mid = None
            if ct1_pend is not None:
                thunk = ct1_pend

                def mid():
                    nonlocal div_done
                    div_done = thunk()
                ct1_pend = None
            v_l1 = {0: emit_sel_l1(ei, 0, mid=mid)}
            v_l1[1] = emit_sel_l1(ei, 1)
            emit_sel_l2(ei, v_l1)
        if ct1_pend is not None:
            div_done = ct1_pend()
        if bi >= 0 and bi < 128 and bi % 4 == 3:
            v_ava = av_kt[(bi, 15)]
            ac(lambda a: a.activation(out=CTE[0:65, :],
                                      in_=PSF[0:65, 3072:4096],
                                      func=AF.Copy, scale=1.0),
               inc="sAC", wait=W(("sPE", v_ava), ("sDV", div_done),
                                 ("sAC", sch.v("sAC"))))
            cte_v = sch.v("sAC")
            cte_gate = cte_v
            dv(lambda v: v.reciprocal(out=RZROW[:], in_=CTE[64:65, :]),
               inc="sDV", wait=W(("sAC", cte_v), ("sDV", sch.v("sDV"))))
            v_rz0 = sch.v("sDV")
            ac(lambda a: a.activation(out=RZROWB[:], in_=RZROW[:],
                                      func=AF.Copy, scale=1.0),
               inc="sAC", wait=W(("sDV", v_rz0), ("sAC", sch.v("sAC"))))
            pend_div = (bi, sch.v("sAC"))

    v_wo = load_weight_direct("wo_t", 1024, wait_pe=sch.v("sPE"),
                              dv_guard=sch.v("sDV"))
    BOROW = ARN[0:1, 50176:51200]
    RQROW = ARF[0:1, 3072:4096]
    v_ac_ph2 = sch.v("sAC")
    gp(lambda g: g.dma_start(out=BOROW, in_=di["bo"]), inc="sGW", n=16,
       wait=W(("sGW", sch.v("sGW")), ("sPE", sch.v("sPE"))))
    v_bo = sch.v("sGW")
    sp(lambda s: s.dma_start(out=RQROW, in_=di["rq"]), inc="sLs", n=16,
       wait=W(("sDV", sch.v("sDV"))))
    v_rq = sch.v("sLs")
    pe_mm(PSF[:, 2048:3072], ONE1F[:], RQROW, start=True, stop=True,
          n_free=1024, inc="sPE",
          wait=W(("sDV", div_done), ("sAC", rcps_gate), ("sLs", v_rq)))
    v_rqbc = sch.v("sPE")
    dv(lambda v: v.tensor_copy(out=R128[:], in_=PSF[:, 2048:3072]),
       inc="sDV", wait=W(("sPE", v_rqbc), ("sDV", sch.v("sDV"))))
    v_r128 = sch.v("sDV")

    for t in range(TT):
        po = 1024 * (t % 2)
        for dc in range(8):
            pe_mm(PSF[:, po:po + 1024],
                  CT1[:, 2048 * dc + 128 * t:2048 * dc + 128 * (t + 1)],
                  WTMP[:, 1024 * dc:1024 * (dc + 1)],
                  start=(dc == 0), stop=False, n_free=1024,
                  wait=(W(v_wo, ("sDV", max(div_done, sch.v("sDV"))))
                        if dc == 0 else ()))
        pe_mm(PSF[:, po:po + 1024], ONE1B[:], BOROW, start=False,
              stop=True, n_free=1024, inc="sPE",
              wait=W(("sGW", v_bo)))
        v_pe = sch.v("sPE")
        dv(lambda v, t=t, po=po: v.scalar_tensor_tensor(
            out=XB[:, D * t:D * (t + 1)], in0=PSF[:, po:po + 1024],
            scalar=1.0, in1=XB[:, D * t:D * (t + 1)], op0=OP.mult,
            op1=OP.add), inc="sDV", wait=W(("sPE", v_pe),
                                           ("sDV", sch.v("sDV"))))
        xt = XB[:, D * t:D * (t + 1)]
        dv(lambda v, xt=xt: v.bn_stats(out=BN12[:, 0:6], in_=xt[:, 0:512]),
           inc="sDV", wait=W(("sDV", sch.v("sDV"))))
        dv(lambda v, xt=xt: v.bn_stats(out=BN12[:, 6:12],
                                       in_=xt[:, 512:1024]), inc="sDV")
        dv(lambda v: v.bn_aggr(out=BN2[:], in_=BN12[:]), inc="sDV",
           wait=W(("sDV", sch.v("sDV"))))
        dv(lambda v: v.tensor_copy(out=MCOL[:], in_=BN2[:, 0:1]), inc="sDV",
           wait=W(("sDV", sch.v("sDV"))))
        v_bn = sch.v("sDV")
        ac(lambda a: a.activation(out=SSQ[:], in_=BN2[:, 1:2], func=AF.Sqrt,
                                  bias=EPSC[:], scale=1.0), inc="sAC",
           wait=W(("sDV", v_bn)))
        dv(lambda v: v.reciprocal(out=RSTD[:], in_=SSQ[:]), inc="sDV",
           wait=W(("sAC", sch.v("sAC"))))
        dv(lambda v, xt=xt: v.tensor_scalar(
            out=XNF[:], in0=xt, scalar1=MCOL[:], scalar2=RSTD[:],
            op0=OP.subtract, op1=OP.mult), inc="sDV",
           wait=W(("sDV", sch.v("sDV"))))
        v_xnf = sch.v("sDV")
        dv(lambda v, t=t: v.scalar_tensor_tensor(
            out=SPROD[:], in0=XNF[:], scalar=1.0, in1=R128[:],
            op0=OP.mult, op1=OP.mult),
           inc="sDV", wait=W(("sDV", max(v_r128, v_xnf))))
        dv(lambda v, t=t: v.reduce_sum(out=SCOL[:, t:t + 1], in_=SPROD[:],
                                       axis=AX),
           inc="sDV", wait=W(("sDV", sch.v("sDV"))))
        ac(lambda a: a.activation(out=SQJ[:], in_=XNF[:], func=AF.Square,
                                  scale=1.0, accum_out=SSQ[:]), inc="sAC",
           wait=W(("sDV", sch.v("sDV"))))
        ac(lambda a: a.activation(out=MCOL[:], in_=SSQ[:], func=AF.Sqrt,
                                  scale=1.0 / D), inc="sAC",
           wait=W(("sAC", sch.v("sAC"))))
        dv(lambda v: v.reciprocal(out=RSN[:], in_=MCOL[:]), inc="sDV",
           wait=W(("sAC", sch.v("sAC"))))
        v_rsn = sch.v("sDV")
        dv(lambda v: v.tensor_scalar(out=RNF[:], in0=XNF[:], scalar1=RSN[:],
                                     scalar2=None, op0=OP.mult),
           inc="sDV", wait=W(("sDV", v_rsn)))
        v_rn = sch.v("sDV")
        for c in range(8):
            pe(lambda p, c=c: p.transpose(
                PSF[:, 3072 + 128 * c:3200 + 128 * c],
                RNF[:, 128 * c:128 * (c + 1)], IDF[:]),
               inc=("sPE" if c == 7 else None),
               wait=(W(("sDV", v_rn), ("sAC", v_ac_ph2 if t == 0 else 0))
                     if c == 0 else ()))
        v_tr = sch.v("sPE")
        dv(lambda v, t=t: v.tensor_copy(
            out=RNT.rearrange("p (c n) -> p c n", n=2048)
                [:, :, 128 * t:128 * (t + 1)],
            in_=PSF[:, 3072:4096].rearrange("p (c n) -> p c n", n=128)),
           inc="sDV", wait=W(("sPE", v_tr)))
    v_ln2 = sch.v("sDV")
    v_x1 = v_ln2


    v_wql = load_weight_direct("wql", 512, gcol=16, wait_pe=sch.v("sPE"),
                               dv_guard=sch.v("sDV"), ac_guard=sch.v("sAC"))
    lp_state = {"vevac": sch.v("sAC"), "vvpl": 0}

    def light_projs():
        vq = v_wql[1]
        for fc in range(4):
            for dc in range(8):
                pe_mm(PSF[:, 0:2048],
                      WTMP[:, 512 * dc + 128 * fc:512 * dc + 128 * (fc + 1)],
                      RNT[:, 2048 * dc:2048 * (dc + 1)],
                      start=(dc == 0), stop=(dc == 7), n_free=2048,
                      inc=("sPE" if dc == 7 else None),
                      wait=(W(("sDV", vq), ("sAC", lp_state["vevac"]))
                            if dc == 0 else ()))
                yield
            v_pe = sch.v("sPE")
            ac(lambda a, fc=fc: a.activation(
                out=QLT[:, 2048 * fc:2048 * (fc + 1)], in_=PSF[:, 0:2048],
                func=AF.Copy, scale=1.0), inc="sAC", wait=W(("sPE", v_pe)))
            lp_state["vevac"] = sch.v("sAC")
            yield
        v_wkvl = load_weight_direct("wkvl", 1024, gcol=16,
                                    wait_pe=sch.v("sPE"),
                                    dv_guard=sch.v("sDV"),
                                    ac_guard=lp_state["vevac"])
        vkv = v_wkvl[1]
        yield
        for fc in range(4):
            for dc in range(8):
                pe_mm(PSF[:, 0:2048],
                      WTMP[:, 1024 * dc + 128 * fc:
                           1024 * dc + 128 * (fc + 1)],
                      RNT[:, 2048 * dc:2048 * (dc + 1)],
                      start=(dc == 0), stop=(dc == 7), n_free=2048,
                      inc=("sPE" if dc == 7 else None),
                      wait=(W(("sDV", vkv), ("sAC", lp_state["vevac"]))
                            if dc == 0 else ()))
                yield
            v_pe = sch.v("sPE")
            ac(lambda a, fc=fc: a.activation(
                out=KLT[:, 2048 * fc:2048 * (fc + 1)], in_=PSF[:, 0:2048],
                func=AF.Copy, scale=1.0), inc="sAC", wait=W(("sPE", v_pe)))
            lp_state["vevac"] = sch.v("sAC")
            yield
        for t in range(TT):
            for dc in range(8):
                pe_mm(PSF[:, 0:512],
                      RNT[:, 2048 * dc + 128 * t:2048 * dc + 128 * (t + 1)],
                      WTMP[:, 1024 * dc + 512:1024 * (dc + 1)],
                      start=(dc == 0), stop=(dc == 7), n_free=512,
                      inc=("sPE" if dc == 7 else None),
                      wait=(W(("sAC", lp_state["vevac"]), ("sGP", gp_init))
                            if dc == 0 else ()))
                yield
            v_pe = sch.v("sPE")
            ac(lambda a, t=t: a.activation(
                out=VPL[:, 520 * t:520 * (t + 1)]
                    .rearrange("p (h c) -> p h c", c=65)[:, :, 0:64],
                in_=PSF[:, 0:512], func=AF.Copy, scale=1.0),
               inc="sAC", wait=W(("sPE", v_pe)))
            lp_state["vevac"] = sch.v("sAC")
            lp_state["vvpl"] = sch.v("sAC")
            yield

    lpg = light_projs()

    def pump(n=1):
        for _ in range(n):
            next(lpg, None)

    v_m = dv(lambda v: v.memset(ANEW[:], A1), inc="sDV")
    dv(lambda v: v.tensor_scalar(out=NANEW[:], in0=ANEW[:], scalar1=-1.0,
                                 scalar2=None, op0=OP.mult), inc="sDV",
       wait=W(("sDV", v_m)))
    dv(lambda v: v.tensor_scalar(out=APLK[:], in0=ANEW[:], scalar1=LOGK,
                                 scalar2=None, op0=OP.add), inc="sDV",
       wait=W(("sDV", v_m)))
    v_a = sch.v("sDV")
    for it in range(NIT - 1):
        pe(lambda p: p.matmul(PSF[:, 2048:2049], ONE1F[:], NANEW[:],
                              start=True, stop=True), inc="sPE",
           wait=W(("sDV", v_a)))
        v_brc = sch.v("sPE")
        ac(lambda a: a.activation(out=NAREP[:], in_=PSF[:, 2048:2049],
                                  func=AF.Copy, scale=1.0), inc="sAC",
           wait=W(("sPE", v_brc)))
        v_nar = sch.v("sAC")
        ac(lambda a: a.activation(out=RSC[:], in_=SCOL[:], func=AF.Relu,
                                  bias=NAREP[:], scale=-1.0), inc="sAC",
           wait=W(("sAC", v_nar), ("sDV", v_ln2)))
        ac(lambda a: a.activation(out=EJ16[:], in_=RSC[:], func=AF.Exp,
                                  scale=-1.0, accum_out=EPART[:]), inc="sAC",
           wait=W(("sAC", sch.v("sAC"))))
        v_ep = sch.v("sAC")
        pe(lambda p: p.matmul(PSF[0:1, 2049:2050], EPART[:], ONECF[:],
                              start=True, stop=True), inc="sPE",
           wait=W(("sAC", v_ep)))
        v_sum = sch.v("sPE")
        ac(lambda a: a.activation(out=LNS[:], in_=PSF[0:1, 2049:2050],
                                  func=AF.Ln, scale=1.0), inc="sAC",
           wait=W(("sPE", v_sum)))
        v_lns = sch.v("sAC")
        dv(lambda v: v.scalar_tensor_tensor(
            out=ANEW[:], in0=LNS[:], scalar=-1.0, in1=APLK[:],
            op0=OP.mult, op1=OP.add), inc="sDV", wait=W(("sAC", v_lns)))
        v_an = sch.v("sDV")
        dv(lambda v: v.tensor_scalar(out=APLK[:], in0=ANEW[:], scalar1=LOGK,
                                     scalar2=None, op0=OP.add), inc="sDV",
           wait=W(("sDV", v_an)))
        dv(lambda v: v.tensor_scalar(out=NANEW[:], in0=ANEW[:], scalar1=-1.0,
                                     scalar2=None, op0=OP.mult), inc="sDV",
           wait=W(("sDV", v_an)))
        v_a = sch.v("sDV")
        pump(6)
    pe(lambda p: p.matmul(PSF[:, 2048:2049], ONE1F[:], ANEW[:],
                          start=True, stop=True), inc="sPE",
       wait=W(("sDV", v_a)))
    ac(lambda a: a.activation(out=AREP[:], in_=PSF[:, 2048:2049],
                              func=AF.Copy, scale=1.0), inc="sAC",
       wait=W(("sPE", sch.v("sPE"))))
    dv(lambda v: v.tensor_scalar(out=UCOL[:], in0=SCOL[:], scalar1=AREP[:],
                                 scalar2=0.0, op0=OP.add, op1=OP.min),
       inc="sDV", wait=W(("sAC", sch.v("sAC"))))
    v_ucol = sch.v("sDV")
    pe(lambda p: p.transpose(PSF[0:16, 2048:2176], UCOL[:], IDF[:]),
       inc="sPE", wait=W(("sDV", v_ucol)))
    dv(lambda v: v.tensor_copy(out=U16[:], in_=PSF[0:16, 2048:2176]),
       inc="sDV", wait=W(("sPE", sch.v("sPE"))))
    v_u16 = sch.v("sDV")
    UROW = ARF[0:1, 0:2048]
    sp(lambda s: s.dma_start(out=UROW, in_=U16[:]), inc="sLs", n=16,
       wait=W(("sDV", v_u16)))
    v_urow = sch.v("sLs")
    v_amat = pe_mm(PSF[:, 2048:4096], ONE1F[:], UROW, start=True, stop=True,
                   n_free=2048, inc="sPE", wait=W(("sLs", v_urow)))
    for tb in range(16):
        dv(lambda v, tb=tb: v.tensor_scalar(
            out=OUTF[:, 0:2048], in0=PSF[:, 2048:4096],
            scalar1=UCOL[:, tb:tb + 1], scalar2=None, op0=OP.is_gt),
           inc="sDV", wait=(W(("sPE", v_amat)) if tb == 0 else
                            W(("sDV", sch.v("sDV")))))
        dv(lambda v, tb=tb: v.reduce_sum(out=CCOL[:, tb:tb + 1],
                                         in_=OUTF[:, 0:2048], axis=AX),
           inc="sDV", wait=W(("sDV", sch.v("sDV"))))
        pump(2)
    v_ccol = sch.v("sDV")
    dv(lambda v: v.tensor_scalar(out=SAT16[:], in0=U16[:], scalar1=0.0,
                                 scalar2=None, op0=OP.is_ge), inc="sDV",
       wait=W(("sDV", v_ccol)))
    dv(lambda v: v.tensor_tensor_scan(out=P16[:], data0=SAT16[:],
                                      data1=SAT16[:], initial=0.0,
                                      op0=OP.add, op1=OP.bypass), inc="sDV",
       wait=W(("sDV", sch.v("sDV"))))
    dv(lambda v: v.reduce_sum(out=RS16[:], in_=SAT16[:], axis=AX), inc="sDV",
       wait=W(("sDV", sch.v("sDV"))))
    v_rs = sch.v("sDV")
    pe(lambda p: p.transpose(PSF[0:1, 2048:2064], RS16[:], IDF[0:16, 0:16]),
       inc="sPE", wait=W(("sDV", v_rs)))
    dv(lambda v: v.tensor_copy(out=OFR[:], in_=PSF[0:1, 2048:2064]),
       inc="sDV", wait=W(("sPE", sch.v("sPE"))))
    dv(lambda v: v.tensor_tensor_scan(out=OFR2[:], data0=OFR[:],
                                      data1=OFR[:], initial=0.0,
                                      op0=OP.add, op1=OP.bypass), inc="sDV",
       wait=W(("sDV", sch.v("sDV"))))
    dv(lambda v: v.memset(OFR[:, 0:1], 0.0), inc="sDV",
       wait=W(("sDV", sch.v("sDV"))))
    dv(lambda v: v.tensor_copy(out=OFR[:, 1:16], in_=OFR2[:, 0:15]),
       inc="sDV", wait=W(("sDV", sch.v("sDV"))))
    v_ofr = sch.v("sDV")
    pe(lambda p: p.transpose(PSF[0:16, 2064:2065], OFR[:], IDF[0:1, 0:1]),
       inc="sPE", wait=W(("sDV", v_ofr)))
    dv(lambda v: v.tensor_copy(out=OFF16[:], in_=PSF[0:16, 2064:2065]),
       inc="sDV", wait=W(("sPE", sch.v("sPE"))))
    v_offev = sch.v("sDV")
    dv(lambda v: v.tensor_sub(out=P16[:], in0=P16[:], in1=SAT16[:]),
       inc="sDV", wait=W(("sDV", sch.v("sDV"))))
    dv(lambda v: v.tensor_scalar(out=P16[:], in0=P16[:], scalar1=OFF16[:],
                                 scalar2=None, op0=OP.add), inc="sDV",
       wait=W(("sDV", sch.v("sDV"))))
    dv(lambda v: v.tensor_mul(out=P16[:], in0=P16[:], in1=SAT16[:]),
       inc="sDV", wait=W(("sDV", sch.v("sDV"))))
    v_p16 = sch.v("sDV")
    pe(lambda p: p.transpose(PSF[0:16, 2176:2304], CCOL[:], IDF[:]),
       inc="sPE", wait=W(("sDV", max(v_ccol, v_offev))))
    dv(lambda v: v.tensor_copy(out=C16[:], in_=PSF[0:16, 2176:2304]),
       inc="sDV", wait=W(("sPE", sch.v("sPE"))))
    dv(lambda v: v.tensor_add(out=R16[:] if False else SEL16[:], in0=C16[:],
                              in1=P16[:]), inc="sDV",
       wait=W(("sDV", sch.v("sDV"))))
    dv(lambda v: v.tensor_scalar(out=SEL16[:], in0=SEL16[:], scalar1=1023.5,
                                 scalar2=None, op0=OP.is_lt), inc="sDV",
       wait=W(("sDV", sch.v("sDV"))))
    v_sel16 = sch.v("sDV")
    pe(lambda p: p.transpose(PSF[:, 2304:2320], SEL16[:], IDF[0:16, 0:16]),
       inc="sPE", wait=W(("sDV", v_sel16)))
    dv(lambda v: v.tensor_copy(out=SELCOL[:], in_=PSF[:, 2304:2320]),
       inc="sDV", wait=W(("sPE", sch.v("sPE"))))
    v_sel = sch.v("sDV")

    pump(400)
    v_vpl = lp_state["vvpl"]

    lt_div = sch.v("sDV")
    cte_prev = sch.v("sAC")
    exp_w = {}
    nw = 0
    for h in range(H2):
        chl, rhl = h // 2, (h % 2) * 64
        QLh = QLT[rhl:rhl + 64, 2048 * chl:2048 * (chl + 1)]
        KLh = KLT[rhl:rhl + 64, 2048 * chl:2048 * (chl + 1)]
        for wg in range(2):
            pend_av = None

            def emit_lav(pa):
                w, h2, wi, expv, fav = pa
                pe(lambda p, w=w, h2=h2, wi=wi: p.matmul(
                    PSF[0:65, 3072 + 128 * wi:3072 + 128 * (wi + 1)],
                    VPL[:, 520 * w + 65 * h2:520 * w + 65 * (h2 + 1)],
                    PTE[:, 128 * wi:128 * (wi + 1)],
                    start=True, stop=True),
                   inc="sPE", wait=W(("sAC", max(expv, cte_prev)),
                                     ("sDV", fav)))

            for wi in range(8):
                w = 8 * wg + wi
                pb = 512 * (wi % 2)
                pe(lambda p, w=w, pb=pb, QLh=QLh, KLh=KLh: p.matmul(
                    PSF[:, pb:pb + 128], KLh[:, 128 * w:128 * (w + 1)],
                    QLh[:, 128 * w:128 * (w + 1)], start=True, stop=True),
                   inc="sPE",
                   wait=W(("sAC", max(v_vpl, exp_w.get(nw - 2, 0)))))
                v_qk = sch.v("sPE")
                ac(lambda a, wi=wi, pb=pb: a.activation(
                    out=PTE[:, 128 * wi:128 * (wi + 1)],
                    in_=PSF[:, pb:pb + 128], func=AF.Exp, scale=SC),
                   inc="sAC", wait=W(("sPE", v_qk)))
                exp_w[nw] = sch.v("sAC")
                nw += 1
                if pend_av is not None:
                    emit_lav(pend_av)
                pend_av = (w, h, wi, exp_w[nw - 1],
                           v_ccol if (h == 0 and wg == 0 and wi == 0) else 0)
            emit_lav(pend_av)
            v_av = sch.v("sPE")
            ac(lambda a: a.activation(out=CTE[0:65, :],
                                      in_=PSF[0:65, 3072:4096],
                                      func=AF.Copy, scale=1.0),
               inc="sAC", wait=W(("sPE", v_av), ("sDV", lt_div)))
            cte_prev = sch.v("sAC")
            dv(lambda v: v.reciprocal(out=RZROW[:], in_=CTE[64:65, :]),
               inc="sDV", wait=W(("sAC", sch.v("sAC")),
                                 ("sDV", sch.v("sDV"))))
            dv(lambda v: v.tensor_copy(out=RZROWB[:], in_=RZROW[:]),
               inc="sDV", wait=W(("sDV", sch.v("sDV"))))
            v_rz = sch.v("sDV")
            pe_mm(PSF[0:64, 2048:3072], ONE1B[0:1, 0:64], RZROWB[:],
                  start=True, stop=True, n_free=1024, inc="sPE",
                  wait=W(("sDV", v_rz)))
            v_rb = sch.v("sPE")
            dv(lambda v: v.tensor_copy(out=RCPS[:], in_=PSF[0:64, 2048:3072]),
               inc="sDV", wait=W(("sPE", v_rb), ("sDV", sch.v("sDV"))))
            v_rc = sch.v("sDV")
            dv(lambda v, h=h, wg=wg: v.tensor_tensor(
                out=CTXL[:, 2048 * (h // 2) + 1024 * wg:
                         2048 * (h // 2) + 1024 * (wg + 1)]
                    [(h % 2) * 64:(h % 2) * 64 + 64, :],
                in0=CTE[0:64, :], in1=RCPS[:], op=OP.mult),
               inc="sDV", wait=W(("sDV", v_rc)))
            lt_div = sch.v("sDV")

    load_weight_direct("wqh", 512, gcol=24, wait_pe=sch.v("sPE"),
                       dv_guard=sch.v("sDV"), ac_guard=sch.v("sAC"))
    proj_T(QHT, None, RNT, nf=4, wcols=512)
    load_weight_direct("wkvh_t", 1024, gcol=24, wait_pe=sch.v("sPE"),
                       dv_guard=sch.v("sDV"), ac_guard=sch.v("sAC"))
    vw = sch.v("sDV")
    for pc in range(4):
        for sub in range(2):
            hh = 2 * pc + sub
            for dc in range(8):
                pe_mm(PSF[64 * sub:64 * (sub + 1), 0:2048],
                      WTMP[:, 1024 * dc + 128 * hh:1024 * dc + 128 * hh + 64],
                      RNT[:, 2048 * dc:2048 * (dc + 1)],
                      start=(dc == 0), stop=(dc == 7), n_free=2048,
                      inc=("sPE" if (dc == 7 and sub == 1) else None),
                      wait=(W(("sDV", vw), ("sAC", sch.v("sAC")))
                            if (dc == 0 and sub == 0) else ()))
        v_pe = sch.v("sPE")
        ac(lambda a, pc=pc: a.activation(
            out=KVHT[:, 2048 * pc:2048 * (pc + 1)], in_=PSF[:, 0:2048],
            func=AF.Copy, scale=1.0), inc="sAC", wait=W(("sPE", v_pe)))
    v_light_done = lt_div
    for t in range(TT):
        for dc in range(8):
            pe(lambda p, t=t, dc=dc: p.matmul(
                PSF[:, 0:512],
                RNT[:, 2048 * dc + 128 * t:2048 * dc + 128 * (t + 1)],
                WTMP[:, 1024 * dc:1024 * (dc + 1)]
                    .rearrange("p (h c) -> p h c", c=128)[:, :, 64:128],
                start=(dc == 0), stop=(dc == 7)),
               inc=("sPE" if dc == 7 else None),
               wait=(W(("sAC", sch.v("sAC")), ("sDV", v_light_done),
                       ("sGP", gp_init)) if dc == 0 else ()))
        v_pe = sch.v("sPE")
        ac(lambda a, t=t: a.activation(
            out=VPH[:, 520 * t:520 * (t + 1)]
                .rearrange("p (h c) -> p h c", c=65)[:, :, 0:64],
            in_=PSF[:, 0:512], func=AF.Copy, scale=1.0),
           inc="sAC", wait=W(("sPE", v_pe)))
    v_vph = sch.v("sAC")

    hv_div = sch.v("sDV")
    cte_prev2 = sch.v("sAC")
    for h in range(H2):
        Kh2 = KVHT[(h % 2) * 64:(h % 2) * 64 + 64,
                   2048 * (h // 2):2048 * (h // 2 + 1)]
        Qh2 = QHT[(h % 2) * 64:(h % 2) * 64 + 64,
                  2048 * (h // 2):2048 * (h // 2 + 1)]
        for qh in range(2):
            exp_v = {}
            av_v = {}

            def emit_hav(kt, h=h):
                for j in range(2):
                    pe(lambda p, kt=kt, h=h, j=j: p.matmul(
                        PSF[0:65, 3072 + 512 * j:3072 + 512 * (j + 1)],
                        VPH[:, 520 * kt + 65 * h:520 * kt + 65 * (h + 1)],
                        PTE[:, 1024 * (kt % 2) + 512 * j:
                            1024 * (kt % 2) + 512 * (j + 1)],
                        start=(kt == 0), stop=(kt == 15)),
                       inc=("sPE" if j == 1 else None),
                       wait=(W(("sAC", max(exp_v[kt], cte_prev2)))
                             if j == 0 else ()))
                av_v[kt] = sch.v("sPE")

            for kt in range(16):
                pb = 1024 * (kt % 2)
                for j in range(2):
                    pe(lambda p, kt=kt, qh=qh, j=j, pb=pb, Kh2=Kh2, Qh2=Qh2:
                       p.matmul(
                           PSF[:, pb + 512 * j:pb + 512 * (j + 1)],
                           Kh2[:, 128 * kt:128 * (kt + 1)],
                           Qh2[:, 1024 * qh + 512 * j:
                               1024 * qh + 512 * (j + 1)],
                           start=True, stop=True),
                       inc=("sPE" if j == 1 else None),
                       wait=(W(("sAC", max(v_vph, exp_v.get(kt - 2, 0))))
                             if j == 0 else ()))
                v_qk = sch.v("sPE")
                ac(lambda a, kt=kt, pb=pb: a.activation(
                    out=PTE[:, 1024 * (kt % 2):1024 * (kt % 2) + 1024],
                    in_=PSF[:, pb:pb + 1024], func=AF.Exp, scale=SC),
                   inc="sAC",
                   wait=W(("sPE", max(v_qk, av_v.get(kt - 2, 0)))))
                exp_v[kt] = sch.v("sAC")
                if kt >= 1:
                    emit_hav(kt - 1)
            emit_hav(15)
            v_av = sch.v("sPE")
            ac(lambda a: a.activation(out=CTE[0:65, :],
                                      in_=PSF[0:65, 3072:4096],
                                      func=AF.Copy, scale=1.0),
               inc="sAC", wait=W(("sPE", v_av), ("sDV", hv_div)))
            cte_prev2 = sch.v("sAC")
            dv(lambda v: v.reciprocal(out=RZROW[:], in_=CTE[64:65, :]),
               inc="sDV", wait=W(("sAC", sch.v("sAC")),
                                 ("sDV", sch.v("sDV"))))
            dv(lambda v: v.tensor_copy(out=RZROWB[:], in_=RZROW[:]),
               inc="sDV", wait=W(("sDV", sch.v("sDV"))))
            v_rz = sch.v("sDV")
            pe_mm(PSF[0:64, 2048:3072], ONE1B[0:1, 0:64], RZROWB[:],
                  start=True, stop=True, n_free=1024, inc="sPE",
                  wait=W(("sDV", v_rz)))
            v_rb = sch.v("sPE")
            dv(lambda v: v.tensor_copy(out=RCPS[:], in_=PSF[0:64, 2048:3072]),
               inc="sDV", wait=W(("sPE", v_rb), ("sDV", sch.v("sDV"))))
            v_rc = sch.v("sDV")
            dv(lambda v, h=h, qh=qh: v.tensor_tensor(
                out=CTXH[:, 2048 * (h // 2) + 1024 * qh:
                         2048 * (h // 2) + 1024 * (qh + 1)]
                    [(h % 2) * 64:(h % 2) * 64 + 64, :],
                in0=CTE[0:64, :], in1=RCPS[:], op=OP.mult),
               inc="sDV", wait=W(("sDV", v_rc)))
            hv_div = sch.v("sDV")

    load_weight_direct("wol", 1024, rows=512, woff=0,
                       wait_pe=sch.v("sPE"), dv_guard=sch.v("sDV"))
    v_wol = load_weight_direct("woh", 1024, rows=512, woff=4096)
    st_done = {0: 0, 1: 0}
    for t in range(TT):
        po = 2048 * (t % 2)
        so = 1024 * (t % 2)
        stx = "sSTa" if t % 2 == 0 else "sSTb"
        for dc in range(4):
            pe_mm(PSF[:, po:po + 1024],
                  CTXL[:, 2048 * dc + 128 * t:2048 * dc + 128 * (t + 1)],
                  WTMP[:, 1024 * dc:1024 * (dc + 1)],
                  start=(dc == 0), stop=(dc == 3), n_free=1024,
                  wait=(W(v_wol, ("sDV", max(hv_div, sch.v("sDV"))))
                        if dc == 0 else ()))
        for dc in range(4):
            pe_mm(PSF[:, po + 1024:po + 2048],
                  CTXH[:, 2048 * dc + 128 * t:2048 * dc + 128 * (t + 1)],
                  WTMP[:, 4096 + 1024 * dc:4096 + 1024 * (dc + 1)],
                  start=(dc == 0), stop=(dc == 3), n_free=1024,
                  inc=("sPE" if dc == 3 else None))
        v_pe = sch.v("sPE")
        dv(lambda v, t=t, po=po, so=so: v.scalar_tensor_tensor(
            out=OUTF[:, so:so + 1024], in0=PSF[:, po + 1024:po + 2048],
            scalar=SELCOL[:, t:t + 1], in1=XB[:, D * t:D * (t + 1)],
            op0=OP.mult, op1=OP.add),
           inc="sDV",
           wait=W(("sPE", v_pe), ("sDV", sch.v("sDV")), ("sDV", v_sel),
                  (stx, st_done[t % 2])))
        dv(lambda v, po=po, so=so: v.tensor_tensor(
            out=OUTF[:, so:so + 1024], in0=PSF[:, po:po + 1024],
            in1=OUTF[:, so:so + 1024], op=OP.add),
           inc="sDV", wait=W(("sDV", sch.v("sDV"))))
        v_of = sch.v("sDV")
        sp(lambda s, t=t, so=so: s.dma_start(
            out=out_t[128 * t:128 * (t + 1), :], in_=OUTF[:, so:so + 1024]),
           inc=stx, n=16, wait=W(("sDV", v_of)))
        st_done[t % 2] = sch.v(stx)
    if upto == 'fulldbg':
        v_fin = sch.v("sDV")
        for a in range(8):
            gp(lambda g, a=a: g.dma_start(
                out=dbg_t[128 * a:128 * (a + 1), :],
                in_=CTXL[:, 1024 * a:1024 * (a + 1)]),
               inc="sGW", n=16, wait=W(("sDV", v_fin), ("sGW", sch.v("sGW"))))
        for a in range(8):
            gp(lambda g, a=a: g.dma_start(
                out=dbg_t[1024 + 128 * a:1024 + 128 * (a + 1), :],
                in_=CTXH[:, 1024 * a:1024 * (a + 1)]),
               inc="sGW", n=16, wait=W(("sGW", sch.v("sGW"))))
        for src_ap, c0 in ((SELCOL, 0), (UCOL, 16), (CCOL, 32), (SCOL, 48)):
            gp(lambda g, src_ap=src_ap, c0=c0: g.dma_start(
                out=dbg_t[2048:2176, c0:c0 + 16], in_=src_ap[:, :],
                allow_slow_non_contiguous=True),
               inc="sGW", n=16, wait=W(("sGW", sch.v("sGW"))))
        for j, a in enumerate((6, 7)):
            gp(lambda g, j=j, a=a: g.dma_start(
                out=dbg_t[2176 + 128 * j:2176 + 128 * (j + 1), :],
                in_=XB[:, 1024 * a:1024 * (a + 1)]),
               inc="sGW", n=16, wait=W(("sGW", sch.v("sGW"))))
        sch.waitq("sp", ("sSTa", sch.v("sSTa")), ("sSTb", sch.v("sSTb")),
                  ("sGW", sch.v("sGW")))
    else:
        sch.waitq("sp", ("sSTa", sch.v("sSTa")), ("sSTb", sch.v("sSTb")))

    with nc.Block() as block:
        @block.sync
        def _(h):
            for f in sch.q["sp"]:
                f(h)

        @block.gpsimd
        def _(h):
            for f in sch.q["gp"]:
                f(h)

        @block.vector
        def _(h):
            for f in sch.q["dv"]:
                f(h)

        @block.scalar
        def _(h):
            for f in sch.q["ac"]:
                f(h)

        @block.tensor
        def _(h):
            for f in sch.q["pe"]:
                f(h)

    es.close()
    return nc


_NC_CACHE = [None]


def kernel(L0_attn_x, L0_attn_Wq, L0_attn_bq, L0_attn_Wk, L0_attn_bk,
           L0_attn_Wv, L0_attn_bv, L0_attn_Wo, L0_attn_bo,
           L0_ffn_gamma_light, L0_ffn_wq_light, L0_ffn_wkv_light,
           L0_ffn_wo_light, L0_ffn_q_routing_token, L0_ffn_kv_routing_token,
           L0_ffn_gamma_heavy, L0_ffn_wq_heavy, L0_ffn_wkv_heavy,
           L0_ffn_wo_heavy):
    x = np.ascontiguousarray(np.asarray(L0_attn_x, dtype=np.float32))
    f = lambda a: np.ascontiguousarray(np.asarray(a, dtype=np.float32))
    row = lambda a: np.ascontiguousarray(
        np.asarray(a, dtype=np.float32).reshape(1, -1))
    base = {
        "wq_t": np.ascontiguousarray(f(L0_attn_Wq).T),
        "wk_t": np.ascontiguousarray(f(L0_attn_Wk).T),
        "wv_t": np.ascontiguousarray(f(L0_attn_Wv).T),
        "wo_t": np.ascontiguousarray(f(L0_attn_Wo).T),
        "wkvl": f(L0_ffn_wkv_light),
        "wkvh_t": np.ascontiguousarray(f(L0_ffn_wkv_heavy).T),
        "wql": f(L0_ffn_wq_light), "wqh": f(L0_ffn_wq_heavy),
        "wol": f(L0_ffn_wo_light), "woh": f(L0_ffn_wo_heavy),
        "bq": row(L0_attn_bq), "bk": row(L0_attn_bk), "bv": row(L0_attn_bv),
        "bo": row(L0_attn_bo), "g_l": row(L0_ffn_gamma_light),
        "g_h": row(L0_ffn_gamma_heavy), "rq": row(L0_ffn_q_routing_token),
    }
    if _NC_CACHE[0] is None:
        _NC_CACHE[0] = build_program()
    nc = _NC_CACHE[0]
    in_maps = [dict(base, x=np.ascontiguousarray(x[b])) for b in range(8)]
    res = run_bass_kernel_spmd(nc, in_maps, core_ids=list(range(8)))
    return np.stack([np.asarray(r["out"], dtype=np.float32)
                     for r in res.results], axis=0)


# version 15
# speedup vs baseline: 1.3561x; 1.0025x over previous
import math
from contextlib import ExitStack

import numpy as np
import concourse.bass as bass
import concourse.mybir as mybir
from concourse.bass_utils import run_bass_kernel_spmd

F32 = mybir.dt.float32
BF16 = mybir.dt.bfloat16
AX = mybir.AxisListType.X
OP = mybir.AluOpType
AF = mybir.ActivationFunctionType

T, D, TT = 2048, 1024, 16
H1, H2, HID = 16, 8, 512
NEG = -1e38
SC = 0.125
ROUTE_K = 1152.0
LOGK = float(np.log(np.float32(ROUTE_K)))
A1 = float(np.log(np.float32(ROUTE_K)) - np.log(np.float32(T)))
NIT = 50


class Sched:

    ENGS = ("sp", "gp", "dv", "ac", "pe")

    def __init__(self):
        self.q = {e: [] for e in self.ENGS}
        self.sem = {}
        self.semh = {}

    def op(self, eng, fn, inc=None, n=1, wait=()):
        semh = self.semh
        wait = tuple((s, v) for s, v in wait if v > 0)
        tgt = None
        if inc is not None:
            self.sem[inc] += n
            tgt = (semh[inc], n)

        def run(h, fn=fn, wait=wait, tgt=tgt):
            for s, v in wait:
                h.wait_ge(semh[s], v)
            inst = fn(h)
            if tgt is not None and inst is not None:
                inst.then_inc(*tgt)

        self.q[eng].append(run)
        return self.sem[inc] if inc is not None else None

    def waitq(self, eng, *sv):
        semh = self.semh
        sv = tuple(sv)

        def run(h, sv=sv):
            for s, v in sv:
                if v > 0:
                    h.wait_ge(semh[s], v)

        self.q[eng].append(run)

    def v(self, s):
        return self.sem[s]


def build_program(upto='full'):
    nc = bass.Bass("TRN2", target_bir_lowering=False, debug=False)
    sch = Sched()

    di = {}
    for name, shape in (
        ("x", [T, D]), ("wq_t", [D, D]), ("wk_t", [D, D]), ("wv_t", [D, D]),
        ("wo_t", [D, D]), ("wkvl", [D, D]), ("wkvh_t", [D, D]),
        ("wql", [D, HID]), ("wqh", [D, HID]), ("wol", [HID, D]),
        ("woh", [HID, D]), ("bq", [1, D]), ("bk", [1, D]), ("bv", [1, D]),
        ("bo", [1, D]), ("g_l", [1, D]), ("g_h", [1, D]), ("rq", [1, D]),
    ):
        di[name] = nc.dram_tensor(name, shape, F32, kind="ExternalInput").ap()
    out_t = nc.dram_tensor("out", [T, D], F32, kind="ExternalOutput").ap()
    dbg_t = (nc.dram_tensor("dbg", [2560, 1024], F32, kind="ExternalOutput")
             .ap() if upto == 'fulldbg' else None)

    es = ExitStack()
    Sb = lambda nm, sh, dt=F32: es.enter_context(nc.sbuf_tensor(nm, sh, dt))

    ARN = Sb("ARN", [128, 90384], BF16)
    XB = ARN[:, 0:16384]
    XNT = ARN[:, 16384:32768]
    CT1 = ARN[:, 16384:32768]
    CTXL = ARN[:, 16384:24576]
    CTXH = ARN[:, 24576:32768]
    QT = ARN[:, 32768:49152]
    RNT = ARN[:, 32768:49152]
    KT = ARN[:, 49152:65536]
    QLT = ARN[:, 49152:57344]
    KLT = ARN[:, 57344:65536]
    KVHT = ARN[:, 49152:57344]
    QHT = ARN[:, 57344:65536]
    VP = ARN[:, 65536:82176]
    VPL = ARN[:, 65536:73856]
    VPH = ARN[:, 73856:82176]
    SBT = ARN[:, 82176:86272]
    SBX = ARN[:, 86272:86784]
    PTE = ARN[:, 86784:88832]
    CND = ARN[:, 88832:89088]
    CN2 = ARN[:, 89088:89344]
    R8 = ARN[:, 89344:89360]
    CTE = ARN[:, 89360:90384]
    WTMP = ARN[:, 82176:90368]

    ARF = Sb("ARF", [128, 4096], F32)
    SBMB = Sb("SBMB", [128, 4096], BF16)
    NEGI = Sb("NEGI", [128, 128], BF16)
    SBM = ARF[:, 0:4096]
    OUTF = ARF[:, 0:2048]
    SPROD = ARF[:, 1024:2048]
    XNF = ARF[:, 2048:3072]
    R128 = ARF[:, 3072:4096]

    IDF = Sb("IDF", [128, 128], F32)
    BIASC = Sb("BIASC", [128, 32], F32)
    BVROW = ARF[0:1, 3072:4096]
    BOROW = None
    ONE1F = Sb("ONE1F", [1, 128], F32)
    ONE1B = Sb("ONE1B", [1, 128], BF16)
    ONECF = Sb("ONECF", [128, 1], F32)
    EPSC = Sb("EPSC", [128, 1], F32)
    TCOL = Sb("TCOL", [128, 2], F32)
    RZROWB = Sb("RZROWB", [1, 1024], BF16)
    RCPS = Sb("RCPS", [64, 1024], BF16)
    MCOL = Sb("MCOL", [128, 1], F32)
    RSTD = Sb("RSTD", [128, 1], F32)
    SSQ = Sb("SSQ", [128, 1], F32)
    RSN = Sb("RSN", [128, 1], F32)
    BN12 = Sb("BN12", [128, 12], F32)
    BN2 = Sb("BN2", [128, 2], F32)
    RNF = ARF[:, 0:1024]
    SQJ = ARN[:, 49152:50176]
    SCOL = Sb("SCOL", [128, 16], F32)
    UCOL = Sb("UCOL", [128, 16], F32)
    CCOL = Sb("CCOL", [128, 16], F32)
    SELCOL = Sb("SELCOL", [128, 16], F32)
    U16 = ARF[0:16, 3072:3200]
    SAT16 = ARF[0:16, 3200:3328]
    P16 = ARF[0:16, 3328:3456]
    SEL16 = ARF[0:16, 3456:3584]
    C16 = U16
    RS16 = ARF[0:16, 3584:3585]
    OFF16 = ARF[0:16, 3585:3586]
    OFR = ARF[0:1, 3586:3602]
    OFR2 = ARF[0:1, 3602:3618]
    AREP = ARF[:, 3618:3619]
    NAREP = ARF[:, 3619:3620]
    RSC = ARF[:, 3620:3636]
    EJ16 = Sb("EJ16", [128, 16], BF16)
    EPART = Sb("EPART", [128, 1], F32)
    LNS = Sb("LNS", [1, 1], F32)
    ANEW = Sb("ANEW", [1, 1], F32)
    NANEW = Sb("NANEW", [1, 1], F32)
    APLK = Sb("APLK", [1, 1], F32)

    PSF = es.enter_context(nc.psum_tensor("PSF", [128, 4096], F32))
    RZROW = PSF[64:65, 2048:3072]

    for s in ("sLs", "sLx0", "sLx1", "sLw", "sSTa", "sSTb",
              "sPE", "sDV", "sAC", "sGP", "sGW"):
        sch.semh[s] = es.enter_context(nc.semaphore(s))
        sch.sem[s] = 0

    W = lambda *sv: tuple(sv)
    sp = lambda fn, **k: sch.op("sp", fn, **k)
    gp = lambda fn, **k: sch.op("gp", fn, **k)
    dv = lambda fn, **k: sch.op("dv", fn, **k)
    ac = lambda fn, **k: sch.op("ac", fn, **k)
    pe = lambda fn, **k: sch.op("pe", fn, **k)

    def pe_mm(out_ap, lhs, rhs, start, stop, n_free, is_tr=False,
              inc=None, wait=()):
        nslc = (n_free + 511) // 512
        for j in range(nslc):
            c0, c1 = 512 * j, min(512 * (j + 1), n_free)
            pe(lambda p, c0=c0, c1=c1, out_ap=out_ap, lhs=lhs, rhs=rhs,
               start=start, stop=stop, is_tr=is_tr: p.matmul(
                   out_ap[:, c0:c1], lhs, rhs[:, c0:c1], start=start,
                   stop=stop, is_transpose=is_tr),
               inc=(inc if j == nslc - 1 else None),
               wait=(wait if j == 0 else ()))
        return sch.v(inc) if inc else None

    gp(lambda g: g.memset(IDF[:], 0.0), inc="sGP")
    gp(lambda g: g.affine_select(
        out=IDF[:], in_=IDF[:], compare_op=OP.not_equal, fill=1.0, base=0,
        pattern=[[-1, 128]], channel_multiplier=1), inc="sGP",
        wait=W(("sGP", sch.v("sGP"))))
    gp(lambda g: g.memset(NEGI[:], 0.0), inc="sGP")
    gp(lambda g: g.affine_select(
        out=NEGI[:], in_=NEGI[:], compare_op=OP.not_equal, fill=-1e30,
        base=0, pattern=[[-1, 128]], channel_multiplier=1), inc="sGP",
        wait=W(("sGP", sch.v("sGP"))))
    gp(lambda g: g.memset(ONE1F[:], 1.0), inc="sGP")
    gp(lambda g: g.memset(ONE1B[:], 1.0), inc="sGP")
    gp(lambda g: g.memset(ONECF[:], 1.0), inc="sGP")
    gp(lambda g: g.memset(EPSC[:], 1e-5), inc="sGP")
    gp(lambda g: g.memset(VP[:], 1.0), inc="sGP")
    gp_init = sch.v("sGP")

    BQROW = ARF[32:33, 3072:4096]
    BKROW = ARF[64:65, 3072:4096]
    BQROWB = ARN[0:1, 67584:68608]
    BKROWB = ARN[0:1, 68608:69632]
    ONESROW = VP[0:1, 0:2048]
    for rowt, name in ((BQROW, "bq"), (BKROW, "bk")):
        sp(lambda s, rowt=rowt, name=name: s.dma_start(
            out=rowt, in_=di[name]), inc="sLs", n=16)
    for name, cs in (("g_l", 16), ("g_h", 24)):
        sp(lambda s, name=name, cs=cs: s.dma_start(
            out=BIASC[:, cs:cs + 8],
            in_=di[name].rearrange("o (c p) -> (o p) c", p=128),
            allow_slow_non_contiguous=True), inc="sLs", n=16)
    sp(lambda s: s.dma_start(out=BVROW, in_=di["bv"]), inc="sLs", n=16)
    v_small = sch.v("sLs")
    dv(lambda v: v.tensor_copy(out=BQROWB, in_=BQROW), inc="sDV",
       wait=W(("sLs", v_small), ("sGP", gp_init)))
    dv(lambda v: v.tensor_copy(out=BKROWB, in_=BKROW), inc="sDV",
       wait=W(("sLs", v_small), ("sGP", gp_init)))

    ln_done = {}
    xnf_done = {}
    for t in range(TT):
        slx = "sLx0"
        sp(lambda s, t=t: s.dma_start(
            out=XNF[:], in_=di["x"][128 * t:128 * (t + 1), :]),
           inc=slx, n=16, wait=W(("sDV", xnf_done.get(t - 1, 0))))
        vld = sch.v(slx)
        dv(lambda v, t=t: v.tensor_copy(
            out=XB[:, D * t:D * (t + 1)], in_=XNF[:]),
           inc="sDV", wait=W((slx, vld)))
        dv(lambda v: v.bn_stats(out=BN12[:, 0:6],
                                in_=XNF[:, 0:512]), inc="sDV")
        dv(lambda v: v.bn_stats(out=BN12[:, 6:12],
                                in_=XNF[:, 512:1024]),
           inc="sDV")
        dv(lambda v: v.bn_aggr(out=BN2[:], in_=BN12[:]), inc="sDV",
           wait=W(("sDV", sch.v("sDV"))))
        dv(lambda v: v.tensor_copy(out=MCOL[:], in_=BN2[:, 0:1]), inc="sDV",
           wait=W(("sDV", sch.v("sDV"))))
        v_bn = sch.v("sDV")
        ac(lambda a: a.activation(out=SSQ[:], in_=BN2[:, 1:2], func=AF.Sqrt,
                                  bias=EPSC[:], scale=1.0), inc="sAC",
           wait=W(("sDV", v_bn), ("sGP", gp_init)))
        dv(lambda v: v.reciprocal(out=RSTD[:], in_=SSQ[:]), inc="sDV",
           wait=W(("sAC", sch.v("sAC"))))
        dv(lambda v: v.tensor_scalar(
            out=XNF[:], in0=XNF[:], scalar1=MCOL[:],
            scalar2=RSTD[:], op0=OP.subtract, op1=OP.mult), inc="sDV",
           wait=W(("sDV", sch.v("sDV"))))
        v_xnf = sch.v("sDV")
        for c in range(8):
            pe(lambda p, c=c: p.transpose(
                PSF[:, 3072 + 128 * c:3200 + 128 * c],
                XNF[:, 128 * c:128 * (c + 1)], IDF[:]),
               inc=("sPE" if c == 7 else None),
               wait=(W(("sDV", v_xnf), ("sGP", 2),
                       ("sDV", ln_done.get(t - 1, 0))) if c == 0 else ()))
        v_tr = sch.v("sPE")
        dv(lambda v, t=t: v.tensor_copy(
            out=XNT.rearrange("p (c n) -> p c n", n=2048)
                [:, :, 128 * t:128 * (t + 1)],
            in_=PSF[:, 3072:4096].rearrange("p (c n) -> p c n", n=128)),
           inc="sDV", wait=W(("sPE", v_tr)))
        ln_done[t] = sch.v("sDV")
        xnf_done[t] = sch.v("sDV")

    def load_weight_direct(name, ncols, rows=1024, woff=0, wait_pe=0,
                           dv_guard=0, ac_guard=0, gcol=None):
        vs = []
        for c in range(rows // 128):
            gp(lambda g, c=c, name=name, ncols=ncols, woff=woff: g.dma_start(
                out=WTMP[:, woff + ncols * c:woff + ncols * (c + 1)],
                in_=di[name][128 * c:128 * (c + 1), 0:ncols]),
               inc="sGW", n=16,
               wait=(W(("sGW", sch.v("sGW")), ("sPE", wait_pe),
                       ("sDV", dv_guard), ("sAC", ac_guard)) if c == 0 else
                     W(("sGW", sch.v("sGW")))))
            vs.append(sch.v("sGW"))
        if gcol is None:
            return ("sGW", sch.v("sGW"))
        for c in range(rows // 128):
            dv(lambda v, c=c, ncols=ncols, woff=woff, gcol=gcol:
               v.tensor_scalar(
                   out=WTMP[:, woff + ncols * c:woff + ncols * (c + 1)],
                   in0=WTMP[:, woff + ncols * c:woff + ncols * (c + 1)],
                   scalar1=BIASC[:, gcol + c:gcol + c + 1],
                   scalar2=None, op0=OP.mult),
               inc="sDV", wait=W(("sGW", vs[c]), ("sLs", v_small),
                                 ("sDV", sch.v("sDV"))))
        return ("sDV", sch.v("sDV"))

    def proj_T(dst, bias_row, src_T, nf=8, wcols=1024, wwait=None):
        if wwait is None:
            wwait = ("sDV", sch.v("sDV"))
        for fc in range(nf):
            for dc in range(8):
                pe_mm(PSF[:, 0:2048],
                      WTMP[:, wcols * dc + 128 * fc:
                           wcols * dc + 128 * (fc + 1)],
                      src_T[:, 2048 * dc:2048 * (dc + 1)],
                      start=(dc == 0), stop=(dc == 7 and bias_row is None),
                      n_free=2048,
                      inc=("sPE" if (dc == 7 and bias_row is None) else None),
                      wait=(W(wwait, ("sDV", sch.v("sDV")),
                              ("sAC", sch.v("sAC")),
                              ("sLs", v_small)) if dc == 0 else ()))
            if bias_row is not None:
                pe_mm(PSF[:, 0:2048], bias_row[0:1, 128 * fc:128 * (fc + 1)],
                      ONESROW, start=False, stop=True, n_free=2048,
                      inc="sPE", wait=W(("sDV", sch.v("sDV")),
                                        ("sGP", gp_init)))
            vpe = sch.v("sPE")
            ac(lambda a, fc=fc, dst=dst: a.activation(
                out=dst[:, 2048 * fc:2048 * (fc + 1)], in_=PSF[:, 0:2048],
                func=AF.Copy, scale=1.0), inc="sAC", wait=W(("sPE", vpe)))
        return sch.v("sAC")

    v_w = load_weight_direct("wq_t", 1024)
    proj_T(QT, BQROWB, XNT, wwait=v_w)
    v_w = load_weight_direct("wk_t", 1024, wait_pe=sch.v("sPE"))
    proj_T(KT, BKROWB, XNT, wwait=v_w)
    v_w = load_weight_direct("wv_t", 1024, wait_pe=sch.v("sPE"))
    for t in range(TT):
        for dc in range(8):
            pe_mm(PSF[:, 0:1024],
                  XNT[:, 2048 * dc + 128 * t:2048 * dc + 128 * (t + 1)],
                  WTMP[:, 1024 * dc:1024 * (dc + 1)],
                  start=(dc == 0), stop=False, n_free=1024,
                  wait=(W(v_w, ("sDV", sch.v("sDV")),
                          ("sAC", sch.v("sAC")),
                          ("sLs", v_small)) if dc == 0 else ()))
        pe_mm(PSF[:, 0:1024], ONE1F[:], BVROW[:], start=False, stop=True,
              n_free=1024, inc="sPE")
        vpe = sch.v("sPE")
        ac(lambda a, t=t: a.activation(
            out=VP[:, 1040 * t:1040 * (t + 1)]
                .rearrange("p (h c) -> p h c", c=65)[:, :, 0:64],
            in_=PSF[:, 0:1024], func=AF.Copy, scale=1.0),
           inc="sAC", wait=W(("sPE", vpe), ("sGP", gp_init)))
    v_vp = sch.v("sAC")

    evac_done = {}
    pen_done = {}
    qk_done = {}
    sel_tcol = {}
    pen0_relu = {}
    st_last = {}
    exp_pair = {}
    av_kt = {}
    div_done = 0

    def b1_qh_kh(ei):
        h = ei // 8
        ch, rh = h // 2, (h % 2) * 64
        return (QT[rh:rh + 64, 2048 * ch:2048 * (ch + 1)],
                KT[rh:rh + 64, 2048 * ch:2048 * (ch + 1)])

    def emit_qksel(ei, qi):
        Qh, Kh = b1_qh_kh(ei)
        qt = 2 * (ei % 8) + qi
        ev = (evac_done.get((ei, 0), (0, 0)) if qi == 1 else
              evac_done.get((ei - 1, 1), (0, 0)))[1]
        for j in range(4):
            pe(lambda p, qt=qt, j=j, Qh=Qh, Kh=Kh: p.matmul(
                PSF[:, 512 * j:512 * (j + 1)],
                Qh[:, 128 * qt:128 * (qt + 1)],
                Kh[:, 512 * j:512 * (j + 1)],
                start=True, stop=True),
               inc=("sPE" if j == 3 else None),
               wait=(W(("sAC", max(ev, v_vp))) if j == 0 else ()))
        qk_done[(ei, qi)] = sch.v("sPE")

    def emit_evac(ei, qi):
        sb = SBT[:, 2048 * qi:2048 * (qi + 1)]
        ph0, ph1 = pen_done.get((ei - 1, qi), (0, 0))
        vs = []
        for hh in range(2):
            ac(lambda a, sb=sb, hh=hh: a.activation(
                out=sb[:, 1024 * hh:1024 * (hh + 1)],
                in_=PSF[:, 1024 * hh:1024 * (hh + 1)],
                func=AF.Copy, scale=1.0),
               inc="sAC", wait=W(("sPE", qk_done[(ei, qi)]),
                                 ("sDV", ph0 if hh == 0 else ph1)))
            vs.append(sch.v("sAC"))
        evac_done[(ei, qi)] = tuple(vs)

    def emit_sel_l1(ei, qi, mid=None):
        sb = SBT[:, 2048 * qi:2048 * (qi + 1)]
        cnd = (CND if qi == 0 else CN2)[:, 0:256]
        for c in range(16):
            wl = ()
            if c == 0:
                wl = W(("sAC", evac_done[(ei, qi)][0]),
                       ("sDV", sch.v("sDV")))
            elif c == 8:
                wl = W(("sAC", evac_done[(ei, qi)][1]))
            dv(lambda v, c=c, sb=sb, cnd=cnd: v.max(
                out=cnd[:, 8 * c:8 * (c + 1)],
                in_=sb[:, 128 * c:128 * (c + 1)]),
               inc="sDV", wait=wl)
            if c == 3 and mid is not None:
                mid()
        v_r1 = sch.v("sDV")
        vgrp = v_r1
        for grp in range(4):
            for c in range(4 * grp, 4 * grp + 4):
                dv(lambda v, c=c, sb=sb, cnd=cnd: v.match_replace(
                    out=SBX[:, 128 * (c % 4):128 * (c % 4) + 128],
                    in_to_replace=cnd[:, 8 * c:8 * (c + 1)],
                    in_values=sb[:, 128 * c:128 * (c + 1)],
                    imm_value=NEG),
                   inc="sDV",
                   wait=(W(("sDV", vgrp)) if c % 4 == 0 else ()))
            v_mr = sch.v("sDV")
            for c in range(4 * grp, 4 * grp + 4):
                dv(lambda v, c=c, cnd=cnd: v.max(
                    out=cnd[:, 128 + 8 * c:128 + 8 * (c + 1)],
                    in_=SBX[:, 128 * (c % 4):128 * (c % 4) + 128]),
                   inc="sDV",
                   wait=(W(("sDV", v_mr)) if c % 4 == 0 else ()))
            vgrp = sch.v("sDV")
        return sch.v("sDV")

    def emit_sel_l2(ei, v_l1):
        cands = (CND[:, 0:256], CN2[:, 0:256])
        last = {0: v_l1[0], 1: v_l1[1]}
        for rr in range(5):
            for qi in range(2):
                dv(lambda v, qi=qi: v.max(out=R8[:, 8 * qi:8 * (qi + 1)],
                                          in_=cands[qi]),
                   inc="sDV", wait=W(("sDV", last[qi])))
                last[qi] = sch.v("sDV")
            if rr < 4:
                for qi in range(2):
                    dv(lambda v, qi=qi: v.match_replace(
                        out=cands[qi], in_to_replace=R8[:, 8 * qi:8 * (qi + 1)],
                        in_values=cands[qi], imm_value=NEG),
                       inc="sDV", wait=W(("sDV", last[qi])))
                    last[qi] = sch.v("sDV")
        for qi in range(2):
            dv(lambda v, qi=qi: v.tensor_copy(
                out=TCOL[:, qi:qi + 1], in_=R8[:, 8 * qi + 5:8 * qi + 6]),
               inc="sDV", wait=W(("sDV", last[qi])))
            sel_tcol[(ei, qi)] = sch.v("sDV")
            emit_pen_half(ei, qi, 0)
        emit_pen_half(ei, 0, 1)
        emit_pen_half(ei, 1, 1)

    def emit_pen_half(ei, qi, hh):
        sb = SBT[:, 2048 * qi:2048 * (qi + 1)]
        sm = SBMB[:, 2048 * qi:2048 * (qi + 1)]
        dv(lambda v, sb=sb, sm=sm, qi=qi, hh=hh: v.tensor_scalar(
            out=sm[:, 1024 * hh:1024 * (hh + 1)],
            in0=sb[:, 1024 * hh:1024 * (hh + 1)],
            scalar1=TCOL[:, qi:qi + 1], scalar2=None, op0=OP.is_lt),
           inc="sDV", wait=W(("sDV", sch.v("sDV")),
                             ("sPE", st_last.get(ei - 1, 0))))
        v = sch.v("sDV")
        cur = pen_done.get((ei, qi), (0, 0))
        pen_done[(ei, qi)] = (v, cur[1]) if hh == 0 else (cur[0], v)

    def emit_av(bi, g, h, ctx_c0):
        for sub in range(2):
            kt = 2 * g + sub
            w = (W(("sAC", max(exp_pair[(bi, g)], cte_gate)))
                 if kt == 0 else
                 (W(("sAC", exp_pair[(bi, g)])) if sub == 0 else ()))
            pe(lambda p, kt=kt, h=h, ctx_c0=ctx_c0: p.matmul(
                PSF[0:65, ctx_c0:ctx_c0 + 256],
                VP[:, 1040 * kt + 65 * h:1040 * kt + 65 * (h + 1)],
                PTE[:, 256 * (kt % 8):256 * (kt % 8) + 256],
                start=(kt == 0), stop=(kt == 15)),
               inc="sPE", wait=w)
            av_kt[(bi, kt)] = sch.v("sPE")

    pend_div = None
    rcps_gate = 0
    cte_gate = 0

    def emit_div2(pd):
        dbi, v_rzb = pd
        h, e8 = dbi // 8, dbi % 8
        half = e8 // 4
        pe_mm(PSF[0:64, 2048:3072], ONE1B[0:1, 0:64], RZROWB[:],
              start=True, stop=True, n_free=1024, inc="sPE",
              wait=W(("sAC", v_rzb)))
        v_rb = sch.v("sPE")
        ac(lambda a: a.activation(out=RCPS[:],
                                  in_=PSF[0:64, 2048:3072],
                                  func=AF.Copy, scale=1.0),
           inc="sAC", wait=W(("sPE", v_rb), ("sAC", sch.v("sAC"))))
        v_rc = sch.v("sAC")

        def ct1_thunk():
            dv(lambda v, h=h, half=half: v.tensor_tensor(
                out=CT1[:, 2048 * (h // 2) + 1024 * half:
                        2048 * (h // 2) + 1024 * (half + 1)]
                    [(h % 2) * 64:(h % 2) * 64 + 64, :],
                in0=CTE[0:64, :], in1=RCPS[:], op=OP.mult),
               inc="sDV", wait=W(("sAC", v_rc), ("sDV", sch.v("sDV"))))
            return sch.v("sDV")
        return v_rc, ct1_thunk

    for step in range(130):
        ei, bi = step, step - 1
        if ei == 0:
            emit_qksel(0, 0)
        ct1_pend = None
        if pend_div is not None:
            rcps_gate, ct1_pend = emit_div2(pend_div)
            pend_div = None
        if ei < 128:
            emit_evac(ei, 0)
        if bi >= 0 and bi < 128:
            Qh, Kh = b1_qh_kh(bi)
            h, e8 = bi // 8, bi % 8
            q0 = 256 * e8
            ctx_c0 = 3072 + 256 * (e8 % 4)
            st_v = {}
            p0h0, p0h1 = pen_done[(bi, 0)]
            p1h0, p1h1 = pen_done[(bi, 1)]
            for g in range(8):
                wbase = 2048 + 512 * (g % 2)
                wguard = (exp_pair[(bi, g - 2)] if g >= 2 else
                          exp_pair.get((bi - 1, g + 6), 0))
                for sub in range(2):
                    kt = 2 * g + sub
                    po = wbase + 256 * sub
                    if g == 0 and sub == 0:
                        pw = W(("sDV", max(p0h0, p1h0)),
                               ("sAC", max(wguard, rcps_gate)))
                    elif g == 4 and sub == 0:
                        pw = W(("sDV", max(p0h1, p1h1)),
                               ("sAC", wguard))
                    elif sub == 0:
                        pw = W(("sAC", wguard))
                    else:
                        pw = ()
                    pe(lambda p, kt=kt, po=po, q0=q0, Qh=Qh, Kh=Kh:
                       p.matmul(PSF[:, po:po + 256],
                                Kh[:, 128 * kt:128 * (kt + 1)],
                                Qh[:, q0:q0 + 256],
                                start=True, stop=False),
                       wait=pw)
                    for qi2 in range(2):
                        pe(lambda p, kt=kt, po=po, qi2=qi2: p.matmul(
                            PSF[:, po + 128 * qi2:po + 128 * (qi2 + 1)],
                            SBMB[:, 2048 * qi2 + 128 * kt:
                                 2048 * qi2 + 128 * (kt + 1)],
                            NEGI[:], start=False, stop=(qi2 == 1)),
                           inc=("sPE" if qi2 == 1 else None))
                    st_v[kt] = sch.v("sPE")
                pguard = (av_kt.get((bi, 2 * g - 7), 0) if g >= 4 else
                          av_kt.get((bi - 1, 2 * g + 9), 0))
                ac(lambda a, g=g, wbase=wbase: a.activation(
                    out=PTE[:, 512 * (g % 4):512 * (g % 4) + 512],
                    in_=PSF[:, wbase:wbase + 512], func=AF.Exp, scale=SC),
                   inc="sAC", wait=W(("sPE", max(st_v[2 * g + 1], pguard))))
                exp_pair[(bi, g)] = sch.v("sAC")
                if g >= 1:
                    emit_av(bi, g - 1, h, ctx_c0)
                if g == 2 and ei < 128:
                    emit_qksel(ei, 1)
                    emit_evac(ei, 1)
                if g == 6 and ei + 1 <= 127:
                    emit_qksel(ei + 1, 0)
            emit_av(bi, 7, h, ctx_c0)
            st_last[bi] = st_v[15]
        if ei < 128 and (bi < 0 or bi >= 128):
            emit_qksel(ei, 1)
            emit_evac(ei, 1)
        if bi < 0 and ei + 1 <= 127:
            emit_qksel(ei + 1, 0)
        if ei < 128:
            mid = None
            if ct1_pend is not None:
                thunk = ct1_pend

                def mid():
                    nonlocal div_done
                    div_done = thunk()
                ct1_pend = None
            v_l1 = {0: emit_sel_l1(ei, 0, mid=mid)}
            v_l1[1] = emit_sel_l1(ei, 1)
            emit_sel_l2(ei, v_l1)
        if ct1_pend is not None:
            div_done = ct1_pend()
        if bi >= 0 and bi < 128 and bi % 4 == 3:
            v_ava = av_kt[(bi, 15)]
            ac(lambda a: a.activation(out=CTE[0:65, :],
                                      in_=PSF[0:65, 3072:4096],
                                      func=AF.Copy, scale=1.0),
               inc="sAC", wait=W(("sPE", v_ava), ("sDV", div_done),
                                 ("sAC", sch.v("sAC"))))
            cte_v = sch.v("sAC")
            cte_gate = cte_v
            dv(lambda v: v.reciprocal(out=RZROW[:], in_=CTE[64:65, :]),
               inc="sDV", wait=W(("sAC", cte_v), ("sDV", sch.v("sDV"))))
            v_rz0 = sch.v("sDV")
            ac(lambda a: a.activation(out=RZROWB[:], in_=RZROW[:],
                                      func=AF.Copy, scale=1.0),
               inc="sAC", wait=W(("sDV", v_rz0), ("sAC", sch.v("sAC"))))
            pend_div = (bi, sch.v("sAC"))

    v_wo = load_weight_direct("wo_t", 1024, wait_pe=sch.v("sPE"),
                              dv_guard=sch.v("sDV"))
    BOROW = ARN[0:1, 50176:51200]
    RQROW = ARF[0:1, 3072:4096]
    v_ac_ph2 = sch.v("sAC")
    gp(lambda g: g.dma_start(out=BOROW, in_=di["bo"]), inc="sGW", n=16,
       wait=W(("sGW", sch.v("sGW")), ("sPE", sch.v("sPE"))))
    v_bo = sch.v("sGW")
    sp(lambda s: s.dma_start(out=RQROW, in_=di["rq"]), inc="sLs", n=16,
       wait=W(("sDV", sch.v("sDV"))))
    v_rq = sch.v("sLs")
    pe_mm(PSF[:, 2048:3072], ONE1F[:], RQROW, start=True, stop=True,
          n_free=1024, inc="sPE",
          wait=W(("sDV", div_done), ("sAC", rcps_gate), ("sLs", v_rq)))
    v_rqbc = sch.v("sPE")
    dv(lambda v: v.tensor_copy(out=R128[:], in_=PSF[:, 2048:3072]),
       inc="sDV", wait=W(("sPE", v_rqbc), ("sDV", sch.v("sDV"))))
    v_r128 = sch.v("sDV")

    for t in range(TT):
        po = 1024 * (t % 2)
        for dc in range(8):
            pe_mm(PSF[:, po:po + 1024],
                  CT1[:, 2048 * dc + 128 * t:2048 * dc + 128 * (t + 1)],
                  WTMP[:, 1024 * dc:1024 * (dc + 1)],
                  start=(dc == 0), stop=False, n_free=1024,
                  wait=(W(v_wo, ("sDV", max(div_done, sch.v("sDV"))))
                        if dc == 0 else ()))
        pe_mm(PSF[:, po:po + 1024], ONE1B[:], BOROW, start=False,
              stop=True, n_free=1024, inc="sPE",
              wait=W(("sGW", v_bo)))
        v_pe = sch.v("sPE")
        dv(lambda v, t=t, po=po: v.scalar_tensor_tensor(
            out=XB[:, D * t:D * (t + 1)], in0=PSF[:, po:po + 1024],
            scalar=1.0, in1=XB[:, D * t:D * (t + 1)], op0=OP.mult,
            op1=OP.add), inc="sDV", wait=W(("sPE", v_pe),
                                           ("sDV", sch.v("sDV"))))
        xt = XB[:, D * t:D * (t + 1)]
        dv(lambda v, xt=xt: v.bn_stats(out=BN12[:, 0:6], in_=xt[:, 0:512]),
           inc="sDV", wait=W(("sDV", sch.v("sDV"))))
        dv(lambda v, xt=xt: v.bn_stats(out=BN12[:, 6:12],
                                       in_=xt[:, 512:1024]), inc="sDV")
        dv(lambda v: v.bn_aggr(out=BN2[:], in_=BN12[:]), inc="sDV",
           wait=W(("sDV", sch.v("sDV"))))
        dv(lambda v: v.tensor_copy(out=MCOL[:], in_=BN2[:, 0:1]), inc="sDV",
           wait=W(("sDV", sch.v("sDV"))))
        v_bn = sch.v("sDV")
        ac(lambda a: a.activation(out=SSQ[:], in_=BN2[:, 1:2], func=AF.Sqrt,
                                  bias=EPSC[:], scale=1.0), inc="sAC",
           wait=W(("sDV", v_bn)))
        dv(lambda v: v.reciprocal(out=RSTD[:], in_=SSQ[:]), inc="sDV",
           wait=W(("sAC", sch.v("sAC"))))
        dv(lambda v, xt=xt: v.tensor_scalar(
            out=XNF[:], in0=xt, scalar1=MCOL[:], scalar2=RSTD[:],
            op0=OP.subtract, op1=OP.mult), inc="sDV",
           wait=W(("sDV", sch.v("sDV"))))
        v_xnf = sch.v("sDV")
        dv(lambda v, t=t: v.scalar_tensor_tensor(
            out=SPROD[:], in0=XNF[:], scalar=1.0, in1=R128[:],
            op0=OP.mult, op1=OP.mult),
           inc="sDV", wait=W(("sDV", max(v_r128, v_xnf))))
        dv(lambda v, t=t: v.reduce_sum(out=SCOL[:, t:t + 1], in_=SPROD[:],
                                       axis=AX),
           inc="sDV", wait=W(("sDV", sch.v("sDV"))))
        ac(lambda a: a.activation(out=SQJ[:], in_=XNF[:], func=AF.Square,
                                  scale=1.0, accum_out=SSQ[:]), inc="sAC",
           wait=W(("sDV", sch.v("sDV"))))
        ac(lambda a: a.activation(out=MCOL[:], in_=SSQ[:], func=AF.Sqrt,
                                  scale=1.0 / D), inc="sAC",
           wait=W(("sAC", sch.v("sAC"))))
        dv(lambda v: v.reciprocal(out=RSN[:], in_=MCOL[:]), inc="sDV",
           wait=W(("sAC", sch.v("sAC"))))
        v_rsn = sch.v("sDV")
        dv(lambda v: v.tensor_scalar(out=RNF[:], in0=XNF[:], scalar1=RSN[:],
                                     scalar2=None, op0=OP.mult),
           inc="sDV", wait=W(("sDV", v_rsn)))
        v_rn = sch.v("sDV")
        for c in range(8):
            pe(lambda p, c=c: p.transpose(
                PSF[:, 3072 + 128 * c:3200 + 128 * c],
                RNF[:, 128 * c:128 * (c + 1)], IDF[:]),
               inc=("sPE" if c == 7 else None),
               wait=(W(("sDV", v_rn), ("sAC", v_ac_ph2 if t == 0 else 0))
                     if c == 0 else ()))
        v_tr = sch.v("sPE")
        dv(lambda v, t=t: v.tensor_copy(
            out=RNT.rearrange("p (c n) -> p c n", n=2048)
                [:, :, 128 * t:128 * (t + 1)],
            in_=PSF[:, 3072:4096].rearrange("p (c n) -> p c n", n=128)),
           inc="sDV", wait=W(("sPE", v_tr)))
    v_ln2 = sch.v("sDV")
    v_x1 = v_ln2


    v_wql = load_weight_direct("wql", 512, gcol=16, wait_pe=sch.v("sPE"),
                               dv_guard=sch.v("sDV"), ac_guard=sch.v("sAC"))
    lp_state = {"vevac": sch.v("sAC"), "vvpl": 0}

    def light_projs():
        vq = v_wql[1]
        for fc in range(4):
            for dc in range(8):
                pe_mm(PSF[:, 0:2048],
                      WTMP[:, 512 * dc + 128 * fc:512 * dc + 128 * (fc + 1)],
                      RNT[:, 2048 * dc:2048 * (dc + 1)],
                      start=(dc == 0), stop=(dc == 7), n_free=2048,
                      inc=("sPE" if dc == 7 else None),
                      wait=(W(("sDV", vq), ("sAC", lp_state["vevac"]))
                            if dc == 0 else ()))
                yield
            v_pe = sch.v("sPE")
            ac(lambda a, fc=fc: a.activation(
                out=QLT[:, 2048 * fc:2048 * (fc + 1)], in_=PSF[:, 0:2048],
                func=AF.Copy, scale=1.0), inc="sAC", wait=W(("sPE", v_pe)))
            lp_state["vevac"] = sch.v("sAC")
            yield
        v_wkvl = load_weight_direct("wkvl", 1024, gcol=16,
                                    wait_pe=sch.v("sPE"),
                                    dv_guard=sch.v("sDV"),
                                    ac_guard=lp_state["vevac"])
        vkv = v_wkvl[1]
        yield
        for fc in range(4):
            for dc in range(8):
                pe_mm(PSF[:, 0:2048],
                      WTMP[:, 1024 * dc + 128 * fc:
                           1024 * dc + 128 * (fc + 1)],
                      RNT[:, 2048 * dc:2048 * (dc + 1)],
                      start=(dc == 0), stop=(dc == 7), n_free=2048,
                      inc=("sPE" if dc == 7 else None),
                      wait=(W(("sDV", vkv), ("sAC", lp_state["vevac"]))
                            if dc == 0 else ()))
                yield
            v_pe = sch.v("sPE")
            ac(lambda a, fc=fc: a.activation(
                out=KLT[:, 2048 * fc:2048 * (fc + 1)], in_=PSF[:, 0:2048],
                func=AF.Copy, scale=1.0), inc="sAC", wait=W(("sPE", v_pe)))
            lp_state["vevac"] = sch.v("sAC")
            yield
        for t in range(TT):
            for dc in range(8):
                pe_mm(PSF[:, 0:512],
                      RNT[:, 2048 * dc + 128 * t:2048 * dc + 128 * (t + 1)],
                      WTMP[:, 1024 * dc + 512:1024 * (dc + 1)],
                      start=(dc == 0), stop=(dc == 7), n_free=512,
                      inc=("sPE" if dc == 7 else None),
                      wait=(W(("sAC", lp_state["vevac"]), ("sGP", gp_init))
                            if dc == 0 else ()))
                yield
            v_pe = sch.v("sPE")
            ac(lambda a, t=t: a.activation(
                out=VPL[:, 520 * t:520 * (t + 1)]
                    .rearrange("p (h c) -> p h c", c=65)[:, :, 0:64],
                in_=PSF[:, 0:512], func=AF.Copy, scale=1.0),
               inc="sAC", wait=W(("sPE", v_pe)))
            lp_state["vevac"] = sch.v("sAC")
            lp_state["vvpl"] = sch.v("sAC")
            yield

    lpg = light_projs()

    def pump(n=1):
        for _ in range(n):
            next(lpg, None)

    v_m = dv(lambda v: v.memset(ANEW[:], A1), inc="sDV")
    dv(lambda v: v.tensor_scalar(out=NANEW[:], in0=ANEW[:], scalar1=-1.0,
                                 scalar2=None, op0=OP.mult), inc="sDV",
       wait=W(("sDV", v_m)))
    dv(lambda v: v.tensor_scalar(out=APLK[:], in0=ANEW[:], scalar1=LOGK,
                                 scalar2=None, op0=OP.add), inc="sDV",
       wait=W(("sDV", v_m)))
    v_a = sch.v("sDV")
    for it in range(NIT - 1):
        pe(lambda p: p.matmul(PSF[:, 2048:2049], ONE1F[:], NANEW[:],
                              start=True, stop=True), inc="sPE",
           wait=W(("sDV", v_a)))
        v_brc = sch.v("sPE")
        ac(lambda a: a.activation(out=NAREP[:], in_=PSF[:, 2048:2049],
                                  func=AF.Copy, scale=1.0), inc="sAC",
           wait=W(("sPE", v_brc)))
        v_nar = sch.v("sAC")
        ac(lambda a: a.activation(out=RSC[:], in_=SCOL[:], func=AF.Relu,
                                  bias=NAREP[:], scale=-1.0), inc="sAC",
           wait=W(("sAC", v_nar), ("sDV", v_ln2)))
        ac(lambda a: a.activation(out=EJ16[:], in_=RSC[:], func=AF.Exp,
                                  scale=-1.0, accum_out=EPART[:]), inc="sAC",
           wait=W(("sAC", sch.v("sAC"))))
        v_ep = sch.v("sAC")
        pe(lambda p: p.matmul(PSF[0:1, 2049:2050], EPART[:], ONECF[:],
                              start=True, stop=True), inc="sPE",
           wait=W(("sAC", v_ep)))
        v_sum = sch.v("sPE")
        ac(lambda a: a.activation(out=LNS[:], in_=PSF[0:1, 2049:2050],
                                  func=AF.Ln, scale=1.0), inc="sAC",
           wait=W(("sPE", v_sum)))
        v_lns = sch.v("sAC")
        dv(lambda v: v.scalar_tensor_tensor(
            out=ANEW[:], in0=LNS[:], scalar=-1.0, in1=APLK[:],
            op0=OP.mult, op1=OP.add), inc="sDV", wait=W(("sAC", v_lns)))
        v_an = sch.v("sDV")
        dv(lambda v: v.tensor_scalar(out=APLK[:], in0=ANEW[:], scalar1=LOGK,
                                     scalar2=None, op0=OP.add), inc="sDV",
           wait=W(("sDV", v_an)))
        dv(lambda v: v.tensor_scalar(out=NANEW[:], in0=ANEW[:], scalar1=-1.0,
                                     scalar2=None, op0=OP.mult), inc="sDV",
           wait=W(("sDV", v_an)))
        v_a = sch.v("sDV")
        pump(6)
    pe(lambda p: p.matmul(PSF[:, 2048:2049], ONE1F[:], ANEW[:],
                          start=True, stop=True), inc="sPE",
       wait=W(("sDV", v_a)))
    ac(lambda a: a.activation(out=AREP[:], in_=PSF[:, 2048:2049],
                              func=AF.Copy, scale=1.0), inc="sAC",
       wait=W(("sPE", sch.v("sPE"))))
    dv(lambda v: v.tensor_scalar(out=UCOL[:], in0=SCOL[:], scalar1=AREP[:],
                                 scalar2=0.0, op0=OP.add, op1=OP.min),
       inc="sDV", wait=W(("sAC", sch.v("sAC"))))
    v_ucol = sch.v("sDV")
    pe(lambda p: p.transpose(PSF[0:16, 2048:2176], UCOL[:], IDF[:]),
       inc="sPE", wait=W(("sDV", v_ucol)))
    dv(lambda v: v.tensor_copy(out=U16[:], in_=PSF[0:16, 2048:2176]),
       inc="sDV", wait=W(("sPE", sch.v("sPE"))))
    v_u16 = sch.v("sDV")
    UROW = ARF[0:1, 0:2048]
    sp(lambda s: s.dma_start(out=UROW, in_=U16[:]), inc="sLs", n=16,
       wait=W(("sDV", v_u16)))
    v_urow = sch.v("sLs")
    v_amat = pe_mm(PSF[:, 2048:4096], ONE1F[:], UROW, start=True, stop=True,
                   n_free=2048, inc="sPE", wait=W(("sLs", v_urow)))
    for tb in range(16):
        dv(lambda v, tb=tb: v.tensor_scalar(
            out=OUTF[:, 0:2048], in0=PSF[:, 2048:4096],
            scalar1=UCOL[:, tb:tb + 1], scalar2=None, op0=OP.is_gt),
           inc="sDV", wait=(W(("sPE", v_amat)) if tb == 0 else
                            W(("sDV", sch.v("sDV")))))
        dv(lambda v, tb=tb: v.reduce_sum(out=CCOL[:, tb:tb + 1],
                                         in_=OUTF[:, 0:2048], axis=AX),
           inc="sDV", wait=W(("sDV", sch.v("sDV"))))
        pump(2)
    v_ccol = sch.v("sDV")
    dv(lambda v: v.tensor_scalar(out=SAT16[:], in0=U16[:], scalar1=0.0,
                                 scalar2=None, op0=OP.is_ge), inc="sDV",
       wait=W(("sDV", v_ccol)))
    dv(lambda v: v.tensor_tensor_scan(out=P16[:], data0=SAT16[:],
                                      data1=SAT16[:], initial=0.0,
                                      op0=OP.add, op1=OP.bypass), inc="sDV",
       wait=W(("sDV", sch.v("sDV"))))
    dv(lambda v: v.reduce_sum(out=RS16[:], in_=SAT16[:], axis=AX), inc="sDV",
       wait=W(("sDV", sch.v("sDV"))))
    v_rs = sch.v("sDV")
    pe(lambda p: p.transpose(PSF[0:1, 2048:2064], RS16[:], IDF[0:16, 0:16]),
       inc="sPE", wait=W(("sDV", v_rs)))
    dv(lambda v: v.tensor_copy(out=OFR[:], in_=PSF[0:1, 2048:2064]),
       inc="sDV", wait=W(("sPE", sch.v("sPE"))))
    dv(lambda v: v.tensor_tensor_scan(out=OFR2[:], data0=OFR[:],
                                      data1=OFR[:], initial=0.0,
                                      op0=OP.add, op1=OP.bypass), inc="sDV",
       wait=W(("sDV", sch.v("sDV"))))
    dv(lambda v: v.memset(OFR[:, 0:1], 0.0), inc="sDV",
       wait=W(("sDV", sch.v("sDV"))))
    dv(lambda v: v.tensor_copy(out=OFR[:, 1:16], in_=OFR2[:, 0:15]),
       inc="sDV", wait=W(("sDV", sch.v("sDV"))))
    v_ofr = sch.v("sDV")
    pe(lambda p: p.transpose(PSF[0:16, 2064:2065], OFR[:], IDF[0:1, 0:1]),
       inc="sPE", wait=W(("sDV", v_ofr)))
    dv(lambda v: v.tensor_copy(out=OFF16[:], in_=PSF[0:16, 2064:2065]),
       inc="sDV", wait=W(("sPE", sch.v("sPE"))))
    v_offev = sch.v("sDV")
    dv(lambda v: v.tensor_sub(out=P16[:], in0=P16[:], in1=SAT16[:]),
       inc="sDV", wait=W(("sDV", sch.v("sDV"))))
    dv(lambda v: v.tensor_scalar(out=P16[:], in0=P16[:], scalar1=OFF16[:],
                                 scalar2=None, op0=OP.add), inc="sDV",
       wait=W(("sDV", sch.v("sDV"))))
    dv(lambda v: v.tensor_mul(out=P16[:], in0=P16[:], in1=SAT16[:]),
       inc="sDV", wait=W(("sDV", sch.v("sDV"))))
    v_p16 = sch.v("sDV")
    pe(lambda p: p.transpose(PSF[0:16, 2176:2304], CCOL[:], IDF[:]),
       inc="sPE", wait=W(("sDV", max(v_ccol, v_offev))))
    dv(lambda v: v.tensor_copy(out=C16[:], in_=PSF[0:16, 2176:2304]),
       inc="sDV", wait=W(("sPE", sch.v("sPE"))))
    dv(lambda v: v.tensor_add(out=R16[:] if False else SEL16[:], in0=C16[:],
                              in1=P16[:]), inc="sDV",
       wait=W(("sDV", sch.v("sDV"))))
    dv(lambda v: v.tensor_scalar(out=SEL16[:], in0=SEL16[:], scalar1=1023.5,
                                 scalar2=None, op0=OP.is_lt), inc="sDV",
       wait=W(("sDV", sch.v("sDV"))))
    v_sel16 = sch.v("sDV")
    pe(lambda p: p.transpose(PSF[:, 2304:2320], SEL16[:], IDF[0:16, 0:16]),
       inc="sPE", wait=W(("sDV", v_sel16)))
    dv(lambda v: v.tensor_copy(out=SELCOL[:], in_=PSF[:, 2304:2320]),
       inc="sDV", wait=W(("sPE", sch.v("sPE"))))
    v_sel = sch.v("sDV")

    pump(400)
    v_vpl = lp_state["vvpl"]

    lt_div = sch.v("sDV")
    cte_prev = sch.v("sAC")
    exp_w = {}
    nw = 0
    for h in range(H2):
        chl, rhl = h // 2, (h % 2) * 64
        QLh = QLT[rhl:rhl + 64, 2048 * chl:2048 * (chl + 1)]
        KLh = KLT[rhl:rhl + 64, 2048 * chl:2048 * (chl + 1)]
        for wg in range(2):
            pend_av = None

            def emit_lav(pa):
                w, h2, wi, expv, fav = pa
                pe(lambda p, w=w, h2=h2, wi=wi: p.matmul(
                    PSF[0:65, 3072 + 128 * wi:3072 + 128 * (wi + 1)],
                    VPL[:, 520 * w + 65 * h2:520 * w + 65 * (h2 + 1)],
                    PTE[:, 128 * wi:128 * (wi + 1)],
                    start=True, stop=True),
                   inc="sPE", wait=W(("sAC", max(expv, cte_prev)),
                                     ("sDV", fav)))

            for wi in range(8):
                w = 8 * wg + wi
                pb = 512 * (wi % 2)
                pe(lambda p, w=w, pb=pb, QLh=QLh, KLh=KLh: p.matmul(
                    PSF[:, pb:pb + 128], KLh[:, 128 * w:128 * (w + 1)],
                    QLh[:, 128 * w:128 * (w + 1)], start=True, stop=True),
                   inc="sPE",
                   wait=W(("sAC", max(v_vpl, exp_w.get(nw - 2, 0)))))
                v_qk = sch.v("sPE")
                ac(lambda a, wi=wi, pb=pb: a.activation(
                    out=PTE[:, 128 * wi:128 * (wi + 1)],
                    in_=PSF[:, pb:pb + 128], func=AF.Exp, scale=SC),
                   inc="sAC", wait=W(("sPE", v_qk)))
                exp_w[nw] = sch.v("sAC")
                nw += 1
                if pend_av is not None:
                    emit_lav(pend_av)
                pend_av = (w, h, wi, exp_w[nw - 1],
                           v_ccol if (h == 0 and wg == 0 and wi == 0) else 0)
            emit_lav(pend_av)
            v_av = sch.v("sPE")
            ac(lambda a: a.activation(out=CTE[0:65, :],
                                      in_=PSF[0:65, 3072:4096],
                                      func=AF.Copy, scale=1.0),
               inc="sAC", wait=W(("sPE", v_av), ("sDV", lt_div)))
            cte_prev = sch.v("sAC")
            dv(lambda v: v.reciprocal(out=RZROW[:], in_=CTE[64:65, :]),
               inc="sDV", wait=W(("sAC", sch.v("sAC")),
                                 ("sDV", sch.v("sDV"))))
            dv(lambda v: v.tensor_copy(out=RZROWB[:], in_=RZROW[:]),
               inc="sDV", wait=W(("sDV", sch.v("sDV"))))
            v_rz = sch.v("sDV")
            pe_mm(PSF[0:64, 2048:3072], ONE1B[0:1, 0:64], RZROWB[:],
                  start=True, stop=True, n_free=1024, inc="sPE",
                  wait=W(("sDV", v_rz)))
            v_rb = sch.v("sPE")
            dv(lambda v: v.tensor_copy(out=RCPS[:], in_=PSF[0:64, 2048:3072]),
               inc="sDV", wait=W(("sPE", v_rb), ("sDV", sch.v("sDV"))))
            v_rc = sch.v("sDV")
            dv(lambda v, h=h, wg=wg: v.tensor_tensor(
                out=CTXL[:, 2048 * (h // 2) + 1024 * wg:
                         2048 * (h // 2) + 1024 * (wg + 1)]
                    [(h % 2) * 64:(h % 2) * 64 + 64, :],
                in0=CTE[0:64, :], in1=RCPS[:], op=OP.mult),
               inc="sDV", wait=W(("sDV", v_rc)))
            lt_div = sch.v("sDV")

    load_weight_direct("wqh", 512, gcol=24, wait_pe=sch.v("sPE"),
                       dv_guard=sch.v("sDV"), ac_guard=sch.v("sAC"))
    proj_T(QHT, None, RNT, nf=4, wcols=512)
    load_weight_direct("wkvh_t", 1024, gcol=24, wait_pe=sch.v("sPE"),
                       dv_guard=sch.v("sDV"), ac_guard=sch.v("sAC"))
    vw = sch.v("sDV")
    for pc in range(4):
        for sub in range(2):
            hh = 2 * pc + sub
            for dc in range(8):
                pe_mm(PSF[64 * sub:64 * (sub + 1), 0:2048],
                      WTMP[:, 1024 * dc + 128 * hh:1024 * dc + 128 * hh + 64],
                      RNT[:, 2048 * dc:2048 * (dc + 1)],
                      start=(dc == 0), stop=(dc == 7), n_free=2048,
                      inc=("sPE" if (dc == 7 and sub == 1) else None),
                      wait=(W(("sDV", vw), ("sAC", sch.v("sAC")))
                            if (dc == 0 and sub == 0) else ()))
        v_pe = sch.v("sPE")
        ac(lambda a, pc=pc: a.activation(
            out=KVHT[:, 2048 * pc:2048 * (pc + 1)], in_=PSF[:, 0:2048],
            func=AF.Copy, scale=1.0), inc="sAC", wait=W(("sPE", v_pe)))
    v_light_done = lt_div
    for t in range(TT):
        for dc in range(8):
            pe(lambda p, t=t, dc=dc: p.matmul(
                PSF[:, 0:512],
                RNT[:, 2048 * dc + 128 * t:2048 * dc + 128 * (t + 1)],
                WTMP[:, 1024 * dc:1024 * (dc + 1)]
                    .rearrange("p (h c) -> p h c", c=128)[:, :, 64:128],
                start=(dc == 0), stop=(dc == 7)),
               inc=("sPE" if dc == 7 else None),
               wait=(W(("sAC", sch.v("sAC")), ("sDV", v_light_done),
                       ("sGP", gp_init)) if dc == 0 else ()))
        v_pe = sch.v("sPE")
        ac(lambda a, t=t: a.activation(
            out=VPH[:, 520 * t:520 * (t + 1)]
                .rearrange("p (h c) -> p h c", c=65)[:, :, 0:64],
            in_=PSF[:, 0:512], func=AF.Copy, scale=1.0),
           inc="sAC", wait=W(("sPE", v_pe)))
    v_vph = sch.v("sAC")

    hv_div = sch.v("sDV")
    cte_prev2 = sch.v("sAC")
    for h in range(H2):
        Kh2 = KVHT[(h % 2) * 64:(h % 2) * 64 + 64,
                   2048 * (h // 2):2048 * (h // 2 + 1)]
        Qh2 = QHT[(h % 2) * 64:(h % 2) * 64 + 64,
                  2048 * (h // 2):2048 * (h // 2 + 1)]
        for qh in range(2):
            exp_v = {}
            av_v = {}

            def emit_hav(kt, h=h):
                for j in range(2):
                    pe(lambda p, kt=kt, h=h, j=j: p.matmul(
                        PSF[0:65, 3072 + 512 * j:3072 + 512 * (j + 1)],
                        VPH[:, 520 * kt + 65 * h:520 * kt + 65 * (h + 1)],
                        PTE[:, 1024 * (kt % 2) + 512 * j:
                            1024 * (kt % 2) + 512 * (j + 1)],
                        start=(kt == 0), stop=(kt == 15)),
                       inc=("sPE" if j == 1 else None),
                       wait=(W(("sAC", max(exp_v[kt], cte_prev2)))
                             if j == 0 else ()))
                av_v[kt] = sch.v("sPE")

            for kt in range(16):
                pb = 1024 * (kt % 2)
                for j in range(2):
                    pe(lambda p, kt=kt, qh=qh, j=j, pb=pb, Kh2=Kh2, Qh2=Qh2:
                       p.matmul(
                           PSF[:, pb + 512 * j:pb + 512 * (j + 1)],
                           Kh2[:, 128 * kt:128 * (kt + 1)],
                           Qh2[:, 1024 * qh + 512 * j:
                               1024 * qh + 512 * (j + 1)],
                           start=True, stop=True),
                       inc=("sPE" if j == 1 else None),
                       wait=(W(("sAC", max(v_vph, exp_v.get(kt - 2, 0))))
                             if j == 0 else ()))
                v_qk = sch.v("sPE")
                ac(lambda a, kt=kt, pb=pb: a.activation(
                    out=PTE[:, 1024 * (kt % 2):1024 * (kt % 2) + 1024],
                    in_=PSF[:, pb:pb + 1024], func=AF.Exp, scale=SC),
                   inc="sAC",
                   wait=W(("sPE", max(v_qk, av_v.get(kt - 2, 0)))))
                exp_v[kt] = sch.v("sAC")
                if kt >= 1:
                    emit_hav(kt - 1)
            emit_hav(15)
            v_av = sch.v("sPE")
            ac(lambda a: a.activation(out=CTE[0:65, :],
                                      in_=PSF[0:65, 3072:4096],
                                      func=AF.Copy, scale=1.0),
               inc="sAC", wait=W(("sPE", v_av), ("sDV", hv_div)))
            cte_prev2 = sch.v("sAC")
            dv(lambda v: v.reciprocal(out=RZROW[:], in_=CTE[64:65, :]),
               inc="sDV", wait=W(("sAC", sch.v("sAC")),
                                 ("sDV", sch.v("sDV"))))
            dv(lambda v: v.tensor_copy(out=RZROWB[:], in_=RZROW[:]),
               inc="sDV", wait=W(("sDV", sch.v("sDV"))))
            v_rz = sch.v("sDV")
            pe_mm(PSF[0:64, 2048:3072], ONE1B[0:1, 0:64], RZROWB[:],
                  start=True, stop=True, n_free=1024, inc="sPE",
                  wait=W(("sDV", v_rz)))
            v_rb = sch.v("sPE")
            dv(lambda v: v.tensor_copy(out=RCPS[:], in_=PSF[0:64, 2048:3072]),
               inc="sDV", wait=W(("sPE", v_rb), ("sDV", sch.v("sDV"))))
            v_rc = sch.v("sDV")
            dv(lambda v, h=h, qh=qh: v.tensor_tensor(
                out=CTXH[:, 2048 * (h // 2) + 1024 * qh:
                         2048 * (h // 2) + 1024 * (qh + 1)]
                    [(h % 2) * 64:(h % 2) * 64 + 64, :],
                in0=CTE[0:64, :], in1=RCPS[:], op=OP.mult),
               inc="sDV", wait=W(("sDV", v_rc)))
            hv_div = sch.v("sDV")

    load_weight_direct("wol", 1024, rows=512, woff=0,
                       wait_pe=sch.v("sPE"), dv_guard=sch.v("sDV"))
    v_wol = load_weight_direct("woh", 1024, rows=512, woff=4096)
    st_done = {0: 0, 1: 0}
    for t in range(TT):
        po = 2048 * (t % 2)
        so = 1024 * (t % 2)
        stx = "sSTa" if t % 2 == 0 else "sSTb"
        for dc in range(4):
            pe_mm(PSF[:, po:po + 1024],
                  CTXL[:, 2048 * dc + 128 * t:2048 * dc + 128 * (t + 1)],
                  WTMP[:, 1024 * dc:1024 * (dc + 1)],
                  start=(dc == 0), stop=(dc == 3), n_free=1024,
                  wait=(W(v_wol, ("sDV", max(hv_div, sch.v("sDV"))))
                        if dc == 0 else ()))
        for dc in range(4):
            pe_mm(PSF[:, po + 1024:po + 2048],
                  CTXH[:, 2048 * dc + 128 * t:2048 * dc + 128 * (t + 1)],
                  WTMP[:, 4096 + 1024 * dc:4096 + 1024 * (dc + 1)],
                  start=(dc == 0), stop=(dc == 3), n_free=1024,
                  inc=("sPE" if dc == 3 else None))
        v_pe = sch.v("sPE")
        dv(lambda v, t=t, po=po, so=so: v.scalar_tensor_tensor(
            out=OUTF[:, so:so + 1024], in0=PSF[:, po + 1024:po + 2048],
            scalar=SELCOL[:, t:t + 1], in1=XB[:, D * t:D * (t + 1)],
            op0=OP.mult, op1=OP.add),
           inc="sDV",
           wait=W(("sPE", v_pe), ("sDV", sch.v("sDV")), ("sDV", v_sel),
                  (stx, st_done[t % 2])))
        dv(lambda v, po=po, so=so: v.tensor_tensor(
            out=OUTF[:, so:so + 1024], in0=PSF[:, po:po + 1024],
            in1=OUTF[:, so:so + 1024], op=OP.add),
           inc="sDV", wait=W(("sDV", sch.v("sDV"))))
        v_of = sch.v("sDV")
        sp(lambda s, t=t, so=so: s.dma_start(
            out=out_t[128 * t:128 * (t + 1), :], in_=OUTF[:, so:so + 1024]),
           inc=stx, n=16, wait=W(("sDV", v_of)))
        st_done[t % 2] = sch.v(stx)
    if upto == 'fulldbg':
        v_fin = sch.v("sDV")
        for a in range(8):
            gp(lambda g, a=a: g.dma_start(
                out=dbg_t[128 * a:128 * (a + 1), :],
                in_=CTXL[:, 1024 * a:1024 * (a + 1)]),
               inc="sGW", n=16, wait=W(("sDV", v_fin), ("sGW", sch.v("sGW"))))
        for a in range(8):
            gp(lambda g, a=a: g.dma_start(
                out=dbg_t[1024 + 128 * a:1024 + 128 * (a + 1), :],
                in_=CTXH[:, 1024 * a:1024 * (a + 1)]),
               inc="sGW", n=16, wait=W(("sGW", sch.v("sGW"))))
        for src_ap, c0 in ((SELCOL, 0), (UCOL, 16), (CCOL, 32), (SCOL, 48)):
            gp(lambda g, src_ap=src_ap, c0=c0: g.dma_start(
                out=dbg_t[2048:2176, c0:c0 + 16], in_=src_ap[:, :],
                allow_slow_non_contiguous=True),
               inc="sGW", n=16, wait=W(("sGW", sch.v("sGW"))))
        for j, a in enumerate((6, 7)):
            gp(lambda g, j=j, a=a: g.dma_start(
                out=dbg_t[2176 + 128 * j:2176 + 128 * (j + 1), :],
                in_=XB[:, 1024 * a:1024 * (a + 1)]),
               inc="sGW", n=16, wait=W(("sGW", sch.v("sGW"))))
        sch.waitq("sp", ("sSTa", sch.v("sSTa")), ("sSTb", sch.v("sSTb")),
                  ("sGW", sch.v("sGW")))
    else:
        sch.waitq("sp", ("sSTa", sch.v("sSTa")), ("sSTb", sch.v("sSTb")))

    with nc.Block() as block:
        @block.sync
        def _(h):
            for f in sch.q["sp"]:
                f(h)

        @block.gpsimd
        def _(h):
            for f in sch.q["gp"]:
                f(h)

        @block.vector
        def _(h):
            for f in sch.q["dv"]:
                f(h)

        @block.scalar
        def _(h):
            for f in sch.q["ac"]:
                f(h)

        @block.tensor
        def _(h):
            for f in sch.q["pe"]:
                f(h)

    es.close()
    return nc


_NC_CACHE = [None]


def kernel(L0_attn_x, L0_attn_Wq, L0_attn_bq, L0_attn_Wk, L0_attn_bk,
           L0_attn_Wv, L0_attn_bv, L0_attn_Wo, L0_attn_bo,
           L0_ffn_gamma_light, L0_ffn_wq_light, L0_ffn_wkv_light,
           L0_ffn_wo_light, L0_ffn_q_routing_token, L0_ffn_kv_routing_token,
           L0_ffn_gamma_heavy, L0_ffn_wq_heavy, L0_ffn_wkv_heavy,
           L0_ffn_wo_heavy):
    x = np.ascontiguousarray(np.asarray(L0_attn_x, dtype=np.float32))
    f = lambda a: np.ascontiguousarray(np.asarray(a, dtype=np.float32))
    row = lambda a: np.ascontiguousarray(
        np.asarray(a, dtype=np.float32).reshape(1, -1))
    base = {
        "wq_t": np.ascontiguousarray(f(L0_attn_Wq).T),
        "wk_t": np.ascontiguousarray(f(L0_attn_Wk).T),
        "wv_t": np.ascontiguousarray(f(L0_attn_Wv).T),
        "wo_t": np.ascontiguousarray(f(L0_attn_Wo).T),
        "wkvl": f(L0_ffn_wkv_light),
        "wkvh_t": np.ascontiguousarray(f(L0_ffn_wkv_heavy).T),
        "wql": f(L0_ffn_wq_light), "wqh": f(L0_ffn_wq_heavy),
        "wol": f(L0_ffn_wo_light), "woh": f(L0_ffn_wo_heavy),
        "bq": row(L0_attn_bq), "bk": row(L0_attn_bk), "bv": row(L0_attn_bv),
        "bo": row(L0_attn_bo), "g_l": row(L0_ffn_gamma_light),
        "g_h": row(L0_ffn_gamma_heavy), "rq": row(L0_ffn_q_routing_token),
    }
    if _NC_CACHE[0] is None:
        _NC_CACHE[0] = build_program()
    nc = _NC_CACHE[0]
    in_maps = [dict(base, x=np.ascontiguousarray(x[b])) for b in range(8)]
    res = run_bass_kernel_spmd(nc, in_maps, core_ids=list(range(8)))
    return np.stack([np.asarray(r["out"], dtype=np.float32)
                     for r in res.results], axis=0)
